# version 02
# baseline (speedup 1.0000x reference)
import math
import numpy as np
from contextlib import ExitStack

import concourse.bass as bass
import concourse.mybir as mybir
from concourse.bass_utils import run_bass_kernel_spmd

F32 = mybir.dt.float32
BF16 = mybir.dt.bfloat16
I32 = mybir.dt.int32
AF = mybir.ActivationFunctionType
ALU = mybir.AluOpType
AX = mybir.AxisListType
f32 = np.float32

NCORES = 8
T = 2048
DM = 1024

ENGINES = ("tensor", "vector", "scalar", "gpsimd", "sync")
DMA_SEMS_PER_QUEUE = 12
DMA_QUEUES = tuple(f"q{e}{i}" for e in ENGINES for i in range(DMA_SEMS_PER_QUEUE))


def _ap_range(ap):
    off = ap.offset
    if not isinstance(off, (int, np.integer)):
        return ap.tensor.name, 0, 1 << 60
    lo = hi = int(off)
    for step, count in ap.ap:
        d = int(step) * (int(count) - 1)
        if d >= 0:
            hi += d
        else:
            lo += d
    return ap.tensor.name, lo, hi


class Prog:
    def __init__(self, nc, es):
        self.nc = nc
        self.sems = {}
        for name in ENGINES + DMA_QUEUES:
            self.sems[name] = es.enter_context(nc.semaphore("as_" + name))
        self.count = {name: 0 for name in ENGINES + DMA_QUEUES}
        self.guar = {e: {s: 0 for s in ENGINES + DMA_QUEUES} for e in ENGINES}
        self.acc = {}
        self.eng = {e: getattr(nc, e) for e in ENGINES}
        self.rot = {e: 0 for e in ENGINES}
        self.n_waits = 0
        self.n_ops = 0

    def _needed(self, reads, check_writes):
        need = {}
        def add(semname, cnt):
            if cnt > need.get(semname, 0):
                need[semname] = cnt
        for ap in reads:
            name, lo, hi = _ap_range(ap)
            for rec in self.acc.get(name, ()):
                if rec[2] and not (rec[1] < lo or rec[0] > hi):
                    add(rec[3], rec[4])
        for ap in check_writes:
            name, lo, hi = _ap_range(ap)
            for rec in self.acc.get(name, ()):
                if not (rec[1] < lo or rec[0] > hi):
                    add(rec[3], rec[4])
        return need

    def _record(self, reads, writes, semname, cnt):
        for is_w, aps in ((False, reads), (True, writes)):
            for ap in aps:
                name, lo, hi = _ap_range(ap)
                lst = self.acc.setdefault(name, [])
                if is_w:
                    lst[:] = [r for r in lst if r[0] < lo or r[1] > hi]
                lst.append([lo, hi, is_w, semname, cnt])
                if len(lst) > 64:
                    old, keep = lst[:-40], lst[-40:]
                    blo = min(r[0] for r in old)
                    bhi = max(r[1] for r in old)
                    sems = {}
                    for r in old:
                        sems[r[3]] = max(sems.get(r[3], 0), r[4])
                    lst[:] = [[blo, bhi, True, s, c] for s, c in sems.items()] + keep

    def _wait(self, engine, need):
        g = self.guar[engine]
        e = self.eng[engine]
        for semname, cnt in sorted(need.items()):
            if g[semname] < cnt:
                if semname == engine and cnt > self.count[semname]:
                    raise RuntimeError(
                        f"deadlock: {engine} waits own sem {cnt} > {self.count[semname]}")
                e.wait_ge(self.sems[semname], cnt)
                g[semname] = cnt
                self.n_waits += 1

    def op(self, engine, f, reads, writes, check_writes=None, inc=True):
        if check_writes is None:
            check_writes = writes
        need = self._needed(reads, check_writes)
        self._wait(engine, need)
        inst = f(self.eng[engine])
        self.n_ops += 1
        if inc:
            self.count[engine] += 1
            inst.then_inc(self.sems[engine], 1)
            self._record(reads, writes, engine, self.count[engine])
        else:
            self._record(reads, writes, engine, self.count[engine] + 1)
        return inst

    def dma(self, engine, out, in_, **kwargs):
        need = self._needed([in_], [out])
        q = f"q{engine}{self.rot[engine]}"
        self.rot[engine] = (self.rot[engine] + 1) % DMA_SEMS_PER_QUEUE
        if self.count[q] > 0:
            need[q] = max(need.get(q, 0), self.count[q])
        self._wait(engine, need)
        inst = self.eng[engine].dma_start(out=out, in_=in_, **kwargs)
        self.count[q] += 16
        inst.then_inc(self.sems[q], 16)
        self.n_ops += 1
        self._record([in_], [out], q, self.count[q])
        return inst

    def idma(self, engine, out, out_off, in_, in_off, extra_reads=()):
        need = self._needed([in_] + list(extra_reads), [out])
        q = f"q{engine}{self.rot[engine]}"
        self.rot[engine] = (self.rot[engine] + 1) % DMA_SEMS_PER_QUEUE
        if self.count[q] > 0:
            need[q] = max(need.get(q, 0), self.count[q])
        self._wait(engine, need)
        inst = self.eng[engine].indirect_dma_start(out, out_off, in_, in_off)
        self.count[q] += 16
        inst.then_inc(self.sems[q], 16)
        self.n_ops += 1
        self._record([in_], [out], q, self.count[q])
        return inst

    def barrier_to(self, engine):
        need = {s: c for s, c in self.count.items() if c > 0}
        self._wait(engine, need)


def ceil_div(a, b):
    return (a + b - 1) // b


def blocks512(total):
    out = list(range(0, total - 511, 512))
    covered = out[-1] + 512 if out else 0
    if covered < total:
        out.append(total - 512)
    return out


class Emitter:

    def __init__(self, nc, es, pr, n_psum=8):
        self.nc = nc
        self.es = es
        self.pr = pr
        ec = es.enter_context
        self.ones_col = ec(nc.sbuf_tensor("em_ones_col", [128, 1], F32))
        self.ones_row = ec(nc.sbuf_tensor("em_ones_row", [1, 128], F32))
        self.half_row = ec(nc.sbuf_tensor("em_half_row", [1, 128], F32))
        pr.op("vector", lambda v: v.memset(self.ones_col[:, :], 1.0), [], [self.ones_col[:, :]])
        pr.op("vector", lambda v: v.memset(self.ones_row[:, :], 1.0), [], [self.ones_row[:, :]])
        pr.op("vector", lambda v: v.memset(self.half_row[:, :], 0.5), [], [self.half_row[:, :]])
        self.psum = [ec(nc.psum_tensor(f"em_ps{i}", [128, 512], F32)) for i in range(n_psum)]
        self._ps_i = 0

    def take_psum(self):
        p = self.psum[self._ps_i]
        self._ps_i = (self._ps_i + 1) % len(self.psum)
        return p

    def mm_group(self, pairs, out_ap, out_reads_extra=()):
        pr = self.pr
        n = len(pairs)
        for i, (lt, rh) in enumerate(pairs):
            first = i == 0
            last = i == n - 1
            pr.op("tensor",
                  lambda t, lt=lt, rh=rh, first=first, last=last:
                      t.matmul(out_ap, lt, rh, start=first, stop=last),
                  [lt, rh] + (list(out_reads_extra) if first else []),
                  [out_ap],
                  check_writes=[out_ap] if first else [])

    def bcast_row(self, out_sb, row_ap, ncols, lhs=None):
        pr = self.pr
        lhs = lhs if lhs is not None else self.ones_row
        for c0 in range(0, ncols, 512):
            w = min(512, ncols - c0)
            ps = self.take_psum()
            pr.op("tensor",
                  lambda t, c0=c0, w=w, ps=ps: t.matmul(
                      ps[:, 0:w], lhs[:, :], row_ap[0:1, c0:c0 + w],
                      start=True, stop=True),
                  [lhs[:, :], row_ap[0:1, c0:c0 + w]], [ps[:, 0:w]])
            pr.op("vector",
                  lambda v, c0=c0, w=w, ps=ps: v.tensor_copy(
                      out_sb[:, c0:c0 + w], ps[:, 0:w]),
                  [ps[:, 0:w]], [out_sb[:, c0:c0 + w]])


def _np_f32(x):
    return np.ascontiguousarray(np.asarray(x, dtype=np.float32))


def mamba_A_constants():
    n = np.arange(1, 17, dtype=np.float32)
    return [-float(np.exp(np.log(v).astype(np.float32)).astype(np.float32)) for v in n]


def prep_phase1(inputs):
    x = _np_f32(inputs["L0_attn_x"])
    W_in = _np_f32(inputs["L0_attn_W_in"])
    conv_w = _np_f32(inputs["L0_attn_conv_w"])
    conv_b = _np_f32(inputs["L0_attn_conv_b"])
    W_x = _np_f32(inputs["L0_attn_W_x"])
    W_dt = _np_f32(inputs["L0_attn_W_dt"])
    b_dt = _np_f32(inputs["L0_attn_b_dt"])
    Dv = _np_f32(inputs["L0_attn_D"])
    W_out = _np_f32(inputs["L0_attn_W_out"])

    def cols16(v):
        return np.ascontiguousarray(v.reshape(16, 128).T)

    shared = {
        "W_in": W_in,
        "W_x": W_x,
        "W_dt": W_dt,
        "W_out": W_out,
        "conv_w": np.ascontiguousarray(conv_w.reshape(4, 16, 128).transpose(2, 1, 0).reshape(128, 64)),
        "conv_b": cols16(conv_b),
        "b_dt": cols16(b_dt),
        "Dv": cols16(Dv),
    }
    selcat = np.zeros((32, 32 * 128), np.float32)
    for n in range(16):
        selcat[n, n * 128:(n + 1) * 128] = 1.0
        selcat[16 + n, (16 + n) * 128:(17 + n) * 128] = 0.5
    shared["selcat"] = selcat
    per_core = []
    for c in range(NCORES):
        b, h = c // 2, c % 2
        t0 = h * T
        lo = t0 - 3
        if lo < 0:
            xs = np.concatenate(
                [np.zeros((3, DM), np.float32), x[b, 0:t0 + T]], axis=0)
        else:
            xs = x[b, lo:t0 + T]
        per_core.append({"x_t": np.ascontiguousarray(xs.T)})
    return shared, per_core


AN = mamba_A_constants()
LNK = 27.631021115928547


def emit_phase1(nc, pr, em, x_t, W_in, W_x, W_dt, W_out, conv_w, conv_b, b_dt,
                Dv, carry, sext, selcat, o_x1, o_hfin, o_sumd,
                sp_xi, sp_delta, sp_res, sp_C, sp_bmcm,
                chunk=512, dgroup=4, carry_sel=None, do_passA=True, do_y=True,
                pfx="", halo_off=3):
    TH = T + halo_off
    NCH = ceil_div(T, chunk)
    with ExitStack() as es:
        ec = es.enter_context

        convb = ec(nc.sbuf_tensor(pfx + "convb", [128, 16], F32))
        bdt = ec(nc.sbuf_tensor(pfx + "bdt", [128, 16], F32))
        dvt = ec(nc.sbuf_tensor(pfx + "dvt", [128, 16], F32))
        carr = ec(nc.sbuf_tensor(pfx + "carr", [128, 256], F32))
        sxt = ec(nc.sbuf_tensor(pfx + "sxt", [128, 16], F32))
        convw = ec(nc.sbuf_tensor(pfx + "convw", [128, 64], F32))
        ccar = ec(nc.sbuf_tensor(pfx + "ccar", [128, 16], F32))
        pr.dma("sync", convw[:, :], conv_w[:, :])
        pr.dma("sync", convb[:, :], conv_b[:, :])
        pr.dma("sync", bdt[:, :], b_dt[:, :])
        pr.dma("sync", dvt[:, :], Dv[:, :])
        pr.dma("sync", carr[:, :], carry[:, :])
        pr.dma("sync", sxt[:, :], sext[:, :])
        if carry_sel is not None:
            selc_t = ec(nc.sbuf_tensor(pfx + "selc_t", [128, 2], F32))
            pr.dma("sync", selc_t[:, :], carry_sel[:, :])
            pr.op("vector", lambda v: v.tensor_scalar(carr[:, :], carr[:, :],
                                                      selc_t[:, 0:1], None, ALU.mult),
                  [carr[:, :], selc_t[:, 0:1]], [carr[:, :]])
            pr.op("vector", lambda v: v.tensor_scalar(sxt[:, :], sxt[:, :],
                                                      selc_t[:, 1:2], None, ALU.mult),
                  [sxt[:, :], selc_t[:, 1:2]], [sxt[:, :]])
        if do_passA:
            pr.op("vector", lambda v: v.memset(ccar[:, :], 0.0), [], [ccar[:, :]])
        else:
            pr.dma("sync", ccar[:, :], o_sumd[:, :])

        if do_passA:
            CH = chunk
            with ExitStack() as esa:
                eca = esa.enter_context
                w_in_bf = eca(nc.sbuf_tensor(pfx + "w_in_bf", [128, 8 * 4096], BF16))
                w_x_bf = eca(nc.sbuf_tensor(pfx + "w_x_bf", [128, 16 * 96], BF16))
                w_dt_bf = eca(nc.sbuf_tensor(pfx + "w_dt_bf", [64, 2048], BF16))
                xTbf = eca(nc.sbuf_tensor(pfx + "xTbf", [128, 8 * TH], BF16))
                xld = eca(nc.sbuf_tensor(pfx + "xld", [128, TH], F32))
                xr = eca(nc.sbuf_tensor(pfx + "xr", [128, CH + 3], F32))
                xi_f = eca(nc.sbuf_tensor(pfx + "xi_f", [128, CH], F32))
                xi_bf = eca(nc.sbuf_tensor(pfx + "xi_bf", [128, 16 * CH], BF16))
                resl = eca(nc.sbuf_tensor(pfx + "resl", [128, CH], F32))
                xdbl = eca(nc.sbuf_tensor(pfx + "xdbl", [96, CH], F32))
                dlin_bf = eca(nc.sbuf_tensor(pfx + "dlin_bf", [64, CH], BF16))
                dpre = eca(nc.sbuf_tensor(pfx + "dpre", [128, CH], F32))
                spl1 = eca(nc.sbuf_tensor(pfx + "spl1", [128, CH], F32))
                spl2 = eca(nc.sbuf_tensor(pfx + "spl2", [128, CH], F32))
                Cch = eca(nc.sbuf_tensor(pfx + "Cch", [128, CH], F32))
                onesC = eca(nc.sbuf_tensor(pfx + "onesC", [128, CH], F32))
                pr.op("vector", lambda v: v.memset(onesC[:, :], 1.0), [], [onesC[:, :]])

                for k in range(8):
                    pr.dma("gpsimd", w_in_bf[:, k * 4096:(k + 1) * 4096], W_in[k * 128:(k + 1) * 128, :])
                for k in range(16):
                    pr.dma("gpsimd", w_x_bf[:, k * 96:(k + 1) * 96], W_x[k * 128:(k + 1) * 128, :])
                pr.dma("gpsimd", w_dt_bf[:, :], W_dt[:, :])
                for k in range(8):
                    pr.dma("sync", xld[:, :], x_t[k * 128:(k + 1) * 128, :])
                    pr.op("vector",
                          lambda v, k=k: v.tensor_copy(xTbf[:, k * TH:(k + 1) * TH], xld[:, :]),
                          [xld[:, :]], [xTbf[:, k * TH:(k + 1) * TH]])

                for lc in range(NCH):
                    c0 = lc * CH
                    hw = CH + 3
                    for dt in range(16):
                        for s0 in range(0, hw, 512):
                            w = min(512, hw - s0)
                            ps = em.take_psum()
                            em.mm_group(
                                [(w_in_bf[:, k * 4096 + dt * 128: k * 4096 + dt * 128 + 128],
                                  xTbf[:, k * TH + c0 + s0: k * TH + c0 + s0 + w])
                                 for k in range(8)],
                                ps[:, 0:w])
                            pr.op("vector",
                                  lambda v, s0=s0, w=w, ps=ps: v.tensor_copy(xr[:, s0:s0 + w], ps[:, 0:w]),
                                  [ps[:, 0:w]], [xr[:, s0:s0 + w]])
                        acc = xi_f[:, 0:CH]
                        pr.op("vector",
                              lambda v, dt=dt, acc=acc: v.tensor_scalar(
                                  acc, xr[:, 0:CH], convw[:, dt * 4:dt * 4 + 1], None, ALU.mult),
                              [xr[:, 0:CH], convw[:, dt * 4:dt * 4 + 1]], [acc])
                        for tap in range(1, 4):
                            pr.op("vector",
                                  lambda v, dt=dt, tap=tap, acc=acc: v.scalar_tensor_tensor(
                                      acc, xr[:, tap:tap + CH],
                                      convw[:, dt * 4 + tap:dt * 4 + tap + 1], acc, ALU.mult, ALU.add),
                                  [xr[:, tap:tap + CH], acc,
                                   convw[:, dt * 4 + tap:dt * 4 + tap + 1]], [acc])
                        pr.op("vector",
                              lambda v, dt=dt, acc=acc: v.tensor_scalar(
                                  acc, acc, convb[:, dt:dt + 1], None, ALU.add),
                              [acc, convb[:, dt:dt + 1]], [acc])
                        pr.op("scalar", lambda s, acc=acc: s.activation(acc, acc, AF.Silu),
                              [acc], [acc])
                        pr.op("vector",
                              lambda v, dt=dt, acc=acc: v.tensor_copy(xi_bf[:, dt * CH:(dt + 1) * CH], acc),
                              [acc], [xi_bf[:, dt * CH:(dt + 1) * CH]])
                        pr.dma("gpsimd", sp_xi[dt * 128:(dt + 1) * 128, c0:c0 + CH], acc)
                    for of in range(16):
                        for s0 in range(0, CH, 512):
                            w = min(512, CH - s0)
                            ps = em.take_psum()
                            em.mm_group(
                                [(w_in_bf[:, k * 4096 + 2048 + of * 128: k * 4096 + 2048 + of * 128 + 128],
                                  xTbf[:, k * TH + c0 + 3 + s0: k * TH + c0 + 3 + s0 + w])
                                 for k in range(8)],
                                ps[:, 0:w])
                            pr.op("vector",
                                  lambda v, s0=s0, w=w, ps=ps: v.tensor_copy(resl[:, s0:s0 + w], ps[:, 0:w]),
                                  [ps[:, 0:w]], [resl[:, s0:s0 + w]])
                        pr.dma("gpsimd", sp_res[of * 128:(of + 1) * 128, c0:c0 + CH], resl[:, 0:CH])
                    for s0 in range(0, CH, 512):
                        w = min(512, CH - s0)
                        ps = em.take_psum()
                        em.mm_group(
                            [(w_x_bf[:, k * 96:(k + 1) * 96], xi_bf[:, k * CH + s0:k * CH + s0 + w])
                             for k in range(16)],
                            ps[0:96, 0:w])
                        pr.op("vector",
                              lambda v, s0=s0, w=w, ps=ps: v.tensor_copy(xdbl[:, s0:s0 + w], ps[0:96, 0:w]),
                              [ps[0:96, 0:w]], [xdbl[:, s0:s0 + w]])
                    pr.dma("gpsimd", sp_bmcm[:, c0:c0 + CH], xdbl[64:96, 0:CH])
                    pr.op("vector",
                          lambda v: v.tensor_copy(dlin_bf[:, 0:CH], xdbl[0:64, 0:CH]),
                          [xdbl[0:64, 0:CH]], [dlin_bf[:, 0:CH]])
                    for dt in range(16):
                        for s0 in range(0, CH, 512):
                            w = min(512, CH - s0)
                            ps = em.take_psum()
                            em.mm_group(
                                [(w_dt_bf[:, dt * 128:(dt + 1) * 128], dlin_bf[:, s0:s0 + w])],
                                ps[:, 0:w])
                            pr.op("vector",
                                  lambda v, s0=s0, w=w, dt=dt, ps=ps:
                                  v.tensor_scalar(dpre[:, s0:s0 + w], ps[:, 0:w],
                                                  bdt[:, dt:dt + 1], None, ALU.add),
                                  [ps[:, 0:w], bdt[:, dt:dt + 1]], [dpre[:, s0:s0 + w]])
                        pr.op("scalar", lambda s: s.activation(spl1[:, 0:CH], dpre[:, 0:CH], AF.Abs),
                              [dpre[:, 0:CH]], [spl1[:, 0:CH]])
                        pr.op("scalar", lambda s: s.activation(spl1[:, 0:CH], spl1[:, 0:CH], AF.Exp, scale=f32(-1.0)),
                              [spl1[:, 0:CH]], [spl1[:, 0:CH]])
                        pr.op("scalar", lambda s: s.activation(spl1[:, 0:CH], spl1[:, 0:CH], AF.Ln, bias=f32(1.0)),
                              [spl1[:, 0:CH]], [spl1[:, 0:CH]])
                        pr.op("scalar", lambda s: s.activation(spl2[:, 0:CH], dpre[:, 0:CH], AF.Relu),
                              [dpre[:, 0:CH]], [spl2[:, 0:CH]])
                        pr.op("vector", lambda v: v.tensor_add(dpre[:, 0:CH], spl1[:, 0:CH], spl2[:, 0:CH]),
                              [spl1[:, 0:CH], spl2[:, 0:CH]], [dpre[:, 0:CH]])
                        pr.dma("gpsimd", sp_delta[dt * 128:(dt + 1) * 128, c0:c0 + CH], dpre[:, 0:CH])
                        pr.op("vector",
                              lambda v, dt=dt: v.tensor_tensor_scan(
                                  Cch[:, 0:CH], onesC[:, 0:CH], dpre[:, 0:CH],
                                  ccar[:, dt:dt + 1], ALU.mult, ALU.add),
                              [onesC[:, 0:CH], dpre[:, 0:CH], ccar[:, dt:dt + 1]], [Cch[:, 0:CH]])
                        pr.op("vector",
                              lambda v, dt=dt: v.tensor_copy(ccar[:, dt:dt + 1], Cch[:, CH - 1:CH]),
                              [Cch[:, CH - 1:CH]], [ccar[:, dt:dt + 1]])
                        pr.dma("gpsimd", sp_C[dt * 128:(dt + 1) * 128, c0:c0 + CH], Cch[:, 0:CH])
                for e in ENGINES:
                    pr.barrier_to(e)


        if do_passA:
            pr.dma("gpsimd", o_sumd[:, :], ccar[:, :])

        SC = chunk
        NSC = ceil_div(T, SC)
        w_out_bf = ec(nc.sbuf_tensor(pfx + "w_out_bf", [128, 16 * DM], BF16))
        selc = ec(nc.sbuf_tensor(pfx + "selc", [32, 32 * 128], F32))
        bmrow = ec(nc.sbuf_tensor(pfx + "bmrow", [32, T], F32))
        del_g = ec(nc.sbuf_tensor(pfx + "del_g", [128, dgroup * SC], F32))
        C_g = ec(nc.sbuf_tensor(pfx + "C_g", [128, dgroup * SC], F32))
        du_g = ec(nc.sbuf_tensor(pfx + "du_g", [128, dgroup * SC], F32))
        xi_g = ec(nc.sbuf_tensor(pfx + "xi_g", [128, dgroup * SC], F32))
        res_g = ec(nc.sbuf_tensor(pfx + "res_g", [128, dgroup * SC], F32))
        T_g = ec(nc.sbuf_tensor(pfx + "T_g", [128, dgroup * SC], F32))
        rho = ec(nc.sbuf_tensor(pfx + "rho", [128, SC], F32))
        En = ec(nc.sbuf_tensor(pfx + "En", [128, SC], F32))
        gq = ec(nc.sbuf_tensor(pfx + "gq", [128, SC], F32))
        gC = ec(nc.sbuf_tensor(pfx + "gC", [128, SC], F32))
        duB = ec(nc.sbuf_tensor(pfx + "duB", [128, SC], F32))
        hsc = ec(nc.sbuf_tensor(pfx + "hsc", [128, SC], F32))
        tsc = ec(nc.sbuf_tensor(pfx + "tsc", [128, SC], F32))
        y_g = ec(nc.sbuf_tensor(pfx + "y_g", [128, dgroup * SC], F32))
        yg_bf = ec(nc.sbuf_tensor(pfx + "yg_bf", [128, 16 * SC], BF16))
        bmb = ec(nc.sbuf_tensor(pfx + "bmb", [128, SC], F32))
        cmb = ec(nc.sbuf_tensor(pfx + "cmb", [128, SC], F32))
        hcar = ec(nc.sbuf_tensor(pfx + "hcar", [128, 256], F32))
        ctot = ec(nc.sbuf_tensor(pfx + "ctot", [128, 16], F32))
        x1pre = ec(nc.sbuf_tensor(pfx + "x1pre", [128, 8 * SC], F32))
        x1sq = ec(nc.sbuf_tensor(pfx + "x1sq", [128, 2 * 512], F32))
        strow = ec(nc.sbuf_tensor(pfx + "strow", [1, 4 * 512], F32))
        mub = ec(nc.sbuf_tensor(pfx + "mub", [128, SC], F32))
        rstdb = ec(nc.sbuf_tensor(pfx + "rstdb", [128, SC], F32))
        xres = ec(nc.sbuf_tensor(pfx + "xres", [128, SC], F32))

        for k in range(16):
            pr.dma("gpsimd", w_out_bf[:, k * DM:(k + 1) * DM], W_out[k * 128:(k + 1) * 128, :])
        pr.dma("sync", bmrow[:, :], sp_bmcm[:, :])
        pr.dma("sync", selc[:, :], selcat[:, :])
        pr.op("vector", lambda v: v.tensor_copy(hcar[:, :], carr[:, :]), [carr[:, :]], [hcar[:, :]])
        pr.op("vector", lambda v: v.tensor_add(ctot[:, :], ccar[:, :], sxt[:, :]),
              [ccar[:, :], sxt[:, :]], [ctot[:, :]])

        for sc in range(NSC):
            c0 = sc * SC
            for dgi in range(16 // dgroup):
                dts = [dgi * dgroup + j for j in range(dgroup)]
                for j, dt in enumerate(dts):
                    pr.dma("sync", del_g[:, j * SC:(j + 1) * SC], sp_delta[dt * 128:(dt + 1) * 128, c0:c0 + SC])
                    if do_y:
                        pr.dma("scalar", C_g[:, j * SC:(j + 1) * SC], sp_C[dt * 128:(dt + 1) * 128, c0:c0 + SC])
                    pr.dma("gpsimd", xi_g[:, j * SC:(j + 1) * SC], sp_xi[dt * 128:(dt + 1) * 128, c0:c0 + SC])
                    pr.op("vector",
                          lambda v, j=j: v.tensor_mul(du_g[:, j * SC:(j + 1) * SC],
                                                      del_g[:, j * SC:(j + 1) * SC],
                                                      xi_g[:, j * SC:(j + 1) * SC]),
                          [del_g[:, j * SC:(j + 1) * SC], xi_g[:, j * SC:(j + 1) * SC]],
                          [du_g[:, j * SC:(j + 1) * SC]])
                    if do_y:
                        pr.op("vector",
                              lambda v, j=j:
                              v.tensor_scalar(T_g[:, j * SC:(j + 1) * SC],
                                              C_g[:, j * SC:(j + 1) * SC],
                                              -1.0, None, ALU.mult),
                              [C_g[:, j * SC:(j + 1) * SC]], [T_g[:, j * SC:(j + 1) * SC]])
                        pr.op("vector",
                              lambda v, j=j, dt=dt: v.tensor_scalar(
                                  T_g[:, j * SC:(j + 1) * SC], T_g[:, j * SC:(j + 1) * SC],
                                  ctot[:, dt:dt + 1], None, ALU.add),
                              [T_g[:, j * SC:(j + 1) * SC], ctot[:, dt:dt + 1]],
                              [T_g[:, j * SC:(j + 1) * SC]])
                for n in range(16):
                    an = AN[n]
                    for tgt, blkk in (((bmb, n), (cmb, 16 + n)) if do_y else ((bmb, n),)):
                        for s0 in range(0, SC, 512):
                            w = min(512, SC - s0)
                            ps = em.take_psum()
                            pr.op("tensor",
                                  lambda t, ps=ps, blkk=blkk, s0=s0, w=w: t.matmul(
                                      ps[:, 0:w], selc[:, blkk * 128:(blkk + 1) * 128],
                                      bmrow[0:32, c0 + s0:c0 + s0 + w],
                                      start=True, stop=True),
                                  [selc[:, blkk * 128:(blkk + 1) * 128],
                                   bmrow[0:32, c0 + s0:c0 + s0 + w]], [ps[:, 0:w]])
                            pr.op("vector",
                                  lambda v, tgt=tgt, s0=s0, w=w, ps=ps: v.tensor_copy(
                                      tgt[:, s0:s0 + w], ps[:, 0:w]),
                                  [ps[:, 0:w]], [tgt[:, s0:s0 + w]])
                    for j, dt in enumerate(dts):
                        dsl = del_g[:, j * SC:(j + 1) * SC]
                        Csl = C_g[:, j * SC:(j + 1) * SC]
                        Tsl = T_g[:, j * SC:(j + 1) * SC]
                        dusl = du_g[:, j * SC:(j + 1) * SC]
                        ysl = y_g[:, j * SC:(j + 1) * SC]
                        hc = hcar[:, dt * 16 + n: dt * 16 + n + 1]
                        pr.op("scalar", lambda s, dsl=dsl, an=an: s.activation(rho[:, :], dsl, AF.Exp, scale=f32(an)),
                              [dsl], [rho[:, :]])
                        pr.op("scalar", lambda s, Tsl=Tsl, an=an: s.activation(
                                  gq[:, :], Tsl, AF.Tanh, scale=f32(an * 0.5), bias=f32(LNK * 0.5)),
                              [Tsl], [gq[:, :]])
                        pr.op("vector", lambda v: v.scalar_tensor_tensor(
                                  gC[:, :], gq[:, :], 1.0, cmb[:, :], ALU.add, ALU.mult),
                              [gq[:, :], cmb[:, :]], [gC[:, :]])
                        pr.op("vector", lambda v, dusl=dusl: v.tensor_mul(duB[:, :], dusl, bmb[:, :]),
                              [dusl, bmb[:, :]], [duB[:, :]])
                        pr.op("vector", lambda v, hc=hc: v.tensor_tensor_scan(
                                  hsc[:, :], rho[:, :], duB[:, :], hc, ALU.mult, ALU.add),
                              [rho[:, :], duB[:, :], hc], [hsc[:, :]])
                        pr.op("vector", lambda v, hc=hc: v.tensor_copy(hc, hsc[:, SC - 1:SC]),
                              [hsc[:, SC - 1:SC]], [hc])
                        pr.op("vector", lambda v: v.tensor_mul(tsc[:, :], hsc[:, :], gC[:, :]),
                              [hsc[:, :], gC[:, :]], [tsc[:, :]])
                        if n == 0:
                            pr.op("vector", lambda v, ysl=ysl: v.tensor_copy(ysl, tsc[:, :]),
                                  [tsc[:, :]], [ysl])
                        else:
                            pr.op("vector", lambda v, ysl=ysl: v.tensor_add(ysl, ysl, tsc[:, :]),
                                  [tsc[:, :], ysl], [ysl])
                for j, dt in enumerate(dts) if do_y else ():
                    ysl = y_g[:, j * SC:(j + 1) * SC]
                    xisl = xi_g[:, j * SC:(j + 1) * SC]
                    rsl = res_g[:, j * SC:(j + 1) * SC]
                    pr.dma("scalar", rsl, sp_res[dt * 128:(dt + 1) * 128, c0:c0 + SC])
                    pr.op("vector", lambda v, ysl=ysl, xisl=xisl, dt=dt: v.scalar_tensor_tensor(
                              ysl, xisl, dvt[:, dt:dt + 1], ysl, ALU.mult, ALU.add),
                          [xisl, dvt[:, dt:dt + 1], ysl], [ysl])
                    pr.op("scalar", lambda s, rsl=rsl: s.activation(rsl, rsl, AF.Silu),
                          [rsl], [rsl])
                    pr.op("vector", lambda v, ysl=ysl, rsl=rsl, dt=dt: v.tensor_mul(
                              yg_bf[:, dt * SC:(dt + 1) * SC], ysl, rsl),
                          [ysl, rsl], [yg_bf[:, dt * SC:(dt + 1) * SC]])
            for of in range(8) if do_y else ():
                for s0 in range(0, SC, 512):
                    w = min(512, SC - s0)
                    ps = em.take_psum()
                    em.mm_group(
                        [(w_out_bf[:, k * DM + of * 128: k * DM + of * 128 + 128],
                          yg_bf[:, k * SC + s0:k * SC + s0 + w]) for k in range(16)],
                        ps[:, 0:w])
                    pr.dma("scalar", xres[:, s0:s0 + w], x_t[of * 128:(of + 1) * 128, 3 + c0 + s0: 3 + c0 + s0 + w])
                    pr.op("vector",
                          lambda v, of=of, s0=s0, w=w, ps=ps: v.tensor_add(
                              x1pre[:, of * SC + s0: of * SC + s0 + w], ps[:, 0:w],
                              xres[:, s0:s0 + w]),
                          [ps[:, 0:w], xres[:, s0:s0 + w]],
                          [x1pre[:, of * SC + s0: of * SC + s0 + w]])
            if do_y:
                emit_ln_feature(pr, em, x1pre, SC, strow, x1sq, mub, rstdb, nfeat=8)
                for of in range(8):
                    sl = x1pre[:, of * SC:(of + 1) * SC]
                    pr.op("vector", lambda v, sl=sl: v.tensor_sub(sl, sl, mub[:, 0:SC]),
                          [sl, mub[:, 0:SC]], [sl])
                    pr.op("vector", lambda v, sl=sl: v.tensor_mul(sl, sl, rstdb[:, 0:SC]),
                          [sl, rstdb[:, 0:SC]], [sl])
                    pr.dma("gpsimd", o_x1[of * 128:(of + 1) * 128, c0:c0 + SC], sl)

        pr.dma("gpsimd", o_hfin[:, :], hcar[:, :])
        for e in ENGINES:
            pr.barrier_to(e)
        print(f"phase1 program: {pr.n_ops} ops, {pr.n_waits} waits")
    return nc




def build_phase1(chunk=512, dgroup=4):
    nc = bass.Bass("TRN2", target_bir_lowering=False, debug=False)
    TH = T + 3
    x_t = nc.dram_tensor("x_t", [DM, TH], F32, kind="ExternalInput").ap()
    W_in = nc.dram_tensor("W_in", [DM, 4096], F32, kind="ExternalInput").ap()
    W_x = nc.dram_tensor("W_x", [2048, 96], F32, kind="ExternalInput").ap()
    W_dt = nc.dram_tensor("W_dt", [64, 2048], F32, kind="ExternalInput").ap()
    W_out = nc.dram_tensor("W_out", [2048, DM], F32, kind="ExternalInput").ap()
    conv_w = nc.dram_tensor("conv_w", [128, 64], F32, kind="ExternalInput").ap()
    conv_b = nc.dram_tensor("conv_b", [128, 16], F32, kind="ExternalInput").ap()
    b_dt = nc.dram_tensor("b_dt", [128, 16], F32, kind="ExternalInput").ap()
    Dv = nc.dram_tensor("Dv", [128, 16], F32, kind="ExternalInput").ap()
    carry = nc.dram_tensor("carry", [128, 256], F32, kind="ExternalInput").ap()
    sext = nc.dram_tensor("sext", [128, 16], F32, kind="ExternalInput").ap()
    selcat = nc.dram_tensor("selcat", [32, 32 * 128], F32, kind="ExternalInput").ap()

    o_x1 = nc.dram_tensor("x1_t", [DM, T], F32, kind="ExternalOutput").ap()
    o_hfin = nc.dram_tensor("hfin", [128, 256], F32, kind="ExternalOutput").ap()
    o_sumd = nc.dram_tensor("sumdelta", [128, 16], F32, kind="ExternalOutput").ap()

    sp_xi = nc.dram_tensor("sp_xi", [2048, T], F32).ap()
    sp_delta = nc.dram_tensor("sp_delta", [2048, T], F32).ap()
    sp_res = nc.dram_tensor("sp_res", [2048, T], F32).ap()
    sp_C = nc.dram_tensor("sp_C", [2048, T], F32).ap()
    sp_bmcm = nc.dram_tensor("sp_bmcm", [32, T], F32).ap()

    NCH = ceil_div(T, chunk)

    with ExitStack() as _es0:
        _ec0 = _es0.enter_context
        blk = _ec0(nc.Block())
        pr = Prog(nc, _es0)
        em = Emitter(nc, _es0, pr, n_psum=7)
        psbf = _ec0(nc.psum_tensor("psbf", [128, 1024], BF16))
        emit_phase1(nc, pr, em, x_t, W_in, W_x, W_dt, W_out, conv_w, conv_b, b_dt,
                    Dv, carry, sext, selcat, o_x1, o_hfin, o_sumd,
                    sp_xi, sp_delta, sp_res, sp_C, sp_bmcm, chunk=chunk, dgroup=dgroup)
    return nc


def emit_ln_feature(pr, em, xtiles, width, strow, sqtile, mub, rstdb, nfeat=8, eps=1e-5):
    nc = pr.nc
    dmf = float(nfeat * 128)
    for s0 in range(0, width, 512):
        w = min(512, width - s0)
        ps = em.take_psum()
        ps2 = em.take_psum()
        em.mm_group(
            [(em.ones_col[:, :], xtiles[:, of * width + s0: of * width + s0 + w])
             for of in range(nfeat)],
            ps[0:1, 0:w])
        for of in range(nfeat):
            sq = sqtile[:, (of % 2) * 512:(of % 2) * 512 + w]
            pr.op("scalar",
                  lambda s, of=of, s0=s0, w=w, sq=sq: s.activation(
                      sq, xtiles[:, of * width + s0: of * width + s0 + w], AF.Square),
                  [xtiles[:, of * width + s0: of * width + s0 + w]], [sq])
            pr.op("tensor",
                  lambda t, of=of, w=w, ps2=ps2, sq=sq: t.matmul(
                      ps2[0:1, 0:w], em.ones_col[:, :], sq,
                      start=(of == 0), stop=(of == nfeat - 1)),
                  [em.ones_col[:, :], sq],
                  [ps2[0:1, 0:w]],
                  check_writes=[ps2[0:1, 0:w]] if of == 0 else [])
        mu = strow[0:1, 0:w]
        msq = strow[0:1, 512:512 + w]
        var = strow[0:1, 1024:1024 + w]
        rstd = strow[0:1, 1536:1536 + w]
        pr.op("vector", lambda v, mu=mu, ps=ps, w=w: v.tensor_scalar_mul(mu, ps[0:1, 0:w], 1.0 / dmf),
              [ps[0:1, 0:w]], [mu])
        pr.op("vector", lambda v, msq=msq, ps2=ps2, w=w: v.tensor_scalar_mul(msq, ps2[0:1, 0:w], 1.0 / dmf),
              [ps2[0:1, 0:w]], [msq])
        pr.op("vector", lambda v, var=var, mu=mu: v.tensor_mul(var, mu, mu), [mu], [var])
        pr.op("vector", lambda v, var=var, msq=msq: v.tensor_sub(var, msq, var), [msq, var], [var])
        pr.op("scalar", lambda s, var=var: s.activation(var, var, AF.Ln, bias=f32(eps)),
              [var], [var])
        pr.op("scalar", lambda s, rstd=rstd, var=var: s.activation(rstd, var, AF.Exp, scale=f32(-0.5)),
              [var], [rstd])
        ps3 = em.take_psum()
        pr.op("tensor", lambda t, ps3=ps3, mu=mu, w=w: t.matmul(
                  ps3[:, 0:w], em.ones_row[:, :], mu, start=True, stop=True),
              [em.ones_row[:, :], mu], [ps3[:, 0:w]])
        pr.op("vector", lambda v, s0=s0, w=w, ps3=ps3: v.tensor_copy(mub[:, s0:s0 + w], ps3[:, 0:w]),
              [ps3[:, 0:w]], [mub[:, s0:s0 + w]])
        ps4 = em.take_psum()
        pr.op("tensor", lambda t, ps4=ps4, rstd=rstd, w=w: t.matmul(
                  ps4[:, 0:w], em.ones_row[:, :], rstd, start=True, stop=True),
              [em.ones_row[:, :], rstd], [ps4[:, 0:w]])
        pr.op("vector", lambda v, s0=s0, w=w, ps4=ps4: v.tensor_copy(rstdb[:, s0:s0 + w], ps4[:, 0:w]),
              [ps4[:, 0:w]], [rstdb[:, s0:s0 + w]])


HALO2 = 1026
QW = 512
HIDP = 2816


def prep_phase2(inputs, lay):
    wqkv = _np_f32(inputs["L0_ffn_wqkv"][lay])
    qs = _np_f32(inputs["L0_ffn_q_scale"][lay])
    ks = _np_f32(inputs["L0_ffn_k_scale"][lay])
    wo = _np_f32(inputs["L0_ffn_wo"][lay])
    lg = _np_f32(inputs["L0_ffn_ln_g"][lay])
    lb = _np_f32(inputs["L0_ffn_ln_b"][lay])
    w1 = _np_f32(inputs["L0_ffn_w1"][lay])
    b1 = _np_f32(inputs["L0_ffn_b1"][lay])
    w2 = _np_f32(inputs["L0_ffn_w2"][lay])
    b2 = _np_f32(inputs["L0_ffn_b2"][lay])

    w1p = np.zeros((1024, 2 * HIDP), np.float32)
    b1p = np.zeros((128, 2 * (HIDP // 128)), np.float32)
    for j in range(HIDP // 128):
        a0, a1 = j * 128, min((j + 1) * 128, 2730)
        if a0 < 2730:
            n = a1 - a0
            w1p[:, j * 256: j * 256 + n] = w1[:, a0:a1]
            w1p[:, j * 256 + 128: j * 256 + 128 + n] = w1[:, 2730 + a0:2730 + a1]
            b1p[0:n, 2 * j] = b1[a0:a1]
            b1p[0:n, 2 * j + 1] = b1[2730 + a0:2730 + a1]
    w2p = np.zeros((HIDP, 1024), np.float32)
    w2p[:2730] = w2

    def cols8(v):
        return np.ascontiguousarray(v.reshape(8, 128).T)

    shared = {
        "wqkv": wqkv, "wo": wo, "w1p": w1p, "w2p": w2p, "b1p": b1p,
        "b2c": cols8(b2), "lgc": cols8(lg), "lbc": cols8(lb),
        "qs8": np.ascontiguousarray(np.tile(qs * 8.0, 2).reshape(128, 1) * np.ones((1, 4), np.float32)),
        "ksc": np.ascontiguousarray(np.tile(ks, 2).reshape(128, 1) * np.ones((1, 4), np.float32)),
        "selheadT": np.ascontiguousarray(
            np.stack([np.r_[np.ones(64, np.float32), np.zeros(64, np.float32)],
                      np.r_[np.zeros(64, np.float32), np.ones(64, np.float32)]])),
        "selhead": np.ascontiguousarray(
            np.stack([np.r_[np.ones(64, np.float32), np.zeros(64, np.float32)],
                      np.r_[np.zeros(64, np.float32), np.ones(64, np.float32)]]).T),
    }
    i_rel = np.arange(512)[:, None]
    j_rel = np.arange(1024)[None, :]
    m_norm = np.zeros((512, 1024), np.float32)
    m_norm[:, :512][i_rel > j_rel[:, :512]] = -1e38
    m_norm[:, 512:][i_rel < (j_rel[:, 512:] - 512)] = -1e38
    m_w0 = m_norm.copy()
    m_w0[:, :512] = -1e38
    shared["l0mask"] = m_norm

    def rope_tables(tglob):
        n = len(tglob)
        half_idx = np.arange(0, 64, 2, dtype=np.float32)
        inv_freq = 1.0 / 10000.0 ** (half_idx / 64.0)
        tt = np.asarray(tglob, np.float32)
        freqs = tt[:, None] * inv_freq[None, :]
        freqs = np.concatenate([freqs, freqs], axis=-1)
        base = (half_idx + 0.4 * 64) / (1.4 * 64)
        power = (tt - 4096 // 2) / float(QW // 2)
        scale = base[None, :] ** power[:, None]
        scale = np.concatenate([scale, scale], axis=-1)
        cos, sin = np.cos(freqs), np.sin(freqs)
        sgn = np.r_[-np.ones(32, np.float32), np.ones(32, np.float32)]
        csq = (cos * scale).T
        ssq = (sin * scale).T * sgn[:, None]
        csk = (cos / scale).T
        ssk = (sin / scale).T * sgn[:, None]
        dbl = lambda a: np.ascontiguousarray(np.concatenate([a, a], axis=0).astype(np.float32))
        return dbl(csq), dbl(ssq), dbl(csk), dbl(ssk)

    per_core = []
    for c in range(NCORES):
        b, h = c // 2, c % 2
        t0 = h * T
        tq = np.maximum(np.arange(t0 - QW, t0 + T), 0)
        tk = np.maximum(np.arange(t0 - 2 * QW, t0 + T), 0)
        csq, ssq, _, _ = rope_tables(tq)
        _, _, csk, ssk = rope_tables(tk)
        per_core.append({
            "csq": csq, "ssq": ssq, "csk": csk, "ssk": ssk,
            "l0mask_w1": (m_w0 if h == 0 else m_norm),
        })
    return shared, per_core


def slice_stream(x_full, c, halo, zero_halo_for_first=True):
    b, h = c // 2, c % 2
    t0 = h * T
    lo = t0 - halo
    if lo < 0:
        xs = np.concatenate([np.zeros((-lo, DM), np.float32), x_full[b, 0:t0 + T]], axis=0)
    else:
        xs = x_full[b, lo:t0 + T]
    return np.ascontiguousarray(xs.T)


def emit_phase2(nc, pr, em, psbf, xin_t, xres_t, wqkv, wo, w1p, w2p, b1p, b2c, lgc, lbc, qs8, ksc, selheadT, selhead, csq, ssq, csk, ssk, l0mask, l0mask_w1, idn_in, o_out, o_ln, sp_x2m, pfx=""):
    W = HALO2 + T
    NQ = T // QW + 1
    QTOT = T + QW
    KTOT = T + 2 * QW

    with ExitStack() as es:
        ec = es.enter_context

        shT = ec(nc.sbuf_tensor(pfx + "shT", [2, 128], F32))
        shD = ec(nc.sbuf_tensor(pfx + "shD", [128, 2], F32))
        qsct = ec(nc.sbuf_tensor(pfx + "qsct", [128, 4], F32))
        ksct = ec(nc.sbuf_tensor(pfx + "ksct", [128, 4], F32))
        idn_bf = ec(nc.sbuf_tensor(pfx + "idn_bf", [128, 128], BF16))
        o_bf = ec(nc.sbuf_tensor(pfx + "o_bf", [128, 4 * QTOT], BF16))
        pr.dma("sync", shT[:, :], selheadT[:, :])
        pr.dma("sync", shD[:, :], selhead[:, :])
        pr.dma("sync", qsct[:, :], qs8[:, :])
        pr.dma("sync", ksct[:, :], ksc[:, :])
        pr.dma("gpsimd", idn_bf[:, :], idn_in[:, :])

        with ExitStack() as es12:
            ec12 = es12.enter_context
            q_bf = ec12(nc.sbuf_tensor(pfx + "q_bf", [128, 4 * QTOT], BF16))
            k_bf = ec12(nc.sbuf_tensor(pfx + "k_bf", [128, 4 * KTOT], BF16))
            v_tr = ec12(nc.sbuf_tensor(pfx + "v_tr", [128, (KTOT // 128) * 512], BF16))

            with ExitStack() as esa:
                eca = esa.enter_context
                xin_bf = eca(nc.sbuf_tensor(pfx + "xin_bf", [128, 8 * W], BF16))
                wqs = [eca(nc.sbuf_tensor(pfx + f"wqs{i}", [128, 8 * 128], BF16)) for i in range(3)]
                xld = eca(nc.sbuf_tensor(pfx + "p2xld", [128, 512], F32))
                scr = eca(nc.sbuf_tensor(pfx + "scr", [128, 2 * 512], F32))
                sqs = eca(nc.sbuf_tensor(pfx + "sqs", [128, 2 * 512], F32))
                nrm = eca(nc.sbuf_tensor(pfx + "nrm", [2, 2 * 512], F32))
                invb = eca(nc.sbuf_tensor(pfx + "invb", [128, 2 * 512], F32))
                qn2 = eca(nc.sbuf_tensor(pfx + "qn2", [128, 2 * 512], F32))
                qsw = eca(nc.sbuf_tensor(pfx + "qsw", [128, 2 * 512], F32))
                rot_cs = eca(nc.sbuf_tensor(pfx + "rot_cs", [128, 2 * 512], F32))
                rot_ss = eca(nc.sbuf_tensor(pfx + "rot_ss", [128, 2 * 512], F32))
                vev = eca(nc.sbuf_tensor(pfx + "vev", [128, KTOT], BF16))

                for k in range(8):
                    for s0 in range(0, W, 512):
                        wdt = min(512, W - s0)
                        pr.dma("sync", xld[:, 0:wdt], xin_t[k * 128:(k + 1) * 128, s0:s0 + wdt])
                        pr.op("vector",
                              lambda v, k=k, s0=s0, wdt=wdt: v.tensor_copy(
                                  xin_bf[:, k * W + s0:k * W + s0 + wdt], xld[:, 0:wdt]),
                              [xld[:, 0:wdt]], [xin_bf[:, k * W + s0:k * W + s0 + wdt]])

                def qkv_block(of, wtile, out_ap_f32, col0, wdt):
                    ps = em.take_psum()
                    em.mm_group(
                        [(wtile[:, k * 128:(k + 1) * 128],
                          xin_bf[:, k * W + col0 + (0 if k < 4 else -1):
                                 k * W + col0 + (0 if k < 4 else -1) + wdt])
                         for k in range(8)],
                        ps[:, 0:wdt])
                    pr.op("vector",
                          lambda v, ps=ps, out_ap_f32=out_ap_f32, wdt=wdt: v.tensor_copy(
                              out_ap_f32, ps[:, 0:wdt]),
                          [ps[:, 0:wdt]], [out_ap_f32])

                wi = 0
                for hp in range(4):
                    for kind in range(3):
                        of = kind * 4 + hp
                        wtile = wqs[wi % 3]
                        wi += 1
                        for k in range(8):
                            pr.dma("gpsimd", wtile[:, k * 128:(k + 1) * 128],
                                   wqkv[k * 128:(k + 1) * 128, of * 128:(of + 1) * 128])
                        if kind == 2:
                            for s0 in range(0, KTOT, 512):
                                wdt = min(512, KTOT - s0)
                                pp = (s0 // 512) % 2
                                sl = scr[:, pp * 512: pp * 512 + wdt]
                                qkv_block(of, wtile, sl, HALO2 - 2 * QW + s0, wdt)
                                pr.op("vector", lambda v, sl=sl, s0=s0, wdt=wdt: v.tensor_copy(
                                          vev[:, s0:s0 + wdt], sl),
                                      [sl], [vev[:, s0:s0 + wdt]])
                            for tb in range(KTOT // 128):
                                for hh in range(2):
                                    pst = psbf[:, (tb % 8) * 128:(tb % 8) * 128 + 64]
                                    pr.op("tensor",
                                          lambda t, tb=tb, hh=hh, pst=pst: t.transpose(
                                              pst, vev[hh * 64:hh * 64 + 64, tb * 128:(tb + 1) * 128],
                                              idn_bf[hh * 64:hh * 64 + 64, hh * 64:hh * 64 + 64]),
                                          [vev[hh * 64:hh * 64 + 64, tb * 128:(tb + 1) * 128],
                                           idn_bf[hh * 64:hh * 64 + 64, hh * 64:hh * 64 + 64]],
                                          [pst])
                                    h = hp * 2 + hh
                                    pr.op("vector",
                                          lambda v, tb=tb, h=h, pst=pst: v.tensor_copy(
                                              v_tr[:, tb * 512 + h * 64: tb * 512 + h * 64 + 64], pst),
                                          [pst], [v_tr[:, tb * 512 + h * 64: tb * 512 + h * 64 + 64]])
                            continue
                        is_q = kind == 0
                        ntok = QTOT if is_q else KTOT
                        col0 = HALO2 - (QW if is_q else 2 * QW)
                        out_bf = q_bf if is_q else k_bf
                        sc_col = qsct if is_q else ksct
                        cs_in, ss_in = (csq, ssq) if is_q else (csk, ssk)
                        for s0 in range(0, ntok, 512):
                            wdt = min(512, ntok - s0)
                            pp = (s0 // 512) % 2
                            sl = scr[:, pp * 512: pp * 512 + wdt]
                            sq = sqs[:, pp * 512: pp * 512 + wdt]
                            nr = nrm[:, pp * 512: pp * 512 + wdt]
                            iv = invb[:, pp * 512: pp * 512 + wdt]
                            q2 = qn2[:, pp * 512: pp * 512 + wdt]
                            qw_ = qsw[:, pp * 512: pp * 512 + wdt]
                            rc = rot_cs[:, pp * 512: pp * 512 + wdt]
                            rs = rot_ss[:, pp * 512: pp * 512 + wdt]
                            qkv_block(of, wtile, sl, col0 + s0, wdt)
                            pr.dma("scalar", rc, cs_in[:, s0:s0 + wdt])
                            pr.dma("scalar", rs, ss_in[:, s0:s0 + wdt])
                            pr.op("scalar", lambda s, sq=sq, sl=sl: s.activation(sq, sl, AF.Square),
                                  [sl], [sq])
                            ps = em.take_psum()
                            pr.op("tensor", lambda t, ps=ps, sq=sq, wdt=wdt: t.matmul(
                                      ps[0:2, 0:wdt], shD[:, :], sq, start=True, stop=True),
                                  [shD[:, :], sq], [ps[0:2, 0:wdt]])
                            pr.op("vector", lambda v, ps=ps, nr=nr, wdt=wdt: v.tensor_copy(
                                      nr, ps[0:2, 0:wdt]),
                                  [ps[0:2, 0:wdt]], [nr])
                            pr.op("scalar", lambda s, nr=nr: s.activation(nr, nr, AF.Sqrt),
                                  [nr], [nr])
                            pr.op("vector", lambda v, nr=nr: v.tensor_scalar_max(nr, nr, 1e-12),
                                  [nr], [nr])
                            pr.op("scalar", lambda s, nr=nr: s.activation(nr, nr, AF.Ln),
                                  [nr], [nr])
                            pr.op("scalar", lambda s, nr=nr: s.activation(nr, nr, AF.Exp, scale=f32(-1.0)),
                                  [nr], [nr])
                            ps2 = em.take_psum()
                            pr.op("tensor", lambda t, ps2=ps2, nr=nr, wdt=wdt: t.matmul(
                                      ps2[:, 0:wdt], shT[:, :], nr, start=True, stop=True),
                                  [shT[:, :], nr], [ps2[:, 0:wdt]])
                            pr.op("vector", lambda v, ps2=ps2, iv=iv, wdt=wdt: v.tensor_copy(iv, ps2[:, 0:wdt]),
                                  [ps2[:, 0:wdt]], [iv])
                            pr.op("vector", lambda v, q2=q2, sl=sl, iv=iv: v.tensor_mul(q2, sl, iv),
                                  [sl, iv], [q2])
                            pr.op("vector", lambda v, q2=q2, sc_col=sc_col, hp=hp: v.tensor_scalar(
                                      q2, q2, sc_col[:, hp:hp + 1], None, ALU.mult),
                                  [q2, sc_col[:, hp:hp + 1]], [q2])
                            for blk4 in range(4):
                                pr.dma("gpsimd",
                                       qw_[blk4 * 32:blk4 * 32 + 32, :],
                                       q2[(blk4 * 32) ^ 32:((blk4 * 32) ^ 32) + 32, :])
                            pr.op("vector", lambda v, sl=sl, q2=q2, rc=rc: v.tensor_mul(sl, q2, rc),
                                  [q2, rc], [sl])
                            pr.op("vector", lambda v, qw_=qw_, rs=rs: v.tensor_mul(qw_, qw_, rs),
                                  [qw_, rs], [qw_])
                            pr.op("vector",
                                  lambda v, out_bf=out_bf, hp=hp, ntok=ntok, s0=s0, wdt=wdt, sl=sl, qw_=qw_:
                                  v.tensor_add(out_bf[:, hp * ntok + s0: hp * ntok + s0 + wdt], sl, qw_),
                                  [sl, qw_], [out_bf[:, hp * ntok + s0: hp * ntok + s0 + wdt]])
                for e in ENGINES:
                    pr.barrier_to(e)

            with ExitStack() as esb:
                ecb = esb.enter_context
                mask_n = ecb(nc.sbuf_tensor(pfx + "mask_n", [128, 4 * 1024], F32))
                mask_1 = ecb(nc.sbuf_tensor(pfx + "mask_1", [128, 4 * 1024], F32))
                sc_f = ecb(nc.sbuf_tensor(pfx + "sc_f", [128, 1024], F32))
                att_bf = ecb(nc.sbuf_tensor(pfx + "att_bf", [128, 1024], BF16))
                attT_bf = ecb(nc.sbuf_tensor(pfx + "attT_bf", [128, 8 * 128], BF16))
                mrow = ecb(nc.sbuf_tensor(pfx + "mrow", [128, 2], F32))
                rsum = ecb(nc.sbuf_tensor(pfx + "rsum", [128, 2], F32))
                recb = ecb(nc.sbuf_tensor(pfx + "recb", [128, 2], F32))
                for qt4 in range(4):
                    pr.dma("scalar", mask_n[:, qt4 * 1024:(qt4 + 1) * 1024], l0mask[qt4 * 128:(qt4 + 1) * 128, :])
                    pr.dma("scalar", mask_1[:, qt4 * 1024:(qt4 + 1) * 1024], l0mask_w1[qt4 * 128:(qt4 + 1) * 128, :])
                for wq in range(NQ):
                    for h in range(8):
                        hp, hh = h // 2, h % 2
                        for qt in range(4):
                            q0 = wq * QW + qt * 128
                            mask = mask_1 if wq == 1 else mask_n
                            pshs = []
                            for kh in range(2):
                                k0 = wq * QW + kh * 512
                                ps = em.take_psum()
                                pr.op("tensor",
                                      lambda t, ps=ps, hp=hp, hh=hh, q0=q0, k0=k0: t.matmul(
                                          ps[:, 0:512],
                                          q_bf[hh * 64:hh * 64 + 64, hp * QTOT + q0: hp * QTOT + q0 + 128],
                                          k_bf[hh * 64:hh * 64 + 64, hp * KTOT + k0: hp * KTOT + k0 + 512],
                                          start=True, stop=True),
                                      [q_bf[hh * 64:hh * 64 + 64, hp * QTOT + q0: hp * QTOT + q0 + 128],
                                       k_bf[hh * 64:hh * 64 + 64, hp * KTOT + k0: hp * KTOT + k0 + 512]],
                                      [ps[:, 0:512]])
                                pshs.append(ps)
                            for kh in range(2):
                                pr.op("vector",
                                      lambda v, kh=kh, qt=qt, mask=mask, ps=pshs[kh]: v.tensor_add(
                                          sc_f[:, kh * 512:(kh + 1) * 512], ps[:, 0:512],
                                          mask[:, qt * 1024 + kh * 512: qt * 1024 + kh * 512 + 512]),
                                      [pshs[kh][:, 0:512],
                                       mask[:, qt * 1024 + kh * 512: qt * 1024 + kh * 512 + 512]],
                                      [sc_f[:, kh * 512:(kh + 1) * 512]])
                            pr.op("vector", lambda v: v.reduce_max(mrow[:, 0:1], sc_f[:, 0:1024], axis=AX.X, negate=True),
                                  [sc_f[:, 0:1024]], [mrow[:, 0:1]])
                            pr.op("scalar", lambda s: s.activation(sc_f[:, 0:1024], sc_f[:, 0:1024], AF.Exp,
                                                                   bias=mrow[:, 0:1], accum_out=rsum[:, 0:1]),
                                  [sc_f[:, 0:1024], mrow[:, 0:1]], [sc_f[:, 0:1024], rsum[:, 0:1]])
                            pr.op("vector", lambda v: v.reciprocal(recb[:, 0:1], rsum[:, 0:1]),
                                  [rsum[:, 0:1]], [recb[:, 0:1]])
                            pr.op("vector", lambda v: v.tensor_scalar(att_bf[:, 0:1024], sc_f[:, 0:1024],
                                                                      recb[:, 0:1], None, ALU.mult),
                                  [sc_f[:, 0:1024], recb[:, 0:1]], [att_bf[:, 0:1024]])
                            for kt in range(8):
                                pst = psbf[:, kt * 128:(kt + 1) * 128]
                                pr.op("tensor",
                                      lambda t, kt=kt, pst=pst: t.transpose(
                                          pst, att_bf[:, kt * 128:(kt + 1) * 128], idn_bf[:, :]),
                                      [att_bf[:, kt * 128:(kt + 1) * 128], idn_bf[:, :]], [pst])
                                pr.op("vector",
                                      lambda v, kt=kt, pst=pst: v.tensor_copy(
                                          attT_bf[:, kt * 128:(kt + 1) * 128], pst),
                                      [pst], [attT_bf[:, kt * 128:(kt + 1) * 128]])
                            psA = em.take_psum()
                            kb0 = wq * 4
                            em.mm_group(
                                [(v_tr[:, (kb0 + kt) * 512 + h * 64: (kb0 + kt) * 512 + h * 64 + 64],
                                  attT_bf[:, kt * 128:(kt + 1) * 128]) for kt in range(8)],
                                psA[0:64, 0:128])
                            pr.op("vector", lambda v, hp=hp, hh=hh, q0=q0, psA=psA: v.tensor_copy(
                                      o_bf[hh * 64:hh * 64 + 64, hp * QTOT + q0: hp * QTOT + q0 + 128],
                                      psA[0:64, 0:128]),
                                  [psA[0:64, 0:128]],
                                  [o_bf[hh * 64:hh * 64 + 64, hp * QTOT + q0: hp * QTOT + q0 + 128]])
                for e in ENGINES:
                    pr.barrier_to(e)

        TT1 = T + 1
        with ExitStack() as esc:
            ecc = esc.enter_context
            x2m = ecc(nc.sbuf_tensor(pfx + "x2m", [128, 8 * TT1], F32))
            xrl = ecc(nc.sbuf_tensor(pfx + "p2xrl", [128, 512], F32))
            with ExitStack() as esd:
                ecd = esd.enter_context
                wo_bf = ecd(nc.sbuf_tensor(pfx + "wo_bf", [128, 4 * DM], BF16))
                for k in range(4):
                    pr.dma("gpsimd", wo_bf[:, k * DM:(k + 1) * DM], wo[k * 128:(k + 1) * 128, :])
                for of in range(8):
                    for s0 in blocks512(TT1):
                        wdt = 512
                        ps = em.take_psum()
                        em.mm_group(
                            [(wo_bf[:, k * DM + of * 128: k * DM + of * 128 + 128],
                              o_bf[:, k * QTOT + QW - 1 + s0: k * QTOT + QW - 1 + s0 + wdt])
                             for k in range(4)],
                            ps[:, 0:wdt])
                        pr.dma("scalar", xrl[:, 0:wdt], xin_t[of * 128:(of + 1) * 128,
                                                             HALO2 - 1 + s0: HALO2 - 1 + s0 + wdt])
                        pr.op("vector",
                              lambda v, of=of, s0=s0, wdt=wdt, ps=ps: v.tensor_add(
                                  x2m[:, of * TT1 + s0: of * TT1 + s0 + wdt], ps[:, 0:wdt],
                                  xrl[:, 0:wdt]),
                              [ps[:, 0:wdt], xrl[:, 0:wdt]],
                              [x2m[:, of * TT1 + s0: of * TT1 + s0 + wdt]])
                for e in ENGINES:
                    pr.barrier_to(e)
            for of in range(8):
                pr.dma("gpsimd", sp_x2m[of * 128:(of + 1) * 128, :], x2m[:, of * TT1: of * TT1 + TT1])

            lgt = ecc(nc.sbuf_tensor(pfx + "lgt", [128, 8], F32))
            lbt = ecc(nc.sbuf_tensor(pfx + "lbt", [128, 8], F32))
            b1t = ecc(nc.sbuf_tensor(pfx + "b1t", [128, 2 * (HIDP // 128)], F32))
            b2t = ecc(nc.sbuf_tensor(pfx + "b2t", [128, 8], F32))
            pr.dma("sync", lgt[:, :], lgc[:, :])
            pr.dma("sync", lbt[:, :], lbc[:, :])
            pr.dma("sync", b1t[:, :], b1p[:, :])
            pr.dma("sync", b2t[:, :], b2c[:, :])
            FCH = 512
            hsh = ecc(nc.sbuf_tensor(pfx + "hsh", [128, 8 * FCH], F32))
            hln_bf = ecc(nc.sbuf_tensor(pfx + "hln_bf", [128, 8 * FCH], BF16))
            h2_bf = ecc(nc.sbuf_tensor(pfx + "h2_bf", [128, (HIDP // 128) * FCH], BF16))
            sqt = ecc(nc.sbuf_tensor(pfx + "p2sq", [128, 2 * 512], F32))
            strow2 = ecc(nc.sbuf_tensor(pfx + "strow2", [1, 4 * 512], F32))
            mub2 = ecc(nc.sbuf_tensor(pfx + "mub2", [128, FCH], F32))
            rstdb2 = ecc(nc.sbuf_tensor(pfx + "rstdb2", [128, FCH], F32))
            wstr = [ecc(nc.sbuf_tensor(pfx + f"wstr{i}", [128, 8 * 128], BF16)) for i in range(4)]
            gel = ecc(nc.sbuf_tensor(pfx + "gel", [128, 512], F32))
            aev = ecc(nc.sbuf_tensor(pfx + "aev", [128, 512], F32))
            xq2 = ecc(nc.sbuf_tensor(pfx + "xq2", [128, 512], F32))
            shsave = ecc(nc.sbuf_tensor(pfx + "shsave", [128, 8], F32))
            shsave_b = ecc(nc.sbuf_tensor(pfx + "shsave_b", [128, 8], F32))
            shsave2 = None
            NJ = HIDP // 128
            for fc in range(T // FCH):
                c0 = fc * FCH
                for of in range(4, 8):
                    pr.op("vector",
                          lambda v, of=of, c0=c0: v.tensor_copy(
                              shsave[:, of:of + 1], x2m[:, of * TT1 + c0 + FCH: of * TT1 + c0 + FCH + 1]),
                          [x2m[:, of * TT1 + c0 + FCH: of * TT1 + c0 + FCH + 1]],
                          [shsave[:, of:of + 1]])
                for of in range(8):
                    off = 1 if of < 4 else 0
                    pr.op("vector",
                          lambda v, of=of, off=off, c0=c0: v.tensor_copy(
                              hsh[:, of * FCH:(of + 1) * FCH],
                              x2m[:, of * TT1 + c0 + off: of * TT1 + c0 + off + FCH]),
                          [x2m[:, of * TT1 + c0 + off: of * TT1 + c0 + off + FCH]],
                          [hsh[:, of * FCH:(of + 1) * FCH]])
                if fc > 0:
                    for of in range(4, 8):
                        pr.op("vector",
                              lambda v, of=of: v.tensor_copy(
                                  hsh[:, of * FCH: of * FCH + 1], shsave2[:, of:of + 1]),
                              [shsave2[:, of:of + 1]], [hsh[:, of * FCH: of * FCH + 1]])
                shsave, shsave2 = (shsave2, shsave) if fc > 0 else (shsave_b, shsave)
                emit_ln_feature(pr, em, hsh, FCH, strow2, sqt, mub2, rstdb2, nfeat=8)
                for of in range(8):
                    sl = hsh[:, of * FCH:(of + 1) * FCH]
                    pr.op("vector", lambda v, sl=sl: v.tensor_sub(sl, sl, mub2[:, 0:FCH]),
                          [sl, mub2[:, 0:FCH]], [sl])
                    pr.op("vector", lambda v, sl=sl: v.tensor_mul(sl, sl, rstdb2[:, 0:FCH]),
                          [sl, rstdb2[:, 0:FCH]], [sl])
                    pr.op("vector", lambda v, sl=sl, of=of: v.tensor_scalar(
                              hln_bf[:, of * FCH:(of + 1) * FCH], sl,
                              lgt[:, of:of + 1], lbt[:, of:of + 1], ALU.mult, ALU.add),
                          [sl, lgt[:, of:of + 1], lbt[:, of:of + 1]],
                          [hln_bf[:, of * FCH:(of + 1) * FCH]])
                for j in range(NJ):
                    wa = wstr[(2 * j) % 4]
                    wg = wstr[(2 * j + 1) % 4]
                    for k in range(8):
                        pr.dma("gpsimd", wa[:, k * 128:(k + 1) * 128],
                               w1p[k * 128:(k + 1) * 128, j * 256: j * 256 + 128])
                        pr.dma("gpsimd", wg[:, k * 128:(k + 1) * 128],
                               w1p[k * 128:(k + 1) * 128, j * 256 + 128: j * 256 + 256])
                    for s0 in range(0, FCH, 512):
                        wdt = min(512, FCH - s0)
                        psa = em.take_psum()
                        em.mm_group(
                            [(wa[:, k * 128:(k + 1) * 128], hln_bf[:, k * FCH + s0:k * FCH + s0 + wdt])
                             for k in range(8)], psa[:, 0:wdt])
                        psg = em.take_psum()
                        em.mm_group(
                            [(wg[:, k * 128:(k + 1) * 128], hln_bf[:, k * FCH + s0:k * FCH + s0 + wdt])
                             for k in range(8)], psg[:, 0:wdt])
                        pr.op("vector", lambda v, psa=psa, wdt=wdt, j=j: v.tensor_scalar(
                                  aev[:, 0:wdt], psa[:, 0:wdt], b1t[:, 2 * j:2 * j + 1], None, ALU.add),
                              [psa[:, 0:wdt], b1t[:, 2 * j:2 * j + 1]], [aev[:, 0:wdt]])
                        pr.op("scalar", lambda s, psg=psg, wdt=wdt, j=j: s.activation(
                                  gel[:, 0:wdt], psg[:, 0:wdt], AF.Gelu, bias=b1t[:, 2 * j + 1:2 * j + 2]),
                              [psg[:, 0:wdt], b1t[:, 2 * j + 1:2 * j + 2]], [gel[:, 0:wdt]])
                        pr.op("vector", lambda v, j=j, s0=s0, wdt=wdt: v.tensor_mul(
                                  h2_bf[:, j * FCH + s0: j * FCH + s0 + wdt], aev[:, 0:wdt], gel[:, 0:wdt]),
                              [aev[:, 0:wdt], gel[:, 0:wdt]],
                              [h2_bf[:, j * FCH + s0: j * FCH + s0 + wdt]])
                for of in range(8):
                    for s0 in range(0, FCH, 512):
                        wdt = min(512, FCH - s0)
                        ps = em.take_psum()
                        for k in range(NJ):
                            wv = wstr[k % 4]
                            pr.dma("gpsimd", wv[:, 0:128],
                                   w2p[k * 128:(k + 1) * 128, of * 128:(of + 1) * 128])
                            pr.op("tensor",
                                  lambda t, ps=ps, wv=wv, k=k, s0=s0, wdt=wdt:
                                  t.matmul(ps[:, 0:wdt], wv[:, 0:128],
                                           h2_bf[:, k * FCH + s0:k * FCH + s0 + wdt],
                                           start=(k == 0), stop=(k == NJ - 1)),
                                  [wv[:, 0:128], h2_bf[:, k * FCH + s0:k * FCH + s0 + wdt]],
                                  [ps[:, 0:wdt]],
                                  check_writes=[ps[:, 0:wdt]] if k == 0 else [])
                        pr.op("vector", lambda v, ps=ps, wdt=wdt, of=of: v.tensor_scalar(
                                  xq2[:, 0:wdt], ps[:, 0:wdt], b2t[:, of:of + 1], None, ALU.add),
                              [ps[:, 0:wdt], b2t[:, of:of + 1]], [xq2[:, 0:wdt]])
                        pr.op("vector",
                              lambda v, of=of, c0=c0, s0=s0, wdt=wdt: v.tensor_add(
                                  x2m[:, of * TT1 + 1 + c0 + s0: of * TT1 + 1 + c0 + s0 + wdt],
                                  x2m[:, of * TT1 + 1 + c0 + s0: of * TT1 + 1 + c0 + s0 + wdt],
                                  xq2[:, 0:wdt]),
                              [xq2[:, 0:wdt],
                               x2m[:, of * TT1 + 1 + c0 + s0: of * TT1 + 1 + c0 + s0 + wdt]],
                              [x2m[:, of * TT1 + 1 + c0 + s0: of * TT1 + 1 + c0 + s0 + wdt]])
            strow3 = ecc(nc.sbuf_tensor(pfx + "strow3", [1, 4 * 512], F32))
            mub3 = ecc(nc.sbuf_tensor(pfx + "mub3", [128, 512], F32))
            rstdb3 = ecc(nc.sbuf_tensor(pfx + "rstdb3", [128, 512], F32))
            xlnb = ecc(nc.sbuf_tensor(pfx + "xlnb", [128, 8 * 512], F32))
            for of in range(8):
                pr.dma("gpsimd", o_out[of * 128:(of + 1) * 128, :], x2m[:, of * TT1 + 1: of * TT1 + TT1])
            for s0 in range(0, T, 512):
                for of in range(8):
                    pr.dma("scalar", xrl[:, 0:512], xres_t[of * 128:(of + 1) * 128, s0:s0 + 512])
                    pr.op("vector",
                          lambda v, of=of, s0=s0: v.tensor_add(
                              xlnb[:, of * 512:(of + 1) * 512],
                              x2m[:, of * TT1 + 1 + s0: of * TT1 + 1 + s0 + 512], xrl[:, 0:512]),
                          [x2m[:, of * TT1 + 1 + s0: of * TT1 + 1 + s0 + 512], xrl[:, 0:512]],
                          [xlnb[:, of * 512:(of + 1) * 512]])
                emit_ln_feature(pr, em, xlnb, 512, strow3, sqt, mub3, rstdb3, nfeat=8)
                for of in range(8):
                    sl = xlnb[:, of * 512:(of + 1) * 512]
                    pr.op("vector", lambda v, sl=sl: v.tensor_sub(sl, sl, mub3[:, 0:512]),
                          [sl, mub3[:, 0:512]], [sl])
                    pr.op("vector", lambda v, sl=sl: v.tensor_mul(sl, sl, rstdb3[:, 0:512]),
                          [sl, rstdb3[:, 0:512]], [sl])
                    pr.dma("gpsimd", o_ln[of * 128:(of + 1) * 128, s0:s0 + 512], sl)
        for e in ENGINES:
            pr.barrier_to(e)
        print(f"phase2 program: {pr.n_ops} ops, {pr.n_waits} waits")
    return nc


NXK = 32



def build_phase2():
    nc = bass.Bass("TRN2", target_bir_lowering=False, debug=False)
    W = HALO2 + T
    NQ = T // QW + 1
    QTOT = T + QW
    KTOT = T + 2 * QW

    xin_t = nc.dram_tensor("xin_t", [DM, W], F32, kind="ExternalInput").ap()
    xres_t = nc.dram_tensor("xres_t", [DM, T], F32, kind="ExternalInput").ap()
    wqkv = nc.dram_tensor("wqkv", [DM, 1536], F32, kind="ExternalInput").ap()
    wo = nc.dram_tensor("wo", [512, DM], F32, kind="ExternalInput").ap()
    w1p = nc.dram_tensor("w1p", [DM, 2 * HIDP], F32, kind="ExternalInput").ap()
    w2p = nc.dram_tensor("w2p", [HIDP, DM], F32, kind="ExternalInput").ap()
    b1p = nc.dram_tensor("b1p", [128, 2 * (HIDP // 128)], F32, kind="ExternalInput").ap()
    b2c = nc.dram_tensor("b2c", [128, 8], F32, kind="ExternalInput").ap()
    lgc = nc.dram_tensor("lgc", [128, 8], F32, kind="ExternalInput").ap()
    lbc = nc.dram_tensor("lbc", [128, 8], F32, kind="ExternalInput").ap()
    qs8 = nc.dram_tensor("qs8", [128, 4], F32, kind="ExternalInput").ap()
    ksc = nc.dram_tensor("ksc", [128, 4], F32, kind="ExternalInput").ap()
    selheadT = nc.dram_tensor("selheadT", [2, 128], F32, kind="ExternalInput").ap()
    selhead = nc.dram_tensor("selhead", [128, 2], F32, kind="ExternalInput").ap()
    csq = nc.dram_tensor("csq", [128, QTOT], F32, kind="ExternalInput").ap()
    ssq = nc.dram_tensor("ssq", [128, QTOT], F32, kind="ExternalInput").ap()
    csk = nc.dram_tensor("csk", [128, KTOT], F32, kind="ExternalInput").ap()
    ssk = nc.dram_tensor("ssk", [128, KTOT], F32, kind="ExternalInput").ap()
    l0mask = nc.dram_tensor("l0mask", [512, 1024], F32, kind="ExternalInput").ap()
    l0mask_w1 = nc.dram_tensor("l0mask_w1", [512, 1024], F32, kind="ExternalInput").ap()
    idn_in = nc.dram_tensor("idn128", [128, 128], F32, kind="ExternalInput").ap()

    o_out = nc.dram_tensor("xout_t", [DM, T], F32, kind="ExternalOutput").ap()
    o_ln = nc.dram_tensor("xln_t", [DM, T], F32, kind="ExternalOutput").ap()
    sp_x2m = nc.dram_tensor("sp_x2m", [DM, T + 1], F32).ap()

    with ExitStack() as _es0:
        _ec0 = _es0.enter_context
        blk = _ec0(nc.Block())
        pr = Prog(nc, _es0)
        em = Emitter(nc, _es0, pr, n_psum=7)
        psbf = _ec0(nc.psum_tensor("psbf", [128, 1024], BF16))
        emit_phase2(nc, pr, em, psbf, xin_t, xres_t, wqkv, wo, w1p, w2p, b1p, b2c, lgc, lbc, qs8, ksc, selheadT, selhead, csq, ssq, csk, ssk, l0mask, l0mask_w1, idn_in, o_out, o_ln, sp_x2m)
    return nc


def _nu2(d, cap=11):
    d = abs(int(d))
    if d == 0:
        return cap
    v = 0
    while d % 2 == 0 and v < cap:
        v += 1
        d //= 2
    return v


def _count_tables_A():
    mul = np.zeros((16, 256, 256), np.float32)
    add = np.full((16, 256, 256), -1e38, np.float32)
    idx = np.arange(256)
    for h in range(16):
        m = np.array([_nu2(i - h, 7) for i in range(256)])
        x = idx[:, None] ^ idx[None, :]
        bl = np.zeros_like(x)
        nz = x > 0
        bl[nz] = (np.floor(np.log2(x[nz]))).astype(np.int64) + 1
        sseg = np.maximum(0, bl - 2)
        cnt = np.maximum(0, np.minimum(m[:, None], m[None, :]) + 1 - sseg)
        valid = (cnt > 0) & (idx[None, :] <= idx[:, None])
        mul[h][valid] = cnt[valid]
        add[h][valid] = 0.0
    return mul, add


def _count_tables_B(t0, half):
    K = T + NXK
    mul = np.zeros((16, 16, K), np.float32)
    add = np.full((16, 16, K), -1e38, np.float32)
    for h in range(16):
        iv = t0 + h + 128 * np.arange(16)
        jl = t0 + np.arange(T)
        jx = np.concatenate([np.arange(16), 1024 + np.arange(16)])
        jv = np.concatenate([jl, jx])
        mi = np.array([_nu2(i - h) for i in iv])
        mj = np.array([_nu2(j - h) for j in jv])
        x = iv[:, None] ^ jv[None, :]
        bl = np.zeros_like(x)
        nz = x > 0
        bl[nz] = (np.floor(np.log2(x[nz]))).astype(np.int64) + 1
        sseg = np.maximum(0, bl - 2)
        cnt = np.maximum(0, np.minimum(mi[:, None], mj[None, :]) + 1 - sseg)
        valid = (cnt > 0) & (jv[None, :] <= iv[:, None])
        if half == 0:
            valid[:, T:] = False
        else:
            keep = np.zeros(NXK, bool)
            keep[h] = True
            keep[16 + h] = True
            valid[:, T:] &= keep[None, :]
        mul[h][valid] = cnt[valid]
        add[h][valid] = 0.0
    return mul, add


def prep_phase3(inputs):
    wq_t = np.ascontiguousarray(_np_f32(inputs["L1_attn_Wq"]).T)
    wk_t = np.ascontiguousarray(_np_f32(inputs["L1_attn_Wk"]).T)
    wv_t = np.ascontiguousarray(_np_f32(inputs["L1_attn_Wv"]).T)
    wo_t = np.ascontiguousarray(_np_f32(inputs["L1_attn_Wo"]).T)

    def cols8(v):
        return np.ascontiguousarray(_np_f32(v).reshape(8, 128).T)

    tamul, taadd = _count_tables_A()
    shared = {
        "wq_t": wq_t, "wk_t": wk_t, "wv_t": wv_t, "wo_t": wo_t,
        "bqc": cols8(inputs["L1_attn_bq"]), "bkc": cols8(inputs["L1_attn_bk"]),
        "bvc": cols8(inputs["L1_attn_bv"]), "boc": cols8(inputs["L1_attn_bo"]),
        "tamul": tamul.reshape(16 * 256, 256), "taadd": taadd.reshape(16 * 256, 256),
    }

    d = 1024
    inv_freq = 1.0 / 15000.0 ** (np.arange(0, d, 2, dtype=np.float32) / d)

    def rope(tpos):
        freqs = np.asarray(tpos, np.float32)[:, None] * inv_freq[None, :]
        emb = np.concatenate([freqs, freqs], axis=-1)
        cos, sin = np.cos(emb), np.sin(emb)
        sgn = np.r_[-np.ones(512, np.float32), np.ones(512, np.float32)]
        return (np.ascontiguousarray(cos.T.astype(np.float32)),
                np.ascontiguousarray((sin.T * sgn[:, None]).astype(np.float32)))

    per_core = []
    for c in range(NCORES):
        b, h = c // 2, c % 2
        t0 = h * T
        cs, ssg = rope(np.arange(t0, t0 + T))
        csx, ssgx = rope(np.concatenate([np.arange(16), 1024 + np.arange(16)]))
        tbmul, tbadd = _count_tables_B(t0, h)
        per_core.append({
            "cs_t": cs, "ssg_t": ssg, "csx_t": csx, "ssgx_t": ssgx,
            "tbmul": tbmul.reshape(16 * 16, T + NXK),
            "tbadd": tbadd.reshape(16 * 16, T + NXK),
        })
    return shared, per_core


def emit_phase3(nc, pr, em, psbf, xin_t, x2x_t, wq_t, wk_t, wv_t, wo_t, bqc, bkc, bvc, boc, cs_t, ssg_t, csx_t, ssgx_t, tamul, taadd, tbmul, tbadd, idn_in, o_x3, pfx=""):
    KB = T + NXK

    with ExitStack() as es:
        ec = es.enter_context
        idn_bf = ec(nc.sbuf_tensor(pfx + "p3idn", [128, 128], BF16))
        pr.dma("gpsimd", idn_bf[:, :], idn_in[:, :])

        qr_bf = ec(nc.sbuf_tensor(pfx + "qr_bf", [128, 8 * T], BF16))
        kr_bf = ec(nc.sbuf_tensor(pfx + "kr_bf", [128, 8 * T], BF16))
        v_tr = ec(nc.sbuf_tensor(pfx + "v_tr", [128, (T // 128) * 1024], BF16))
        o_bf = ec(nc.sbuf_tensor(pfx + "o_bf", [128, 8 * T], BF16))
        kx_bf = ec(nc.sbuf_tensor(pfx + "kx_bf", [128, 8 * NXK], BF16))
        vx_tr = ec(nc.sbuf_tensor(pfx + "vx_tr", [32, 16 * 64], BF16))

        with ExitStack() as esa:
            eca = esa.enter_context
            xin_bf = eca(nc.sbuf_tensor(pfx + "p3xin_bf", [128, 8 * T], BF16))
            xld = eca(nc.sbuf_tensor(pfx + "p3xld", [128, 512], F32))
            w_bf = [eca(nc.sbuf_tensor(pfx + f"p3w{i}", [128, 8 * 128], BF16)) for i in range(3)]
            qT = eca(nc.sbuf_tensor(pfx + "p3qT", [128, 2 * T], F32))
            bcol = eca(nc.sbuf_tensor(pfx + "p3bcol", [128, 8 * 3], F32))
            cstl = eca(nc.sbuf_tensor(pfx + "p3cs", [128, 2 * 512], F32))
            x2xT = eca(nc.sbuf_tensor(pfx + "p3x2x", [128, 8 * NXK], F32))
            x2x_bf = eca(nc.sbuf_tensor(pfx + "p3x2xb", [128, 8 * NXK], BF16))
            kxf = eca(nc.sbuf_tensor(pfx + "p3kxf", [128, 8 * NXK], F32))
            csx_s = eca(nc.sbuf_tensor(pfx + "p3csx", [128, 8 * NXK], F32))
            ssx_s = eca(nc.sbuf_tensor(pfx + "p3ssx", [128, 8 * NXK], F32))
            vev3 = eca(nc.sbuf_tensor(pfx + "vev3", [128, T], BF16))

            pr.dma("sync", bcol[:, 0:8], bqc[:, :])
            pr.dma("sync", bcol[:, 8:16], bkc[:, :])
            pr.dma("sync", bcol[:, 16:24], bvc[:, :])
            for k in range(8):
                for s0 in range(0, T, 512):
                    pr.dma("sync", xld[:, :], xin_t[k * 128:(k + 1) * 128, s0:s0 + 512])
                    pr.op("vector",
                          lambda v, k=k, s0=s0: v.tensor_copy(
                              xin_bf[:, k * T + s0:k * T + s0 + 512], xld[:, :]),
                          [xld[:, :]], [xin_bf[:, k * T + s0:k * T + s0 + 512]])
                pr.dma("sync", x2xT[:, k * NXK:(k + 1) * NXK], x2x_t[k * 128:(k + 1) * 128, :])
                pr.op("vector",
                      lambda v, k=k: v.tensor_copy(x2x_bf[:, k * NXK:(k + 1) * NXK],
                                                   x2xT[:, k * NXK:(k + 1) * NXK]),
                      [x2xT[:, k * NXK:(k + 1) * NXK]], [x2x_bf[:, k * NXK:(k + 1) * NXK]])
                pr.dma("sync", csx_s[:, k * NXK:(k + 1) * NXK], csx_t[k * 128:(k + 1) * 128, :])
                pr.dma("sync", ssx_s[:, k * NXK:(k + 1) * NXK], ssgx_t[k * 128:(k + 1) * 128, :])

            for kind, wsrc, bof, out_bf in ((0, wq_t, 0, qr_bf), (1, wk_t, 8, kr_bf)):
                for ofp in range(4):
                    for half, of in ((0, ofp), (1, ofp + 4)):
                        wt = w_bf[of % 3]
                        for k in range(8):
                            pr.dma("gpsimd", wt[:, k * 128:(k + 1) * 128],
                                   wsrc[k * 128:(k + 1) * 128, of * 128:(of + 1) * 128])
                        for s0 in range(0, T, 512):
                            ps = em.take_psum()
                            em.mm_group(
                                [(wt[:, k * 128:(k + 1) * 128], xin_bf[:, k * T + s0:k * T + s0 + 512])
                                 for k in range(8)],
                                ps[:, 0:512])
                            pr.op("vector",
                                  lambda v, half=half, s0=s0, ps=ps, bof=bof, of=of: v.tensor_scalar(
                                      qT[:, half * T + s0: half * T + s0 + 512], ps[:, 0:512],
                                      bcol[:, bof + of:bof + of + 1], None, ALU.add),
                                  [ps[:, 0:512], bcol[:, bof + of:bof + of + 1]],
                                  [qT[:, half * T + s0: half * T + s0 + 512]])
                        psx = em.take_psum()
                        em.mm_group(
                            [(wt[:, k * 128:(k + 1) * 128], x2x_bf[:, k * NXK:(k + 1) * NXK])
                             for k in range(8)],
                            psx[:, 0:NXK])
                        pr.op("vector",
                              lambda v, of=of, psx=psx, bof=bof: v.tensor_scalar(
                                  kxf[:, of * NXK:(of + 1) * NXK], psx[:, 0:NXK],
                                  bcol[:, bof + of:bof + of + 1], None, ALU.add),
                              [psx[:, 0:NXK], bcol[:, bof + of:bof + of + 1]],
                              [kxf[:, of * NXK:(of + 1) * NXK]])
                    for half, of in ((0, ofp), (1, ofp + 4)):
                        for s0 in range(0, T, 512):
                            pr.dma("scalar", cstl[:, 0:512], cs_t[of * 128:(of + 1) * 128, s0:s0 + 512])
                            pr.dma("scalar", cstl[:, 512:1024], ssg_t[of * 128:(of + 1) * 128, s0:s0 + 512])
                            pr.op("vector",
                                  lambda v, half=half, s0=s0: v.tensor_mul(
                                      cstl[:, 0:512],
                                      qT[:, half * T + s0: half * T + s0 + 512], cstl[:, 0:512]),
                                  [qT[:, half * T + s0: half * T + s0 + 512], cstl[:, 0:512]],
                                  [cstl[:, 0:512]])
                            pr.op("vector",
                                  lambda v, half=half, s0=s0: v.tensor_mul(
                                      cstl[:, 512:1024],
                                      qT[:, (1 - half) * T + s0: (1 - half) * T + s0 + 512], cstl[:, 512:1024]),
                                  [qT[:, (1 - half) * T + s0: (1 - half) * T + s0 + 512], cstl[:, 512:1024]],
                                  [cstl[:, 512:1024]])
                            pr.op("vector",
                                  lambda v, of=of, s0=s0, out_bf=out_bf: v.tensor_add(
                                      out_bf[:, of * T + s0: of * T + s0 + 512],
                                      cstl[:, 0:512], cstl[:, 512:1024]),
                                  [cstl[:, 0:512], cstl[:, 512:1024]],
                                  [out_bf[:, of * T + s0: of * T + s0 + 512]])
                if kind == 1:
                    for of in range(8):
                        pr.op("vector",
                              lambda v, of=of: v.tensor_mul(
                                  csx_s[:, of * NXK:(of + 1) * NXK],
                                  kxf[:, of * NXK:(of + 1) * NXK], csx_s[:, of * NXK:(of + 1) * NXK]),
                              [kxf[:, of * NXK:(of + 1) * NXK], csx_s[:, of * NXK:(of + 1) * NXK]],
                              [csx_s[:, of * NXK:(of + 1) * NXK]])
                        pr.op("vector",
                              lambda v, of=of: v.tensor_mul(
                                  ssx_s[:, of * NXK:(of + 1) * NXK],
                                  kxf[:, (of ^ 4) * NXK:((of ^ 4) + 1) * NXK], ssx_s[:, of * NXK:(of + 1) * NXK]),
                              [kxf[:, (of ^ 4) * NXK:((of ^ 4) + 1) * NXK], ssx_s[:, of * NXK:(of + 1) * NXK]],
                              [ssx_s[:, of * NXK:(of + 1) * NXK]])
                        pr.op("vector",
                              lambda v, of=of: v.tensor_add(
                                  kx_bf[:, of * NXK:(of + 1) * NXK],
                                  csx_s[:, of * NXK:(of + 1) * NXK], ssx_s[:, of * NXK:(of + 1) * NXK]),
                              [csx_s[:, of * NXK:(of + 1) * NXK], ssx_s[:, of * NXK:(of + 1) * NXK]],
                              [kx_bf[:, of * NXK:(of + 1) * NXK]])
            for of in range(8):
                wt = w_bf[of % 3]
                for k in range(8):
                    pr.dma("gpsimd", wt[:, k * 128:(k + 1) * 128],
                           wv_t[k * 128:(k + 1) * 128, of * 128:(of + 1) * 128])
                for s0 in range(0, T, 512):
                    ps = em.take_psum()
                    em.mm_group(
                        [(wt[:, k * 128:(k + 1) * 128], xin_bf[:, k * T + s0:k * T + s0 + 512])
                         for k in range(8)],
                        ps[:, 0:512])
                    pr.op("vector",
                          lambda v, of=of, s0=s0, ps=ps: v.tensor_scalar(
                              vev3[:, s0:s0 + 512], ps[:, 0:512],
                              bcol[:, 16 + of:16 + of + 1], None, ALU.add),
                          [ps[:, 0:512], bcol[:, 16 + of:16 + of + 1]],
                          [vev3[:, s0:s0 + 512]])
                for tb in range(T // 128):
                    for hh in range(2):
                        pst = psbf[:, (tb % 8) * 128:(tb % 8) * 128 + 64]
                        pr.op("tensor",
                              lambda t, tb=tb, hh=hh, pst=pst: t.transpose(
                                  pst, vev3[hh * 64:hh * 64 + 64, tb * 128:(tb + 1) * 128],
                                  idn_bf[hh * 64:hh * 64 + 64, hh * 64:hh * 64 + 64]),
                              [vev3[hh * 64:hh * 64 + 64, tb * 128:(tb + 1) * 128],
                               idn_bf[hh * 64:hh * 64 + 64, hh * 64:hh * 64 + 64]],
                              [pst])
                        h = of * 2 + hh
                        pr.op("vector",
                              lambda v, tb=tb, h=h, pst=pst: v.tensor_copy(
                                  v_tr[:, tb * 1024 + h * 64: tb * 1024 + h * 64 + 64], pst),
                              [pst], [v_tr[:, tb * 1024 + h * 64: tb * 1024 + h * 64 + 64]])
                psx = em.take_psum()
                em.mm_group(
                    [(wt[:, k * 128:(k + 1) * 128], x2x_bf[:, k * NXK:(k + 1) * NXK])
                     for k in range(8)],
                    psx[:, 0:NXK])
                pr.op("vector",
                      lambda v, of=of, psx=psx: v.tensor_scalar(
                          kxf[:, of * NXK:(of + 1) * NXK], psx[:, 0:NXK],
                          bcol[:, 16 + of:16 + of + 1], None, ALU.add),
                      [psx[:, 0:NXK], bcol[:, 16 + of:16 + of + 1]],
                      [kxf[:, of * NXK:(of + 1) * NXK]])
                pr.op("vector",
                      lambda v, of=of: v.tensor_copy(vev3[:, 0:NXK], kxf[:, of * NXK:(of + 1) * NXK]),
                      [kxf[:, of * NXK:(of + 1) * NXK]], [vev3[:, 0:NXK]])
                for hh in range(2):
                    pst = psbf[0:32, 0:64]
                    pr.op("tensor",
                          lambda t, hh=hh, pst=pst: t.transpose(
                              pst, vev3[hh * 64:hh * 64 + 64, 0:NXK],
                              idn_bf[hh * 64:hh * 64 + 64, hh * 64:hh * 64 + 64]),
                          [vev3[hh * 64:hh * 64 + 64, 0:NXK],
                           idn_bf[hh * 64:hh * 64 + 64, hh * 64:hh * 64 + 64]],
                          [pst])
                    h = of * 2 + hh
                    pr.op("vector",
                          lambda v, h=h, pst=pst: v.tensor_copy(
                              vx_tr[:, h * 64:(h + 1) * 64], pst),
                          [pst], [vx_tr[:, h * 64:(h + 1) * 64]])
            for e in ENGINES:
                pr.barrier_to(e)

        with ExitStack() as esb:
            ecb = esb.enter_context
            ta_m = ecb(nc.sbuf_tensor(pfx + "ta_m", [128, 2 * 256], F32))
            ta_a = ecb(nc.sbuf_tensor(pfx + "ta_a", [128, 2 * 256], F32))
            sc3 = ecb(nc.sbuf_tensor(pfx + "sc3", [128, 256], F32))
            att3 = ecb(nc.sbuf_tensor(pfx + "att3", [128, 256], BF16))
            attT3 = ecb(nc.sbuf_tensor(pfx + "attT3", [128, 2 * 128], BF16))
            mr3 = ecb(nc.sbuf_tensor(pfx + "mr3", [128, 2], F32))
            rs3 = ecb(nc.sbuf_tensor(pfx + "rs3", [128, 2], F32))
            rc3 = ecb(nc.sbuf_tensor(pfx + "rc3", [128, 2], F32))
            for h in range(16):
                for pt in range(2):
                    pr.dma("scalar", ta_m[:, pt * 256:(pt + 1) * 256],
                           tamul[h * 256 + pt * 128: h * 256 + (pt + 1) * 128, :])
                    pr.dma("scalar", ta_a[:, pt * 256:(pt + 1) * 256],
                           taadd[h * 256 + pt * 128: h * 256 + (pt + 1) * 128, :])
                hf, hh = h // 2, h % 2
                for seg in range(T // 256):
                    for qt in range(2):
                        q0 = seg * 256 + qt * 128
                        ps = em.take_psum()
                        pr.op("tensor",
                              lambda t, ps=ps, hf=hf, hh=hh, q0=q0, seg=seg: t.matmul(
                                  ps[:, 0:256],
                                  qr_bf[hh * 64:hh * 64 + 64, hf * T + q0: hf * T + q0 + 128],
                                  kr_bf[hh * 64:hh * 64 + 64, hf * T + seg * 256: hf * T + seg * 256 + 256],
                                  start=True, stop=True),
                              [qr_bf[hh * 64:hh * 64 + 64, hf * T + q0: hf * T + q0 + 128],
                               kr_bf[hh * 64:hh * 64 + 64, hf * T + seg * 256: hf * T + seg * 256 + 256]],
                              [ps[:, 0:256]])
                        pr.op("vector",
                              lambda v, ps=ps, qt=qt: v.tensor_mul(
                                  sc3[:, 0:256], ps[:, 0:256], ta_m[:, qt * 256:(qt + 1) * 256]),
                              [ps[:, 0:256], ta_m[:, qt * 256:(qt + 1) * 256]], [sc3[:, 0:256]])
                        pr.op("vector",
                              lambda v, qt=qt: v.tensor_add(
                                  sc3[:, 0:256], sc3[:, 0:256], ta_a[:, qt * 256:(qt + 1) * 256]),
                              [sc3[:, 0:256], ta_a[:, qt * 256:(qt + 1) * 256]], [sc3[:, 0:256]])
                        pr.op("vector", lambda v: v.reduce_max(mr3[:, 0:1], sc3[:, 0:256], axis=AX.X, negate=True),
                              [sc3[:, 0:256]], [mr3[:, 0:1]])
                        pr.op("scalar", lambda s: s.activation(sc3[:, 0:256], sc3[:, 0:256], AF.Exp,
                                                               bias=mr3[:, 0:1], accum_out=rs3[:, 0:1]),
                              [sc3[:, 0:256], mr3[:, 0:1]], [sc3[:, 0:256], rs3[:, 0:1]])
                        pr.op("vector", lambda v: v.reciprocal(rc3[:, 0:1], rs3[:, 0:1]),
                              [rs3[:, 0:1]], [rc3[:, 0:1]])
                        pr.op("vector", lambda v: v.tensor_scalar(att3[:, 0:256], sc3[:, 0:256],
                                                                  rc3[:, 0:1], None, ALU.mult),
                              [sc3[:, 0:256], rc3[:, 0:1]], [att3[:, 0:256]])
                        for kt in range(2):
                            pst = psbf[:, kt * 128:(kt + 1) * 128]
                            pr.op("tensor",
                                  lambda t, kt=kt, pst=pst: t.transpose(
                                      pst, att3[:, kt * 128:(kt + 1) * 128], idn_bf[:, :]),
                                  [att3[:, kt * 128:(kt + 1) * 128], idn_bf[:, :]], [pst])
                            pr.op("vector",
                                  lambda v, kt=kt, pst=pst: v.tensor_copy(
                                      attT3[:, kt * 128:(kt + 1) * 128], pst),
                                  [pst], [attT3[:, kt * 128:(kt + 1) * 128]])
                        psA = em.take_psum()
                        em.mm_group(
                            [(v_tr[:, (seg * 2 + kt) * 1024 + h * 64: (seg * 2 + kt) * 1024 + h * 64 + 64],
                              attT3[:, kt * 128:(kt + 1) * 128]) for kt in range(2)],
                            psA[0:64, 0:128])
                        pr.op("vector", lambda v, hf=hf, hh=hh, q0=q0, psA=psA: v.tensor_copy(
                                  o_bf[hh * 64:hh * 64 + 64, hf * T + q0: hf * T + q0 + 128],
                                  psA[0:64, 0:128]),
                              [psA[0:64, 0:128]],
                              [o_bf[hh * 64:hh * 64 + 64, hf * T + q0: hf * T + q0 + 128]])
            for e in ENGINES:
                pr.barrier_to(e)

        with ExitStack() as esc:
            ecc = esc.enter_context
            qbx = ecc(nc.sbuf_tensor(pfx + "qbx", [128, 16], BF16))
            tb_m = ecc(nc.sbuf_tensor(pfx + "tb_m", [16, KB], F32))
            tb_a = ecc(nc.sbuf_tensor(pfx + "tb_a", [16, KB], F32))
            scb = ecc(nc.sbuf_tensor(pfx + "scb", [16, KB], F32))
            attb = ecc(nc.sbuf_tensor(pfx + "attb", [16, KB], BF16))
            attbT = ecc(nc.sbuf_tensor(pfx + "attbT", [128, 17 * 16], BF16))
            mrb = ecc(nc.sbuf_tensor(pfx + "mrb", [16, 2], F32))
            rsb = ecc(nc.sbuf_tensor(pfx + "rsb", [16, 2], F32))
            rcb2 = ecc(nc.sbuf_tensor(pfx + "rcb2", [16, 2], F32))
            for h in range(16):
                hf, hh = h // 2, h % 2
                pr.dma("scalar", tb_m[:, :], tbmul[h * 16:(h + 1) * 16, :])
                pr.dma("scalar", tb_a[:, :], tbadd[h * 16:(h + 1) * 16, :])
                pr.dma("gpsimd", qbx[hh * 64:hh * 64 + 64, 0:16],
                       qr_bf[hh * 64:hh * 64 + 64, hf * T + h::128][:, 0:16],
                       allow_slow_non_contiguous=True)
                for s0 in range(0, T, 512):
                    ps = em.take_psum()
                    pr.op("tensor",
                          lambda t, ps=ps, hf=hf, hh=hh, s0=s0: t.matmul(
                              ps[0:16, 0:512], qbx[hh * 64:hh * 64 + 64, 0:16],
                              kr_bf[hh * 64:hh * 64 + 64, hf * T + s0: hf * T + s0 + 512],
                              start=True, stop=True),
                          [qbx[hh * 64:hh * 64 + 64, 0:16],
                           kr_bf[hh * 64:hh * 64 + 64, hf * T + s0: hf * T + s0 + 512]],
                          [ps[0:16, 0:512]])
                    pr.op("vector",
                          lambda v, ps=ps, s0=s0: v.tensor_mul(
                              scb[:, s0:s0 + 512], ps[0:16, 0:512], tb_m[:, s0:s0 + 512]),
                          [ps[0:16, 0:512], tb_m[:, s0:s0 + 512]], [scb[:, s0:s0 + 512]])
                    pr.op("vector",
                          lambda v, s0=s0: v.tensor_add(
                              scb[:, s0:s0 + 512], scb[:, s0:s0 + 512], tb_a[:, s0:s0 + 512]),
                          [scb[:, s0:s0 + 512], tb_a[:, s0:s0 + 512]], [scb[:, s0:s0 + 512]])
                psx = em.take_psum()
                pr.op("tensor",
                      lambda t, psx=psx, hf=hf, hh=hh: t.matmul(
                          psx[0:16, 0:NXK], qbx[hh * 64:hh * 64 + 64, 0:16],
                          kx_bf[hh * 64:hh * 64 + 64, hf * NXK: hf * NXK + NXK],
                          start=True, stop=True),
                      [qbx[hh * 64:hh * 64 + 64, 0:16], kx_bf[hh * 64:hh * 64 + 64, hf * NXK: hf * NXK + NXK]],
                      [psx[0:16, 0:NXK]])
                pr.op("vector",
                      lambda v, psx=psx: v.tensor_mul(
                          scb[:, T:T + NXK], psx[0:16, 0:NXK], tb_m[:, T:T + NXK]),
                      [psx[0:16, 0:NXK], tb_m[:, T:T + NXK]], [scb[:, T:T + NXK]])
                pr.op("vector",
                      lambda v: v.tensor_add(scb[:, T:T + NXK], scb[:, T:T + NXK], tb_a[:, T:T + NXK]),
                      [scb[:, T:T + NXK], tb_a[:, T:T + NXK]], [scb[:, T:T + NXK]])
                pr.op("vector", lambda v: v.reduce_max(mrb[:, 0:1], scb[:, 0:KB], axis=AX.X, negate=True),
                      [scb[:, 0:KB]], [mrb[:, 0:1]])
                pr.op("scalar", lambda s: s.activation(scb[:, 0:KB], scb[:, 0:KB], AF.Exp,
                                                       bias=mrb[:, 0:1], accum_out=rsb[:, 0:1]),
                      [scb[:, 0:KB], mrb[:, 0:1]], [scb[:, 0:KB], rsb[:, 0:1]])
                pr.op("vector", lambda v: v.reciprocal(rcb2[:, 0:1], rsb[:, 0:1]),
                      [rsb[:, 0:1]], [rcb2[:, 0:1]])
                pr.op("vector", lambda v: v.tensor_scalar(attb[:, 0:KB], scb[:, 0:KB],
                                                          rcb2[:, 0:1], None, ALU.mult),
                      [scb[:, 0:KB], rcb2[:, 0:1]], [attb[:, 0:KB]])
                for kt in range(17):
                    wk_ = 128 if kt < 16 else NXK
                    pst = psbf[:, (kt % 8) * 128:(kt % 8) * 128 + 16]
                    pr.op("tensor",
                          lambda t, kt=kt, wk_=wk_, pst=pst: t.transpose(
                              pst[0:wk_, 0:16], attb[0:16, kt * 128:kt * 128 + wk_],
                              idn_bf[0:16, 0:16]),
                          [attb[0:16, kt * 128:kt * 128 + wk_], idn_bf[0:16, 0:16]],
                          [pst[0:wk_, 0:16]])
                    pr.op("vector",
                          lambda v, kt=kt, wk_=wk_, pst=pst: v.tensor_copy(
                              attbT[0:wk_, kt * 16:(kt + 1) * 16], pst[0:wk_, 0:16]),
                          [pst[0:wk_, 0:16]], [attbT[0:wk_, kt * 16:(kt + 1) * 16]])
                psB = em.take_psum()
                pairs = [(v_tr[:, tb * 1024 + h * 64: tb * 1024 + h * 64 + 64],
                          attbT[:, tb * 16:(tb + 1) * 16]) for tb in range(16)]
                pairs.append((vx_tr[0:32, h * 64:(h + 1) * 64], attbT[0:32, 16 * 16:17 * 16]))
                em.mm_group(pairs, psB[0:64, 0:16])
                pr.op("vector",
                      lambda v, hf=hf, hh=hh, h=h, psB=psB: v.tensor_copy(
                          o_bf[hh * 64:hh * 64 + 64, hf * T + h::128][:, 0:16], psB[0:64, 0:16]),
                      [psB[0:64, 0:16]],
                      [o_bf[hh * 64:hh * 64 + 64, hf * T + h: hf * T + h + 128 * 15 + 1]])
            for e in ENGINES:
                pr.barrier_to(e)

        with ExitStack() as esd:
            ecd = esd.enter_context
            wot_bf = ecd(nc.sbuf_tensor(pfx + "wot_bf", [128, 8 * 128], BF16))
            bocl = ecd(nc.sbuf_tensor(pfx + "bocl", [128, 8], F32))
            x3p = ecd(nc.sbuf_tensor(pfx + "x3p", [128, 8 * 512], F32))
            xrl3 = ecd(nc.sbuf_tensor(pfx + "xrl3", [128, 512], F32))
            sq3 = ecd(nc.sbuf_tensor(pfx + "sq3", [128, 2 * 512], F32))
            st3 = ecd(nc.sbuf_tensor(pfx + "st3", [1, 4 * 512], F32))
            mu3 = ecd(nc.sbuf_tensor(pfx + "mu3", [128, 512], F32))
            rst3 = ecd(nc.sbuf_tensor(pfx + "rst3", [128, 512], F32))
            pr.dma("sync", bocl[:, :], boc[:, :])
            for s0 in range(0, T, 512):
                for of in range(8):
                    for k in range(8):
                        pr.dma("gpsimd", wot_bf[:, k * 128:(k + 1) * 128],
                               wo_t[k * 128:(k + 1) * 128, of * 128:(of + 1) * 128])
                    ps = em.take_psum()
                    em.mm_group(
                        [(wot_bf[:, k * 128:(k + 1) * 128], o_bf[:, k * T + s0:k * T + s0 + 512])
                         for k in range(8)],
                        ps[:, 0:512])
                    pr.dma("scalar", xrl3[:, :], xin_t[of * 128:(of + 1) * 128, s0:s0 + 512])
                    pr.op("vector",
                          lambda v, of=of, ps=ps: v.tensor_scalar(
                              x3p[:, of * 512:(of + 1) * 512], ps[:, 0:512],
                              bocl[:, of:of + 1], None, ALU.add),
                          [ps[:, 0:512], bocl[:, of:of + 1]], [x3p[:, of * 512:(of + 1) * 512]])
                    pr.op("vector",
                          lambda v, of=of: v.tensor_add(
                              x3p[:, of * 512:(of + 1) * 512], x3p[:, of * 512:(of + 1) * 512],
                              xrl3[:, :]),
                          [x3p[:, of * 512:(of + 1) * 512], xrl3[:, :]],
                          [x3p[:, of * 512:(of + 1) * 512]])
                emit_ln_feature(pr, em, x3p, 512, st3, sq3, mu3, rst3, nfeat=8)
                for of in range(8):
                    sl = x3p[:, of * 512:(of + 1) * 512]
                    pr.op("vector", lambda v, sl=sl: v.tensor_sub(sl, sl, mu3[:, 0:512]),
                          [sl, mu3[:, 0:512]], [sl])
                    pr.op("vector", lambda v, sl=sl: v.tensor_mul(sl, sl, rst3[:, 0:512]),
                          [sl, rst3[:, 0:512]], [sl])
                    pr.dma("gpsimd", o_x3[of * 128:(of + 1) * 128, s0:s0 + 512], sl)
        for e in ENGINES:
            pr.barrier_to(e)
        print(f"phase3 program: {pr.n_ops} ops, {pr.n_waits} waits")
    return nc


NEXP = 10



def build_phase3():
    nc = bass.Bass("TRN2", target_bir_lowering=False, debug=False)
    KB = T + NXK

    xin_t = nc.dram_tensor("xin_t", [DM, T], F32, kind="ExternalInput").ap()
    x2x_t = nc.dram_tensor("x2x_t", [DM, NXK], F32, kind="ExternalInput").ap()
    wq_t = nc.dram_tensor("wq_t", [DM, DM], F32, kind="ExternalInput").ap()
    wk_t = nc.dram_tensor("wk_t", [DM, DM], F32, kind="ExternalInput").ap()
    wv_t = nc.dram_tensor("wv_t", [DM, DM], F32, kind="ExternalInput").ap()
    wo_t = nc.dram_tensor("wo_t", [DM, DM], F32, kind="ExternalInput").ap()
    bqc = nc.dram_tensor("bqc", [128, 8], F32, kind="ExternalInput").ap()
    bkc = nc.dram_tensor("bkc", [128, 8], F32, kind="ExternalInput").ap()
    bvc = nc.dram_tensor("bvc", [128, 8], F32, kind="ExternalInput").ap()
    boc = nc.dram_tensor("boc", [128, 8], F32, kind="ExternalInput").ap()
    cs_t = nc.dram_tensor("cs_t", [DM, T], F32, kind="ExternalInput").ap()
    ssg_t = nc.dram_tensor("ssg_t", [DM, T], F32, kind="ExternalInput").ap()
    csx_t = nc.dram_tensor("csx_t", [DM, NXK], F32, kind="ExternalInput").ap()
    ssgx_t = nc.dram_tensor("ssgx_t", [DM, NXK], F32, kind="ExternalInput").ap()
    tamul = nc.dram_tensor("tamul", [16 * 256, 256], F32, kind="ExternalInput").ap()
    taadd = nc.dram_tensor("taadd", [16 * 256, 256], F32, kind="ExternalInput").ap()
    tbmul = nc.dram_tensor("tbmul", [16 * 16, KB], F32, kind="ExternalInput").ap()
    tbadd = nc.dram_tensor("tbadd", [16 * 16, KB], F32, kind="ExternalInput").ap()
    idn_in = nc.dram_tensor("idn128", [128, 128], F32, kind="ExternalInput").ap()

    o_x3 = nc.dram_tensor("x3_t", [DM, T], F32, kind="ExternalOutput").ap()

    with ExitStack() as _es0:
        _ec0 = _es0.enter_context
        blk = _ec0(nc.Block())
        pr = Prog(nc, _es0)
        em = Emitter(nc, _es0, pr, n_psum=7)
        psbf = _ec0(nc.psum_tensor("psbf", [128, 1024], BF16))
        emit_phase3(nc, pr, em, psbf, xin_t, x2x_t, wq_t, wk_t, wv_t, wo_t, bqc, bkc, bvc, boc, cs_t, ssg_t, csx_t, ssgx_t, tamul, taadd, tbmul, tbadd, idn_in, o_x3)
    return nc


def prep_phase4(inputs):
    w1 = _np_f32(inputs["L1_ffn_w1"])
    w3 = _np_f32(inputs["L1_ffn_w3"])
    w2 = _np_f32(inputs["L1_ffn_w2"])
    sw1 = _np_f32(inputs["L1_ffn_sw1"])
    sw3 = _np_f32(inputs["L1_ffn_sw3"])
    sw2 = _np_f32(inputs["L1_ffn_sw2"])
    m1 = np.concatenate([w1, sw1], axis=0).transpose(0, 2, 1)
    m3 = np.concatenate([w3, sw3], axis=0).transpose(0, 2, 1)
    m2 = np.concatenate([w2, sw2], axis=0).transpose(0, 2, 1)
    sel8 = np.zeros((8, 8 * 128), np.float32)
    for e in range(8):
        sel8[e, e * 128:(e + 1) * 128] = 1.0
    shared = {
        "m1_t": np.ascontiguousarray(m1.reshape(NEXP * 1024, 2048)),
        "m3_t": np.ascontiguousarray(m3.reshape(NEXP * 1024, 2048)),
        "m2_t": np.ascontiguousarray(m2.reshape(NEXP * 2048, 1024)),
        "rw_t": np.ascontiguousarray(_np_f32(inputs["L1_ffn_router_w"]).T),
        "ebc": np.ascontiguousarray(np.pad(_np_f32(inputs["L1_ffn_expert_bias"]),
                                           (0, 120)).reshape(128, 1)),
        "sel8": sel8,
    }
    return shared


def emit_phase4(nc, pr, em, psbf, xin_t, m1_t, m3_t, m2_t, rw_t, ebc, sel8, idn_in, o_out, pfx=""):
    with ExitStack() as es:
        ec = es.enter_context

        xin_bf = ec(nc.sbuf_tensor(pfx + "p4xin", [128, 8 * T], BF16))
        acc = ec(nc.sbuf_tensor(pfx + "p4acc", [128, 8 * T], F32))
        xld = ec(nc.sbuf_tensor(pfx + "p4xld", [128, 512], F32))
        idn_f = ec(nc.sbuf_tensor(pfx + "p4idn", [128, 128], F32))
        pr.dma("gpsimd", idn_f[:, :], idn_in[:, :])
        for k in range(8):
            for s0 in range(0, T, 512):
                pr.dma("sync", xld[:, :], xin_t[k * 128:(k + 1) * 128, s0:s0 + 512])
                pr.op("vector",
                      lambda v, k=k, s0=s0: v.tensor_copy(
                          xin_bf[:, k * T + s0:k * T + s0 + 512], xld[:, :]),
                      [xld[:, :]], [xin_bf[:, k * T + s0:k * T + s0 + 512]])

        crT = ec(nc.sbuf_tensor(pfx + "p4crT", [8, T], F32))
        with ExitStack() as esr:
            ecr = esr.enter_context
            rwt = ecr(nc.sbuf_tensor(pfx + "p4rw", [128, 8], BF16))
            ebt = ecr(nc.sbuf_tensor(pfx + "p4eb", [128, 1], F32))
            lgT = ecr(nc.sbuf_tensor(pfx + "p4lgT", [8, T], F32))
            lgm = ecr(nc.sbuf_tensor(pfx + "p4lgm", [128, 16 * 8], F32))
            lg2 = ecr(nc.sbuf_tensor(pfx + "p4lg2", [128, 8], F32))
            is1 = ecr(nc.sbuf_tensor(pfx + "p4is1", [128, 8], F32))
            is2 = ecr(nc.sbuf_tensor(pfx + "p4is2", [128, 8], F32))
            mcol = ecr(nc.sbuf_tensor(pfx + "p4m", [128, 4], F32))
            rwk = ecr(nc.sbuf_tensor(pfx + "p4rwk", [128, 8 * 8], F32))
            xf32 = [ecr(nc.sbuf_tensor(pfx + f"p4xf{i}", [128, 512], F32)) for i in range(2)]
            for k in range(8):
                pr.dma("gpsimd", rwk[:, k * 8:(k + 1) * 8], rw_t[k * 128:(k + 1) * 128, :])
            pr.dma("sync", ebt[:, :], ebc[:, :])
            for s0 in range(0, T, 512):
                ps = em.take_psum()
                for k in range(8):
                    xt_ = xf32[k % 2]
                    pr.dma("sync", xt_[:, :], xin_t[k * 128:(k + 1) * 128, s0:s0 + 512])
                    pr.op("tensor",
                          lambda t, ps=ps, k=k, xt_=xt_: t.matmul(
                              ps[0:8, 0:512], rwk[:, k * 8:(k + 1) * 8], xt_[:, :],
                              start=(k == 0), stop=(k == 7)),
                          [rwk[:, k * 8:(k + 1) * 8], xt_[:, :]],
                          [ps[0:8, 0:512]],
                          check_writes=[ps[0:8, 0:512]] if k == 0 else [])
                pr.op("vector",
                      lambda v, ps=ps, s0=s0: v.tensor_scalar(
                          lgT[:, s0:s0 + 512], ps[0:8, 0:512], ebt[0:8, 0:1], None, ALU.add),
                      [ps[0:8, 0:512], ebt[0:8, 0:1]], [lgT[:, s0:s0 + 512]])
            for tt in range(T // 128):
                ps = em.take_psum()
                pr.op("tensor",
                      lambda t, tt=tt, ps=ps: t.transpose(
                          ps[:, 0:8], lgT[0:8, tt * 128:(tt + 1) * 128], idn_f[0:8, 0:8]),
                      [lgT[0:8, tt * 128:(tt + 1) * 128], idn_f[0:8, 0:8]], [ps[:, 0:8]])
                lgs = lgm[:, tt * 8:(tt + 1) * 8]
                pr.op("vector", lambda v, ps=ps, lgs=lgs: v.tensor_copy(lgs, ps[:, 0:8]),
                      [ps[:, 0:8]], [lgs])
                pr.op("vector", lambda v, lgs=lgs: v.reduce_max(mcol[:, 0:1], lgs, axis=AX.X),
                      [lgs], [mcol[:, 0:1]])
                pr.op("vector", lambda v, lgs=lgs: v.tensor_scalar(is1[:, :], lgs, mcol[:, 0:1], None, ALU.is_ge),
                      [lgs, mcol[:, 0:1]], [is1[:, :]])
                pr.op("vector", lambda v, lgs=lgs: v.scalar_tensor_tensor(
                          lg2[:, :], is1[:, :], -1e38, lgs, ALU.mult, ALU.add),
                      [is1[:, :], lgs], [lg2[:, :]])
                pr.op("vector", lambda v: v.reduce_max(mcol[:, 1:2], lg2[:, :], axis=AX.X),
                      [lg2[:, :]], [mcol[:, 1:2]])
                pr.op("vector", lambda v: v.tensor_scalar(is2[:, :], lg2[:, :], mcol[:, 1:2], None, ALU.is_ge),
                      [lg2[:, :], mcol[:, 1:2]], [is2[:, :]])
                pr.op("vector", lambda v: v.tensor_sub(mcol[:, 2:3], mcol[:, 0:1], mcol[:, 1:2]),
                      [mcol[:, 0:1], mcol[:, 1:2]], [mcol[:, 2:3]])
                pr.op("scalar", lambda s: s.activation(mcol[:, 2:3], mcol[:, 2:3], AF.Sigmoid),
                      [mcol[:, 2:3]], [mcol[:, 2:3]])
                pr.op("vector", lambda v: v.tensor_scalar(mcol[:, 3:4], mcol[:, 2:3], -1.0, 1.0, ALU.mult, ALU.add),
                      [mcol[:, 2:3]], [mcol[:, 3:4]])
                pr.op("vector", lambda v: v.tensor_scalar(is1[:, :], is1[:, :], mcol[:, 2:3], None, ALU.mult),
                      [is1[:, :], mcol[:, 2:3]], [is1[:, :]])
                pr.op("vector", lambda v: v.scalar_tensor_tensor(
                          is1[:, :], is2[:, :], mcol[:, 3:4], is1[:, :], ALU.mult, ALU.add),
                      [is2[:, :], mcol[:, 3:4], is1[:, :]], [is1[:, :]])
                ps2 = em.take_psum()
                pr.op("tensor",
                      lambda t, ps2=ps2: t.transpose(ps2[0:8, 0:128], is1[:, 0:8], idn_f[:, :]),
                      [is1[:, 0:8], idn_f[:, :]], [ps2[0:8, 0:128]])
                pr.op("vector",
                      lambda v, tt=tt, ps2=ps2: v.tensor_copy(crT[:, tt * 128:(tt + 1) * 128], ps2[0:8, 0:128]),
                      [ps2[0:8, 0:128]], [crT[:, tt * 128:(tt + 1) * 128]])
            for e in ENGINES:
                pr.barrier_to(e)

        ese = es.enter_context(ExitStack())
        ece = ese.enter_context
        sel8t = ece(nc.sbuf_tensor(pfx + "p4sel8", [8, 8 * 128], F32))
        cbe = ece(nc.sbuf_tensor(pfx + "p4cbe", [128, T], F32))
        hscr = ece(nc.sbuf_tensor(pfx + "p4hscr", [128, 512], F32))
        h3scr = ece(nc.sbuf_tensor(pfx + "p4h3", [128, 512], F32))
        h_bf = ece(nc.sbuf_tensor(pfx + "p4h", [128, 16 * T], BF16))
        wpool = [ece(nc.sbuf_tensor(pfx + f"p4w{i}", [128, 8 * 128], BF16)) for i in range(4)]
        w2pool = [ece(nc.sbuf_tensor(pfx + f"p4w2_{i}", [128, 16 * 128], BF16)) for i in range(2)]
        pr.dma("sync", sel8t[:, :], sel8[:, :])
        for e in range(NEXP):
            routed = e < 8
            if routed:
                for s0 in range(0, T, 512):
                    ps = em.take_psum()
                    pr.op("tensor",
                          lambda t, ps=ps, e=e, s0=s0: t.matmul(
                              ps[:, 0:512], sel8t[:, e * 128:(e + 1) * 128],
                              crT[0:8, s0:s0 + 512], start=True, stop=True),
                          [sel8t[:, e * 128:(e + 1) * 128], crT[0:8, s0:s0 + 512]],
                          [ps[:, 0:512]])
                    pr.op("vector",
                          lambda v, ps=ps, s0=s0: v.tensor_copy(cbe[:, s0:s0 + 512], ps[:, 0:512]),
                          [ps[:, 0:512]], [cbe[:, s0:s0 + 512]])
            for j in range(16):
                w1t = wpool[(2 * j) % 4]
                w3t = wpool[(2 * j + 1) % 4]
                for k in range(8):
                    pr.dma("gpsimd", w1t[:, k * 128:(k + 1) * 128],
                           m1_t[e * 1024 + k * 128: e * 1024 + (k + 1) * 128, j * 128:(j + 1) * 128])
                    pr.dma("gpsimd", w3t[:, k * 128:(k + 1) * 128],
                           m3_t[e * 1024 + k * 128: e * 1024 + (k + 1) * 128, j * 128:(j + 1) * 128])
                for s0 in range(0, T, 512):
                    ps1 = em.take_psum()
                    em.mm_group(
                        [(w1t[:, k * 128:(k + 1) * 128], xin_bf[:, k * T + s0:k * T + s0 + 512])
                         for k in range(8)], ps1[:, 0:512])
                    ps3 = em.take_psum()
                    em.mm_group(
                        [(w3t[:, k * 128:(k + 1) * 128], xin_bf[:, k * T + s0:k * T + s0 + 512])
                         for k in range(8)], ps3[:, 0:512])
                    pr.op("scalar", lambda s, ps1=ps1: s.activation(hscr[:, 0:512], ps1[:, 0:512], AF.Silu),
                          [ps1[:, 0:512]], [hscr[:, 0:512]])
                    pr.op("vector", lambda v, ps3=ps3: v.tensor_mul(h3scr[:, 0:512], hscr[:, 0:512], ps3[:, 0:512]),
                          [hscr[:, 0:512], ps3[:, 0:512]], [h3scr[:, 0:512]])
                    if routed:
                        pr.op("vector",
                              lambda v, j=j, s0=s0: v.tensor_mul(
                                  h_bf[:, j * T + s0:j * T + s0 + 512], h3scr[:, 0:512], cbe[:, s0:s0 + 512]),
                              [h3scr[:, 0:512], cbe[:, s0:s0 + 512]],
                              [h_bf[:, j * T + s0:j * T + s0 + 512]])
                    else:
                        pr.op("vector",
                              lambda v, j=j, s0=s0: v.tensor_copy(
                                  h_bf[:, j * T + s0:j * T + s0 + 512], h3scr[:, 0:512]),
                              [h3scr[:, 0:512]], [h_bf[:, j * T + s0:j * T + s0 + 512]])
            for of in range(8):
                w2t = w2pool[of % 2]
                for k in range(16):
                    pr.dma("gpsimd", w2t[:, k * 128:(k + 1) * 128],
                           m2_t[e * 2048 + k * 128: e * 2048 + (k + 1) * 128, of * 128:(of + 1) * 128])
                for s0 in range(0, T, 512):
                    ps = em.take_psum()
                    em.mm_group(
                        [(w2t[:, k * 128:(k + 1) * 128], h_bf[:, k * T + s0:k * T + s0 + 512])
                         for k in range(16)], ps[:, 0:512])
                    if e == 0:
                        pr.op("vector",
                              lambda v, of=of, s0=s0, ps=ps: v.tensor_copy(
                                  acc[:, of * T + s0: of * T + s0 + 512], ps[:, 0:512]),
                              [ps[:, 0:512]], [acc[:, of * T + s0: of * T + s0 + 512]])
                    else:
                        pr.op("vector",
                              lambda v, of=of, s0=s0, ps=ps: v.tensor_add(
                                  acc[:, of * T + s0: of * T + s0 + 512],
                                  acc[:, of * T + s0: of * T + s0 + 512], ps[:, 0:512]),
                              [ps[:, 0:512], acc[:, of * T + s0: of * T + s0 + 512]],
                              [acc[:, of * T + s0: of * T + s0 + 512]])

        for e in ENGINES:
            pr.barrier_to(e)
        ese.close()

        sq4 = ec(nc.sbuf_tensor(pfx + "p4sq", [128, 2 * 512], F32))
        st4 = ec(nc.sbuf_tensor(pfx + "p4st", [1, 4 * 512], F32))
        mu4 = ec(nc.sbuf_tensor(pfx + "p4mu", [128, 512], F32))
        rst4 = ec(nc.sbuf_tensor(pfx + "p4rst", [128, 512], F32))
        lnb4 = ec(nc.sbuf_tensor(pfx + "p4lnb", [128, 8 * 512], F32))
        for s0 in range(0, T, 512):
            for of in range(8):
                pr.dma("scalar", xld[:, :], xin_t[of * 128:(of + 1) * 128, s0:s0 + 512])
                pr.op("vector",
                      lambda v, of=of, s0=s0: v.tensor_add(
                          lnb4[:, of * 512:(of + 1) * 512],
                          acc[:, of * T + s0: of * T + s0 + 512], xld[:, :]),
                      [acc[:, of * T + s0: of * T + s0 + 512], xld[:, :]],
                      [lnb4[:, of * 512:(of + 1) * 512]])
            emit_ln_feature(pr, em, lnb4, 512, st4, sq4, mu4, rst4, nfeat=8)
            for of in range(8):
                sl = lnb4[:, of * 512:(of + 1) * 512]
                pr.op("vector", lambda v, sl=sl: v.tensor_sub(sl, sl, mu4[:, 0:512]),
                      [sl, mu4[:, 0:512]], [sl])
                pr.op("vector", lambda v, sl=sl: v.tensor_mul(sl, sl, rst4[:, 0:512]),
                      [sl, rst4[:, 0:512]], [sl])
                pr.dma("gpsimd", o_out[of * 128:(of + 1) * 128, s0:s0 + 512], sl)
        for e in ENGINES:
            pr.barrier_to(e)
        print(f"phase4 program: {pr.n_ops} ops, {pr.n_waits} waits")
    return nc


MAGIC0 = 0x5A100000
MAGIC1 = 0x3C700000


def build_fused():
    nc = bass.Bass("TRN2", target_bir_lowering=False, debug=False)
    TH = T + 3
    KB = T + NXK

    def inp(name, shp):
        return nc.dram_tensor(name, shp, F32, kind="ExternalInput").ap()

    x_t = inp("x_t", [DM, TH])
    W_in = inp("W_in", [DM, 4096]); W_x = inp("W_x", [2048, 96])
    W_dt = inp("W_dt", [64, 2048]); W_out = inp("W_out", [2048, DM])
    conv_w = inp("conv_w", [128, 64]); conv_b = inp("conv_b", [128, 16])
    b_dt = inp("b_dt", [128, 16]); Dv = inp("Dv", [128, 16])
    selcat = inp("selcat", [32, 32 * 128])
    zeros256 = inp("zeros256", [128, 256]); zeros16 = inp("zeros16", [128, 16])
    selcol = inp("selcol", [128, 2])
    hsel = inp("hsel", [128, 1])
    pairinfo = nc.dram_tensor("pairinfo", [1, 4], I32, kind="ExternalInput").ap()
    xidx = nc.dram_tensor("xidx", [128, 20], I32, kind="ExternalInput").ap()
    fidx = nc.dram_tensor("fidx", [1, 2], I32, kind="ExternalInput").ap()
    p2w = {}
    for lay in (0, 1):
        sfx = f"_l{lay}"
        p2w[lay] = dict(
            wqkv=inp("wqkv" + sfx, [DM, 1536]), wo=inp("wo" + sfx, [512, DM]),
            w1p=inp("w1p" + sfx, [DM, 2 * HIDP]), w2p=inp("w2p" + sfx, [HIDP, DM]),
            b1p=inp("b1p" + sfx, [128, 2 * (HIDP // 128)]), b2c=inp("b2c" + sfx, [128, 8]),
            lgc=inp("lgc" + sfx, [128, 8]), lbc=inp("lbc" + sfx, [128, 8]),
            qs8=inp("qs8" + sfx, [128, 4]), ksc=inp("ksc" + sfx, [128, 4]))
    selheadT = inp("selheadT", [2, 128]); selhead = inp("selhead", [128, 2])
    csq = inp("csq", [DM // 8, T + QW]) if False else inp("csq", [128, T + QW])
    ssq = inp("ssq", [128, T + QW])
    csk = inp("csk", [128, T + 2 * QW]); ssk = inp("ssk", [128, T + 2 * QW])
    l0mask = inp("l0mask", [512, 1024]); l0mask_w1 = inp("l0mask_w1", [512, 1024])
    idn_in = inp("idn128", [128, 128])
    wq_t = inp("wq_t", [DM, DM]); wk_t = inp("wk_t", [DM, DM])
    wv_t = inp("wv_t", [DM, DM]); wo_t = inp("wo_t", [DM, DM])
    bqc = inp("bqc", [128, 8]); bkc = inp("bkc", [128, 8])
    bvc = inp("bvc", [128, 8]); boc = inp("boc", [128, 8])
    cs_t = inp("cs_t", [DM, T]); ssg_t = inp("ssg_t", [DM, T])
    csx_t = inp("csx_t", [DM, NXK]); ssgx_t = inp("ssgx_t", [DM, NXK])
    tamul = inp("tamul", [16 * 256, 256]); taadd = inp("taadd", [16 * 256, 256])
    tbmul = inp("tbmul", [16 * 16, KB]); tbadd = inp("tbadd", [16 * 16, KB])
    m1_t = inp("m1_t", [NEXP * 1024, 2048]); m3_t = inp("m3_t", [NEXP * 1024, 2048])
    m2_t = inp("m2_t", [NEXP * 2048, 1024])
    rw_t = inp("rw_t", [DM, 8]); ebc = inp("ebc", [128, 1]); sel8 = inp("sel8", [8, 8 * 128])

    out_t = nc.dram_tensor("out_t", [DM, T], F32, kind="ExternalOutput").ap()

    def internal(name, shp, dt=F32):
        return nc.dram_tensor(name, shp, dt).ap()

    x1sp = internal("x1sp", [DM, HALO2 + T])
    x2asp = internal("x2asp", [DM, HALO2 + T])
    x2sp = internal("x2sp", [DM, T])
    x3sp = internal("x3sp", [DM, T])
    x2xsp = internal("x2xsp", [DM, NXK])
    junksp = internal("junksp", [DM, T])
    hfin_i = internal("hfin_i", [128, 256])
    sumd_i = internal("sumd_i", [128, 16])
    carry_i = internal("carry_i", [128, 256])
    sext_i = internal("sext_i", [128, 16])
    sp_xi = internal("sp_xi", [2048, T]); sp_delta = internal("sp_delta", [2048, T])
    sp_res = internal("sp_res", [2048, T]); sp_C = internal("sp_C", [2048, T])
    sp_bmcm = internal("sp_bmcm", [32, T])
    sp_x2m0 = internal("sp_x2m0", [DM, T + 1]); sp_x2m1 = internal("sp_x2m1", [DM, T + 1])

    def shared_t(name, shp, dt=F32):
        return nc.dram_tensor(name, shp, dt, addr_space="Shared").ap()

    xg_sd = shared_t("xg_sd", [256, 16])
    xg_hf = shared_t("xg_hf", [256, 256])
    xg_h1 = shared_t("xg_h1", [2 * 8 * 128, HALO2])
    xg_h2 = shared_t("xg_h2", [2 * 8 * 128, HALO2])
    xg_xx = shared_t("xg_xx", [2 * 8 * 128, NXK])
    shflag = shared_t("shflag", [256, 16], I32)

    with ExitStack() as es:
        ec = es.enter_context
        blk = ec(nc.Block())
        pr = Prog(nc, es)
        em = Emitter(nc, es, pr, n_psum=7)
        psbf = ec(nc.psum_tensor("psbf", [128, 1024], BF16))
        g = nc.gpsimd

        myfl = ec(nc.sbuf_tensor("myfl", [128, 16], I32))
        flrd = ec(nc.sbuf_tensor("flrd", [128, 16], I32))
        hselt = ec(nc.sbuf_tensor("hselt", [128, 1], F32))
        himp = ec(nc.sbuf_tensor("himp", [128, HALO2], F32))
        xixt = ec(nc.sbuf_tensor("xixt", [128, 20], I32))
        psem = ec(nc.semaphore("pollsem"))
        r_rc = ec(g.register("r_rc"))
        r_tmp = ec(g.register("r_tmp"))
        r_cnt = ec(g.register("r_cnt"))
        r_v = ec(g.register("r_v"))
        pr.dma("gpsimd", xixt[:, :], xidx[:, :])
        pr.dma("sync", hselt[:, :], hsel[:, :])
        pr.barrier_to("gpsimd")
        g.reg_mov(r_cnt, 0)
        bar_state = {"k": 0}
        IOA = bass.IndirectOffsetOnAxis if hasattr(bass, "IndirectOffsetOnAxis") else None
        if IOA is None:
            from concourse.bass_core_rust import IndirectOffsetOnAxis as IOA

        def pair_barrier():
            bar_state["k"] += 1
            k = bar_state["k"]
            pr.barrier_to("gpsimd")
            pr.op("gpsimd", lambda gg: gg.memset(myfl[:, 0:8], MAGIC0 + k), [], [myfl[:, 0:8]])
            pr.op("gpsimd", lambda gg: gg.memset(myfl[:, 8:16], MAGIC1 + k), [], [myfl[:, 8:16]])
            pr.idma("gpsimd", shflag[:, :], IOA(xixt[:, 16:17], 0), myfl[:, :], None,
                    extra_reads=[xixt[:, 16:17]])
            g.reg_mov(r_rc, 1)
            with g.While(r_rc):
                g.indirect_dma_start(flrd[:, :], None, shflag[:, :],
                                     IOA(xixt[:, 17:18], 0)).then_inc(psem, 16)
                g.reg_add(r_cnt, r_cnt, 16)
                g.wait_ge(psem, r_cnt)
                g.reg_load(r_v, flrd[0:1, 0:1])
                g.reg_alu(r_rc, r_v, MAGIC0 + k, ALU.not_equal)
                g.reg_load(r_v, flrd[0:1, 8:9])
                g.reg_alu(r_tmp, r_v, MAGIC1 + k, ALU.not_equal)
                g.reg_alu(r_rc, r_rc, r_tmp, ALU.bitwise_or)
            pr._record([], [flrd[0:1, :]], "gpsimd", pr.count["gpsimd"])

        def export_rows(dst_shared, src_ap_fn, width):
            for of in range(8):
                pr.dma("gpsimd", himp[:, 0:width], src_ap_fn(of))
                pr.idma("gpsimd", dst_shared[:, :], IOA(xixt[:, of:of + 1], 0),
                        himp[:, 0:width], None, extra_reads=[xixt[:, of:of + 1]])

        def import_rows(src_shared, dst_dram, width, col0):
            for of in range(8):
                pr.idma("gpsimd", himp[:, 0:width], None,
                        src_shared[:, :], IOA(xixt[:, 8 + of:9 + of], 0),
                        extra_reads=[xixt[:, 8 + of:9 + of]])
                pr.op("vector",
                      lambda v, w=width: v.tensor_scalar(himp[:, 0:w], himp[:, 0:w],
                                                         hselt[:, 0:1], None, ALU.mult),
                      [himp[:, 0:width], hselt[:, 0:1]], [himp[:, 0:width]])
                pr.dma("gpsimd", dst_dram[of * 128:(of + 1) * 128, col0:col0 + width],
                       himp[:, 0:width])

        emit_phase1(nc, pr, em, x_t, W_in, W_x, W_dt, W_out, conv_w, conv_b, b_dt,
                    Dv, zeros256, zeros16, selcat, junksp, hfin_i, sumd_i,
                    sp_xi, sp_delta, sp_res, sp_C, sp_bmcm,
                    do_passA=True, do_y=False, pfx="p1a")
        pr.dma("gpsimd", himp[:, 0:16], sumd_i[:, :])
        pr.idma("gpsimd", xg_sd[:, :], IOA(xixt[:, 16:17], 0), himp[:, 0:16], None,
                extra_reads=[xixt[:, 16:17]])
        pr.dma("gpsimd", himp[:, 16:272], hfin_i[:, :])
        pr.idma("gpsimd", xg_hf[:, :], IOA(xixt[:, 16:17], 0), himp[:, 16:272], None,
                extra_reads=[xixt[:, 16:17]])
        pair_barrier()
        pr.idma("gpsimd", himp[:, 0:16], None, xg_sd[:, :], IOA(xixt[:, 17:18], 0),
                extra_reads=[xixt[:, 17:18]])
        pr.dma("gpsimd", sext_i[:, :], himp[:, 0:16])
        pr.idma("gpsimd", himp[:, 16:272], None, xg_hf[:, :], IOA(xixt[:, 17:18], 0),
                extra_reads=[xixt[:, 17:18]])
        pr.dma("gpsimd", carry_i[:, :], himp[:, 16:272])
        emit_phase1(nc, pr, em, x_t, W_in, W_x, W_dt, W_out, conv_w, conv_b, b_dt,
                    Dv, carry_i, sext_i, selcat, x1sp[:, HALO2:], hfin_i, sumd_i,
                    sp_xi, sp_delta, sp_res, sp_C, sp_bmcm,
                    do_passA=False, do_y=True, carry_sel=selcol, pfx="p1b")
        export_rows(xg_h1, lambda of: x1sp[of * 128:(of + 1) * 128, T:T + HALO2], HALO2)
        pair_barrier()
        import_rows(xg_h1, x1sp, HALO2, 0)

        emit_phase2(nc, pr, em, psbf, x1sp, x1sp[:, HALO2:], p2w[0]["wqkv"], p2w[0]["wo"],
                    p2w[0]["w1p"], p2w[0]["w2p"], p2w[0]["b1p"], p2w[0]["b2c"],
                    p2w[0]["lgc"], p2w[0]["lbc"], p2w[0]["qs8"], p2w[0]["ksc"],
                    selheadT, selhead, csq, ssq, csk, ssk, l0mask, l0mask_w1,
                    idn_in, x2asp[:, HALO2:], junksp, sp_x2m0, pfx="A")
        export_rows(xg_h2, lambda of: x2asp[of * 128:(of + 1) * 128, T:T + HALO2], HALO2)
        pair_barrier()
        import_rows(xg_h2, x2asp, HALO2, 0)
        emit_phase2(nc, pr, em, psbf, x2asp, x1sp[:, HALO2:], p2w[1]["wqkv"], p2w[1]["wo"],
                    p2w[1]["w1p"], p2w[1]["w2p"], p2w[1]["b1p"], p2w[1]["b2c"],
                    p2w[1]["lgc"], p2w[1]["lbc"], p2w[1]["qs8"], p2w[1]["ksc"],
                    selheadT, selhead, csq, ssq, csk, ssk, l0mask, l0mask_w1,
                    idn_in, junksp, x2sp, sp_x2m1, pfx="B")
        for of in range(8):
            pr.dma("gpsimd", himp[:, 0:16], x2sp[of * 128:(of + 1) * 128, 0:16])
            pr.dma("gpsimd", himp[:, 16:32], x2sp[of * 128:(of + 1) * 128, 1024:1040])
            pr.idma("gpsimd", xg_xx[:, :], IOA(xixt[:, of:of + 1], 0), himp[:, 0:32], None,
                    extra_reads=[xixt[:, of:of + 1]])
        pair_barrier()
        import_rows(xg_xx, x2xsp, NXK, 0)

        emit_phase3(nc, pr, em, psbf, x2sp, x2xsp, wq_t, wk_t, wv_t, wo_t,
                    bqc, bkc, bvc, boc, cs_t, ssg_t, csx_t, ssgx_t,
                    tamul, taadd, tbmul, tbadd, idn_in, x3sp, pfx="C")
        emit_phase4(nc, pr, em, psbf, x3sp, m1_t, m3_t, m2_t, rw_t, ebc, sel8,
                    idn_in, out_t, pfx="D")
        for e in ENGINES:
            pr.barrier_to(e)
        print(f"fused program: {pr.n_ops} ops, {pr.n_waits} waits")
    return nc


def prep_fused(inputs):
    sh1, pc1 = prep_phase1(inputs)
    sh3, pc3 = prep_phase3(inputs)
    sh4 = prep_phase4(inputs)
    idn = np.eye(128, dtype=np.float32)
    maps = []
    p2l = [prep_phase2(inputs, 0), prep_phase2(inputs, 1)]
    for c in range(NCORES):
        b, h = c // 2, c % 2
        m = {}
        m.update(sh1); m.update(pc1[c])
        m.pop("carry", None); m.pop("sext", None)
        m["zeros256"] = np.zeros((128, 256), np.float32)
        m["zeros16"] = np.zeros((128, 16), np.float32)
        sc = np.zeros((128, 2), np.float32)
        sc[:, 0] = float(h)
        sc[:, 1] = float(1 - h)
        m["selcol"] = sc
        m["hsel"] = np.full((128, 1), float(h), np.float32)
        m["pairinfo"] = np.array([[h, 1 - h, 0, 0]], np.int32)
        xidx = np.zeros((128, 20), np.int32)
        pp = np.arange(128)
        for of in range(8):
            xidx[:, of] = h * 1024 + of * 128 + pp
            xidx[:, 8 + of] = (1 - h) * 1024 + of * 128 + pp
        xidx[:, 16] = h * 128 + pp
        xidx[:, 17] = (1 - h) * 128 + pp
        m["xidx"] = xidx
        m["fidx"] = np.array([[h, 1 - h]], np.int32)
        for lay in (0, 1):
            sh2, pc2 = p2l[lay]
            for k in ("wqkv", "wo", "w1p", "w2p", "b1p", "b2c", "lgc", "lbc", "qs8", "ksc"):
                m[k + f"_l{lay}"] = sh2[k]
        sh2, pc2 = p2l[0]
        for k in ("selheadT", "selhead", "l0mask"):
            m[k] = sh2[k]
        m.update({k: v for k, v in pc2[c].items()})
        m["idn128"] = idn
        m.update(sh3); m.update(pc3[c])
        m.update(sh4)
        maps.append(m)
    return maps


def kernel_fused(**inputs):
    nc = build_fused()
    maps = prep_fused(inputs)
    r = run_bass_kernel_spmd(nc, maps, core_ids=list(range(NCORES)))
    out = np.zeros((4, 4096, 1024), np.float32)
    for c in range(NCORES):
        b, h = c // 2, c % 2
        out[b, h * T:(h + 1) * T] = np.asarray(r.results[c]["out_t"]).T
    return out



def build_phase4():
    nc = bass.Bass("TRN2", target_bir_lowering=False, debug=False)
    xin_t = nc.dram_tensor("xin_t", [DM, T], F32, kind="ExternalInput").ap()
    m1_t = nc.dram_tensor("m1_t", [NEXP * 1024, 2048], F32, kind="ExternalInput").ap()
    m3_t = nc.dram_tensor("m3_t", [NEXP * 1024, 2048], F32, kind="ExternalInput").ap()
    m2_t = nc.dram_tensor("m2_t", [NEXP * 2048, 1024], F32, kind="ExternalInput").ap()
    rw_t = nc.dram_tensor("rw_t", [DM, 8], F32, kind="ExternalInput").ap()
    ebc = nc.dram_tensor("ebc", [128, 1], F32, kind="ExternalInput").ap()
    sel8 = nc.dram_tensor("sel8", [8, 8 * 128], F32, kind="ExternalInput").ap()
    idn_in = nc.dram_tensor("idn128", [128, 128], F32, kind="ExternalInput").ap()
    o_out = nc.dram_tensor("out_t", [DM, T], F32, kind="ExternalOutput").ap()

    with ExitStack() as _es0:
        _ec0 = _es0.enter_context
        blk = _ec0(nc.Block())
        pr = Prog(nc, _es0)
        em = Emitter(nc, _es0, pr, n_psum=7)
        psbf = _ec0(nc.psum_tensor("psbf", [128, 1024], BF16))
        emit_phase4(nc, pr, em, psbf, xin_t, m1_t, m3_t, m2_t, rw_t, ebc, sel8, idn_in, o_out)
    return nc


def kernel(**inputs):
    idn = np.eye(128, dtype=np.float32)
    x_full = _np_f32(inputs["L0_attn_x"])

    sh1, pc1 = prep_phase1(inputs)
    nc1 = build_phase1()
    zc = np.zeros((128, 256), np.float32)
    zs = np.zeros((128, 16), np.float32)
    maps = []
    for c in range(NCORES):
        m = dict(sh1); m.update(pc1[c]); m["carry"] = zc; m["sext"] = zs
        maps.append(m)
    r1 = run_bass_kernel_spmd(nc1, maps, core_ids=list(range(NCORES)))
    maps2 = []
    for c in range(NCORES):
        m = dict(maps[c])
        if c % 2 == 1:
            m["carry"] = np.asarray(r1.results[c - 1]["hfin"])
        else:
            m["sext"] = np.asarray(r1.results[c + 1]["sumdelta"])
        maps2.append(m)
    r2 = run_bass_kernel_spmd(nc1, maps2, core_ids=list(range(NCORES)))
    x1 = np.zeros((4, 4096, 1024), np.float32)
    for c in range(NCORES):
        b, h = c // 2, c % 2
        x1[b, h * T:(h + 1) * T] = np.asarray(r2.results[c]["x1_t"]).T

    nc2 = build_phase2()

    def run_p2(lay, xin_full, xres_full):
        sh2, pc2 = prep_phase2(inputs, lay)
        mm = []
        for c in range(NCORES):
            m = dict(sh2); m.update(pc2[c])
            m["xin_t"] = slice_stream(xin_full, c, HALO2)
            m["xres_t"] = slice_stream(xres_full, c, 0)
            m["idn128"] = idn
            mm.append(m)
        r = run_bass_kernel_spmd(nc2, mm, core_ids=list(range(NCORES)))
        xo = np.zeros((4, 4096, 1024), np.float32)
        xl = np.zeros((4, 4096, 1024), np.float32)
        for c in range(NCORES):
            b, h = c // 2, c % 2
            xo[b, h * T:(h + 1) * T] = np.asarray(r.results[c]["xout_t"]).T
            xl[b, h * T:(h + 1) * T] = np.asarray(r.results[c]["xln_t"]).T
        return xo, xl

    x2a, _ = run_p2(0, x1, x1)
    _, x2 = run_p2(1, x2a, x1)

    nc3 = build_phase3()
    sh3, pc3 = prep_phase3(inputs)
    rows = np.concatenate([np.arange(16), 1024 + np.arange(16)])
    mm = []
    for c in range(NCORES):
        b, h = c // 2, c % 2
        m = dict(sh3); m.update(pc3[c])
        m["xin_t"] = slice_stream(x2, c, 0)
        m["x2x_t"] = (np.ascontiguousarray(x2[b, rows].T) if h == 1
                      else np.zeros((DM, NXK), np.float32))
        m["idn128"] = idn
        mm.append(m)
    r = run_bass_kernel_spmd(nc3, mm, core_ids=list(range(NCORES)))
    x3 = np.zeros((4, 4096, 1024), np.float32)
    for c in range(NCORES):
        b, h = c // 2, c % 2
        x3[b, h * T:(h + 1) * T] = np.asarray(r.results[c]["x3_t"]).T

    nc4 = build_phase4()
    sh4 = prep_phase4(inputs)
    mm = []
    for c in range(NCORES):
        m = dict(sh4)
        m["xin_t"] = slice_stream(x3, c, 0)
        m["idn128"] = idn
        mm.append(m)
    r = run_bass_kernel_spmd(nc4, mm, core_ids=list(range(NCORES)))
    out = np.zeros((4, 4096, 1024), np.float32)
    for c in range(NCORES):
        b, h = c // 2, c % 2
        out[b, h * T:(h + 1) * T] = np.asarray(r.results[c]["out_t"]).T
    return out


# version 03
# speedup vs baseline: 1.2173x; 1.2173x over previous
import math
import numpy as np
from contextlib import ExitStack

import concourse.bass as bass
import concourse.mybir as mybir
from concourse.bass_utils import run_bass_kernel_spmd

F32 = mybir.dt.float32
BF16 = mybir.dt.bfloat16
I32 = mybir.dt.int32
AF = mybir.ActivationFunctionType
ALU = mybir.AluOpType
AX = mybir.AxisListType
f32 = np.float32

NCORES = 8
T = 2048
DM = 1024

ENGINES = ("tensor", "vector", "scalar", "gpsimd", "sync")
DMA_SEMS_PER_QUEUE = 12
DMA_QUEUES = tuple(f"q{e}{i}" for e in ENGINES for i in range(DMA_SEMS_PER_QUEUE))


def _ap_range(ap):
    off = ap.offset
    if not isinstance(off, (int, np.integer)):
        return ap.tensor.name, 0, 1 << 60
    lo = hi = int(off)
    for step, count in ap.ap:
        d = int(step) * (int(count) - 1)
        if d >= 0:
            hi += d
        else:
            lo += d
    return ap.tensor.name, lo, hi


class Prog:
    def __init__(self, nc, es):
        self.nc = nc
        self.sems = {}
        for name in ENGINES + DMA_QUEUES:
            self.sems[name] = es.enter_context(nc.semaphore("as_" + name))
        self.count = {name: 0 for name in ENGINES + DMA_QUEUES}
        self.guar = {e: {s: 0 for s in ENGINES + DMA_QUEUES} for e in ENGINES}
        self.acc = {}
        self.eng = {e: getattr(nc, e) for e in ENGINES}
        self.rot = {e: 0 for e in ENGINES}
        self.n_waits = 0
        self.n_ops = 0

    def _needed(self, reads, check_writes):
        need = {}
        def add(semname, cnt):
            if cnt > need.get(semname, 0):
                need[semname] = cnt
        for ap in reads:
            name, lo, hi = _ap_range(ap)
            for rec in self.acc.get(name, ()):
                if rec[2] and not (rec[1] < lo or rec[0] > hi):
                    add(rec[3], rec[4])
        for ap in check_writes:
            name, lo, hi = _ap_range(ap)
            for rec in self.acc.get(name, ()):
                if not (rec[1] < lo or rec[0] > hi):
                    add(rec[3], rec[4])
        return need

    def _record(self, reads, writes, semname, cnt):
        for is_w, aps in ((False, reads), (True, writes)):
            for ap in aps:
                name, lo, hi = _ap_range(ap)
                lst = self.acc.setdefault(name, [])
                if is_w:
                    lst[:] = [r for r in lst if r[0] < lo or r[1] > hi]
                lst.append([lo, hi, is_w, semname, cnt])
                if len(lst) > 64:
                    old, keep = lst[:-40], lst[-40:]
                    blo = min(r[0] for r in old)
                    bhi = max(r[1] for r in old)
                    sems = {}
                    for r in old:
                        sems[r[3]] = max(sems.get(r[3], 0), r[4])
                    lst[:] = [[blo, bhi, True, s, c] for s, c in sems.items()] + keep

    def _wait(self, engine, need):
        g = self.guar[engine]
        e = self.eng[engine]
        for semname, cnt in sorted(need.items()):
            if g[semname] < cnt:
                if semname == engine and cnt > self.count[semname]:
                    raise RuntimeError(
                        f"deadlock: {engine} waits own sem {cnt} > {self.count[semname]}")
                e.wait_ge(self.sems[semname], cnt)
                g[semname] = cnt
                self.n_waits += 1

    def op(self, engine, f, reads, writes, check_writes=None, inc=True):
        if check_writes is None:
            check_writes = writes
        need = self._needed(reads, check_writes)
        self._wait(engine, need)
        inst = f(self.eng[engine])
        self.n_ops += 1
        if inc:
            self.count[engine] += 1
            inst.then_inc(self.sems[engine], 1)
            self._record(reads, writes, engine, self.count[engine])
        else:
            self._record(reads, writes, engine, self.count[engine] + 1)
        return inst

    def dma(self, engine, out, in_, **kwargs):
        need = self._needed([in_], [out])
        q = f"q{engine}{self.rot[engine]}"
        self.rot[engine] = (self.rot[engine] + 1) % DMA_SEMS_PER_QUEUE
        if self.count[q] > 0:
            need[q] = max(need.get(q, 0), self.count[q])
        self._wait(engine, need)
        inst = self.eng[engine].dma_start(out=out, in_=in_, **kwargs)
        self.count[q] += 16
        inst.then_inc(self.sems[q], 16)
        self.n_ops += 1
        self._record([in_], [out], q, self.count[q])
        return inst

    def idma(self, engine, out, out_off, in_, in_off, extra_reads=()):
        need = self._needed([in_] + list(extra_reads), [out])
        q = f"q{engine}{self.rot[engine]}"
        self.rot[engine] = (self.rot[engine] + 1) % DMA_SEMS_PER_QUEUE
        if self.count[q] > 0:
            need[q] = max(need.get(q, 0), self.count[q])
        self._wait(engine, need)
        inst = self.eng[engine].indirect_dma_start(out, out_off, in_, in_off)
        self.count[q] += 16
        inst.then_inc(self.sems[q], 16)
        self.n_ops += 1
        self._record([in_], [out], q, self.count[q])
        return inst

    def barrier_to(self, engine):
        need = {s: c for s, c in self.count.items() if c > 0}
        self._wait(engine, need)


def ceil_div(a, b):
    return (a + b - 1) // b


def blocks512(total):
    out = list(range(0, total - 511, 512))
    covered = out[-1] + 512 if out else 0
    if covered < total:
        out.append(total - 512)
    return out


class Emitter:

    def __init__(self, nc, es, pr, n_psum=8):
        self.nc = nc
        self.es = es
        self.pr = pr
        ec = es.enter_context
        self.ones_col = ec(nc.sbuf_tensor("em_ones_col", [128, 1], F32))
        self.ones_row = ec(nc.sbuf_tensor("em_ones_row", [1, 128], F32))
        self.half_row = ec(nc.sbuf_tensor("em_half_row", [1, 128], F32))
        pr.op("vector", lambda v: v.memset(self.ones_col[:, :], 1.0), [], [self.ones_col[:, :]])
        pr.op("vector", lambda v: v.memset(self.ones_row[:, :], 1.0), [], [self.ones_row[:, :]])
        pr.op("vector", lambda v: v.memset(self.half_row[:, :], 0.5), [], [self.half_row[:, :]])
        self.psum = [ec(nc.psum_tensor(f"em_ps{i}", [128, 512], F32)) for i in range(n_psum)]
        self._ps_i = 0

    def take_psum(self):
        p = self.psum[self._ps_i]
        self._ps_i = (self._ps_i + 1) % len(self.psum)
        return p

    def mm_group(self, pairs, out_ap, out_reads_extra=()):
        pr = self.pr
        n = len(pairs)
        for i, (lt, rh) in enumerate(pairs):
            first = i == 0
            last = i == n - 1
            pr.op("tensor",
                  lambda t, lt=lt, rh=rh, first=first, last=last:
                      t.matmul(out_ap, lt, rh, start=first, stop=last),
                  [lt, rh] + (list(out_reads_extra) if first else []),
                  [out_ap],
                  check_writes=[out_ap] if first else [])

    def bcast_row(self, out_sb, row_ap, ncols, lhs=None):
        pr = self.pr
        lhs = lhs if lhs is not None else self.ones_row
        for c0 in range(0, ncols, 512):
            w = min(512, ncols - c0)
            ps = self.take_psum()
            pr.op("tensor",
                  lambda t, c0=c0, w=w, ps=ps: t.matmul(
                      ps[:, 0:w], lhs[:, :], row_ap[0:1, c0:c0 + w],
                      start=True, stop=True),
                  [lhs[:, :], row_ap[0:1, c0:c0 + w]], [ps[:, 0:w]])
            pr.op("vector",
                  lambda v, c0=c0, w=w, ps=ps: v.tensor_copy(
                      out_sb[:, c0:c0 + w], ps[:, 0:w]),
                  [ps[:, 0:w]], [out_sb[:, c0:c0 + w]])


def _np_f32(x):
    return np.ascontiguousarray(np.asarray(x, dtype=np.float32))


def mamba_A_constants():
    n = np.arange(1, 17, dtype=np.float32)
    return [-float(np.exp(np.log(v).astype(np.float32)).astype(np.float32)) for v in n]


def prep_phase1(inputs):
    x = _np_f32(inputs["L0_attn_x"])
    W_in = _np_f32(inputs["L0_attn_W_in"])
    conv_w = _np_f32(inputs["L0_attn_conv_w"])
    conv_b = _np_f32(inputs["L0_attn_conv_b"])
    W_x = _np_f32(inputs["L0_attn_W_x"])
    W_dt = _np_f32(inputs["L0_attn_W_dt"])
    b_dt = _np_f32(inputs["L0_attn_b_dt"])
    Dv = _np_f32(inputs["L0_attn_D"])
    W_out = _np_f32(inputs["L0_attn_W_out"])

    def cols16(v):
        return np.ascontiguousarray(v.reshape(16, 128).T)

    shared = {
        "W_in": W_in,
        "W_x": W_x,
        "W_dt": W_dt,
        "W_out": W_out,
        "conv_w": np.ascontiguousarray(conv_w.reshape(4, 16, 128).transpose(2, 1, 0).reshape(128, 64)),
        "conv_b": cols16(conv_b),
        "b_dt": cols16(b_dt),
        "Dv": cols16(Dv),
    }
    selcat = np.zeros((32, 32 * 128), np.float32)
    for n in range(16):
        selcat[n, n * 128:(n + 1) * 128] = 1.0
        selcat[16 + n, (16 + n) * 128:(17 + n) * 128] = 0.5
    shared["selcat"] = selcat
    per_core = []
    for c in range(NCORES):
        b, h = c // 2, c % 2
        t0 = h * T
        lo = t0 - 3
        if lo < 0:
            xs = np.concatenate(
                [np.zeros((3, DM), np.float32), x[b, 0:t0 + T]], axis=0)
        else:
            xs = x[b, lo:t0 + T]
        per_core.append({"x_t": np.ascontiguousarray(xs.T)})
    return shared, per_core


AN = mamba_A_constants()
LNK = 27.631021115928547


def emit_phase1(nc, pr, em, x_t, W_in, W_x, W_dt, W_out, conv_w, conv_b, b_dt,
                Dv, carry, sext, selcat, o_x1, o_hfin, o_sumd,
                sp_xi, sp_delta, sp_res, sp_C, sp_bmcm,
                chunk=512, dgroup=4, carry_sel=None, do_passA=True, do_y=True,
                pfx="", halo_off=3):
    TH = T + halo_off
    NCH = ceil_div(T, chunk)
    with ExitStack() as es:
        ec = es.enter_context

        convb = ec(nc.sbuf_tensor(pfx + "convb", [128, 16], F32))
        bdt = ec(nc.sbuf_tensor(pfx + "bdt", [128, 16], F32))
        dvt = ec(nc.sbuf_tensor(pfx + "dvt", [128, 16], F32))
        carr = ec(nc.sbuf_tensor(pfx + "carr", [128, 256], F32))
        sxt = ec(nc.sbuf_tensor(pfx + "sxt", [128, 16], F32))
        convw = ec(nc.sbuf_tensor(pfx + "convw", [128, 64], F32))
        ccar = ec(nc.sbuf_tensor(pfx + "ccar", [128, 16], F32))
        pr.dma("sync", convw[:, :], conv_w[:, :])
        pr.dma("sync", convb[:, :], conv_b[:, :])
        pr.dma("sync", bdt[:, :], b_dt[:, :])
        pr.dma("sync", dvt[:, :], Dv[:, :])
        pr.dma("sync", carr[:, :], carry[:, :])
        pr.dma("sync", sxt[:, :], sext[:, :])
        if carry_sel is not None:
            selc_t = ec(nc.sbuf_tensor(pfx + "selc_t", [128, 2], F32))
            pr.dma("sync", selc_t[:, :], carry_sel[:, :])
            pr.op("vector", lambda v: v.tensor_scalar(carr[:, :], carr[:, :],
                                                      selc_t[:, 0:1], None, ALU.mult),
                  [carr[:, :], selc_t[:, 0:1]], [carr[:, :]])
            pr.op("vector", lambda v: v.tensor_scalar(sxt[:, :], sxt[:, :],
                                                      selc_t[:, 1:2], None, ALU.mult),
                  [sxt[:, :], selc_t[:, 1:2]], [sxt[:, :]])
        if do_passA:
            pr.op("vector", lambda v: v.memset(ccar[:, :], 0.0), [], [ccar[:, :]])
        else:
            pr.dma("sync", ccar[:, :], o_sumd[:, :])

        if do_passA:
            CH = chunk
            with ExitStack() as esa:
                eca = esa.enter_context
                w_in_bf = eca(nc.sbuf_tensor(pfx + "w_in_bf", [128, 8 * 4096], BF16))
                w_x_bf = eca(nc.sbuf_tensor(pfx + "w_x_bf", [128, 16 * 96], BF16))
                w_dt_bf = eca(nc.sbuf_tensor(pfx + "w_dt_bf", [64, 2048], BF16))
                xTbf = eca(nc.sbuf_tensor(pfx + "xTbf", [128, 8 * TH], BF16))
                xld = eca(nc.sbuf_tensor(pfx + "xld", [128, TH], F32))
                xr = eca(nc.sbuf_tensor(pfx + "xr", [128, CH + 3], F32))
                xi_f = eca(nc.sbuf_tensor(pfx + "xi_f", [128, CH], F32))
                xi_bf = eca(nc.sbuf_tensor(pfx + "xi_bf", [128, 16 * CH], BF16))
                resl = eca(nc.sbuf_tensor(pfx + "resl", [128, CH], F32))
                xdbl = eca(nc.sbuf_tensor(pfx + "xdbl", [96, CH], F32))
                dlin_bf = eca(nc.sbuf_tensor(pfx + "dlin_bf", [64, CH], BF16))
                dpre = eca(nc.sbuf_tensor(pfx + "dpre", [128, CH], F32))
                spl1 = eca(nc.sbuf_tensor(pfx + "spl1", [128, CH], F32))
                spl2 = eca(nc.sbuf_tensor(pfx + "spl2", [128, CH], F32))
                Cch = eca(nc.sbuf_tensor(pfx + "Cch", [128, CH], F32))
                onesC = eca(nc.sbuf_tensor(pfx + "onesC", [128, CH], F32))
                pr.op("vector", lambda v: v.memset(onesC[:, :], 1.0), [], [onesC[:, :]])

                for k in range(8):
                    pr.dma("gpsimd", w_in_bf[:, k * 4096:(k + 1) * 4096], W_in[k * 128:(k + 1) * 128, :])
                for k in range(16):
                    pr.dma("gpsimd", w_x_bf[:, k * 96:(k + 1) * 96], W_x[k * 128:(k + 1) * 128, :])
                pr.dma("gpsimd", w_dt_bf[:, :], W_dt[:, :])
                for k in range(8):
                    pr.dma("sync", xld[:, :], x_t[k * 128:(k + 1) * 128, :])
                    pr.op("vector",
                          lambda v, k=k: v.tensor_copy(xTbf[:, k * TH:(k + 1) * TH], xld[:, :]),
                          [xld[:, :]], [xTbf[:, k * TH:(k + 1) * TH]])

                for lc in range(NCH):
                    c0 = lc * CH
                    hw = CH + 3
                    for dt in range(16):
                        for s0 in range(0, hw, 512):
                            w = min(512, hw - s0)
                            ps = em.take_psum()
                            em.mm_group(
                                [(w_in_bf[:, k * 4096 + dt * 128: k * 4096 + dt * 128 + 128],
                                  xTbf[:, k * TH + c0 + s0: k * TH + c0 + s0 + w])
                                 for k in range(8)],
                                ps[:, 0:w])
                            pr.op("vector",
                                  lambda v, s0=s0, w=w, ps=ps: v.tensor_copy(xr[:, s0:s0 + w], ps[:, 0:w]),
                                  [ps[:, 0:w]], [xr[:, s0:s0 + w]])
                        acc = xi_f[:, 0:CH]
                        pr.op("vector",
                              lambda v, dt=dt, acc=acc: v.tensor_scalar(
                                  acc, xr[:, 0:CH], convw[:, dt * 4:dt * 4 + 1], None, ALU.mult),
                              [xr[:, 0:CH], convw[:, dt * 4:dt * 4 + 1]], [acc])
                        for tap in range(1, 4):
                            pr.op("vector",
                                  lambda v, dt=dt, tap=tap, acc=acc: v.scalar_tensor_tensor(
                                      acc, xr[:, tap:tap + CH],
                                      convw[:, dt * 4 + tap:dt * 4 + tap + 1], acc, ALU.mult, ALU.add),
                                  [xr[:, tap:tap + CH], acc,
                                   convw[:, dt * 4 + tap:dt * 4 + tap + 1]], [acc])
                        pr.op("vector",
                              lambda v, dt=dt, acc=acc: v.tensor_scalar(
                                  acc, acc, convb[:, dt:dt + 1], None, ALU.add),
                              [acc, convb[:, dt:dt + 1]], [acc])
                        pr.op("scalar", lambda s, acc=acc: s.activation(acc, acc, AF.Silu),
                              [acc], [acc])
                        pr.op("vector",
                              lambda v, dt=dt, acc=acc: v.tensor_copy(xi_bf[:, dt * CH:(dt + 1) * CH], acc),
                              [acc], [xi_bf[:, dt * CH:(dt + 1) * CH]])
                        pr.dma("gpsimd", sp_xi[dt * 128:(dt + 1) * 128, c0:c0 + CH], acc)
                    for of in range(16):
                        for s0 in range(0, CH, 512):
                            w = min(512, CH - s0)
                            ps = em.take_psum()
                            em.mm_group(
                                [(w_in_bf[:, k * 4096 + 2048 + of * 128: k * 4096 + 2048 + of * 128 + 128],
                                  xTbf[:, k * TH + c0 + 3 + s0: k * TH + c0 + 3 + s0 + w])
                                 for k in range(8)],
                                ps[:, 0:w])
                            pr.op("vector",
                                  lambda v, s0=s0, w=w, ps=ps: v.tensor_copy(resl[:, s0:s0 + w], ps[:, 0:w]),
                                  [ps[:, 0:w]], [resl[:, s0:s0 + w]])
                        pr.dma("gpsimd", sp_res[of * 128:(of + 1) * 128, c0:c0 + CH], resl[:, 0:CH])
                    for s0 in range(0, CH, 512):
                        w = min(512, CH - s0)
                        ps = em.take_psum()
                        em.mm_group(
                            [(w_x_bf[:, k * 96:(k + 1) * 96], xi_bf[:, k * CH + s0:k * CH + s0 + w])
                             for k in range(16)],
                            ps[0:96, 0:w])
                        pr.op("vector",
                              lambda v, s0=s0, w=w, ps=ps: v.tensor_copy(xdbl[:, s0:s0 + w], ps[0:96, 0:w]),
                              [ps[0:96, 0:w]], [xdbl[:, s0:s0 + w]])
                    pr.dma("gpsimd", sp_bmcm[:, c0:c0 + CH], xdbl[64:96, 0:CH])
                    pr.op("vector",
                          lambda v: v.tensor_copy(dlin_bf[:, 0:CH], xdbl[0:64, 0:CH]),
                          [xdbl[0:64, 0:CH]], [dlin_bf[:, 0:CH]])
                    for dt in range(16):
                        for s0 in range(0, CH, 512):
                            w = min(512, CH - s0)
                            ps = em.take_psum()
                            em.mm_group(
                                [(w_dt_bf[:, dt * 128:(dt + 1) * 128], dlin_bf[:, s0:s0 + w])],
                                ps[:, 0:w])
                            pr.op("vector",
                                  lambda v, s0=s0, w=w, dt=dt, ps=ps:
                                  v.tensor_scalar(dpre[:, s0:s0 + w], ps[:, 0:w],
                                                  bdt[:, dt:dt + 1], None, ALU.add),
                                  [ps[:, 0:w], bdt[:, dt:dt + 1]], [dpre[:, s0:s0 + w]])
                        pr.op("scalar", lambda s: s.activation(spl1[:, 0:CH], dpre[:, 0:CH], AF.Abs),
                              [dpre[:, 0:CH]], [spl1[:, 0:CH]])
                        pr.op("scalar", lambda s: s.activation(spl1[:, 0:CH], spl1[:, 0:CH], AF.Exp, scale=f32(-1.0)),
                              [spl1[:, 0:CH]], [spl1[:, 0:CH]])
                        pr.op("scalar", lambda s: s.activation(spl1[:, 0:CH], spl1[:, 0:CH], AF.Ln, bias=f32(1.0)),
                              [spl1[:, 0:CH]], [spl1[:, 0:CH]])
                        pr.op("scalar", lambda s: s.activation(spl2[:, 0:CH], dpre[:, 0:CH], AF.Relu),
                              [dpre[:, 0:CH]], [spl2[:, 0:CH]])
                        pr.op("vector", lambda v: v.tensor_add(dpre[:, 0:CH], spl1[:, 0:CH], spl2[:, 0:CH]),
                              [spl1[:, 0:CH], spl2[:, 0:CH]], [dpre[:, 0:CH]])
                        pr.dma("gpsimd", sp_delta[dt * 128:(dt + 1) * 128, c0:c0 + CH], dpre[:, 0:CH])
                        pr.op("vector",
                              lambda v, dt=dt: v.tensor_tensor_scan(
                                  Cch[:, 0:CH], onesC[:, 0:CH], dpre[:, 0:CH],
                                  ccar[:, dt:dt + 1], ALU.mult, ALU.add),
                              [onesC[:, 0:CH], dpre[:, 0:CH], ccar[:, dt:dt + 1]], [Cch[:, 0:CH]])
                        pr.op("vector",
                              lambda v, dt=dt: v.tensor_copy(ccar[:, dt:dt + 1], Cch[:, CH - 1:CH]),
                              [Cch[:, CH - 1:CH]], [ccar[:, dt:dt + 1]])
                        pr.dma("gpsimd", sp_C[dt * 128:(dt + 1) * 128, c0:c0 + CH], Cch[:, 0:CH])
                for e in ENGINES:
                    pr.barrier_to(e)


        if do_passA:
            pr.dma("gpsimd", o_sumd[:, :], ccar[:, :])

        SC = chunk
        NSC = ceil_div(T, SC)
        w_out_bf = ec(nc.sbuf_tensor(pfx + "w_out_bf", [128, 16 * DM], BF16))
        selc = ec(nc.sbuf_tensor(pfx + "selc", [32, 32 * 128], F32))
        bmrow = ec(nc.sbuf_tensor(pfx + "bmrow", [32, T], F32))
        del_g = ec(nc.sbuf_tensor(pfx + "del_g", [128, dgroup * SC], F32))
        C_g = ec(nc.sbuf_tensor(pfx + "C_g", [128, dgroup * SC], F32))
        du_g = ec(nc.sbuf_tensor(pfx + "du_g", [128, dgroup * SC], F32))
        xi_g = ec(nc.sbuf_tensor(pfx + "xi_g", [128, dgroup * SC], F32))
        res_g = ec(nc.sbuf_tensor(pfx + "res_g", [128, dgroup * SC], F32))
        T_g = ec(nc.sbuf_tensor(pfx + "T_g", [128, dgroup * SC], F32))
        rho = ec(nc.sbuf_tensor(pfx + "rho", [128, SC], F32))
        En = ec(nc.sbuf_tensor(pfx + "En", [128, SC], F32))
        gq = ec(nc.sbuf_tensor(pfx + "gq", [128, SC], F32))
        gC = ec(nc.sbuf_tensor(pfx + "gC", [128, SC], F32))
        duB = ec(nc.sbuf_tensor(pfx + "duB", [128, SC], F32))
        hsc = ec(nc.sbuf_tensor(pfx + "hsc", [128, SC], F32))
        tsc = ec(nc.sbuf_tensor(pfx + "tsc", [128, SC], F32))
        y_g = ec(nc.sbuf_tensor(pfx + "y_g", [128, dgroup * SC], F32))
        yg_bf = ec(nc.sbuf_tensor(pfx + "yg_bf", [128, 16 * SC], BF16))
        bmb = ec(nc.sbuf_tensor(pfx + "bmb", [128, SC], F32))
        cmb = ec(nc.sbuf_tensor(pfx + "cmb", [128, SC], F32))
        hcar = ec(nc.sbuf_tensor(pfx + "hcar", [128, 256], F32))
        ctot = ec(nc.sbuf_tensor(pfx + "ctot", [128, 16], F32))
        x1pre = ec(nc.sbuf_tensor(pfx + "x1pre", [128, 8 * SC], F32))
        x1sq = ec(nc.sbuf_tensor(pfx + "x1sq", [128, 2 * 512], F32))
        strow = ec(nc.sbuf_tensor(pfx + "strow", [1, 4 * 512], F32))
        mub = ec(nc.sbuf_tensor(pfx + "mub", [128, SC], F32))
        rstdb = ec(nc.sbuf_tensor(pfx + "rstdb", [128, SC], F32))
        xres = ec(nc.sbuf_tensor(pfx + "xres", [128, SC], F32))

        for k in range(16):
            pr.dma("gpsimd", w_out_bf[:, k * DM:(k + 1) * DM], W_out[k * 128:(k + 1) * 128, :])
        pr.dma("sync", bmrow[:, :], sp_bmcm[:, :])
        pr.dma("sync", selc[:, :], selcat[:, :])
        pr.op("vector", lambda v: v.tensor_copy(hcar[:, :], carr[:, :]), [carr[:, :]], [hcar[:, :]])
        pr.op("vector", lambda v: v.tensor_add(ctot[:, :], ccar[:, :], sxt[:, :]),
              [ccar[:, :], sxt[:, :]], [ctot[:, :]])

        for sc in range(NSC):
            c0 = sc * SC
            for dgi in range(16 // dgroup):
                dts = [dgi * dgroup + j for j in range(dgroup)]
                for j, dt in enumerate(dts):
                    pr.dma("sync", del_g[:, j * SC:(j + 1) * SC], sp_delta[dt * 128:(dt + 1) * 128, c0:c0 + SC])
                    if do_y:
                        pr.dma("scalar", C_g[:, j * SC:(j + 1) * SC], sp_C[dt * 128:(dt + 1) * 128, c0:c0 + SC])
                    pr.dma("gpsimd", xi_g[:, j * SC:(j + 1) * SC], sp_xi[dt * 128:(dt + 1) * 128, c0:c0 + SC])
                    pr.op("vector",
                          lambda v, j=j: v.tensor_mul(du_g[:, j * SC:(j + 1) * SC],
                                                      del_g[:, j * SC:(j + 1) * SC],
                                                      xi_g[:, j * SC:(j + 1) * SC]),
                          [del_g[:, j * SC:(j + 1) * SC], xi_g[:, j * SC:(j + 1) * SC]],
                          [du_g[:, j * SC:(j + 1) * SC]])
                    if do_y:
                        pr.op("vector",
                              lambda v, j=j:
                              v.tensor_scalar(T_g[:, j * SC:(j + 1) * SC],
                                              C_g[:, j * SC:(j + 1) * SC],
                                              -1.0, None, ALU.mult),
                              [C_g[:, j * SC:(j + 1) * SC]], [T_g[:, j * SC:(j + 1) * SC]])
                        pr.op("vector",
                              lambda v, j=j, dt=dt: v.tensor_scalar(
                                  T_g[:, j * SC:(j + 1) * SC], T_g[:, j * SC:(j + 1) * SC],
                                  ctot[:, dt:dt + 1], None, ALU.add),
                              [T_g[:, j * SC:(j + 1) * SC], ctot[:, dt:dt + 1]],
                              [T_g[:, j * SC:(j + 1) * SC]])
                for n in range(16):
                    an = AN[n]
                    for tgt, blkk in (((bmb, n), (cmb, 16 + n)) if do_y else ((bmb, n),)):
                        for s0 in range(0, SC, 512):
                            w = min(512, SC - s0)
                            ps = em.take_psum()
                            pr.op("tensor",
                                  lambda t, ps=ps, blkk=blkk, s0=s0, w=w: t.matmul(
                                      ps[:, 0:w], selc[:, blkk * 128:(blkk + 1) * 128],
                                      bmrow[0:32, c0 + s0:c0 + s0 + w],
                                      start=True, stop=True),
                                  [selc[:, blkk * 128:(blkk + 1) * 128],
                                   bmrow[0:32, c0 + s0:c0 + s0 + w]], [ps[:, 0:w]])
                            pr.op("vector",
                                  lambda v, tgt=tgt, s0=s0, w=w, ps=ps: v.tensor_copy(
                                      tgt[:, s0:s0 + w], ps[:, 0:w]),
                                  [ps[:, 0:w]], [tgt[:, s0:s0 + w]])
                    for j, dt in enumerate(dts):
                        dsl = del_g[:, j * SC:(j + 1) * SC]
                        Csl = C_g[:, j * SC:(j + 1) * SC]
                        Tsl = T_g[:, j * SC:(j + 1) * SC]
                        dusl = du_g[:, j * SC:(j + 1) * SC]
                        ysl = y_g[:, j * SC:(j + 1) * SC]
                        hc = hcar[:, dt * 16 + n: dt * 16 + n + 1]
                        pr.op("scalar", lambda s, dsl=dsl, an=an: s.activation(rho[:, :], dsl, AF.Exp, scale=f32(an)),
                              [dsl], [rho[:, :]])
                        pr.op("scalar", lambda s, Tsl=Tsl, an=an: s.activation(
                                  gq[:, :], Tsl, AF.Tanh, scale=f32(an * 0.5), bias=f32(LNK * 0.5)),
                              [Tsl], [gq[:, :]])
                        pr.op("vector", lambda v: v.scalar_tensor_tensor(
                                  gC[:, :], gq[:, :], 1.0, cmb[:, :], ALU.add, ALU.mult),
                              [gq[:, :], cmb[:, :]], [gC[:, :]])
                        pr.op("vector", lambda v, dusl=dusl: v.tensor_mul(duB[:, :], dusl, bmb[:, :]),
                              [dusl, bmb[:, :]], [duB[:, :]])
                        pr.op("vector", lambda v, hc=hc: v.tensor_tensor_scan(
                                  hsc[:, :], rho[:, :], duB[:, :], hc, ALU.mult, ALU.add),
                              [rho[:, :], duB[:, :], hc], [hsc[:, :]])
                        pr.op("vector", lambda v, hc=hc: v.tensor_copy(hc, hsc[:, SC - 1:SC]),
                              [hsc[:, SC - 1:SC]], [hc])
                        pr.op("vector", lambda v: v.tensor_mul(tsc[:, :], hsc[:, :], gC[:, :]),
                              [hsc[:, :], gC[:, :]], [tsc[:, :]])
                        if n == 0:
                            pr.op("vector", lambda v, ysl=ysl: v.tensor_copy(ysl, tsc[:, :]),
                                  [tsc[:, :]], [ysl])
                        else:
                            pr.op("vector", lambda v, ysl=ysl: v.tensor_add(ysl, ysl, tsc[:, :]),
                                  [tsc[:, :], ysl], [ysl])
                for j, dt in enumerate(dts) if do_y else ():
                    ysl = y_g[:, j * SC:(j + 1) * SC]
                    xisl = xi_g[:, j * SC:(j + 1) * SC]
                    rsl = res_g[:, j * SC:(j + 1) * SC]
                    pr.dma("scalar", rsl, sp_res[dt * 128:(dt + 1) * 128, c0:c0 + SC])
                    pr.op("vector", lambda v, ysl=ysl, xisl=xisl, dt=dt: v.scalar_tensor_tensor(
                              ysl, xisl, dvt[:, dt:dt + 1], ysl, ALU.mult, ALU.add),
                          [xisl, dvt[:, dt:dt + 1], ysl], [ysl])
                    pr.op("scalar", lambda s, rsl=rsl: s.activation(rsl, rsl, AF.Silu),
                          [rsl], [rsl])
                    pr.op("vector", lambda v, ysl=ysl, rsl=rsl, dt=dt: v.tensor_mul(
                              yg_bf[:, dt * SC:(dt + 1) * SC], ysl, rsl),
                          [ysl, rsl], [yg_bf[:, dt * SC:(dt + 1) * SC]])
            for of in range(8) if do_y else ():
                for s0 in range(0, SC, 512):
                    w = min(512, SC - s0)
                    ps = em.take_psum()
                    em.mm_group(
                        [(w_out_bf[:, k * DM + of * 128: k * DM + of * 128 + 128],
                          yg_bf[:, k * SC + s0:k * SC + s0 + w]) for k in range(16)],
                        ps[:, 0:w])
                    pr.dma("scalar", xres[:, s0:s0 + w], x_t[of * 128:(of + 1) * 128, 3 + c0 + s0: 3 + c0 + s0 + w])
                    pr.op("vector",
                          lambda v, of=of, s0=s0, w=w, ps=ps: v.tensor_add(
                              x1pre[:, of * SC + s0: of * SC + s0 + w], ps[:, 0:w],
                              xres[:, s0:s0 + w]),
                          [ps[:, 0:w], xres[:, s0:s0 + w]],
                          [x1pre[:, of * SC + s0: of * SC + s0 + w]])
            if do_y:
                emit_ln_feature(pr, em, x1pre, SC, strow, x1sq, mub, rstdb, nfeat=8)
                for of in range(8):
                    sl = x1pre[:, of * SC:(of + 1) * SC]
                    pr.op("vector", lambda v, sl=sl: v.tensor_sub(sl, sl, mub[:, 0:SC]),
                          [sl, mub[:, 0:SC]], [sl])
                    pr.op("vector", lambda v, sl=sl: v.tensor_mul(sl, sl, rstdb[:, 0:SC]),
                          [sl, rstdb[:, 0:SC]], [sl])
                    pr.dma("gpsimd", o_x1[of * 128:(of + 1) * 128, c0:c0 + SC], sl)

        pr.dma("gpsimd", o_hfin[:, :], hcar[:, :])
        for e in ENGINES:
            pr.barrier_to(e)
        print(f"phase1 program: {pr.n_ops} ops, {pr.n_waits} waits")
    return nc




def build_phase1(chunk=512, dgroup=4):
    nc = bass.Bass("TRN2", target_bir_lowering=False, debug=False)
    TH = T + 3
    x_t = nc.dram_tensor("x_t", [DM, TH], F32, kind="ExternalInput").ap()
    W_in = nc.dram_tensor("W_in", [DM, 4096], F32, kind="ExternalInput").ap()
    W_x = nc.dram_tensor("W_x", [2048, 96], F32, kind="ExternalInput").ap()
    W_dt = nc.dram_tensor("W_dt", [64, 2048], F32, kind="ExternalInput").ap()
    W_out = nc.dram_tensor("W_out", [2048, DM], F32, kind="ExternalInput").ap()
    conv_w = nc.dram_tensor("conv_w", [128, 64], F32, kind="ExternalInput").ap()
    conv_b = nc.dram_tensor("conv_b", [128, 16], F32, kind="ExternalInput").ap()
    b_dt = nc.dram_tensor("b_dt", [128, 16], F32, kind="ExternalInput").ap()
    Dv = nc.dram_tensor("Dv", [128, 16], F32, kind="ExternalInput").ap()
    carry = nc.dram_tensor("carry", [128, 256], F32, kind="ExternalInput").ap()
    sext = nc.dram_tensor("sext", [128, 16], F32, kind="ExternalInput").ap()
    selcat = nc.dram_tensor("selcat", [32, 32 * 128], F32, kind="ExternalInput").ap()

    o_x1 = nc.dram_tensor("x1_t", [DM, T], F32, kind="ExternalOutput").ap()
    o_hfin = nc.dram_tensor("hfin", [128, 256], F32, kind="ExternalOutput").ap()
    o_sumd = nc.dram_tensor("sumdelta", [128, 16], F32, kind="ExternalOutput").ap()

    sp_xi = nc.dram_tensor("sp_xi", [2048, T], F32).ap()
    sp_delta = nc.dram_tensor("sp_delta", [2048, T], F32).ap()
    sp_res = nc.dram_tensor("sp_res", [2048, T], F32).ap()
    sp_C = nc.dram_tensor("sp_C", [2048, T], F32).ap()
    sp_bmcm = nc.dram_tensor("sp_bmcm", [32, T], F32).ap()

    NCH = ceil_div(T, chunk)

    with ExitStack() as _es0:
        _ec0 = _es0.enter_context
        blk = _ec0(nc.Block())
        pr = Prog(nc, _es0)
        em = Emitter(nc, _es0, pr, n_psum=7)
        psbf = _ec0(nc.psum_tensor("psbf", [128, 1024], BF16))
        emit_phase1(nc, pr, em, x_t, W_in, W_x, W_dt, W_out, conv_w, conv_b, b_dt,
                    Dv, carry, sext, selcat, o_x1, o_hfin, o_sumd,
                    sp_xi, sp_delta, sp_res, sp_C, sp_bmcm, chunk=chunk, dgroup=dgroup)
    return nc


def emit_ln_feature(pr, em, xtiles, width, strow, sqtile, mub, rstdb, nfeat=8, eps=1e-5):
    nc = pr.nc
    dmf = float(nfeat * 128)
    for s0 in range(0, width, 512):
        w = min(512, width - s0)
        ps = em.take_psum()
        ps2 = em.take_psum()
        em.mm_group(
            [(em.ones_col[:, :], xtiles[:, of * width + s0: of * width + s0 + w])
             for of in range(nfeat)],
            ps[0:1, 0:w])
        for of in range(nfeat):
            sq = sqtile[:, (of % 2) * 512:(of % 2) * 512 + w]
            pr.op("scalar",
                  lambda s, of=of, s0=s0, w=w, sq=sq: s.activation(
                      sq, xtiles[:, of * width + s0: of * width + s0 + w], AF.Square),
                  [xtiles[:, of * width + s0: of * width + s0 + w]], [sq])
            pr.op("tensor",
                  lambda t, of=of, w=w, ps2=ps2, sq=sq: t.matmul(
                      ps2[0:1, 0:w], em.ones_col[:, :], sq,
                      start=(of == 0), stop=(of == nfeat - 1)),
                  [em.ones_col[:, :], sq],
                  [ps2[0:1, 0:w]],
                  check_writes=[ps2[0:1, 0:w]] if of == 0 else [])
        mu = strow[0:1, 0:w]
        msq = strow[0:1, 512:512 + w]
        var = strow[0:1, 1024:1024 + w]
        rstd = strow[0:1, 1536:1536 + w]
        pr.op("vector", lambda v, mu=mu, ps=ps, w=w: v.tensor_scalar_mul(mu, ps[0:1, 0:w], 1.0 / dmf),
              [ps[0:1, 0:w]], [mu])
        pr.op("vector", lambda v, msq=msq, ps2=ps2, w=w: v.tensor_scalar_mul(msq, ps2[0:1, 0:w], 1.0 / dmf),
              [ps2[0:1, 0:w]], [msq])
        pr.op("vector", lambda v, var=var, mu=mu: v.tensor_mul(var, mu, mu), [mu], [var])
        pr.op("vector", lambda v, var=var, msq=msq: v.tensor_sub(var, msq, var), [msq, var], [var])
        pr.op("scalar", lambda s, var=var: s.activation(var, var, AF.Ln, bias=f32(eps)),
              [var], [var])
        pr.op("scalar", lambda s, rstd=rstd, var=var: s.activation(rstd, var, AF.Exp, scale=f32(-0.5)),
              [var], [rstd])
        ps3 = em.take_psum()
        pr.op("tensor", lambda t, ps3=ps3, mu=mu, w=w: t.matmul(
                  ps3[:, 0:w], em.ones_row[:, :], mu, start=True, stop=True),
              [em.ones_row[:, :], mu], [ps3[:, 0:w]])
        pr.op("vector", lambda v, s0=s0, w=w, ps3=ps3: v.tensor_copy(mub[:, s0:s0 + w], ps3[:, 0:w]),
              [ps3[:, 0:w]], [mub[:, s0:s0 + w]])
        ps4 = em.take_psum()
        pr.op("tensor", lambda t, ps4=ps4, rstd=rstd, w=w: t.matmul(
                  ps4[:, 0:w], em.ones_row[:, :], rstd, start=True, stop=True),
              [em.ones_row[:, :], rstd], [ps4[:, 0:w]])
        pr.op("vector", lambda v, s0=s0, w=w, ps4=ps4: v.tensor_copy(rstdb[:, s0:s0 + w], ps4[:, 0:w]),
              [ps4[:, 0:w]], [rstdb[:, s0:s0 + w]])


HALO2 = 1026
QW = 512
HIDP = 2816


def prep_phase2(inputs, lay):
    wqkv = _np_f32(inputs["L0_ffn_wqkv"][lay])
    qs = _np_f32(inputs["L0_ffn_q_scale"][lay])
    ks = _np_f32(inputs["L0_ffn_k_scale"][lay])
    wo = _np_f32(inputs["L0_ffn_wo"][lay])
    lg = _np_f32(inputs["L0_ffn_ln_g"][lay])
    lb = _np_f32(inputs["L0_ffn_ln_b"][lay])
    w1 = _np_f32(inputs["L0_ffn_w1"][lay])
    b1 = _np_f32(inputs["L0_ffn_b1"][lay])
    w2 = _np_f32(inputs["L0_ffn_w2"][lay])
    b2 = _np_f32(inputs["L0_ffn_b2"][lay])

    w1p = np.zeros((1024, 2 * HIDP), np.float32)
    b1p = np.zeros((128, 2 * (HIDP // 128)), np.float32)
    for j in range(HIDP // 128):
        a0, a1 = j * 128, min((j + 1) * 128, 2730)
        if a0 < 2730:
            n = a1 - a0
            w1p[:, j * 256: j * 256 + n] = w1[:, a0:a1]
            w1p[:, j * 256 + 128: j * 256 + 128 + n] = w1[:, 2730 + a0:2730 + a1]
            b1p[0:n, 2 * j] = b1[a0:a1]
            b1p[0:n, 2 * j + 1] = b1[2730 + a0:2730 + a1]
    w2p = np.zeros((HIDP, 1024), np.float32)
    w2p[:2730] = w2

    def cols8(v):
        return np.ascontiguousarray(v.reshape(8, 128).T)

    shared = {
        "wqkv": wqkv, "wo": wo, "w1p": w1p, "w2p": w2p, "b1p": b1p,
        "b2c": cols8(b2), "lgc": cols8(lg), "lbc": cols8(lb),
        "qs8": np.ascontiguousarray(np.tile(qs * 8.0, 2).reshape(128, 1) * np.ones((1, 4), np.float32)),
        "ksc": np.ascontiguousarray(np.tile(ks, 2).reshape(128, 1) * np.ones((1, 4), np.float32)),
        "selheadT": np.ascontiguousarray(
            np.stack([np.r_[np.ones(64, np.float32), np.zeros(64, np.float32)],
                      np.r_[np.zeros(64, np.float32), np.ones(64, np.float32)]])),
        "selhead": np.ascontiguousarray(
            np.stack([np.r_[np.ones(64, np.float32), np.zeros(64, np.float32)],
                      np.r_[np.zeros(64, np.float32), np.ones(64, np.float32)]]).T),
    }
    i_rel = np.arange(512)[:, None]
    j_rel = np.arange(1024)[None, :]
    m_norm = np.zeros((512, 1024), np.float32)
    m_norm[:, :512][i_rel > j_rel[:, :512]] = -1e38
    m_norm[:, 512:][i_rel < (j_rel[:, 512:] - 512)] = -1e38
    m_w0 = m_norm.copy()
    m_w0[:, :512] = -1e38
    shared["l0mask"] = m_norm

    def rope_tables(tglob):
        n = len(tglob)
        half_idx = np.arange(0, 64, 2, dtype=np.float32)
        inv_freq = 1.0 / 10000.0 ** (half_idx / 64.0)
        tt = np.asarray(tglob, np.float32)
        freqs = tt[:, None] * inv_freq[None, :]
        freqs = np.concatenate([freqs, freqs], axis=-1)
        base = (half_idx + 0.4 * 64) / (1.4 * 64)
        power = (tt - 4096 // 2) / float(QW // 2)
        scale = base[None, :] ** power[:, None]
        scale = np.concatenate([scale, scale], axis=-1)
        cos, sin = np.cos(freqs), np.sin(freqs)
        sgn = np.r_[-np.ones(32, np.float32), np.ones(32, np.float32)]
        csq = (cos * scale).T
        ssq = (sin * scale).T * sgn[:, None]
        csk = (cos / scale).T
        ssk = (sin / scale).T * sgn[:, None]
        dbl = lambda a: np.ascontiguousarray(np.concatenate([a, a], axis=0).astype(np.float32))
        return dbl(csq), dbl(ssq), dbl(csk), dbl(ssk)

    per_core = []
    for c in range(NCORES):
        b, h = c // 2, c % 2
        t0 = h * T
        tq = np.maximum(np.arange(t0 - QW, t0 + T), 0)
        tk = np.maximum(np.arange(t0 - 2 * QW, t0 + T), 0)
        csq, ssq, _, _ = rope_tables(tq)
        _, _, csk, ssk = rope_tables(tk)
        per_core.append({
            "csq": csq, "ssq": ssq, "csk": csk, "ssk": ssk,
            "l0mask_w1": (m_w0 if h == 0 else m_norm),
        })
    return shared, per_core


def slice_stream(x_full, c, halo, zero_halo_for_first=True):
    b, h = c // 2, c % 2
    t0 = h * T
    lo = t0 - halo
    if lo < 0:
        xs = np.concatenate([np.zeros((-lo, DM), np.float32), x_full[b, 0:t0 + T]], axis=0)
    else:
        xs = x_full[b, lo:t0 + T]
    return np.ascontiguousarray(xs.T)


def emit_phase2(nc, pr, em, psbf, xin_t, xres_t, wqkv, wo, w1p, w2p, b1p, b2c, lgc, lbc, qs8, ksc, selheadT, selhead, csq, ssq, csk, ssk, l0mask, l0mask_w1, idn_in, o_out, o_ln, sp_x2m, pfx=""):
    W = HALO2 + T
    NQ = T // QW + 1
    QTOT = T + QW
    KTOT = T + 2 * QW

    with ExitStack() as es:
        ec = es.enter_context

        shT = ec(nc.sbuf_tensor(pfx + "shT", [2, 128], F32))
        shD = ec(nc.sbuf_tensor(pfx + "shD", [128, 2], F32))
        qsct = ec(nc.sbuf_tensor(pfx + "qsct", [128, 4], F32))
        ksct = ec(nc.sbuf_tensor(pfx + "ksct", [128, 4], F32))
        idn_bf = ec(nc.sbuf_tensor(pfx + "idn_bf", [128, 128], BF16))
        o_bf = ec(nc.sbuf_tensor(pfx + "o_bf", [128, 4 * QTOT], BF16))
        pr.dma("sync", shT[:, :], selheadT[:, :])
        pr.dma("sync", shD[:, :], selhead[:, :])
        pr.dma("sync", qsct[:, :], qs8[:, :])
        pr.dma("sync", ksct[:, :], ksc[:, :])
        pr.dma("gpsimd", idn_bf[:, :], idn_in[:, :])

        with ExitStack() as es12:
            ec12 = es12.enter_context
            q_bf = ec12(nc.sbuf_tensor(pfx + "q_bf", [128, 4 * QTOT], BF16))
            k_bf = ec12(nc.sbuf_tensor(pfx + "k_bf", [128, 4 * KTOT], BF16))
            v_tr = ec12(nc.sbuf_tensor(pfx + "v_tr", [128, (KTOT // 128) * 512], BF16))

            with ExitStack() as esa:
                eca = esa.enter_context
                xin_bf = eca(nc.sbuf_tensor(pfx + "xin_bf", [128, 8 * W], BF16))
                wqs = [eca(nc.sbuf_tensor(pfx + f"wqs{i}", [128, 8 * 128], BF16)) for i in range(3)]
                xld = eca(nc.sbuf_tensor(pfx + "p2xld", [128, 512], F32))
                scr = eca(nc.sbuf_tensor(pfx + "scr", [128, 2 * 512], F32))
                sqs = eca(nc.sbuf_tensor(pfx + "sqs", [128, 2 * 512], F32))
                nrm = eca(nc.sbuf_tensor(pfx + "nrm", [2, 2 * 512], F32))
                invb = eca(nc.sbuf_tensor(pfx + "invb", [128, 2 * 512], F32))
                qn2 = eca(nc.sbuf_tensor(pfx + "qn2", [128, 2 * 512], F32))
                qsw = eca(nc.sbuf_tensor(pfx + "qsw", [128, 2 * 512], F32))
                rot_cs = eca(nc.sbuf_tensor(pfx + "rot_cs", [128, 2 * 512], F32))
                rot_ss = eca(nc.sbuf_tensor(pfx + "rot_ss", [128, 2 * 512], F32))
                vev = eca(nc.sbuf_tensor(pfx + "vev", [128, KTOT], BF16))

                for k in range(8):
                    for s0 in range(0, W, 512):
                        wdt = min(512, W - s0)
                        pr.dma("sync", xld[:, 0:wdt], xin_t[k * 128:(k + 1) * 128, s0:s0 + wdt])
                        pr.op("vector",
                              lambda v, k=k, s0=s0, wdt=wdt: v.tensor_copy(
                                  xin_bf[:, k * W + s0:k * W + s0 + wdt], xld[:, 0:wdt]),
                              [xld[:, 0:wdt]], [xin_bf[:, k * W + s0:k * W + s0 + wdt]])

                def qkv_block(of, wtile, out_ap_f32, col0, wdt):
                    ps = em.take_psum()
                    em.mm_group(
                        [(wtile[:, k * 128:(k + 1) * 128],
                          xin_bf[:, k * W + col0 + (0 if k < 4 else -1):
                                 k * W + col0 + (0 if k < 4 else -1) + wdt])
                         for k in range(8)],
                        ps[:, 0:wdt])
                    pr.op("vector",
                          lambda v, ps=ps, out_ap_f32=out_ap_f32, wdt=wdt: v.tensor_copy(
                              out_ap_f32, ps[:, 0:wdt]),
                          [ps[:, 0:wdt]], [out_ap_f32])

                wi = 0
                for hp in range(4):
                    for kind in range(3):
                        of = kind * 4 + hp
                        wtile = wqs[wi % 3]
                        wi += 1
                        for k in range(8):
                            pr.dma("gpsimd", wtile[:, k * 128:(k + 1) * 128],
                                   wqkv[k * 128:(k + 1) * 128, of * 128:(of + 1) * 128])
                        if kind == 2:
                            for s0 in range(0, KTOT, 512):
                                wdt = min(512, KTOT - s0)
                                pp = (s0 // 512) % 2
                                sl = scr[:, pp * 512: pp * 512 + wdt]
                                qkv_block(of, wtile, sl, HALO2 - 2 * QW + s0, wdt)
                                pr.op("vector", lambda v, sl=sl, s0=s0, wdt=wdt: v.tensor_copy(
                                          vev[:, s0:s0 + wdt], sl),
                                      [sl], [vev[:, s0:s0 + wdt]])
                            for tb in range(KTOT // 128):
                                for hh in range(2):
                                    pst = psbf[:, (tb % 8) * 128:(tb % 8) * 128 + 64]
                                    pr.op("tensor",
                                          lambda t, tb=tb, hh=hh, pst=pst: t.transpose(
                                              pst, vev[hh * 64:hh * 64 + 64, tb * 128:(tb + 1) * 128],
                                              idn_bf[hh * 64:hh * 64 + 64, hh * 64:hh * 64 + 64]),
                                          [vev[hh * 64:hh * 64 + 64, tb * 128:(tb + 1) * 128],
                                           idn_bf[hh * 64:hh * 64 + 64, hh * 64:hh * 64 + 64]],
                                          [pst])
                                    h = hp * 2 + hh
                                    pr.op("vector",
                                          lambda v, tb=tb, h=h, pst=pst: v.tensor_copy(
                                              v_tr[:, tb * 512 + h * 64: tb * 512 + h * 64 + 64], pst),
                                          [pst], [v_tr[:, tb * 512 + h * 64: tb * 512 + h * 64 + 64]])
                            continue
                        is_q = kind == 0
                        ntok = QTOT if is_q else KTOT
                        col0 = HALO2 - (QW if is_q else 2 * QW)
                        out_bf = q_bf if is_q else k_bf
                        sc_col = qsct if is_q else ksct
                        cs_in, ss_in = (csq, ssq) if is_q else (csk, ssk)
                        for s0 in range(0, ntok, 512):
                            wdt = min(512, ntok - s0)
                            pp = (s0 // 512) % 2
                            sl = scr[:, pp * 512: pp * 512 + wdt]
                            sq = sqs[:, pp * 512: pp * 512 + wdt]
                            nr = nrm[:, pp * 512: pp * 512 + wdt]
                            iv = invb[:, pp * 512: pp * 512 + wdt]
                            q2 = qn2[:, pp * 512: pp * 512 + wdt]
                            qw_ = qsw[:, pp * 512: pp * 512 + wdt]
                            rc = rot_cs[:, pp * 512: pp * 512 + wdt]
                            rs = rot_ss[:, pp * 512: pp * 512 + wdt]
                            qkv_block(of, wtile, sl, col0 + s0, wdt)
                            pr.dma("scalar", rc, cs_in[:, s0:s0 + wdt])
                            pr.dma("scalar", rs, ss_in[:, s0:s0 + wdt])
                            pr.op("scalar", lambda s, sq=sq, sl=sl: s.activation(sq, sl, AF.Square),
                                  [sl], [sq])
                            ps = em.take_psum()
                            pr.op("tensor", lambda t, ps=ps, sq=sq, wdt=wdt: t.matmul(
                                      ps[0:2, 0:wdt], shD[:, :], sq, start=True, stop=True),
                                  [shD[:, :], sq], [ps[0:2, 0:wdt]])
                            pr.op("vector", lambda v, ps=ps, nr=nr, wdt=wdt: v.tensor_copy(
                                      nr, ps[0:2, 0:wdt]),
                                  [ps[0:2, 0:wdt]], [nr])
                            pr.op("scalar", lambda s, nr=nr: s.activation(nr, nr, AF.Sqrt),
                                  [nr], [nr])
                            pr.op("vector", lambda v, nr=nr: v.tensor_scalar_max(nr, nr, 1e-12),
                                  [nr], [nr])
                            pr.op("scalar", lambda s, nr=nr: s.activation(nr, nr, AF.Ln),
                                  [nr], [nr])
                            pr.op("scalar", lambda s, nr=nr: s.activation(nr, nr, AF.Exp, scale=f32(-1.0)),
                                  [nr], [nr])
                            ps2 = em.take_psum()
                            pr.op("tensor", lambda t, ps2=ps2, nr=nr, wdt=wdt: t.matmul(
                                      ps2[:, 0:wdt], shT[:, :], nr, start=True, stop=True),
                                  [shT[:, :], nr], [ps2[:, 0:wdt]])
                            pr.op("vector", lambda v, ps2=ps2, iv=iv, wdt=wdt: v.tensor_copy(iv, ps2[:, 0:wdt]),
                                  [ps2[:, 0:wdt]], [iv])
                            pr.op("vector", lambda v, q2=q2, sl=sl, iv=iv: v.tensor_mul(q2, sl, iv),
                                  [sl, iv], [q2])
                            pr.op("vector", lambda v, q2=q2, sc_col=sc_col, hp=hp: v.tensor_scalar(
                                      q2, q2, sc_col[:, hp:hp + 1], None, ALU.mult),
                                  [q2, sc_col[:, hp:hp + 1]], [q2])
                            for blk4 in range(4):
                                pr.dma("gpsimd",
                                       qw_[blk4 * 32:blk4 * 32 + 32, :],
                                       q2[(blk4 * 32) ^ 32:((blk4 * 32) ^ 32) + 32, :])
                            pr.op("vector", lambda v, sl=sl, q2=q2, rc=rc: v.tensor_mul(sl, q2, rc),
                                  [q2, rc], [sl])
                            pr.op("vector", lambda v, qw_=qw_, rs=rs: v.tensor_mul(qw_, qw_, rs),
                                  [qw_, rs], [qw_])
                            pr.op("vector",
                                  lambda v, out_bf=out_bf, hp=hp, ntok=ntok, s0=s0, wdt=wdt, sl=sl, qw_=qw_:
                                  v.tensor_add(out_bf[:, hp * ntok + s0: hp * ntok + s0 + wdt], sl, qw_),
                                  [sl, qw_], [out_bf[:, hp * ntok + s0: hp * ntok + s0 + wdt]])
                for e in ENGINES:
                    pr.barrier_to(e)

            with ExitStack() as esb:
                ecb = esb.enter_context
                mask_n = ecb(nc.sbuf_tensor(pfx + "mask_n", [128, 4 * 1024], F32))
                mask_1 = ecb(nc.sbuf_tensor(pfx + "mask_1", [128, 4 * 1024], F32))
                sc_f = ecb(nc.sbuf_tensor(pfx + "sc_f", [128, 1024], F32))
                att_bf = ecb(nc.sbuf_tensor(pfx + "att_bf", [128, 1024], BF16))
                attT_bf = ecb(nc.sbuf_tensor(pfx + "attT_bf", [128, 8 * 128], BF16))
                mrow = ecb(nc.sbuf_tensor(pfx + "mrow", [128, 2], F32))
                rsum = ecb(nc.sbuf_tensor(pfx + "rsum", [128, 2], F32))
                recb = ecb(nc.sbuf_tensor(pfx + "recb", [128, 2], F32))
                for qt4 in range(4):
                    pr.dma("scalar", mask_n[:, qt4 * 1024:(qt4 + 1) * 1024], l0mask[qt4 * 128:(qt4 + 1) * 128, :])
                    pr.dma("scalar", mask_1[:, qt4 * 1024:(qt4 + 1) * 1024], l0mask_w1[qt4 * 128:(qt4 + 1) * 128, :])
                for wq in range(NQ):
                    for h in range(8):
                        hp, hh = h // 2, h % 2
                        for qt in range(4):
                            q0 = wq * QW + qt * 128
                            mask = mask_1 if wq == 1 else mask_n
                            pshs = []
                            for kh in range(2):
                                k0 = wq * QW + kh * 512
                                ps = em.take_psum()
                                pr.op("tensor",
                                      lambda t, ps=ps, hp=hp, hh=hh, q0=q0, k0=k0: t.matmul(
                                          ps[:, 0:512],
                                          q_bf[hh * 64:hh * 64 + 64, hp * QTOT + q0: hp * QTOT + q0 + 128],
                                          k_bf[hh * 64:hh * 64 + 64, hp * KTOT + k0: hp * KTOT + k0 + 512],
                                          start=True, stop=True),
                                      [q_bf[hh * 64:hh * 64 + 64, hp * QTOT + q0: hp * QTOT + q0 + 128],
                                       k_bf[hh * 64:hh * 64 + 64, hp * KTOT + k0: hp * KTOT + k0 + 512]],
                                      [ps[:, 0:512]])
                                pshs.append(ps)
                            for kh in range(2):
                                pr.op("vector",
                                      lambda v, kh=kh, qt=qt, mask=mask, ps=pshs[kh]: v.tensor_add(
                                          sc_f[:, kh * 512:(kh + 1) * 512], ps[:, 0:512],
                                          mask[:, qt * 1024 + kh * 512: qt * 1024 + kh * 512 + 512]),
                                      [pshs[kh][:, 0:512],
                                       mask[:, qt * 1024 + kh * 512: qt * 1024 + kh * 512 + 512]],
                                      [sc_f[:, kh * 512:(kh + 1) * 512]])
                            pr.op("vector", lambda v: v.reduce_max(mrow[:, 0:1], sc_f[:, 0:1024], axis=AX.X, negate=True),
                                  [sc_f[:, 0:1024]], [mrow[:, 0:1]])
                            pr.op("scalar", lambda s: s.activation(sc_f[:, 0:1024], sc_f[:, 0:1024], AF.Exp,
                                                                   bias=mrow[:, 0:1], accum_out=rsum[:, 0:1]),
                                  [sc_f[:, 0:1024], mrow[:, 0:1]], [sc_f[:, 0:1024], rsum[:, 0:1]])
                            pr.op("vector", lambda v: v.reciprocal(recb[:, 0:1], rsum[:, 0:1]),
                                  [rsum[:, 0:1]], [recb[:, 0:1]])
                            pr.op("vector", lambda v: v.tensor_scalar(att_bf[:, 0:1024], sc_f[:, 0:1024],
                                                                      recb[:, 0:1], None, ALU.mult),
                                  [sc_f[:, 0:1024], recb[:, 0:1]], [att_bf[:, 0:1024]])
                            for kt in range(8):
                                pst = psbf[:, kt * 128:(kt + 1) * 128]
                                pr.op("tensor",
                                      lambda t, kt=kt, pst=pst: t.transpose(
                                          pst, att_bf[:, kt * 128:(kt + 1) * 128], idn_bf[:, :]),
                                      [att_bf[:, kt * 128:(kt + 1) * 128], idn_bf[:, :]], [pst])
                                pr.op("vector",
                                      lambda v, kt=kt, pst=pst: v.tensor_copy(
                                          attT_bf[:, kt * 128:(kt + 1) * 128], pst),
                                      [pst], [attT_bf[:, kt * 128:(kt + 1) * 128]])
                            psA = em.take_psum()
                            kb0 = wq * 4
                            em.mm_group(
                                [(v_tr[:, (kb0 + kt) * 512 + h * 64: (kb0 + kt) * 512 + h * 64 + 64],
                                  attT_bf[:, kt * 128:(kt + 1) * 128]) for kt in range(8)],
                                psA[0:64, 0:128])
                            pr.op("vector", lambda v, hp=hp, hh=hh, q0=q0, psA=psA: v.tensor_copy(
                                      o_bf[hh * 64:hh * 64 + 64, hp * QTOT + q0: hp * QTOT + q0 + 128],
                                      psA[0:64, 0:128]),
                                  [psA[0:64, 0:128]],
                                  [o_bf[hh * 64:hh * 64 + 64, hp * QTOT + q0: hp * QTOT + q0 + 128]])
                for e in ENGINES:
                    pr.barrier_to(e)

        TT1 = T + 1
        with ExitStack() as esc:
            ecc = esc.enter_context
            x2m = ecc(nc.sbuf_tensor(pfx + "x2m", [128, 8 * TT1], F32))
            xrl = ecc(nc.sbuf_tensor(pfx + "p2xrl", [128, 512], F32))
            with ExitStack() as esd:
                ecd = esd.enter_context
                wo_bf = ecd(nc.sbuf_tensor(pfx + "wo_bf", [128, 4 * DM], BF16))
                for k in range(4):
                    pr.dma("gpsimd", wo_bf[:, k * DM:(k + 1) * DM], wo[k * 128:(k + 1) * 128, :])
                for of in range(8):
                    for s0 in blocks512(TT1):
                        wdt = 512
                        ps = em.take_psum()
                        em.mm_group(
                            [(wo_bf[:, k * DM + of * 128: k * DM + of * 128 + 128],
                              o_bf[:, k * QTOT + QW - 1 + s0: k * QTOT + QW - 1 + s0 + wdt])
                             for k in range(4)],
                            ps[:, 0:wdt])
                        pr.dma("scalar", xrl[:, 0:wdt], xin_t[of * 128:(of + 1) * 128,
                                                             HALO2 - 1 + s0: HALO2 - 1 + s0 + wdt])
                        pr.op("vector",
                              lambda v, of=of, s0=s0, wdt=wdt, ps=ps: v.tensor_add(
                                  x2m[:, of * TT1 + s0: of * TT1 + s0 + wdt], ps[:, 0:wdt],
                                  xrl[:, 0:wdt]),
                              [ps[:, 0:wdt], xrl[:, 0:wdt]],
                              [x2m[:, of * TT1 + s0: of * TT1 + s0 + wdt]])
                for e in ENGINES:
                    pr.barrier_to(e)
            for of in range(8):
                pr.dma("gpsimd", sp_x2m[of * 128:(of + 1) * 128, :], x2m[:, of * TT1: of * TT1 + TT1])

            lgt = ecc(nc.sbuf_tensor(pfx + "lgt", [128, 8], F32))
            lbt = ecc(nc.sbuf_tensor(pfx + "lbt", [128, 8], F32))
            b1t = ecc(nc.sbuf_tensor(pfx + "b1t", [128, 2 * (HIDP // 128)], F32))
            b2t = ecc(nc.sbuf_tensor(pfx + "b2t", [128, 8], F32))
            pr.dma("sync", lgt[:, :], lgc[:, :])
            pr.dma("sync", lbt[:, :], lbc[:, :])
            pr.dma("sync", b1t[:, :], b1p[:, :])
            pr.dma("sync", b2t[:, :], b2c[:, :])
            FCH = 512
            hsh = ecc(nc.sbuf_tensor(pfx + "hsh", [128, 8 * FCH], F32))
            hln_bf = ecc(nc.sbuf_tensor(pfx + "hln_bf", [128, 8 * FCH], BF16))
            h2_bf = ecc(nc.sbuf_tensor(pfx + "h2_bf", [128, (HIDP // 128) * FCH], BF16))
            sqt = ecc(nc.sbuf_tensor(pfx + "p2sq", [128, 2 * 512], F32))
            strow2 = ecc(nc.sbuf_tensor(pfx + "strow2", [1, 4 * 512], F32))
            mub2 = ecc(nc.sbuf_tensor(pfx + "mub2", [128, FCH], F32))
            rstdb2 = ecc(nc.sbuf_tensor(pfx + "rstdb2", [128, FCH], F32))
            wstr = [ecc(nc.sbuf_tensor(pfx + f"wstr{i}", [128, 8 * 128], BF16)) for i in range(4)]
            gel = ecc(nc.sbuf_tensor(pfx + "gel", [128, 512], F32))
            aev = ecc(nc.sbuf_tensor(pfx + "aev", [128, 512], F32))
            xq2 = ecc(nc.sbuf_tensor(pfx + "xq2", [128, 512], F32))
            shsave = ecc(nc.sbuf_tensor(pfx + "shsave", [128, 8], F32))
            shsave_b = ecc(nc.sbuf_tensor(pfx + "shsave_b", [128, 8], F32))
            shsave2 = None
            NJ = HIDP // 128
            for fc in range(T // FCH):
                c0 = fc * FCH
                for of in range(4, 8):
                    pr.op("vector",
                          lambda v, of=of, c0=c0: v.tensor_copy(
                              shsave[:, of:of + 1], x2m[:, of * TT1 + c0 + FCH: of * TT1 + c0 + FCH + 1]),
                          [x2m[:, of * TT1 + c0 + FCH: of * TT1 + c0 + FCH + 1]],
                          [shsave[:, of:of + 1]])
                for of in range(8):
                    off = 1 if of < 4 else 0
                    pr.op("vector",
                          lambda v, of=of, off=off, c0=c0: v.tensor_copy(
                              hsh[:, of * FCH:(of + 1) * FCH],
                              x2m[:, of * TT1 + c0 + off: of * TT1 + c0 + off + FCH]),
                          [x2m[:, of * TT1 + c0 + off: of * TT1 + c0 + off + FCH]],
                          [hsh[:, of * FCH:(of + 1) * FCH]])
                if fc > 0:
                    for of in range(4, 8):
                        pr.op("vector",
                              lambda v, of=of: v.tensor_copy(
                                  hsh[:, of * FCH: of * FCH + 1], shsave2[:, of:of + 1]),
                              [shsave2[:, of:of + 1]], [hsh[:, of * FCH: of * FCH + 1]])
                shsave, shsave2 = (shsave2, shsave) if fc > 0 else (shsave_b, shsave)
                emit_ln_feature(pr, em, hsh, FCH, strow2, sqt, mub2, rstdb2, nfeat=8)
                for of in range(8):
                    sl = hsh[:, of * FCH:(of + 1) * FCH]
                    pr.op("vector", lambda v, sl=sl: v.tensor_sub(sl, sl, mub2[:, 0:FCH]),
                          [sl, mub2[:, 0:FCH]], [sl])
                    pr.op("vector", lambda v, sl=sl: v.tensor_mul(sl, sl, rstdb2[:, 0:FCH]),
                          [sl, rstdb2[:, 0:FCH]], [sl])
                    pr.op("vector", lambda v, sl=sl, of=of: v.tensor_scalar(
                              hln_bf[:, of * FCH:(of + 1) * FCH], sl,
                              lgt[:, of:of + 1], lbt[:, of:of + 1], ALU.mult, ALU.add),
                          [sl, lgt[:, of:of + 1], lbt[:, of:of + 1]],
                          [hln_bf[:, of * FCH:(of + 1) * FCH]])
                for j in range(NJ):
                    wa = wstr[(2 * j) % 4]
                    wg = wstr[(2 * j + 1) % 4]
                    for k in range(8):
                        pr.dma("gpsimd", wa[:, k * 128:(k + 1) * 128],
                               w1p[k * 128:(k + 1) * 128, j * 256: j * 256 + 128])
                        pr.dma("gpsimd", wg[:, k * 128:(k + 1) * 128],
                               w1p[k * 128:(k + 1) * 128, j * 256 + 128: j * 256 + 256])
                    for s0 in range(0, FCH, 512):
                        wdt = min(512, FCH - s0)
                        psa = em.take_psum()
                        em.mm_group(
                            [(wa[:, k * 128:(k + 1) * 128], hln_bf[:, k * FCH + s0:k * FCH + s0 + wdt])
                             for k in range(8)], psa[:, 0:wdt])
                        psg = em.take_psum()
                        em.mm_group(
                            [(wg[:, k * 128:(k + 1) * 128], hln_bf[:, k * FCH + s0:k * FCH + s0 + wdt])
                             for k in range(8)], psg[:, 0:wdt])
                        pr.op("vector", lambda v, psa=psa, wdt=wdt, j=j: v.tensor_scalar(
                                  aev[:, 0:wdt], psa[:, 0:wdt], b1t[:, 2 * j:2 * j + 1], None, ALU.add),
                              [psa[:, 0:wdt], b1t[:, 2 * j:2 * j + 1]], [aev[:, 0:wdt]])
                        pr.op("scalar", lambda s, psg=psg, wdt=wdt, j=j: s.activation(
                                  gel[:, 0:wdt], psg[:, 0:wdt], AF.Gelu, bias=b1t[:, 2 * j + 1:2 * j + 2]),
                              [psg[:, 0:wdt], b1t[:, 2 * j + 1:2 * j + 2]], [gel[:, 0:wdt]])
                        pr.op("vector", lambda v, j=j, s0=s0, wdt=wdt: v.tensor_mul(
                                  h2_bf[:, j * FCH + s0: j * FCH + s0 + wdt], aev[:, 0:wdt], gel[:, 0:wdt]),
                              [aev[:, 0:wdt], gel[:, 0:wdt]],
                              [h2_bf[:, j * FCH + s0: j * FCH + s0 + wdt]])
                for of in range(8):
                    for s0 in range(0, FCH, 512):
                        wdt = min(512, FCH - s0)
                        ps = em.take_psum()
                        for k in range(NJ):
                            wv = wstr[k % 4]
                            pr.dma("gpsimd", wv[:, 0:128],
                                   w2p[k * 128:(k + 1) * 128, of * 128:(of + 1) * 128])
                            pr.op("tensor",
                                  lambda t, ps=ps, wv=wv, k=k, s0=s0, wdt=wdt:
                                  t.matmul(ps[:, 0:wdt], wv[:, 0:128],
                                           h2_bf[:, k * FCH + s0:k * FCH + s0 + wdt],
                                           start=(k == 0), stop=(k == NJ - 1)),
                                  [wv[:, 0:128], h2_bf[:, k * FCH + s0:k * FCH + s0 + wdt]],
                                  [ps[:, 0:wdt]],
                                  check_writes=[ps[:, 0:wdt]] if k == 0 else [])
                        pr.op("vector", lambda v, ps=ps, wdt=wdt, of=of: v.tensor_scalar(
                                  xq2[:, 0:wdt], ps[:, 0:wdt], b2t[:, of:of + 1], None, ALU.add),
                              [ps[:, 0:wdt], b2t[:, of:of + 1]], [xq2[:, 0:wdt]])
                        pr.op("vector",
                              lambda v, of=of, c0=c0, s0=s0, wdt=wdt: v.tensor_add(
                                  x2m[:, of * TT1 + 1 + c0 + s0: of * TT1 + 1 + c0 + s0 + wdt],
                                  x2m[:, of * TT1 + 1 + c0 + s0: of * TT1 + 1 + c0 + s0 + wdt],
                                  xq2[:, 0:wdt]),
                              [xq2[:, 0:wdt],
                               x2m[:, of * TT1 + 1 + c0 + s0: of * TT1 + 1 + c0 + s0 + wdt]],
                              [x2m[:, of * TT1 + 1 + c0 + s0: of * TT1 + 1 + c0 + s0 + wdt]])
            strow3 = ecc(nc.sbuf_tensor(pfx + "strow3", [1, 4 * 512], F32))
            mub3 = ecc(nc.sbuf_tensor(pfx + "mub3", [128, 512], F32))
            rstdb3 = ecc(nc.sbuf_tensor(pfx + "rstdb3", [128, 512], F32))
            xlnb = ecc(nc.sbuf_tensor(pfx + "xlnb", [128, 8 * 512], F32))
            for of in range(8):
                pr.dma("gpsimd", o_out[of * 128:(of + 1) * 128, :], x2m[:, of * TT1 + 1: of * TT1 + TT1])
            for s0 in range(0, T, 512):
                for of in range(8):
                    pr.dma("scalar", xrl[:, 0:512], xres_t[of * 128:(of + 1) * 128, s0:s0 + 512])
                    pr.op("vector",
                          lambda v, of=of, s0=s0: v.tensor_add(
                              xlnb[:, of * 512:(of + 1) * 512],
                              x2m[:, of * TT1 + 1 + s0: of * TT1 + 1 + s0 + 512], xrl[:, 0:512]),
                          [x2m[:, of * TT1 + 1 + s0: of * TT1 + 1 + s0 + 512], xrl[:, 0:512]],
                          [xlnb[:, of * 512:(of + 1) * 512]])
                emit_ln_feature(pr, em, xlnb, 512, strow3, sqt, mub3, rstdb3, nfeat=8)
                for of in range(8):
                    sl = xlnb[:, of * 512:(of + 1) * 512]
                    pr.op("vector", lambda v, sl=sl: v.tensor_sub(sl, sl, mub3[:, 0:512]),
                          [sl, mub3[:, 0:512]], [sl])
                    pr.op("vector", lambda v, sl=sl: v.tensor_mul(sl, sl, rstdb3[:, 0:512]),
                          [sl, rstdb3[:, 0:512]], [sl])
                    pr.dma("gpsimd", o_ln[of * 128:(of + 1) * 128, s0:s0 + 512], sl)
        for e in ENGINES:
            pr.barrier_to(e)
        print(f"phase2 program: {pr.n_ops} ops, {pr.n_waits} waits")
    return nc


NXK = 32



def build_phase2():
    nc = bass.Bass("TRN2", target_bir_lowering=False, debug=False)
    W = HALO2 + T
    NQ = T // QW + 1
    QTOT = T + QW
    KTOT = T + 2 * QW

    xin_t = nc.dram_tensor("xin_t", [DM, W], F32, kind="ExternalInput").ap()
    xres_t = nc.dram_tensor("xres_t", [DM, T], F32, kind="ExternalInput").ap()
    wqkv = nc.dram_tensor("wqkv", [DM, 1536], F32, kind="ExternalInput").ap()
    wo = nc.dram_tensor("wo", [512, DM], F32, kind="ExternalInput").ap()
    w1p = nc.dram_tensor("w1p", [DM, 2 * HIDP], F32, kind="ExternalInput").ap()
    w2p = nc.dram_tensor("w2p", [HIDP, DM], F32, kind="ExternalInput").ap()
    b1p = nc.dram_tensor("b1p", [128, 2 * (HIDP // 128)], F32, kind="ExternalInput").ap()
    b2c = nc.dram_tensor("b2c", [128, 8], F32, kind="ExternalInput").ap()
    lgc = nc.dram_tensor("lgc", [128, 8], F32, kind="ExternalInput").ap()
    lbc = nc.dram_tensor("lbc", [128, 8], F32, kind="ExternalInput").ap()
    qs8 = nc.dram_tensor("qs8", [128, 4], F32, kind="ExternalInput").ap()
    ksc = nc.dram_tensor("ksc", [128, 4], F32, kind="ExternalInput").ap()
    selheadT = nc.dram_tensor("selheadT", [2, 128], F32, kind="ExternalInput").ap()
    selhead = nc.dram_tensor("selhead", [128, 2], F32, kind="ExternalInput").ap()
    csq = nc.dram_tensor("csq", [128, QTOT], F32, kind="ExternalInput").ap()
    ssq = nc.dram_tensor("ssq", [128, QTOT], F32, kind="ExternalInput").ap()
    csk = nc.dram_tensor("csk", [128, KTOT], F32, kind="ExternalInput").ap()
    ssk = nc.dram_tensor("ssk", [128, KTOT], F32, kind="ExternalInput").ap()
    l0mask = nc.dram_tensor("l0mask", [512, 1024], F32, kind="ExternalInput").ap()
    l0mask_w1 = nc.dram_tensor("l0mask_w1", [512, 1024], F32, kind="ExternalInput").ap()
    idn_in = nc.dram_tensor("idn128", [128, 128], F32, kind="ExternalInput").ap()

    o_out = nc.dram_tensor("xout_t", [DM, T], F32, kind="ExternalOutput").ap()
    o_ln = nc.dram_tensor("xln_t", [DM, T], F32, kind="ExternalOutput").ap()
    sp_x2m = nc.dram_tensor("sp_x2m", [DM, T + 1], F32).ap()

    with ExitStack() as _es0:
        _ec0 = _es0.enter_context
        blk = _ec0(nc.Block())
        pr = Prog(nc, _es0)
        em = Emitter(nc, _es0, pr, n_psum=7)
        psbf = _ec0(nc.psum_tensor("psbf", [128, 1024], BF16))
        emit_phase2(nc, pr, em, psbf, xin_t, xres_t, wqkv, wo, w1p, w2p, b1p, b2c, lgc, lbc, qs8, ksc, selheadT, selhead, csq, ssq, csk, ssk, l0mask, l0mask_w1, idn_in, o_out, o_ln, sp_x2m)
    return nc


def _nu2(d, cap=11):
    d = abs(int(d))
    if d == 0:
        return cap
    v = 0
    while d % 2 == 0 and v < cap:
        v += 1
        d //= 2
    return v


def _count_tables_A():
    mul = np.zeros((16, 256, 256), np.float32)
    add = np.full((16, 256, 256), -1e38, np.float32)
    idx = np.arange(256)
    for h in range(16):
        m = np.array([_nu2(i - h, 7) for i in range(256)])
        x = idx[:, None] ^ idx[None, :]
        bl = np.zeros_like(x)
        nz = x > 0
        bl[nz] = (np.floor(np.log2(x[nz]))).astype(np.int64) + 1
        sseg = np.maximum(0, bl - 2)
        cnt = np.maximum(0, np.minimum(m[:, None], m[None, :]) + 1 - sseg)
        valid = (cnt > 0) & (idx[None, :] <= idx[:, None])
        mul[h][valid] = cnt[valid]
        add[h][valid] = 0.0
    return mul, add


def _count_tables_B(t0, half):
    K = T + NXK
    mul = np.zeros((16, 16, K), np.float32)
    add = np.full((16, 16, K), -1e38, np.float32)
    for h in range(16):
        iv = t0 + h + 128 * np.arange(16)
        jl = t0 + np.arange(T)
        jx = np.concatenate([np.arange(16), 1024 + np.arange(16)])
        jv = np.concatenate([jl, jx])
        mi = np.array([_nu2(i - h) for i in iv])
        mj = np.array([_nu2(j - h) for j in jv])
        x = iv[:, None] ^ jv[None, :]
        bl = np.zeros_like(x)
        nz = x > 0
        bl[nz] = (np.floor(np.log2(x[nz]))).astype(np.int64) + 1
        sseg = np.maximum(0, bl - 2)
        cnt = np.maximum(0, np.minimum(mi[:, None], mj[None, :]) + 1 - sseg)
        valid = (cnt > 0) & (jv[None, :] <= iv[:, None])
        if half == 0:
            valid[:, T:] = False
        else:
            keep = np.zeros(NXK, bool)
            keep[h] = True
            keep[16 + h] = True
            valid[:, T:] &= keep[None, :]
        mul[h][valid] = cnt[valid]
        add[h][valid] = 0.0
    return mul, add


def prep_phase3(inputs):
    wq_t = np.ascontiguousarray(_np_f32(inputs["L1_attn_Wq"]).T)
    wk_t = np.ascontiguousarray(_np_f32(inputs["L1_attn_Wk"]).T)
    wv_t = np.ascontiguousarray(_np_f32(inputs["L1_attn_Wv"]).T)
    wo_t = np.ascontiguousarray(_np_f32(inputs["L1_attn_Wo"]).T)

    def cols8(v):
        return np.ascontiguousarray(_np_f32(v).reshape(8, 128).T)

    tamul, taadd = _count_tables_A()
    shared = {
        "wq_t": wq_t, "wk_t": wk_t, "wv_t": wv_t, "wo_t": wo_t,
        "bqc": cols8(inputs["L1_attn_bq"]), "bkc": cols8(inputs["L1_attn_bk"]),
        "bvc": cols8(inputs["L1_attn_bv"]), "boc": cols8(inputs["L1_attn_bo"]),
        "tamul": tamul.reshape(16 * 256, 256), "taadd": taadd.reshape(16 * 256, 256),
    }

    d = 1024
    inv_freq = 1.0 / 15000.0 ** (np.arange(0, d, 2, dtype=np.float32) / d)

    def rope(tpos):
        freqs = np.asarray(tpos, np.float32)[:, None] * inv_freq[None, :]
        emb = np.concatenate([freqs, freqs], axis=-1)
        cos, sin = np.cos(emb), np.sin(emb)
        sgn = np.r_[-np.ones(512, np.float32), np.ones(512, np.float32)]
        return (np.ascontiguousarray(cos.T.astype(np.float32)),
                np.ascontiguousarray((sin.T * sgn[:, None]).astype(np.float32)))

    per_core = []
    for c in range(NCORES):
        b, h = c // 2, c % 2
        t0 = h * T
        cs, ssg = rope(np.arange(t0, t0 + T))
        csx, ssgx = rope(np.concatenate([np.arange(16), 1024 + np.arange(16)]))
        tbmul, tbadd = _count_tables_B(t0, h)
        per_core.append({
            "cs_t": cs, "ssg_t": ssg, "csx_t": csx, "ssgx_t": ssgx,
            "tbmul": tbmul.reshape(16 * 16, T + NXK),
            "tbadd": tbadd.reshape(16 * 16, T + NXK),
        })
    return shared, per_core


def emit_phase3(nc, pr, em, psbf, xin_t, x2x_t, wq_t, wk_t, wv_t, wo_t, bqc, bkc, bvc, boc, cs_t, ssg_t, csx_t, ssgx_t, tamul, taadd, tbmul, tbadd, idn_in, o_x3, pfx=""):
    KB = T + NXK

    with ExitStack() as es:
        ec = es.enter_context
        idn_bf = ec(nc.sbuf_tensor(pfx + "p3idn", [128, 128], BF16))
        pr.dma("gpsimd", idn_bf[:, :], idn_in[:, :])

        qr_bf = ec(nc.sbuf_tensor(pfx + "qr_bf", [128, 8 * T], BF16))
        kr_bf = ec(nc.sbuf_tensor(pfx + "kr_bf", [128, 8 * T], BF16))
        v_tr = ec(nc.sbuf_tensor(pfx + "v_tr", [128, (T // 128) * 1024], BF16))
        o_bf = ec(nc.sbuf_tensor(pfx + "o_bf", [128, 8 * T], BF16))
        kx_bf = ec(nc.sbuf_tensor(pfx + "kx_bf", [128, 8 * NXK], BF16))
        vx_tr = ec(nc.sbuf_tensor(pfx + "vx_tr", [32, 16 * 64], BF16))

        with ExitStack() as esa:
            eca = esa.enter_context
            xin_bf = eca(nc.sbuf_tensor(pfx + "p3xin_bf", [128, 8 * T], BF16))
            xld = eca(nc.sbuf_tensor(pfx + "p3xld", [128, 512], F32))
            w_bf = [eca(nc.sbuf_tensor(pfx + f"p3w{i}", [128, 8 * 128], BF16)) for i in range(3)]
            qT = eca(nc.sbuf_tensor(pfx + "p3qT", [128, 2 * T], F32))
            bcol = eca(nc.sbuf_tensor(pfx + "p3bcol", [128, 8 * 3], F32))
            cstl = eca(nc.sbuf_tensor(pfx + "p3cs", [128, 2 * 512], F32))
            x2xT = eca(nc.sbuf_tensor(pfx + "p3x2x", [128, 8 * NXK], F32))
            x2x_bf = eca(nc.sbuf_tensor(pfx + "p3x2xb", [128, 8 * NXK], BF16))
            kxf = eca(nc.sbuf_tensor(pfx + "p3kxf", [128, 8 * NXK], F32))
            csx_s = eca(nc.sbuf_tensor(pfx + "p3csx", [128, 8 * NXK], F32))
            ssx_s = eca(nc.sbuf_tensor(pfx + "p3ssx", [128, 8 * NXK], F32))
            vev3 = eca(nc.sbuf_tensor(pfx + "vev3", [128, T], BF16))

            pr.dma("sync", bcol[:, 0:8], bqc[:, :])
            pr.dma("sync", bcol[:, 8:16], bkc[:, :])
            pr.dma("sync", bcol[:, 16:24], bvc[:, :])
            for k in range(8):
                for s0 in range(0, T, 512):
                    pr.dma("sync", xld[:, :], xin_t[k * 128:(k + 1) * 128, s0:s0 + 512])
                    pr.op("vector",
                          lambda v, k=k, s0=s0: v.tensor_copy(
                              xin_bf[:, k * T + s0:k * T + s0 + 512], xld[:, :]),
                          [xld[:, :]], [xin_bf[:, k * T + s0:k * T + s0 + 512]])
                pr.dma("sync", x2xT[:, k * NXK:(k + 1) * NXK], x2x_t[k * 128:(k + 1) * 128, :])
                pr.op("vector",
                      lambda v, k=k: v.tensor_copy(x2x_bf[:, k * NXK:(k + 1) * NXK],
                                                   x2xT[:, k * NXK:(k + 1) * NXK]),
                      [x2xT[:, k * NXK:(k + 1) * NXK]], [x2x_bf[:, k * NXK:(k + 1) * NXK]])
                pr.dma("sync", csx_s[:, k * NXK:(k + 1) * NXK], csx_t[k * 128:(k + 1) * 128, :])
                pr.dma("sync", ssx_s[:, k * NXK:(k + 1) * NXK], ssgx_t[k * 128:(k + 1) * 128, :])

            for kind, wsrc, bof, out_bf in ((0, wq_t, 0, qr_bf), (1, wk_t, 8, kr_bf)):
                for ofp in range(4):
                    for half, of in ((0, ofp), (1, ofp + 4)):
                        wt = w_bf[of % 3]
                        for k in range(8):
                            pr.dma("gpsimd", wt[:, k * 128:(k + 1) * 128],
                                   wsrc[k * 128:(k + 1) * 128, of * 128:(of + 1) * 128])
                        for s0 in range(0, T, 512):
                            ps = em.take_psum()
                            em.mm_group(
                                [(wt[:, k * 128:(k + 1) * 128], xin_bf[:, k * T + s0:k * T + s0 + 512])
                                 for k in range(8)],
                                ps[:, 0:512])
                            pr.op("vector",
                                  lambda v, half=half, s0=s0, ps=ps, bof=bof, of=of: v.tensor_scalar(
                                      qT[:, half * T + s0: half * T + s0 + 512], ps[:, 0:512],
                                      bcol[:, bof + of:bof + of + 1], None, ALU.add),
                                  [ps[:, 0:512], bcol[:, bof + of:bof + of + 1]],
                                  [qT[:, half * T + s0: half * T + s0 + 512]])
                        psx = em.take_psum()
                        em.mm_group(
                            [(wt[:, k * 128:(k + 1) * 128], x2x_bf[:, k * NXK:(k + 1) * NXK])
                             for k in range(8)],
                            psx[:, 0:NXK])
                        pr.op("vector",
                              lambda v, of=of, psx=psx, bof=bof: v.tensor_scalar(
                                  kxf[:, of * NXK:(of + 1) * NXK], psx[:, 0:NXK],
                                  bcol[:, bof + of:bof + of + 1], None, ALU.add),
                              [psx[:, 0:NXK], bcol[:, bof + of:bof + of + 1]],
                              [kxf[:, of * NXK:(of + 1) * NXK]])
                    for half, of in ((0, ofp), (1, ofp + 4)):
                        for s0 in range(0, T, 512):
                            pr.dma("scalar", cstl[:, 0:512], cs_t[of * 128:(of + 1) * 128, s0:s0 + 512])
                            pr.dma("scalar", cstl[:, 512:1024], ssg_t[of * 128:(of + 1) * 128, s0:s0 + 512])
                            pr.op("vector",
                                  lambda v, half=half, s0=s0: v.tensor_mul(
                                      cstl[:, 0:512],
                                      qT[:, half * T + s0: half * T + s0 + 512], cstl[:, 0:512]),
                                  [qT[:, half * T + s0: half * T + s0 + 512], cstl[:, 0:512]],
                                  [cstl[:, 0:512]])
                            pr.op("vector",
                                  lambda v, half=half, s0=s0: v.tensor_mul(
                                      cstl[:, 512:1024],
                                      qT[:, (1 - half) * T + s0: (1 - half) * T + s0 + 512], cstl[:, 512:1024]),
                                  [qT[:, (1 - half) * T + s0: (1 - half) * T + s0 + 512], cstl[:, 512:1024]],
                                  [cstl[:, 512:1024]])
                            pr.op("vector",
                                  lambda v, of=of, s0=s0, out_bf=out_bf: v.tensor_add(
                                      out_bf[:, of * T + s0: of * T + s0 + 512],
                                      cstl[:, 0:512], cstl[:, 512:1024]),
                                  [cstl[:, 0:512], cstl[:, 512:1024]],
                                  [out_bf[:, of * T + s0: of * T + s0 + 512]])
                if kind == 1:
                    for of in range(8):
                        pr.op("vector",
                              lambda v, of=of: v.tensor_mul(
                                  csx_s[:, of * NXK:(of + 1) * NXK],
                                  kxf[:, of * NXK:(of + 1) * NXK], csx_s[:, of * NXK:(of + 1) * NXK]),
                              [kxf[:, of * NXK:(of + 1) * NXK], csx_s[:, of * NXK:(of + 1) * NXK]],
                              [csx_s[:, of * NXK:(of + 1) * NXK]])
                        pr.op("vector",
                              lambda v, of=of: v.tensor_mul(
                                  ssx_s[:, of * NXK:(of + 1) * NXK],
                                  kxf[:, (of ^ 4) * NXK:((of ^ 4) + 1) * NXK], ssx_s[:, of * NXK:(of + 1) * NXK]),
                              [kxf[:, (of ^ 4) * NXK:((of ^ 4) + 1) * NXK], ssx_s[:, of * NXK:(of + 1) * NXK]],
                              [ssx_s[:, of * NXK:(of + 1) * NXK]])
                        pr.op("vector",
                              lambda v, of=of: v.tensor_add(
                                  kx_bf[:, of * NXK:(of + 1) * NXK],
                                  csx_s[:, of * NXK:(of + 1) * NXK], ssx_s[:, of * NXK:(of + 1) * NXK]),
                              [csx_s[:, of * NXK:(of + 1) * NXK], ssx_s[:, of * NXK:(of + 1) * NXK]],
                              [kx_bf[:, of * NXK:(of + 1) * NXK]])
            for of in range(8):
                wt = w_bf[of % 3]
                for k in range(8):
                    pr.dma("gpsimd", wt[:, k * 128:(k + 1) * 128],
                           wv_t[k * 128:(k + 1) * 128, of * 128:(of + 1) * 128])
                for s0 in range(0, T, 512):
                    ps = em.take_psum()
                    em.mm_group(
                        [(wt[:, k * 128:(k + 1) * 128], xin_bf[:, k * T + s0:k * T + s0 + 512])
                         for k in range(8)],
                        ps[:, 0:512])
                    pr.op("vector",
                          lambda v, of=of, s0=s0, ps=ps: v.tensor_scalar(
                              vev3[:, s0:s0 + 512], ps[:, 0:512],
                              bcol[:, 16 + of:16 + of + 1], None, ALU.add),
                          [ps[:, 0:512], bcol[:, 16 + of:16 + of + 1]],
                          [vev3[:, s0:s0 + 512]])
                for tb in range(T // 128):
                    for hh in range(2):
                        pst = psbf[:, (tb % 8) * 128:(tb % 8) * 128 + 64]
                        pr.op("tensor",
                              lambda t, tb=tb, hh=hh, pst=pst: t.transpose(
                                  pst, vev3[hh * 64:hh * 64 + 64, tb * 128:(tb + 1) * 128],
                                  idn_bf[hh * 64:hh * 64 + 64, hh * 64:hh * 64 + 64]),
                              [vev3[hh * 64:hh * 64 + 64, tb * 128:(tb + 1) * 128],
                               idn_bf[hh * 64:hh * 64 + 64, hh * 64:hh * 64 + 64]],
                              [pst])
                        h = of * 2 + hh
                        pr.op("vector",
                              lambda v, tb=tb, h=h, pst=pst: v.tensor_copy(
                                  v_tr[:, tb * 1024 + h * 64: tb * 1024 + h * 64 + 64], pst),
                              [pst], [v_tr[:, tb * 1024 + h * 64: tb * 1024 + h * 64 + 64]])
                psx = em.take_psum()
                em.mm_group(
                    [(wt[:, k * 128:(k + 1) * 128], x2x_bf[:, k * NXK:(k + 1) * NXK])
                     for k in range(8)],
                    psx[:, 0:NXK])
                pr.op("vector",
                      lambda v, of=of, psx=psx: v.tensor_scalar(
                          kxf[:, of * NXK:(of + 1) * NXK], psx[:, 0:NXK],
                          bcol[:, 16 + of:16 + of + 1], None, ALU.add),
                      [psx[:, 0:NXK], bcol[:, 16 + of:16 + of + 1]],
                      [kxf[:, of * NXK:(of + 1) * NXK]])
                pr.op("vector",
                      lambda v, of=of: v.tensor_copy(vev3[:, 0:NXK], kxf[:, of * NXK:(of + 1) * NXK]),
                      [kxf[:, of * NXK:(of + 1) * NXK]], [vev3[:, 0:NXK]])
                for hh in range(2):
                    pst = psbf[0:32, 0:64]
                    pr.op("tensor",
                          lambda t, hh=hh, pst=pst: t.transpose(
                              pst, vev3[hh * 64:hh * 64 + 64, 0:NXK],
                              idn_bf[hh * 64:hh * 64 + 64, hh * 64:hh * 64 + 64]),
                          [vev3[hh * 64:hh * 64 + 64, 0:NXK],
                           idn_bf[hh * 64:hh * 64 + 64, hh * 64:hh * 64 + 64]],
                          [pst])
                    h = of * 2 + hh
                    pr.op("vector",
                          lambda v, h=h, pst=pst: v.tensor_copy(
                              vx_tr[:, h * 64:(h + 1) * 64], pst),
                          [pst], [vx_tr[:, h * 64:(h + 1) * 64]])
            for e in ENGINES:
                pr.barrier_to(e)

        with ExitStack() as esb:
            ecb = esb.enter_context
            ta_m = ecb(nc.sbuf_tensor(pfx + "ta_m", [128, 2 * 256], F32))
            ta_a = ecb(nc.sbuf_tensor(pfx + "ta_a", [128, 2 * 256], F32))
            sc3 = ecb(nc.sbuf_tensor(pfx + "sc3", [128, 256], F32))
            att3 = ecb(nc.sbuf_tensor(pfx + "att3", [128, 256], BF16))
            attT3 = ecb(nc.sbuf_tensor(pfx + "attT3", [128, 2 * 128], BF16))
            mr3 = ecb(nc.sbuf_tensor(pfx + "mr3", [128, 2], F32))
            rs3 = ecb(nc.sbuf_tensor(pfx + "rs3", [128, 2], F32))
            rc3 = ecb(nc.sbuf_tensor(pfx + "rc3", [128, 2], F32))
            for h in range(16):
                for pt in range(2):
                    pr.dma("scalar", ta_m[:, pt * 256:(pt + 1) * 256],
                           tamul[h * 256 + pt * 128: h * 256 + (pt + 1) * 128, :])
                    pr.dma("scalar", ta_a[:, pt * 256:(pt + 1) * 256],
                           taadd[h * 256 + pt * 128: h * 256 + (pt + 1) * 128, :])
                hf, hh = h // 2, h % 2
                for seg in range(T // 256):
                    for qt in range(2):
                        q0 = seg * 256 + qt * 128
                        ps = em.take_psum()
                        pr.op("tensor",
                              lambda t, ps=ps, hf=hf, hh=hh, q0=q0, seg=seg: t.matmul(
                                  ps[:, 0:256],
                                  qr_bf[hh * 64:hh * 64 + 64, hf * T + q0: hf * T + q0 + 128],
                                  kr_bf[hh * 64:hh * 64 + 64, hf * T + seg * 256: hf * T + seg * 256 + 256],
                                  start=True, stop=True),
                              [qr_bf[hh * 64:hh * 64 + 64, hf * T + q0: hf * T + q0 + 128],
                               kr_bf[hh * 64:hh * 64 + 64, hf * T + seg * 256: hf * T + seg * 256 + 256]],
                              [ps[:, 0:256]])
                        pr.op("vector",
                              lambda v, ps=ps, qt=qt: v.tensor_mul(
                                  sc3[:, 0:256], ps[:, 0:256], ta_m[:, qt * 256:(qt + 1) * 256]),
                              [ps[:, 0:256], ta_m[:, qt * 256:(qt + 1) * 256]], [sc3[:, 0:256]])
                        pr.op("vector",
                              lambda v, qt=qt: v.tensor_add(
                                  sc3[:, 0:256], sc3[:, 0:256], ta_a[:, qt * 256:(qt + 1) * 256]),
                              [sc3[:, 0:256], ta_a[:, qt * 256:(qt + 1) * 256]], [sc3[:, 0:256]])
                        pr.op("vector", lambda v: v.reduce_max(mr3[:, 0:1], sc3[:, 0:256], axis=AX.X, negate=True),
                              [sc3[:, 0:256]], [mr3[:, 0:1]])
                        pr.op("scalar", lambda s: s.activation(sc3[:, 0:256], sc3[:, 0:256], AF.Exp,
                                                               bias=mr3[:, 0:1], accum_out=rs3[:, 0:1]),
                              [sc3[:, 0:256], mr3[:, 0:1]], [sc3[:, 0:256], rs3[:, 0:1]])
                        pr.op("vector", lambda v: v.reciprocal(rc3[:, 0:1], rs3[:, 0:1]),
                              [rs3[:, 0:1]], [rc3[:, 0:1]])
                        pr.op("vector", lambda v: v.tensor_scalar(att3[:, 0:256], sc3[:, 0:256],
                                                                  rc3[:, 0:1], None, ALU.mult),
                              [sc3[:, 0:256], rc3[:, 0:1]], [att3[:, 0:256]])
                        for kt in range(2):
                            pst = psbf[:, kt * 128:(kt + 1) * 128]
                            pr.op("tensor",
                                  lambda t, kt=kt, pst=pst: t.transpose(
                                      pst, att3[:, kt * 128:(kt + 1) * 128], idn_bf[:, :]),
                                  [att3[:, kt * 128:(kt + 1) * 128], idn_bf[:, :]], [pst])
                            pr.op("vector",
                                  lambda v, kt=kt, pst=pst: v.tensor_copy(
                                      attT3[:, kt * 128:(kt + 1) * 128], pst),
                                  [pst], [attT3[:, kt * 128:(kt + 1) * 128]])
                        psA = em.take_psum()
                        em.mm_group(
                            [(v_tr[:, (seg * 2 + kt) * 1024 + h * 64: (seg * 2 + kt) * 1024 + h * 64 + 64],
                              attT3[:, kt * 128:(kt + 1) * 128]) for kt in range(2)],
                            psA[0:64, 0:128])
                        pr.op("vector", lambda v, hf=hf, hh=hh, q0=q0, psA=psA: v.tensor_copy(
                                  o_bf[hh * 64:hh * 64 + 64, hf * T + q0: hf * T + q0 + 128],
                                  psA[0:64, 0:128]),
                              [psA[0:64, 0:128]],
                              [o_bf[hh * 64:hh * 64 + 64, hf * T + q0: hf * T + q0 + 128]])
            for e in ENGINES:
                pr.barrier_to(e)

        with ExitStack() as esc:
            ecc = esc.enter_context
            qbx = ecc(nc.sbuf_tensor(pfx + "qbx", [128, 16], BF16))
            tb_m = ecc(nc.sbuf_tensor(pfx + "tb_m", [16, KB], F32))
            tb_a = ecc(nc.sbuf_tensor(pfx + "tb_a", [16, KB], F32))
            scb = ecc(nc.sbuf_tensor(pfx + "scb", [16, KB], F32))
            attb = ecc(nc.sbuf_tensor(pfx + "attb", [16, KB], BF16))
            attbT = ecc(nc.sbuf_tensor(pfx + "attbT", [128, 17 * 16], BF16))
            mrb = ecc(nc.sbuf_tensor(pfx + "mrb", [16, 2], F32))
            rsb = ecc(nc.sbuf_tensor(pfx + "rsb", [16, 2], F32))
            rcb2 = ecc(nc.sbuf_tensor(pfx + "rcb2", [16, 2], F32))
            for h in range(16):
                hf, hh = h // 2, h % 2
                pr.dma("scalar", tb_m[:, :], tbmul[h * 16:(h + 1) * 16, :])
                pr.dma("scalar", tb_a[:, :], tbadd[h * 16:(h + 1) * 16, :])
                pr.dma("gpsimd", qbx[hh * 64:hh * 64 + 64, 0:16],
                       qr_bf[hh * 64:hh * 64 + 64, hf * T + h::128][:, 0:16],
                       allow_slow_non_contiguous=True)
                for s0 in range(0, T, 512):
                    ps = em.take_psum()
                    pr.op("tensor",
                          lambda t, ps=ps, hf=hf, hh=hh, s0=s0: t.matmul(
                              ps[0:16, 0:512], qbx[hh * 64:hh * 64 + 64, 0:16],
                              kr_bf[hh * 64:hh * 64 + 64, hf * T + s0: hf * T + s0 + 512],
                              start=True, stop=True),
                          [qbx[hh * 64:hh * 64 + 64, 0:16],
                           kr_bf[hh * 64:hh * 64 + 64, hf * T + s0: hf * T + s0 + 512]],
                          [ps[0:16, 0:512]])
                    pr.op("vector",
                          lambda v, ps=ps, s0=s0: v.tensor_mul(
                              scb[:, s0:s0 + 512], ps[0:16, 0:512], tb_m[:, s0:s0 + 512]),
                          [ps[0:16, 0:512], tb_m[:, s0:s0 + 512]], [scb[:, s0:s0 + 512]])
                    pr.op("vector",
                          lambda v, s0=s0: v.tensor_add(
                              scb[:, s0:s0 + 512], scb[:, s0:s0 + 512], tb_a[:, s0:s0 + 512]),
                          [scb[:, s0:s0 + 512], tb_a[:, s0:s0 + 512]], [scb[:, s0:s0 + 512]])
                psx = em.take_psum()
                pr.op("tensor",
                      lambda t, psx=psx, hf=hf, hh=hh: t.matmul(
                          psx[0:16, 0:NXK], qbx[hh * 64:hh * 64 + 64, 0:16],
                          kx_bf[hh * 64:hh * 64 + 64, hf * NXK: hf * NXK + NXK],
                          start=True, stop=True),
                      [qbx[hh * 64:hh * 64 + 64, 0:16], kx_bf[hh * 64:hh * 64 + 64, hf * NXK: hf * NXK + NXK]],
                      [psx[0:16, 0:NXK]])
                pr.op("vector",
                      lambda v, psx=psx: v.tensor_mul(
                          scb[:, T:T + NXK], psx[0:16, 0:NXK], tb_m[:, T:T + NXK]),
                      [psx[0:16, 0:NXK], tb_m[:, T:T + NXK]], [scb[:, T:T + NXK]])
                pr.op("vector",
                      lambda v: v.tensor_add(scb[:, T:T + NXK], scb[:, T:T + NXK], tb_a[:, T:T + NXK]),
                      [scb[:, T:T + NXK], tb_a[:, T:T + NXK]], [scb[:, T:T + NXK]])
                pr.op("vector", lambda v: v.reduce_max(mrb[:, 0:1], scb[:, 0:KB], axis=AX.X, negate=True),
                      [scb[:, 0:KB]], [mrb[:, 0:1]])
                pr.op("scalar", lambda s: s.activation(scb[:, 0:KB], scb[:, 0:KB], AF.Exp,
                                                       bias=mrb[:, 0:1], accum_out=rsb[:, 0:1]),
                      [scb[:, 0:KB], mrb[:, 0:1]], [scb[:, 0:KB], rsb[:, 0:1]])
                pr.op("vector", lambda v: v.reciprocal(rcb2[:, 0:1], rsb[:, 0:1]),
                      [rsb[:, 0:1]], [rcb2[:, 0:1]])
                pr.op("vector", lambda v: v.tensor_scalar(attb[:, 0:KB], scb[:, 0:KB],
                                                          rcb2[:, 0:1], None, ALU.mult),
                      [scb[:, 0:KB], rcb2[:, 0:1]], [attb[:, 0:KB]])
                for kt in range(17):
                    wk_ = 128 if kt < 16 else NXK
                    pst = psbf[:, (kt % 8) * 128:(kt % 8) * 128 + 16]
                    pr.op("tensor",
                          lambda t, kt=kt, wk_=wk_, pst=pst: t.transpose(
                              pst[0:wk_, 0:16], attb[0:16, kt * 128:kt * 128 + wk_],
                              idn_bf[0:16, 0:16]),
                          [attb[0:16, kt * 128:kt * 128 + wk_], idn_bf[0:16, 0:16]],
                          [pst[0:wk_, 0:16]])
                    pr.op("vector",
                          lambda v, kt=kt, wk_=wk_, pst=pst: v.tensor_copy(
                              attbT[0:wk_, kt * 16:(kt + 1) * 16], pst[0:wk_, 0:16]),
                          [pst[0:wk_, 0:16]], [attbT[0:wk_, kt * 16:(kt + 1) * 16]])
                psB = em.take_psum()
                pairs = [(v_tr[:, tb * 1024 + h * 64: tb * 1024 + h * 64 + 64],
                          attbT[:, tb * 16:(tb + 1) * 16]) for tb in range(16)]
                pairs.append((vx_tr[0:32, h * 64:(h + 1) * 64], attbT[0:32, 16 * 16:17 * 16]))
                em.mm_group(pairs, psB[0:64, 0:16])
                pr.op("vector",
                      lambda v, hf=hf, hh=hh, h=h, psB=psB: v.tensor_copy(
                          o_bf[hh * 64:hh * 64 + 64, hf * T + h::128][:, 0:16], psB[0:64, 0:16]),
                      [psB[0:64, 0:16]],
                      [o_bf[hh * 64:hh * 64 + 64, hf * T + h: hf * T + h + 128 * 15 + 1]])
            for e in ENGINES:
                pr.barrier_to(e)

        with ExitStack() as esd:
            ecd = esd.enter_context
            wot_bf = ecd(nc.sbuf_tensor(pfx + "wot_bf", [128, 8 * 128], BF16))
            bocl = ecd(nc.sbuf_tensor(pfx + "bocl", [128, 8], F32))
            x3p = ecd(nc.sbuf_tensor(pfx + "x3p", [128, 8 * 512], F32))
            xrl3 = ecd(nc.sbuf_tensor(pfx + "xrl3", [128, 512], F32))
            sq3 = ecd(nc.sbuf_tensor(pfx + "sq3", [128, 2 * 512], F32))
            st3 = ecd(nc.sbuf_tensor(pfx + "st3", [1, 4 * 512], F32))
            mu3 = ecd(nc.sbuf_tensor(pfx + "mu3", [128, 512], F32))
            rst3 = ecd(nc.sbuf_tensor(pfx + "rst3", [128, 512], F32))
            pr.dma("sync", bocl[:, :], boc[:, :])
            for s0 in range(0, T, 512):
                for of in range(8):
                    for k in range(8):
                        pr.dma("gpsimd", wot_bf[:, k * 128:(k + 1) * 128],
                               wo_t[k * 128:(k + 1) * 128, of * 128:(of + 1) * 128])
                    ps = em.take_psum()
                    em.mm_group(
                        [(wot_bf[:, k * 128:(k + 1) * 128], o_bf[:, k * T + s0:k * T + s0 + 512])
                         for k in range(8)],
                        ps[:, 0:512])
                    pr.dma("scalar", xrl3[:, :], xin_t[of * 128:(of + 1) * 128, s0:s0 + 512])
                    pr.op("vector",
                          lambda v, of=of, ps=ps: v.tensor_scalar(
                              x3p[:, of * 512:(of + 1) * 512], ps[:, 0:512],
                              bocl[:, of:of + 1], None, ALU.add),
                          [ps[:, 0:512], bocl[:, of:of + 1]], [x3p[:, of * 512:(of + 1) * 512]])
                    pr.op("vector",
                          lambda v, of=of: v.tensor_add(
                              x3p[:, of * 512:(of + 1) * 512], x3p[:, of * 512:(of + 1) * 512],
                              xrl3[:, :]),
                          [x3p[:, of * 512:(of + 1) * 512], xrl3[:, :]],
                          [x3p[:, of * 512:(of + 1) * 512]])
                emit_ln_feature(pr, em, x3p, 512, st3, sq3, mu3, rst3, nfeat=8)
                for of in range(8):
                    sl = x3p[:, of * 512:(of + 1) * 512]
                    pr.op("vector", lambda v, sl=sl: v.tensor_sub(sl, sl, mu3[:, 0:512]),
                          [sl, mu3[:, 0:512]], [sl])
                    pr.op("vector", lambda v, sl=sl: v.tensor_mul(sl, sl, rst3[:, 0:512]),
                          [sl, rst3[:, 0:512]], [sl])
                    pr.dma("gpsimd", o_x3[of * 128:(of + 1) * 128, s0:s0 + 512], sl)
        for e in ENGINES:
            pr.barrier_to(e)
        print(f"phase3 program: {pr.n_ops} ops, {pr.n_waits} waits")
    return nc


NEXP = 10



def build_phase3():
    nc = bass.Bass("TRN2", target_bir_lowering=False, debug=False)
    KB = T + NXK

    xin_t = nc.dram_tensor("xin_t", [DM, T], F32, kind="ExternalInput").ap()
    x2x_t = nc.dram_tensor("x2x_t", [DM, NXK], F32, kind="ExternalInput").ap()
    wq_t = nc.dram_tensor("wq_t", [DM, DM], F32, kind="ExternalInput").ap()
    wk_t = nc.dram_tensor("wk_t", [DM, DM], F32, kind="ExternalInput").ap()
    wv_t = nc.dram_tensor("wv_t", [DM, DM], F32, kind="ExternalInput").ap()
    wo_t = nc.dram_tensor("wo_t", [DM, DM], F32, kind="ExternalInput").ap()
    bqc = nc.dram_tensor("bqc", [128, 8], F32, kind="ExternalInput").ap()
    bkc = nc.dram_tensor("bkc", [128, 8], F32, kind="ExternalInput").ap()
    bvc = nc.dram_tensor("bvc", [128, 8], F32, kind="ExternalInput").ap()
    boc = nc.dram_tensor("boc", [128, 8], F32, kind="ExternalInput").ap()
    cs_t = nc.dram_tensor("cs_t", [DM, T], F32, kind="ExternalInput").ap()
    ssg_t = nc.dram_tensor("ssg_t", [DM, T], F32, kind="ExternalInput").ap()
    csx_t = nc.dram_tensor("csx_t", [DM, NXK], F32, kind="ExternalInput").ap()
    ssgx_t = nc.dram_tensor("ssgx_t", [DM, NXK], F32, kind="ExternalInput").ap()
    tamul = nc.dram_tensor("tamul", [16 * 256, 256], F32, kind="ExternalInput").ap()
    taadd = nc.dram_tensor("taadd", [16 * 256, 256], F32, kind="ExternalInput").ap()
    tbmul = nc.dram_tensor("tbmul", [16 * 16, KB], F32, kind="ExternalInput").ap()
    tbadd = nc.dram_tensor("tbadd", [16 * 16, KB], F32, kind="ExternalInput").ap()
    idn_in = nc.dram_tensor("idn128", [128, 128], F32, kind="ExternalInput").ap()

    o_x3 = nc.dram_tensor("x3_t", [DM, T], F32, kind="ExternalOutput").ap()

    with ExitStack() as _es0:
        _ec0 = _es0.enter_context
        blk = _ec0(nc.Block())
        pr = Prog(nc, _es0)
        em = Emitter(nc, _es0, pr, n_psum=7)
        psbf = _ec0(nc.psum_tensor("psbf", [128, 1024], BF16))
        emit_phase3(nc, pr, em, psbf, xin_t, x2x_t, wq_t, wk_t, wv_t, wo_t, bqc, bkc, bvc, boc, cs_t, ssg_t, csx_t, ssgx_t, tamul, taadd, tbmul, tbadd, idn_in, o_x3)
    return nc


def prep_phase4(inputs):
    w1 = _np_f32(inputs["L1_ffn_w1"])
    w3 = _np_f32(inputs["L1_ffn_w3"])
    w2 = _np_f32(inputs["L1_ffn_w2"])
    sw1 = _np_f32(inputs["L1_ffn_sw1"])
    sw3 = _np_f32(inputs["L1_ffn_sw3"])
    sw2 = _np_f32(inputs["L1_ffn_sw2"])
    m1 = np.concatenate([w1, sw1], axis=0).transpose(0, 2, 1)
    m3 = np.concatenate([w3, sw3], axis=0).transpose(0, 2, 1)
    m2 = np.concatenate([w2, sw2], axis=0).transpose(0, 2, 1)
    sel8 = np.zeros((8, 8 * 128), np.float32)
    for e in range(8):
        sel8[e, e * 128:(e + 1) * 128] = 1.0
    shared = {
        "m1_t": np.ascontiguousarray(m1.reshape(NEXP * 1024, 2048)),
        "m3_t": np.ascontiguousarray(m3.reshape(NEXP * 1024, 2048)),
        "m2_t": np.ascontiguousarray(m2.reshape(NEXP * 2048, 1024)),
        "rw_t": np.ascontiguousarray(_np_f32(inputs["L1_ffn_router_w"]).T),
        "ebc": np.ascontiguousarray(np.pad(_np_f32(inputs["L1_ffn_expert_bias"]),
                                           (0, 120)).reshape(128, 1)),
        "sel8": sel8,
    }
    return shared


def emit_phase4(nc, pr, em, psbf, xin_t, m1_t, m3_t, m2_t, rw_t, ebc, sel8, idn_in, o_out, pfx=""):
    with ExitStack() as es:
        ec = es.enter_context

        xin_bf = ec(nc.sbuf_tensor(pfx + "p4xin", [128, 8 * T], BF16))
        acc = ec(nc.sbuf_tensor(pfx + "p4acc", [128, 8 * T], F32))
        xld = ec(nc.sbuf_tensor(pfx + "p4xld", [128, 512], F32))
        idn_f = ec(nc.sbuf_tensor(pfx + "p4idn", [128, 128], F32))
        pr.dma("gpsimd", idn_f[:, :], idn_in[:, :])
        for k in range(8):
            for s0 in range(0, T, 512):
                pr.dma("sync", xld[:, :], xin_t[k * 128:(k + 1) * 128, s0:s0 + 512])
                pr.op("vector",
                      lambda v, k=k, s0=s0: v.tensor_copy(
                          xin_bf[:, k * T + s0:k * T + s0 + 512], xld[:, :]),
                      [xld[:, :]], [xin_bf[:, k * T + s0:k * T + s0 + 512]])

        crT = ec(nc.sbuf_tensor(pfx + "p4crT", [8, T], F32))
        with ExitStack() as esr:
            ecr = esr.enter_context
            rwt = ecr(nc.sbuf_tensor(pfx + "p4rw", [128, 8], BF16))
            ebt = ecr(nc.sbuf_tensor(pfx + "p4eb", [128, 1], F32))
            lgT = ecr(nc.sbuf_tensor(pfx + "p4lgT", [8, T], F32))
            lgm = ecr(nc.sbuf_tensor(pfx + "p4lgm", [128, 16 * 8], F32))
            lg2 = ecr(nc.sbuf_tensor(pfx + "p4lg2", [128, 8], F32))
            is1 = ecr(nc.sbuf_tensor(pfx + "p4is1", [128, 8], F32))
            is2 = ecr(nc.sbuf_tensor(pfx + "p4is2", [128, 8], F32))
            mcol = ecr(nc.sbuf_tensor(pfx + "p4m", [128, 4], F32))
            rwk = ecr(nc.sbuf_tensor(pfx + "p4rwk", [128, 8 * 8], F32))
            xf32 = [ecr(nc.sbuf_tensor(pfx + f"p4xf{i}", [128, 512], F32)) for i in range(2)]
            for k in range(8):
                pr.dma("gpsimd", rwk[:, k * 8:(k + 1) * 8], rw_t[k * 128:(k + 1) * 128, :])
            pr.dma("sync", ebt[:, :], ebc[:, :])
            for s0 in range(0, T, 512):
                ps = em.take_psum()
                for k in range(8):
                    xt_ = xf32[k % 2]
                    pr.dma("sync", xt_[:, :], xin_t[k * 128:(k + 1) * 128, s0:s0 + 512])
                    pr.op("tensor",
                          lambda t, ps=ps, k=k, xt_=xt_: t.matmul(
                              ps[0:8, 0:512], rwk[:, k * 8:(k + 1) * 8], xt_[:, :],
                              start=(k == 0), stop=(k == 7)),
                          [rwk[:, k * 8:(k + 1) * 8], xt_[:, :]],
                          [ps[0:8, 0:512]],
                          check_writes=[ps[0:8, 0:512]] if k == 0 else [])
                pr.op("vector",
                      lambda v, ps=ps, s0=s0: v.tensor_scalar(
                          lgT[:, s0:s0 + 512], ps[0:8, 0:512], ebt[0:8, 0:1], None, ALU.add),
                      [ps[0:8, 0:512], ebt[0:8, 0:1]], [lgT[:, s0:s0 + 512]])
            for tt in range(T // 128):
                ps = em.take_psum()
                pr.op("tensor",
                      lambda t, tt=tt, ps=ps: t.transpose(
                          ps[:, 0:8], lgT[0:8, tt * 128:(tt + 1) * 128], idn_f[0:8, 0:8]),
                      [lgT[0:8, tt * 128:(tt + 1) * 128], idn_f[0:8, 0:8]], [ps[:, 0:8]])
                lgs = lgm[:, tt * 8:(tt + 1) * 8]
                pr.op("vector", lambda v, ps=ps, lgs=lgs: v.tensor_copy(lgs, ps[:, 0:8]),
                      [ps[:, 0:8]], [lgs])
                pr.op("vector", lambda v, lgs=lgs: v.reduce_max(mcol[:, 0:1], lgs, axis=AX.X),
                      [lgs], [mcol[:, 0:1]])
                pr.op("vector", lambda v, lgs=lgs: v.tensor_scalar(is1[:, :], lgs, mcol[:, 0:1], None, ALU.is_ge),
                      [lgs, mcol[:, 0:1]], [is1[:, :]])
                pr.op("vector", lambda v, lgs=lgs: v.scalar_tensor_tensor(
                          lg2[:, :], is1[:, :], -1e38, lgs, ALU.mult, ALU.add),
                      [is1[:, :], lgs], [lg2[:, :]])
                pr.op("vector", lambda v: v.reduce_max(mcol[:, 1:2], lg2[:, :], axis=AX.X),
                      [lg2[:, :]], [mcol[:, 1:2]])
                pr.op("vector", lambda v: v.tensor_scalar(is2[:, :], lg2[:, :], mcol[:, 1:2], None, ALU.is_ge),
                      [lg2[:, :], mcol[:, 1:2]], [is2[:, :]])
                pr.op("vector", lambda v: v.tensor_sub(mcol[:, 2:3], mcol[:, 0:1], mcol[:, 1:2]),
                      [mcol[:, 0:1], mcol[:, 1:2]], [mcol[:, 2:3]])
                pr.op("scalar", lambda s: s.activation(mcol[:, 2:3], mcol[:, 2:3], AF.Sigmoid),
                      [mcol[:, 2:3]], [mcol[:, 2:3]])
                pr.op("vector", lambda v: v.tensor_scalar(mcol[:, 3:4], mcol[:, 2:3], -1.0, 1.0, ALU.mult, ALU.add),
                      [mcol[:, 2:3]], [mcol[:, 3:4]])
                pr.op("vector", lambda v: v.tensor_scalar(is1[:, :], is1[:, :], mcol[:, 2:3], None, ALU.mult),
                      [is1[:, :], mcol[:, 2:3]], [is1[:, :]])
                pr.op("vector", lambda v: v.scalar_tensor_tensor(
                          is1[:, :], is2[:, :], mcol[:, 3:4], is1[:, :], ALU.mult, ALU.add),
                      [is2[:, :], mcol[:, 3:4], is1[:, :]], [is1[:, :]])
                ps2 = em.take_psum()
                pr.op("tensor",
                      lambda t, ps2=ps2: t.transpose(ps2[0:8, 0:128], is1[:, 0:8], idn_f[:, :]),
                      [is1[:, 0:8], idn_f[:, :]], [ps2[0:8, 0:128]])
                pr.op("vector",
                      lambda v, tt=tt, ps2=ps2: v.tensor_copy(crT[:, tt * 128:(tt + 1) * 128], ps2[0:8, 0:128]),
                      [ps2[0:8, 0:128]], [crT[:, tt * 128:(tt + 1) * 128]])
            for e in ENGINES:
                pr.barrier_to(e)

        ese = es.enter_context(ExitStack())
        ece = ese.enter_context
        sel8t = ece(nc.sbuf_tensor(pfx + "p4sel8", [8, 8 * 128], F32))
        cbe = ece(nc.sbuf_tensor(pfx + "p4cbe", [128, T], F32))
        hscr = ece(nc.sbuf_tensor(pfx + "p4hscr", [128, 512], F32))
        h3scr = ece(nc.sbuf_tensor(pfx + "p4h3", [128, 512], F32))
        h_bf = ece(nc.sbuf_tensor(pfx + "p4h", [128, 16 * T], BF16))
        wpool = [ece(nc.sbuf_tensor(pfx + f"p4w{i}", [128, 8 * 128], BF16)) for i in range(4)]
        w2pool = [ece(nc.sbuf_tensor(pfx + f"p4w2_{i}", [128, 16 * 128], BF16)) for i in range(2)]
        pr.dma("sync", sel8t[:, :], sel8[:, :])
        for e in range(NEXP):
            routed = e < 8
            if routed:
                for s0 in range(0, T, 512):
                    ps = em.take_psum()
                    pr.op("tensor",
                          lambda t, ps=ps, e=e, s0=s0: t.matmul(
                              ps[:, 0:512], sel8t[:, e * 128:(e + 1) * 128],
                              crT[0:8, s0:s0 + 512], start=True, stop=True),
                          [sel8t[:, e * 128:(e + 1) * 128], crT[0:8, s0:s0 + 512]],
                          [ps[:, 0:512]])
                    pr.op("vector",
                          lambda v, ps=ps, s0=s0: v.tensor_copy(cbe[:, s0:s0 + 512], ps[:, 0:512]),
                          [ps[:, 0:512]], [cbe[:, s0:s0 + 512]])
            for j in range(16):
                w1t = wpool[(2 * j) % 4]
                w3t = wpool[(2 * j + 1) % 4]
                for k in range(8):
                    pr.dma("gpsimd", w1t[:, k * 128:(k + 1) * 128],
                           m1_t[e * 1024 + k * 128: e * 1024 + (k + 1) * 128, j * 128:(j + 1) * 128])
                    pr.dma("gpsimd", w3t[:, k * 128:(k + 1) * 128],
                           m3_t[e * 1024 + k * 128: e * 1024 + (k + 1) * 128, j * 128:(j + 1) * 128])
                for s0 in range(0, T, 512):
                    ps1 = em.take_psum()
                    em.mm_group(
                        [(w1t[:, k * 128:(k + 1) * 128], xin_bf[:, k * T + s0:k * T + s0 + 512])
                         for k in range(8)], ps1[:, 0:512])
                    ps3 = em.take_psum()
                    em.mm_group(
                        [(w3t[:, k * 128:(k + 1) * 128], xin_bf[:, k * T + s0:k * T + s0 + 512])
                         for k in range(8)], ps3[:, 0:512])
                    pr.op("scalar", lambda s, ps1=ps1: s.activation(hscr[:, 0:512], ps1[:, 0:512], AF.Silu),
                          [ps1[:, 0:512]], [hscr[:, 0:512]])
                    pr.op("vector", lambda v, ps3=ps3: v.tensor_mul(h3scr[:, 0:512], hscr[:, 0:512], ps3[:, 0:512]),
                          [hscr[:, 0:512], ps3[:, 0:512]], [h3scr[:, 0:512]])
                    if routed:
                        pr.op("vector",
                              lambda v, j=j, s0=s0: v.tensor_mul(
                                  h_bf[:, j * T + s0:j * T + s0 + 512], h3scr[:, 0:512], cbe[:, s0:s0 + 512]),
                              [h3scr[:, 0:512], cbe[:, s0:s0 + 512]],
                              [h_bf[:, j * T + s0:j * T + s0 + 512]])
                    else:
                        pr.op("vector",
                              lambda v, j=j, s0=s0: v.tensor_copy(
                                  h_bf[:, j * T + s0:j * T + s0 + 512], h3scr[:, 0:512]),
                              [h3scr[:, 0:512]], [h_bf[:, j * T + s0:j * T + s0 + 512]])
            for of in range(8):
                w2t = w2pool[of % 2]
                for k in range(16):
                    pr.dma("gpsimd", w2t[:, k * 128:(k + 1) * 128],
                           m2_t[e * 2048 + k * 128: e * 2048 + (k + 1) * 128, of * 128:(of + 1) * 128])
                for s0 in range(0, T, 512):
                    ps = em.take_psum()
                    em.mm_group(
                        [(w2t[:, k * 128:(k + 1) * 128], h_bf[:, k * T + s0:k * T + s0 + 512])
                         for k in range(16)], ps[:, 0:512])
                    if e == 0:
                        pr.op("vector",
                              lambda v, of=of, s0=s0, ps=ps: v.tensor_copy(
                                  acc[:, of * T + s0: of * T + s0 + 512], ps[:, 0:512]),
                              [ps[:, 0:512]], [acc[:, of * T + s0: of * T + s0 + 512]])
                    else:
                        pr.op("vector",
                              lambda v, of=of, s0=s0, ps=ps: v.tensor_add(
                                  acc[:, of * T + s0: of * T + s0 + 512],
                                  acc[:, of * T + s0: of * T + s0 + 512], ps[:, 0:512]),
                              [ps[:, 0:512], acc[:, of * T + s0: of * T + s0 + 512]],
                              [acc[:, of * T + s0: of * T + s0 + 512]])

        for e in ENGINES:
            pr.barrier_to(e)
        ese.close()

        sq4 = ec(nc.sbuf_tensor(pfx + "p4sq", [128, 2 * 512], F32))
        st4 = ec(nc.sbuf_tensor(pfx + "p4st", [1, 4 * 512], F32))
        mu4 = ec(nc.sbuf_tensor(pfx + "p4mu", [128, 512], F32))
        rst4 = ec(nc.sbuf_tensor(pfx + "p4rst", [128, 512], F32))
        lnb4 = ec(nc.sbuf_tensor(pfx + "p4lnb", [128, 8 * 512], F32))
        for s0 in range(0, T, 512):
            for of in range(8):
                pr.dma("scalar", xld[:, :], xin_t[of * 128:(of + 1) * 128, s0:s0 + 512])
                pr.op("vector",
                      lambda v, of=of, s0=s0: v.tensor_add(
                          lnb4[:, of * 512:(of + 1) * 512],
                          acc[:, of * T + s0: of * T + s0 + 512], xld[:, :]),
                      [acc[:, of * T + s0: of * T + s0 + 512], xld[:, :]],
                      [lnb4[:, of * 512:(of + 1) * 512]])
            emit_ln_feature(pr, em, lnb4, 512, st4, sq4, mu4, rst4, nfeat=8)
            for of in range(8):
                sl = lnb4[:, of * 512:(of + 1) * 512]
                pr.op("vector", lambda v, sl=sl: v.tensor_sub(sl, sl, mu4[:, 0:512]),
                      [sl, mu4[:, 0:512]], [sl])
                pr.op("vector", lambda v, sl=sl: v.tensor_mul(sl, sl, rst4[:, 0:512]),
                      [sl, rst4[:, 0:512]], [sl])
                pr.dma("gpsimd", o_out[of * 128:(of + 1) * 128, s0:s0 + 512], sl)
        for e in ENGINES:
            pr.barrier_to(e)
        print(f"phase4 program: {pr.n_ops} ops, {pr.n_waits} waits")
    return nc


MAGIC0 = 0x5A100000
MAGIC1 = 0x3C700000


def build_fused():
    nc = bass.Bass("TRN2", target_bir_lowering=False, debug=False)
    TH = T + 3
    KB = T + NXK

    def inp(name, shp):
        return nc.dram_tensor(name, shp, F32, kind="ExternalInput").ap()

    x_t = inp("x_t", [DM, TH])
    W_in = inp("W_in", [DM, 4096]); W_x = inp("W_x", [2048, 96])
    W_dt = inp("W_dt", [64, 2048]); W_out = inp("W_out", [2048, DM])
    conv_w = inp("conv_w", [128, 64]); conv_b = inp("conv_b", [128, 16])
    b_dt = inp("b_dt", [128, 16]); Dv = inp("Dv", [128, 16])
    selcat = inp("selcat", [32, 32 * 128])
    zeros256 = inp("zeros256", [128, 256]); zeros16 = inp("zeros16", [128, 16])
    selcol = inp("selcol", [128, 2])
    hsel = inp("hsel", [128, 1])
    pairinfo = nc.dram_tensor("pairinfo", [1, 4], I32, kind="ExternalInput").ap()
    xidx = nc.dram_tensor("xidx", [128, 20], I32, kind="ExternalInput").ap()
    fidx = nc.dram_tensor("fidx", [1, 2], I32, kind="ExternalInput").ap()
    p2w = {}
    for lay in (0, 1):
        sfx = f"_l{lay}"
        p2w[lay] = dict(
            wqkv=inp("wqkv" + sfx, [DM, 1536]), wo=inp("wo" + sfx, [512, DM]),
            w1p=inp("w1p" + sfx, [DM, 2 * HIDP]), w2p=inp("w2p" + sfx, [HIDP, DM]),
            b1p=inp("b1p" + sfx, [128, 2 * (HIDP // 128)]), b2c=inp("b2c" + sfx, [128, 8]),
            lgc=inp("lgc" + sfx, [128, 8]), lbc=inp("lbc" + sfx, [128, 8]),
            qs8=inp("qs8" + sfx, [128, 4]), ksc=inp("ksc" + sfx, [128, 4]))
    selheadT = inp("selheadT", [2, 128]); selhead = inp("selhead", [128, 2])
    csq = inp("csq", [DM // 8, T + QW]) if False else inp("csq", [128, T + QW])
    ssq = inp("ssq", [128, T + QW])
    csk = inp("csk", [128, T + 2 * QW]); ssk = inp("ssk", [128, T + 2 * QW])
    l0mask = inp("l0mask", [512, 1024]); l0mask_w1 = inp("l0mask_w1", [512, 1024])
    idn_in = inp("idn128", [128, 128])
    wq_t = inp("wq_t", [DM, DM]); wk_t = inp("wk_t", [DM, DM])
    wv_t = inp("wv_t", [DM, DM]); wo_t = inp("wo_t", [DM, DM])
    bqc = inp("bqc", [128, 8]); bkc = inp("bkc", [128, 8])
    bvc = inp("bvc", [128, 8]); boc = inp("boc", [128, 8])
    cs_t = inp("cs_t", [DM, T]); ssg_t = inp("ssg_t", [DM, T])
    csx_t = inp("csx_t", [DM, NXK]); ssgx_t = inp("ssgx_t", [DM, NXK])
    tamul = inp("tamul", [16 * 256, 256]); taadd = inp("taadd", [16 * 256, 256])
    tbmul = inp("tbmul", [16 * 16, KB]); tbadd = inp("tbadd", [16 * 16, KB])
    m1_t = inp("m1_t", [NEXP * 1024, 2048]); m3_t = inp("m3_t", [NEXP * 1024, 2048])
    m2_t = inp("m2_t", [NEXP * 2048, 1024])
    rw_t = inp("rw_t", [DM, 8]); ebc = inp("ebc", [128, 1]); sel8 = inp("sel8", [8, 8 * 128])

    out_t = nc.dram_tensor("out_t", [DM, T], F32, kind="ExternalOutput").ap()

    def internal(name, shp, dt=F32):
        return nc.dram_tensor(name, shp, dt).ap()

    x1sp = internal("x1sp", [DM, HALO2 + T])
    x2asp = internal("x2asp", [DM, HALO2 + T])
    x2sp = internal("x2sp", [DM, T])
    x3sp = internal("x3sp", [DM, T])
    x2xsp = internal("x2xsp", [DM, NXK])
    junksp = internal("junksp", [DM, T])
    hfin_i = internal("hfin_i", [128, 256])
    sumd_i = internal("sumd_i", [128, 16])
    carry_i = internal("carry_i", [128, 256])
    sext_i = internal("sext_i", [128, 16])
    sp_xi = internal("sp_xi", [2048, T]); sp_delta = internal("sp_delta", [2048, T])
    sp_res = internal("sp_res", [2048, T]); sp_C = internal("sp_C", [2048, T])
    sp_bmcm = internal("sp_bmcm", [32, T])
    sp_x2m0 = internal("sp_x2m0", [DM, T + 1]); sp_x2m1 = internal("sp_x2m1", [DM, T + 1])

    def shared_t(name, shp, dt=F32):
        return nc.dram_tensor(name, shp, dt, addr_space="Shared").ap()

    xg_sd = shared_t("xg_sd", [256, 16])
    xg_hf = shared_t("xg_hf", [256, 256])
    xg_h1 = shared_t("xg_h1", [2 * 8 * 128, HALO2])
    xg_h2 = shared_t("xg_h2", [2 * 8 * 128, HALO2])
    xg_xx = shared_t("xg_xx", [2 * 8 * 128, NXK])
    shflag = shared_t("shflag", [256, 16], I32)

    with ExitStack() as es:
        ec = es.enter_context
        blk = ec(nc.Block())
        pr = Prog(nc, es)
        em = Emitter(nc, es, pr, n_psum=7)
        psbf = ec(nc.psum_tensor("psbf", [128, 1024], BF16))
        g = nc.gpsimd

        myfl = ec(nc.sbuf_tensor("myfl", [128, 16], I32))
        flrd = ec(nc.sbuf_tensor("flrd", [128, 16], I32))
        hselt = ec(nc.sbuf_tensor("hselt", [128, 1], F32))
        himp = ec(nc.sbuf_tensor("himp", [128, HALO2], F32))
        xixt = ec(nc.sbuf_tensor("xixt", [128, 20], I32))
        psem = ec(nc.semaphore("pollsem"))
        r_rc = ec(g.register("r_rc"))
        r_tmp = ec(g.register("r_tmp"))
        r_cnt = ec(g.register("r_cnt"))
        r_v = ec(g.register("r_v"))
        pr.dma("gpsimd", xixt[:, :], xidx[:, :])
        pr.dma("sync", hselt[:, :], hsel[:, :])
        pr.barrier_to("gpsimd")
        g.reg_mov(r_cnt, 0)
        bar_state = {"k": 0}
        IOA = bass.IndirectOffsetOnAxis if hasattr(bass, "IndirectOffsetOnAxis") else None
        if IOA is None:
            from concourse.bass_core_rust import IndirectOffsetOnAxis as IOA

        def pair_barrier():
            bar_state["k"] += 1
            k = bar_state["k"]
            pr.barrier_to("gpsimd")
            pr.op("gpsimd", lambda gg: gg.memset(myfl[:, 0:8], MAGIC0 + k), [], [myfl[:, 0:8]])
            pr.op("gpsimd", lambda gg: gg.memset(myfl[:, 8:16], MAGIC1 + k), [], [myfl[:, 8:16]])
            pr.idma("gpsimd", shflag[:, :], IOA(xixt[:, 16:17], 0), myfl[:, :], None,
                    extra_reads=[xixt[:, 16:17]])
            g.reg_mov(r_rc, 1)
            with g.While(r_rc):
                g.indirect_dma_start(flrd[:, :], None, shflag[:, :],
                                     IOA(xixt[:, 17:18], 0)).then_inc(psem, 16)
                g.reg_add(r_cnt, r_cnt, 16)
                g.wait_ge(psem, r_cnt)
                g.reg_load(r_v, flrd[0:1, 0:1])
                g.reg_alu(r_rc, r_v, MAGIC0 + k, ALU.not_equal)
                g.reg_load(r_v, flrd[0:1, 8:9])
                g.reg_alu(r_tmp, r_v, MAGIC1 + k, ALU.not_equal)
                g.reg_alu(r_rc, r_rc, r_tmp, ALU.bitwise_or)
            pr._record([], [flrd[0:1, :]], "gpsimd", pr.count["gpsimd"])

        def export_rows(dst_shared, src_ap_fn, width):
            for of in range(8):
                pr.dma("gpsimd", himp[:, 0:width], src_ap_fn(of))
                pr.idma("gpsimd", dst_shared[:, :], IOA(xixt[:, of:of + 1], 0),
                        himp[:, 0:width], None, extra_reads=[xixt[:, of:of + 1]])

        def import_rows(src_shared, dst_dram, width, col0):
            for of in range(8):
                pr.idma("gpsimd", himp[:, 0:width], None,
                        src_shared[:, :], IOA(xixt[:, 8 + of:9 + of], 0),
                        extra_reads=[xixt[:, 8 + of:9 + of]])
                pr.op("vector",
                      lambda v, w=width: v.tensor_scalar(himp[:, 0:w], himp[:, 0:w],
                                                         hselt[:, 0:1], None, ALU.mult),
                      [himp[:, 0:width], hselt[:, 0:1]], [himp[:, 0:width]])
                pr.dma("gpsimd", dst_dram[of * 128:(of + 1) * 128, col0:col0 + width],
                       himp[:, 0:width])

        emit_phase1(nc, pr, em, x_t, W_in, W_x, W_dt, W_out, conv_w, conv_b, b_dt,
                    Dv, zeros256, zeros16, selcat, junksp, hfin_i, sumd_i,
                    sp_xi, sp_delta, sp_res, sp_C, sp_bmcm,
                    do_passA=True, do_y=False, pfx="p1a")
        pr.dma("gpsimd", himp[:, 0:16], sumd_i[:, :])
        pr.idma("gpsimd", xg_sd[:, :], IOA(xixt[:, 16:17], 0), himp[:, 0:16], None,
                extra_reads=[xixt[:, 16:17]])
        pr.dma("gpsimd", himp[:, 16:272], hfin_i[:, :])
        pr.idma("gpsimd", xg_hf[:, :], IOA(xixt[:, 16:17], 0), himp[:, 16:272], None,
                extra_reads=[xixt[:, 16:17]])
        pair_barrier()
        pr.idma("gpsimd", himp[:, 0:16], None, xg_sd[:, :], IOA(xixt[:, 17:18], 0),
                extra_reads=[xixt[:, 17:18]])
        pr.dma("gpsimd", sext_i[:, :], himp[:, 0:16])
        pr.idma("gpsimd", himp[:, 16:272], None, xg_hf[:, :], IOA(xixt[:, 17:18], 0),
                extra_reads=[xixt[:, 17:18]])
        pr.dma("gpsimd", carry_i[:, :], himp[:, 16:272])
        emit_phase1(nc, pr, em, x_t, W_in, W_x, W_dt, W_out, conv_w, conv_b, b_dt,
                    Dv, carry_i, sext_i, selcat, x1sp[:, HALO2:], hfin_i, sumd_i,
                    sp_xi, sp_delta, sp_res, sp_C, sp_bmcm,
                    do_passA=False, do_y=True, carry_sel=selcol, pfx="p1b")
        export_rows(xg_h1, lambda of: x1sp[of * 128:(of + 1) * 128, T:T + HALO2], HALO2)
        pair_barrier()
        import_rows(xg_h1, x1sp, HALO2, 0)

        emit_phase2(nc, pr, em, psbf, x1sp, x1sp[:, HALO2:], p2w[0]["wqkv"], p2w[0]["wo"],
                    p2w[0]["w1p"], p2w[0]["w2p"], p2w[0]["b1p"], p2w[0]["b2c"],
                    p2w[0]["lgc"], p2w[0]["lbc"], p2w[0]["qs8"], p2w[0]["ksc"],
                    selheadT, selhead, csq, ssq, csk, ssk, l0mask, l0mask_w1,
                    idn_in, x2asp[:, HALO2:], junksp, sp_x2m0, pfx="A")
        export_rows(xg_h2, lambda of: x2asp[of * 128:(of + 1) * 128, T:T + HALO2], HALO2)
        pair_barrier()
        import_rows(xg_h2, x2asp, HALO2, 0)
        emit_phase2(nc, pr, em, psbf, x2asp, x1sp[:, HALO2:], p2w[1]["wqkv"], p2w[1]["wo"],
                    p2w[1]["w1p"], p2w[1]["w2p"], p2w[1]["b1p"], p2w[1]["b2c"],
                    p2w[1]["lgc"], p2w[1]["lbc"], p2w[1]["qs8"], p2w[1]["ksc"],
                    selheadT, selhead, csq, ssq, csk, ssk, l0mask, l0mask_w1,
                    idn_in, junksp, x2sp, sp_x2m1, pfx="B")
        for of in range(8):
            pr.dma("gpsimd", himp[:, 0:16], x2sp[of * 128:(of + 1) * 128, 0:16])
            pr.dma("gpsimd", himp[:, 16:32], x2sp[of * 128:(of + 1) * 128, 1024:1040])
            pr.idma("gpsimd", xg_xx[:, :], IOA(xixt[:, of:of + 1], 0), himp[:, 0:32], None,
                    extra_reads=[xixt[:, of:of + 1]])
        pair_barrier()
        import_rows(xg_xx, x2xsp, NXK, 0)

        emit_phase3(nc, pr, em, psbf, x2sp, x2xsp, wq_t, wk_t, wv_t, wo_t,
                    bqc, bkc, bvc, boc, cs_t, ssg_t, csx_t, ssgx_t,
                    tamul, taadd, tbmul, tbadd, idn_in, x3sp, pfx="C")
        emit_phase4(nc, pr, em, psbf, x3sp, m1_t, m3_t, m2_t, rw_t, ebc, sel8,
                    idn_in, out_t, pfx="D")
        for e in ENGINES:
            pr.barrier_to(e)
        print(f"fused program: {pr.n_ops} ops, {pr.n_waits} waits")
    return nc


def prep_fused(inputs):
    sh1, pc1 = prep_phase1(inputs)
    sh3, pc3 = prep_phase3(inputs)
    sh4 = prep_phase4(inputs)
    idn = np.eye(128, dtype=np.float32)
    maps = []
    p2l = [prep_phase2(inputs, 0), prep_phase2(inputs, 1)]
    for c in range(NCORES):
        b, h = c // 2, c % 2
        m = {}
        m.update(sh1); m.update(pc1[c])
        m.pop("carry", None); m.pop("sext", None)
        m["zeros256"] = np.zeros((128, 256), np.float32)
        m["zeros16"] = np.zeros((128, 16), np.float32)
        sc = np.zeros((128, 2), np.float32)
        sc[:, 0] = float(h)
        sc[:, 1] = float(1 - h)
        m["selcol"] = sc
        m["hsel"] = np.full((128, 1), float(h), np.float32)
        m["pairinfo"] = np.array([[h, 1 - h, 0, 0]], np.int32)
        xidx = np.zeros((128, 20), np.int32)
        pp = np.arange(128)
        for of in range(8):
            xidx[:, of] = h * 1024 + of * 128 + pp
            xidx[:, 8 + of] = (1 - h) * 1024 + of * 128 + pp
        xidx[:, 16] = h * 128 + pp
        xidx[:, 17] = (1 - h) * 128 + pp
        m["xidx"] = xidx
        m["fidx"] = np.array([[h, 1 - h]], np.int32)
        for lay in (0, 1):
            sh2, pc2 = p2l[lay]
            for k in ("wqkv", "wo", "w1p", "w2p", "b1p", "b2c", "lgc", "lbc", "qs8", "ksc"):
                m[k + f"_l{lay}"] = sh2[k]
        sh2, pc2 = p2l[0]
        for k in ("selheadT", "selhead", "l0mask"):
            m[k] = sh2[k]
        m.update({k: v for k, v in pc2[c].items()})
        m["idn128"] = idn
        m.update(sh3); m.update(pc3[c])
        m.update(sh4)
        maps.append(m)
    return maps


def kernel_fused(**inputs):
    nc = build_fused()
    maps = prep_fused(inputs)
    r = run_bass_kernel_spmd(nc, maps, core_ids=list(range(NCORES)))
    out = np.zeros((4, 4096, 1024), np.float32)
    for c in range(NCORES):
        b, h = c // 2, c % 2
        out[b, h * T:(h + 1) * T] = np.asarray(r.results[c]["out_t"]).T
    return out



def build_phase4():
    nc = bass.Bass("TRN2", target_bir_lowering=False, debug=False)
    xin_t = nc.dram_tensor("xin_t", [DM, T], F32, kind="ExternalInput").ap()
    m1_t = nc.dram_tensor("m1_t", [NEXP * 1024, 2048], F32, kind="ExternalInput").ap()
    m3_t = nc.dram_tensor("m3_t", [NEXP * 1024, 2048], F32, kind="ExternalInput").ap()
    m2_t = nc.dram_tensor("m2_t", [NEXP * 2048, 1024], F32, kind="ExternalInput").ap()
    rw_t = nc.dram_tensor("rw_t", [DM, 8], F32, kind="ExternalInput").ap()
    ebc = nc.dram_tensor("ebc", [128, 1], F32, kind="ExternalInput").ap()
    sel8 = nc.dram_tensor("sel8", [8, 8 * 128], F32, kind="ExternalInput").ap()
    idn_in = nc.dram_tensor("idn128", [128, 128], F32, kind="ExternalInput").ap()
    o_out = nc.dram_tensor("out_t", [DM, T], F32, kind="ExternalOutput").ap()

    with ExitStack() as _es0:
        _ec0 = _es0.enter_context
        blk = _ec0(nc.Block())
        pr = Prog(nc, _es0)
        em = Emitter(nc, _es0, pr, n_psum=7)
        psbf = _ec0(nc.psum_tensor("psbf", [128, 1024], BF16))
        emit_phase4(nc, pr, em, psbf, xin_t, m1_t, m3_t, m2_t, rw_t, ebc, sel8, idn_in, o_out)
    return nc


def kernel(**inputs):
    return kernel_fused(**inputs)


def kernel_unfused(**inputs):
    idn = np.eye(128, dtype=np.float32)
    x_full = _np_f32(inputs["L0_attn_x"])

    sh1, pc1 = prep_phase1(inputs)
    nc1 = build_phase1()
    zc = np.zeros((128, 256), np.float32)
    zs = np.zeros((128, 16), np.float32)
    maps = []
    for c in range(NCORES):
        m = dict(sh1); m.update(pc1[c]); m["carry"] = zc; m["sext"] = zs
        maps.append(m)
    r1 = run_bass_kernel_spmd(nc1, maps, core_ids=list(range(NCORES)))
    maps2 = []
    for c in range(NCORES):
        m = dict(maps[c])
        if c % 2 == 1:
            m["carry"] = np.asarray(r1.results[c - 1]["hfin"])
        else:
            m["sext"] = np.asarray(r1.results[c + 1]["sumdelta"])
        maps2.append(m)
    r2 = run_bass_kernel_spmd(nc1, maps2, core_ids=list(range(NCORES)))
    x1 = np.zeros((4, 4096, 1024), np.float32)
    for c in range(NCORES):
        b, h = c // 2, c % 2
        x1[b, h * T:(h + 1) * T] = np.asarray(r2.results[c]["x1_t"]).T

    nc2 = build_phase2()

    def run_p2(lay, xin_full, xres_full):
        sh2, pc2 = prep_phase2(inputs, lay)
        mm = []
        for c in range(NCORES):
            m = dict(sh2); m.update(pc2[c])
            m["xin_t"] = slice_stream(xin_full, c, HALO2)
            m["xres_t"] = slice_stream(xres_full, c, 0)
            m["idn128"] = idn
            mm.append(m)
        r = run_bass_kernel_spmd(nc2, mm, core_ids=list(range(NCORES)))
        xo = np.zeros((4, 4096, 1024), np.float32)
        xl = np.zeros((4, 4096, 1024), np.float32)
        for c in range(NCORES):
            b, h = c // 2, c % 2
            xo[b, h * T:(h + 1) * T] = np.asarray(r.results[c]["xout_t"]).T
            xl[b, h * T:(h + 1) * T] = np.asarray(r.results[c]["xln_t"]).T
        return xo, xl

    x2a, _ = run_p2(0, x1, x1)
    _, x2 = run_p2(1, x2a, x1)

    nc3 = build_phase3()
    sh3, pc3 = prep_phase3(inputs)
    rows = np.concatenate([np.arange(16), 1024 + np.arange(16)])
    mm = []
    for c in range(NCORES):
        b, h = c // 2, c % 2
        m = dict(sh3); m.update(pc3[c])
        m["xin_t"] = slice_stream(x2, c, 0)
        m["x2x_t"] = (np.ascontiguousarray(x2[b, rows].T) if h == 1
                      else np.zeros((DM, NXK), np.float32))
        m["idn128"] = idn
        mm.append(m)
    r = run_bass_kernel_spmd(nc3, mm, core_ids=list(range(NCORES)))
    x3 = np.zeros((4, 4096, 1024), np.float32)
    for c in range(NCORES):
        b, h = c // 2, c % 2
        x3[b, h * T:(h + 1) * T] = np.asarray(r.results[c]["x3_t"]).T

    nc4 = build_phase4()
    sh4 = prep_phase4(inputs)
    mm = []
    for c in range(NCORES):
        m = dict(sh4)
        m["xin_t"] = slice_stream(x3, c, 0)
        m["idn128"] = idn
        mm.append(m)
    r = run_bass_kernel_spmd(nc4, mm, core_ids=list(range(NCORES)))
    out = np.zeros((4, 4096, 1024), np.float32)
    for c in range(NCORES):
        b, h = c // 2, c % 2
        out[b, h * T:(h + 1) * T] = np.asarray(r.results[c]["out_t"]).T
    return out


# version 04
# speedup vs baseline: 1.2701x; 1.0434x over previous
import math
import numpy as np
from contextlib import ExitStack

import concourse.bass as bass
import concourse.mybir as mybir
from concourse.bass_utils import run_bass_kernel_spmd

F32 = mybir.dt.float32
BF16 = mybir.dt.bfloat16
I32 = mybir.dt.int32
AF = mybir.ActivationFunctionType
ALU = mybir.AluOpType
AX = mybir.AxisListType
f32 = np.float32

NCORES = 8
T = 2048
DM = 1024

ENGINES = ("tensor", "vector", "scalar", "gpsimd", "sync")
DMA_SEMS_PER_QUEUE = 12
DMA_QUEUES = tuple(f"q{e}{i}" for e in ENGINES for i in range(DMA_SEMS_PER_QUEUE))


def _ap_range(ap):
    off = ap.offset
    if not isinstance(off, (int, np.integer)):
        return ap.tensor.name, 0, 1 << 60
    lo = hi = int(off)
    for step, count in ap.ap:
        d = int(step) * (int(count) - 1)
        if d >= 0:
            hi += d
        else:
            lo += d
    return ap.tensor.name, lo, hi


class Prog:
    def __init__(self, nc, es):
        self.nc = nc
        self.sems = {}
        for name in ENGINES + DMA_QUEUES:
            self.sems[name] = es.enter_context(nc.semaphore("as_" + name))
        self.count = {name: 0 for name in ENGINES + DMA_QUEUES}
        self.guar = {e: {s: 0 for s in ENGINES + DMA_QUEUES} for e in ENGINES}
        self.acc = {}
        self.eng = {e: getattr(nc, e) for e in ENGINES}
        self.rot = {e: 0 for e in ENGINES}
        self.n_waits = 0
        self.n_ops = 0

    def _needed(self, reads, check_writes):
        need = {}
        def add(semname, cnt):
            if cnt > need.get(semname, 0):
                need[semname] = cnt
        for ap in reads:
            name, lo, hi = _ap_range(ap)
            for rec in self.acc.get(name, ()):
                if rec[2] and not (rec[1] < lo or rec[0] > hi):
                    add(rec[3], rec[4])
        for ap in check_writes:
            name, lo, hi = _ap_range(ap)
            for rec in self.acc.get(name, ()):
                if not (rec[1] < lo or rec[0] > hi):
                    add(rec[3], rec[4])
        return need

    def _record(self, reads, writes, semname, cnt):
        for is_w, aps in ((False, reads), (True, writes)):
            for ap in aps:
                name, lo, hi = _ap_range(ap)
                lst = self.acc.setdefault(name, [])
                if is_w:
                    lst[:] = [r for r in lst if r[0] < lo or r[1] > hi]
                lst.append([lo, hi, is_w, semname, cnt])
                if len(lst) > 64:
                    old, keep = lst[:-40], lst[-40:]
                    blo = min(r[0] for r in old)
                    bhi = max(r[1] for r in old)
                    sems = {}
                    for r in old:
                        sems[r[3]] = max(sems.get(r[3], 0), r[4])
                    lst[:] = [[blo, bhi, True, s, c] for s, c in sems.items()] + keep

    def _wait(self, engine, need):
        g = self.guar[engine]
        e = self.eng[engine]
        for semname, cnt in sorted(need.items()):
            if g[semname] < cnt:
                if semname == engine and cnt > self.count[semname]:
                    raise RuntimeError(
                        f"deadlock: {engine} waits own sem {cnt} > {self.count[semname]}")
                e.wait_ge(self.sems[semname], cnt)
                g[semname] = cnt
                self.n_waits += 1

    def op(self, engine, f, reads, writes, check_writes=None, inc=True):
        if check_writes is None:
            check_writes = writes
        need = self._needed(reads, check_writes)
        self._wait(engine, need)
        inst = f(self.eng[engine])
        self.n_ops += 1
        if inc:
            self.count[engine] += 1
            inst.then_inc(self.sems[engine], 1)
            self._record(reads, writes, engine, self.count[engine])
        else:
            self._record(reads, writes, engine, self.count[engine] + 1)
        return inst

    def dma(self, engine, out, in_, **kwargs):
        need = self._needed([in_], [out])
        q = f"q{engine}{self.rot[engine]}"
        self.rot[engine] = (self.rot[engine] + 1) % DMA_SEMS_PER_QUEUE
        if self.count[q] > 0:
            need[q] = max(need.get(q, 0), self.count[q])
        self._wait(engine, need)
        inst = self.eng[engine].dma_start(out=out, in_=in_, **kwargs)
        self.count[q] += 16
        inst.then_inc(self.sems[q], 16)
        self.n_ops += 1
        self._record([in_], [out], q, self.count[q])
        return inst

    def idma(self, engine, out, out_off, in_, in_off, extra_reads=()):
        need = self._needed([in_] + list(extra_reads), [out])
        q = f"q{engine}{self.rot[engine]}"
        self.rot[engine] = (self.rot[engine] + 1) % DMA_SEMS_PER_QUEUE
        if self.count[q] > 0:
            need[q] = max(need.get(q, 0), self.count[q])
        self._wait(engine, need)
        inst = self.eng[engine].indirect_dma_start(out, out_off, in_, in_off)
        self.count[q] += 16
        inst.then_inc(self.sems[q], 16)
        self.n_ops += 1
        self._record([in_], [out], q, self.count[q])
        return inst

    def barrier_to(self, engine):
        need = {s: c for s, c in self.count.items() if c > 0}
        self._wait(engine, need)


def ceil_div(a, b):
    return (a + b - 1) // b


def blocks512(total):
    out = list(range(0, total - 511, 512))
    covered = out[-1] + 512 if out else 0
    if covered < total:
        out.append(total - 512)
    return out


class Emitter:

    def __init__(self, nc, es, pr, n_psum=8):
        self.nc = nc
        self.es = es
        self.pr = pr
        ec = es.enter_context
        self.ones_col = ec(nc.sbuf_tensor("em_ones_col", [128, 1], F32))
        self.ones_row = ec(nc.sbuf_tensor("em_ones_row", [1, 128], F32))
        self.half_row = ec(nc.sbuf_tensor("em_half_row", [1, 128], F32))
        pr.op("vector", lambda v: v.memset(self.ones_col[:, :], 1.0), [], [self.ones_col[:, :]])
        pr.op("vector", lambda v: v.memset(self.ones_row[:, :], 1.0), [], [self.ones_row[:, :]])
        pr.op("vector", lambda v: v.memset(self.half_row[:, :], 0.5), [], [self.half_row[:, :]])
        self.psum = [ec(nc.psum_tensor(f"em_ps{i}", [128, 512], F32)) for i in range(n_psum)]
        self._ps_i = 0

    def take_psum(self):
        p = self.psum[self._ps_i]
        self._ps_i = (self._ps_i + 1) % len(self.psum)
        return p

    def mm_group(self, pairs, out_ap, out_reads_extra=()):
        pr = self.pr
        n = len(pairs)
        for i, (lt, rh) in enumerate(pairs):
            first = i == 0
            last = i == n - 1
            pr.op("tensor",
                  lambda t, lt=lt, rh=rh, first=first, last=last:
                      t.matmul(out_ap, lt, rh, start=first, stop=last),
                  [lt, rh] + (list(out_reads_extra) if first else []),
                  [out_ap],
                  check_writes=[out_ap] if first else [])

    def bcast_row(self, out_sb, row_ap, ncols, lhs=None):
        pr = self.pr
        lhs = lhs if lhs is not None else self.ones_row
        for c0 in range(0, ncols, 512):
            w = min(512, ncols - c0)
            ps = self.take_psum()
            pr.op("tensor",
                  lambda t, c0=c0, w=w, ps=ps: t.matmul(
                      ps[:, 0:w], lhs[:, :], row_ap[0:1, c0:c0 + w],
                      start=True, stop=True),
                  [lhs[:, :], row_ap[0:1, c0:c0 + w]], [ps[:, 0:w]])
            pr.op("vector",
                  lambda v, c0=c0, w=w, ps=ps: v.tensor_copy(
                      out_sb[:, c0:c0 + w], ps[:, 0:w]),
                  [ps[:, 0:w]], [out_sb[:, c0:c0 + w]])


def _np_f32(x):
    return np.ascontiguousarray(np.asarray(x, dtype=np.float32))


def mamba_A_constants():
    n = np.arange(1, 17, dtype=np.float32)
    return [-float(np.exp(np.log(v).astype(np.float32)).astype(np.float32)) for v in n]


def prep_phase1(inputs):
    x = _np_f32(inputs["L0_attn_x"])
    W_in = _np_f32(inputs["L0_attn_W_in"])
    conv_w = _np_f32(inputs["L0_attn_conv_w"])
    conv_b = _np_f32(inputs["L0_attn_conv_b"])
    W_x = _np_f32(inputs["L0_attn_W_x"])
    W_dt = _np_f32(inputs["L0_attn_W_dt"])
    b_dt = _np_f32(inputs["L0_attn_b_dt"])
    Dv = _np_f32(inputs["L0_attn_D"])
    W_out = _np_f32(inputs["L0_attn_W_out"])

    def cols16(v):
        return np.ascontiguousarray(v.reshape(16, 128).T)

    shared = {
        "W_in": W_in,
        "W_x": W_x,
        "W_dt": W_dt,
        "W_out": W_out,
        "conv_w": np.ascontiguousarray(conv_w.reshape(4, 16, 128).transpose(2, 1, 0).reshape(128, 64)),
        "conv_b": cols16(conv_b),
        "b_dt": cols16(b_dt),
        "Dv": cols16(Dv),
    }
    selcat = np.zeros((32, 32 * 128), np.float32)
    for n in range(16):
        selcat[n, n * 128:(n + 1) * 128] = 1.0
        selcat[16 + n, (16 + n) * 128:(17 + n) * 128] = 0.5
    shared["selcat"] = selcat
    per_core = []
    for c in range(NCORES):
        b, h = c // 2, c % 2
        t0 = h * T
        lo = t0 - 3
        if lo < 0:
            xs = np.concatenate(
                [np.zeros((3, DM), np.float32), x[b, 0:t0 + T]], axis=0)
        else:
            xs = x[b, lo:t0 + T]
        per_core.append({"x_t": np.ascontiguousarray(xs.T)})
    return shared, per_core


AN = mamba_A_constants()
LNK = 27.631021115928547


def emit_phase1(nc, pr, em, x_t, W_in, W_x, W_dt, W_out, conv_w, conv_b, b_dt,
                Dv, carry, sext, selcat, o_x1, o_hfin, o_sumd,
                sp_xi, sp_delta, sp_res, sp_C, sp_bmcm,
                chunk=512, dgroup=4, carry_sel=None, do_passA=True, do_y=True,
                pfx="", halo_off=3):
    TH = T + halo_off
    NCH = ceil_div(T, chunk)
    with ExitStack() as es:
        ec = es.enter_context

        convb = ec(nc.sbuf_tensor(pfx + "convb", [128, 16], F32))
        bdt = ec(nc.sbuf_tensor(pfx + "bdt", [128, 16], F32))
        dvt = ec(nc.sbuf_tensor(pfx + "dvt", [128, 16], F32))
        carr = ec(nc.sbuf_tensor(pfx + "carr", [128, 256], F32))
        sxt = ec(nc.sbuf_tensor(pfx + "sxt", [128, 16], F32))
        convw = ec(nc.sbuf_tensor(pfx + "convw", [128, 64], F32))
        ccar = ec(nc.sbuf_tensor(pfx + "ccar", [128, 16], F32))
        pr.dma("sync", convw[:, :], conv_w[:, :])
        pr.dma("sync", convb[:, :], conv_b[:, :])
        pr.dma("sync", bdt[:, :], b_dt[:, :])
        pr.dma("sync", dvt[:, :], Dv[:, :])
        pr.dma("sync", carr[:, :], carry[:, :])
        pr.dma("sync", sxt[:, :], sext[:, :])
        if carry_sel is not None:
            selc_t = ec(nc.sbuf_tensor(pfx + "selc_t", [128, 2], F32))
            pr.dma("sync", selc_t[:, :], carry_sel[:, :])
            pr.op("vector", lambda v: v.tensor_scalar(carr[:, :], carr[:, :],
                                                      selc_t[:, 0:1], None, ALU.mult),
                  [carr[:, :], selc_t[:, 0:1]], [carr[:, :]])
            pr.op("vector", lambda v: v.tensor_scalar(sxt[:, :], sxt[:, :],
                                                      selc_t[:, 1:2], None, ALU.mult),
                  [sxt[:, :], selc_t[:, 1:2]], [sxt[:, :]])
        if do_passA:
            pr.op("vector", lambda v: v.memset(ccar[:, :], 0.0), [], [ccar[:, :]])
        else:
            pr.dma("sync", ccar[:, :], o_sumd[:, :])

        if do_passA:
            CH = chunk
            with ExitStack() as esa:
                eca = esa.enter_context
                w_in_bf = eca(nc.sbuf_tensor(pfx + "w_in_bf", [128, 8 * 4096], BF16))
                w_x_bf = eca(nc.sbuf_tensor(pfx + "w_x_bf", [128, 16 * 96], BF16))
                w_dt_bf = eca(nc.sbuf_tensor(pfx + "w_dt_bf", [64, 2048], BF16))
                xTbf = eca(nc.sbuf_tensor(pfx + "xTbf", [128, 8 * TH], BF16))
                xld = eca(nc.sbuf_tensor(pfx + "xld", [128, TH], F32))
                xr = eca(nc.sbuf_tensor(pfx + "xr", [128, CH + 3], F32))
                xi_f = eca(nc.sbuf_tensor(pfx + "xi_f", [128, CH], F32))
                xi_bf = eca(nc.sbuf_tensor(pfx + "xi_bf", [128, 16 * CH], BF16))
                resl = eca(nc.sbuf_tensor(pfx + "resl", [128, CH], F32))
                xdbl = eca(nc.sbuf_tensor(pfx + "xdbl", [96, CH], F32))
                dlin_bf = eca(nc.sbuf_tensor(pfx + "dlin_bf", [64, CH], BF16))
                dpre = eca(nc.sbuf_tensor(pfx + "dpre", [128, CH], F32))
                spl1 = eca(nc.sbuf_tensor(pfx + "spl1", [128, CH], F32))
                spl2 = eca(nc.sbuf_tensor(pfx + "spl2", [128, CH], F32))
                Cch = eca(nc.sbuf_tensor(pfx + "Cch", [128, CH], F32))
                onesC = eca(nc.sbuf_tensor(pfx + "onesC", [128, CH], F32))
                pr.op("vector", lambda v: v.memset(onesC[:, :], 1.0), [], [onesC[:, :]])

                for k in range(8):
                    pr.dma("gpsimd", w_in_bf[:, k * 4096:(k + 1) * 4096], W_in[k * 128:(k + 1) * 128, :])
                for k in range(16):
                    pr.dma("gpsimd", w_x_bf[:, k * 96:(k + 1) * 96], W_x[k * 128:(k + 1) * 128, :])
                pr.dma("gpsimd", w_dt_bf[:, :], W_dt[:, :])
                for k in range(8):
                    pr.dma("sync", xld[:, :], x_t[k * 128:(k + 1) * 128, :])
                    pr.op("vector",
                          lambda v, k=k: v.tensor_copy(xTbf[:, k * TH:(k + 1) * TH], xld[:, :]),
                          [xld[:, :]], [xTbf[:, k * TH:(k + 1) * TH]])

                for lc in range(NCH):
                    c0 = lc * CH
                    hw = CH + 3
                    for dt in range(16):
                        for s0 in range(0, hw, 512):
                            w = min(512, hw - s0)
                            ps = em.take_psum()
                            em.mm_group(
                                [(w_in_bf[:, k * 4096 + dt * 128: k * 4096 + dt * 128 + 128],
                                  xTbf[:, k * TH + c0 + s0: k * TH + c0 + s0 + w])
                                 for k in range(8)],
                                ps[:, 0:w])
                            pr.op("vector",
                                  lambda v, s0=s0, w=w, ps=ps: v.tensor_copy(xr[:, s0:s0 + w], ps[:, 0:w]),
                                  [ps[:, 0:w]], [xr[:, s0:s0 + w]])
                        acc = xi_f[:, 0:CH]
                        pr.op("vector",
                              lambda v, dt=dt, acc=acc: v.tensor_scalar(
                                  acc, xr[:, 0:CH], convw[:, dt * 4:dt * 4 + 1], None, ALU.mult),
                              [xr[:, 0:CH], convw[:, dt * 4:dt * 4 + 1]], [acc])
                        for tap in range(1, 4):
                            pr.op("vector",
                                  lambda v, dt=dt, tap=tap, acc=acc: v.scalar_tensor_tensor(
                                      acc, xr[:, tap:tap + CH],
                                      convw[:, dt * 4 + tap:dt * 4 + tap + 1], acc, ALU.mult, ALU.add),
                                  [xr[:, tap:tap + CH], acc,
                                   convw[:, dt * 4 + tap:dt * 4 + tap + 1]], [acc])
                        pr.op("vector",
                              lambda v, dt=dt, acc=acc: v.tensor_scalar(
                                  acc, acc, convb[:, dt:dt + 1], None, ALU.add),
                              [acc, convb[:, dt:dt + 1]], [acc])
                        pr.op("scalar", lambda s, acc=acc: s.activation(acc, acc, AF.Silu),
                              [acc], [acc])
                        pr.op("vector",
                              lambda v, dt=dt, acc=acc: v.tensor_copy(xi_bf[:, dt * CH:(dt + 1) * CH], acc),
                              [acc], [xi_bf[:, dt * CH:(dt + 1) * CH]])
                        pr.dma("gpsimd", sp_xi[dt * 128:(dt + 1) * 128, c0:c0 + CH], acc)
                    for of in range(16):
                        for s0 in range(0, CH, 512):
                            w = min(512, CH - s0)
                            ps = em.take_psum()
                            em.mm_group(
                                [(w_in_bf[:, k * 4096 + 2048 + of * 128: k * 4096 + 2048 + of * 128 + 128],
                                  xTbf[:, k * TH + c0 + 3 + s0: k * TH + c0 + 3 + s0 + w])
                                 for k in range(8)],
                                ps[:, 0:w])
                            pr.op("vector",
                                  lambda v, s0=s0, w=w, ps=ps: v.tensor_copy(resl[:, s0:s0 + w], ps[:, 0:w]),
                                  [ps[:, 0:w]], [resl[:, s0:s0 + w]])
                        pr.dma("gpsimd", sp_res[of * 128:(of + 1) * 128, c0:c0 + CH], resl[:, 0:CH])
                    for s0 in range(0, CH, 512):
                        w = min(512, CH - s0)
                        ps = em.take_psum()
                        em.mm_group(
                            [(w_x_bf[:, k * 96:(k + 1) * 96], xi_bf[:, k * CH + s0:k * CH + s0 + w])
                             for k in range(16)],
                            ps[0:96, 0:w])
                        pr.op("vector",
                              lambda v, s0=s0, w=w, ps=ps: v.tensor_copy(xdbl[:, s0:s0 + w], ps[0:96, 0:w]),
                              [ps[0:96, 0:w]], [xdbl[:, s0:s0 + w]])
                    pr.dma("gpsimd", sp_bmcm[:, c0:c0 + CH], xdbl[64:96, 0:CH])
                    pr.op("vector",
                          lambda v: v.tensor_copy(dlin_bf[:, 0:CH], xdbl[0:64, 0:CH]),
                          [xdbl[0:64, 0:CH]], [dlin_bf[:, 0:CH]])
                    for dt in range(16):
                        for s0 in range(0, CH, 512):
                            w = min(512, CH - s0)
                            ps = em.take_psum()
                            em.mm_group(
                                [(w_dt_bf[:, dt * 128:(dt + 1) * 128], dlin_bf[:, s0:s0 + w])],
                                ps[:, 0:w])
                            pr.op("vector",
                                  lambda v, s0=s0, w=w, dt=dt, ps=ps:
                                  v.tensor_scalar(dpre[:, s0:s0 + w], ps[:, 0:w],
                                                  bdt[:, dt:dt + 1], None, ALU.add),
                                  [ps[:, 0:w], bdt[:, dt:dt + 1]], [dpre[:, s0:s0 + w]])
                        pr.op("scalar", lambda s: s.activation(spl1[:, 0:CH], dpre[:, 0:CH], AF.Abs),
                              [dpre[:, 0:CH]], [spl1[:, 0:CH]])
                        pr.op("scalar", lambda s: s.activation(spl1[:, 0:CH], spl1[:, 0:CH], AF.Exp, scale=f32(-1.0)),
                              [spl1[:, 0:CH]], [spl1[:, 0:CH]])
                        pr.op("scalar", lambda s: s.activation(spl1[:, 0:CH], spl1[:, 0:CH], AF.Ln, bias=f32(1.0)),
                              [spl1[:, 0:CH]], [spl1[:, 0:CH]])
                        pr.op("scalar", lambda s: s.activation(spl2[:, 0:CH], dpre[:, 0:CH], AF.Relu),
                              [dpre[:, 0:CH]], [spl2[:, 0:CH]])
                        pr.op("vector", lambda v: v.tensor_add(dpre[:, 0:CH], spl1[:, 0:CH], spl2[:, 0:CH]),
                              [spl1[:, 0:CH], spl2[:, 0:CH]], [dpre[:, 0:CH]])
                        pr.dma("gpsimd", sp_delta[dt * 128:(dt + 1) * 128, c0:c0 + CH], dpre[:, 0:CH])
                        pr.op("vector",
                              lambda v, dt=dt: v.tensor_tensor_scan(
                                  Cch[:, 0:CH], onesC[:, 0:CH], dpre[:, 0:CH],
                                  ccar[:, dt:dt + 1], ALU.mult, ALU.add),
                              [onesC[:, 0:CH], dpre[:, 0:CH], ccar[:, dt:dt + 1]], [Cch[:, 0:CH]])
                        pr.op("vector",
                              lambda v, dt=dt: v.tensor_copy(ccar[:, dt:dt + 1], Cch[:, CH - 1:CH]),
                              [Cch[:, CH - 1:CH]], [ccar[:, dt:dt + 1]])
                        pr.dma("gpsimd", sp_C[dt * 128:(dt + 1) * 128, c0:c0 + CH], Cch[:, 0:CH])
                for e in ENGINES:
                    pr.barrier_to(e)


        if do_passA:
            pr.dma("gpsimd", o_sumd[:, :], ccar[:, :])

        SC = chunk
        NSC = ceil_div(T, SC)
        G = dgroup * SC
        selc = ec(nc.sbuf_tensor(pfx + "selc", [32, 32 * 128], F32))
        bmrow = ec(nc.sbuf_tensor(pfx + "bmrow", [32, T], F32))
        del_g = ec(nc.sbuf_tensor(pfx + "del_g", [128, G], F32))
        du_g = ec(nc.sbuf_tensor(pfx + "du_g", [128, G], F32))
        xi_g = ec(nc.sbuf_tensor(pfx + "xi_g", [128, G], F32))
        cdub = ec(nc.sbuf_tensor(pfx + "cdub", [128, G], F32))
        hs_g = ec(nc.sbuf_tensor(pfx + "hs_g", [128, G], F32))
        rho_g = [ec(nc.sbuf_tensor(pfx + f"rho_g{r}", [128, G], F32)) for r in range(2)]
        bmb = [ec(nc.sbuf_tensor(pfx + f"bmb{r}", [128, SC], F32)) for r in range(2)]
        hcar = ec(nc.sbuf_tensor(pfx + "hcar", [128, 256], F32))
        ctot = ec(nc.sbuf_tensor(pfx + "ctot", [128, 16], F32))
        if do_y:
            w_out_bf = ec(nc.sbuf_tensor(pfx + "w_out_bf", [128, 16 * DM], BF16))
            T_g = ec(nc.sbuf_tensor(pfx + "T_g", [128, G], F32))
            gq_g = [ec(nc.sbuf_tensor(pfx + f"gq_g{r}", [128, G], F32)) for r in range(2)]
            cmb = [ec(nc.sbuf_tensor(pfx + f"cmb{r}", [128, SC], F32)) for r in range(2)]
            res_s = [ec(nc.sbuf_tensor(pfx + f"res_s{r}", [128, SC], F32)) for r in range(2)]
            y_g = ec(nc.sbuf_tensor(pfx + "y_g", [128, G], F32))
            yg_bf = ec(nc.sbuf_tensor(pfx + "yg_bf", [128, 16 * SC], BF16))
            x1pre = ec(nc.sbuf_tensor(pfx + "x1pre", [128, 8 * SC], F32))
            strow = ec(nc.sbuf_tensor(pfx + "strow", [1, 3 * 512], F32))
            x1sq = y_g[:, 0:1024]
            mub = y_g[:, 1024:1536]
            rstdb = y_g[:, 1536:2048]
            xres = res_s
            for k in range(16):
                pr.dma("gpsimd", w_out_bf[:, k * DM:(k + 1) * DM], W_out[k * 128:(k + 1) * 128, :])
        pr.dma("sync", bmrow[:, :], sp_bmcm[:, :])
        pr.dma("sync", selc[:, :], selcat[:, :])
        pr.op("vector", lambda v: v.tensor_copy(hcar[:, :], carr[:, :]), [carr[:, :]], [hcar[:, :]])
        pr.op("vector", lambda v: v.tensor_add(ctot[:, :], ccar[:, :], sxt[:, :]),
              [ccar[:, :], sxt[:, :]], [ctot[:, :]])
        hcar_s = hcar[:, :].rearrange("p (d s) -> p d s", s=16)

        def shp(t):
            return t[:, 0:G].rearrange("p (a c) -> p a c", a=dgroup)

        def rep_sc(t):
            return t[:, 0:SC].rearrange("p (u c) -> p u c", u=1).broadcast_to([128, dgroup, SC])

        def rep_in(t, dt0):
            return t[:, dt0:dt0 + dgroup].rearrange("p (a u) -> p a u", u=1).broadcast_to(
                [128, dgroup, SC])

        for sc in range(NSC):
            c0 = sc * SC
            for dgi in range(16 // dgroup):
                dt0 = dgi * dgroup
                dts = [dt0 + j for j in range(dgroup)]
                for j, dt in enumerate(dts):
                    pr.dma("sync", del_g[:, j * SC:(j + 1) * SC], sp_delta[dt * 128:(dt + 1) * 128, c0:c0 + SC])
                    if do_y:
                        pr.dma("sync", cdub[:, j * SC:(j + 1) * SC], sp_C[dt * 128:(dt + 1) * 128, c0:c0 + SC])
                    pr.dma("gpsimd", xi_g[:, j * SC:(j + 1) * SC], sp_xi[dt * 128:(dt + 1) * 128, c0:c0 + SC])
                pr.op("vector", lambda v: v.tensor_mul(du_g[:, :], del_g[:, :], xi_g[:, :]),
                      [del_g[:, :], xi_g[:, :]], [du_g[:, :]])
                if do_y:
                    pr.op("vector", lambda v, dt0=dt0: v.tensor_tensor(
                              shp(T_g), rep_in(ctot, dt0), shp(cdub), ALU.subtract),
                          [ctot[:, dt0:dt0 + dgroup], cdub[:, :]], [T_g[:, :]])
                for n in range(16):
                    an = AN[n]
                    r = n % 2
                    for tgt, blkk in (((bmb[r], n), (cmb[r], 16 + n)) if do_y else ((bmb[r], n),)):
                        for s0 in range(0, SC, 512):
                            w = min(512, SC - s0)
                            ps = em.take_psum()
                            pr.op("tensor",
                                  lambda t, ps=ps, blkk=blkk, s0=s0, w=w: t.matmul(
                                      ps[:, 0:w], selc[:, blkk * 128:(blkk + 1) * 128],
                                      bmrow[0:32, c0 + s0:c0 + s0 + w],
                                      start=True, stop=True),
                                  [selc[:, blkk * 128:(blkk + 1) * 128],
                                   bmrow[0:32, c0 + s0:c0 + s0 + w]], [ps[:, 0:w]])
                            pr.op("scalar",
                                  lambda s, tgt=tgt, s0=s0, w=w, ps=ps: s.copy(
                                      tgt[:, s0:s0 + w], ps[:, 0:w]),
                                  [ps[:, 0:w]], [tgt[:, s0:s0 + w]])
                    pr.op("scalar", lambda s, r=r, an=an: s.activation(
                              rho_g[r][:, :], del_g[:, :], AF.Exp, scale=f32(an)),
                          [del_g[:, :]], [rho_g[r][:, :]])
                    if do_y:
                        pr.op("scalar", lambda s, r=r, an=an: s.activation(
                                  gq_g[r][:, :], T_g[:, :], AF.Tanh,
                                  scale=f32(an * 0.5), bias=f32(LNK * 0.5)),
                              [T_g[:, :]], [gq_g[r][:, :]])
                        pr.op("vector", lambda v, r=r: v.scalar_tensor_tensor(
                                  shp(gq_g[r]), shp(gq_g[r]), 1.0, rep_sc(cmb[r]),
                                  ALU.add, ALU.mult),
                              [gq_g[r][:, :], cmb[r][:, :]], [gq_g[r][:, :]])
                    pr.op("vector", lambda v, r=r: v.tensor_tensor(
                              shp(cdub), shp(du_g), rep_sc(bmb[r]), ALU.mult),
                          [du_g[:, :], bmb[r][:, :]], [cdub[:, :]])
                    for j, dt in enumerate(dts):
                        hc = hcar[:, dt * 16 + n: dt * 16 + n + 1]
                        pr.op("vector", lambda v, j=j, r=r, hc=hc: v.tensor_tensor_scan(
                                  hs_g[:, j * SC:(j + 1) * SC], rho_g[r][:, j * SC:(j + 1) * SC],
                                  cdub[:, j * SC:(j + 1) * SC], hc, ALU.mult, ALU.add),
                              [rho_g[r][:, j * SC:(j + 1) * SC], cdub[:, j * SC:(j + 1) * SC], hc],
                              [hs_g[:, j * SC:(j + 1) * SC]])
                    pr.op("vector", lambda v, dt0=dt0, n=n: v.tensor_copy(
                              hcar_s[:, dt0:dt0 + dgroup, n:n + 1],
                              shp(hs_g)[:, :, SC - 1:SC]),
                          [hs_g[:, :]], [hcar[:, dt0 * 16 + n:(dt0 + dgroup - 1) * 16 + n + 1]])
                    if do_y:
                        if n == 0:
                            pr.op("vector", lambda v, r=r: v.tensor_mul(
                                      y_g[:, :], hs_g[:, :], gq_g[r][:, :]),
                                  [hs_g[:, :], gq_g[r][:, :]], [y_g[:, :]])
                        else:
                            pr.op("vector", lambda v, r=r: v.tensor_mul(
                                      cdub[:, :], hs_g[:, :], gq_g[r][:, :]),
                                  [hs_g[:, :], gq_g[r][:, :]], [cdub[:, :]])
                            pr.op("vector", lambda v: v.tensor_add(y_g[:, :], y_g[:, :], cdub[:, :]),
                                  [cdub[:, :], y_g[:, :]], [y_g[:, :]])
                if do_y:
                    pr.op("vector", lambda v, dt0=dt0: v.tensor_tensor(
                              shp(hs_g), shp(xi_g), rep_in(dvt, dt0), ALU.mult),
                          [xi_g[:, :], dvt[:, dt0:dt0 + dgroup]], [hs_g[:, :]])
                    pr.op("vector", lambda v: v.tensor_add(y_g[:, :], y_g[:, :], hs_g[:, :]),
                          [hs_g[:, :], y_g[:, :]], [y_g[:, :]])
                    for j, dt in enumerate(dts):
                        rsl = res_s[j % 2][:, 0:SC]
                        pr.dma("sync", rsl, sp_res[dt * 128:(dt + 1) * 128, c0:c0 + SC])
                        pr.op("scalar", lambda s, rsl=rsl: s.activation(rsl, rsl, AF.Silu),
                              [rsl], [rsl])
                        pr.op("vector", lambda v, j=j, rsl=rsl, dt=dt: v.tensor_mul(
                                  yg_bf[:, dt * SC:(dt + 1) * SC], y_g[:, j * SC:(j + 1) * SC], rsl),
                              [y_g[:, j * SC:(j + 1) * SC], rsl], [yg_bf[:, dt * SC:(dt + 1) * SC]])
            for of in range(8) if do_y else ():
                for s0 in range(0, SC, 512):
                    w = min(512, SC - s0)
                    ps = em.take_psum()
                    em.mm_group(
                        [(w_out_bf[:, k * DM + of * 128: k * DM + of * 128 + 128],
                          yg_bf[:, k * SC + s0:k * SC + s0 + w]) for k in range(16)],
                        ps[:, 0:w])
                    xr2 = xres[of % 2]
                    pr.dma("sync", xr2[:, s0:s0 + w],
                           x_t[of * 128:(of + 1) * 128, 3 + c0 + s0: 3 + c0 + s0 + w])
                    pr.op("vector",
                          lambda v, of=of, s0=s0, w=w, ps=ps, xr2=xr2: v.tensor_add(
                              x1pre[:, of * SC + s0: of * SC + s0 + w], ps[:, 0:w],
                              xr2[:, s0:s0 + w]),
                          [ps[:, 0:w], xr2[:, s0:s0 + w]],
                          [x1pre[:, of * SC + s0: of * SC + s0 + w]])
            if do_y:
                emit_ln_feature(pr, em, x1pre, SC, strow, x1sq, mub, rstdb, nfeat=8)
                for of in range(8):
                    sl = x1pre[:, of * SC:(of + 1) * SC]
                    pr.op("vector", lambda v, sl=sl: v.tensor_sub(sl, sl, mub[:, 0:SC]),
                          [sl, mub[:, 0:SC]], [sl])
                    pr.op("vector", lambda v, sl=sl: v.tensor_mul(sl, sl, rstdb[:, 0:SC]),
                          [sl, rstdb[:, 0:SC]], [sl])
                    pr.dma("gpsimd", o_x1[of * 128:(of + 1) * 128, c0:c0 + SC], sl)

        pr.dma("gpsimd", o_hfin[:, :], hcar[:, :])
        for e in ENGINES:
            pr.barrier_to(e)
        print(f"phase1 program: {pr.n_ops} ops, {pr.n_waits} waits")
    return nc




def build_phase1(chunk=512, dgroup=4):
    nc = bass.Bass("TRN2", target_bir_lowering=False, debug=False)
    TH = T + 3
    x_t = nc.dram_tensor("x_t", [DM, TH], F32, kind="ExternalInput").ap()
    W_in = nc.dram_tensor("W_in", [DM, 4096], F32, kind="ExternalInput").ap()
    W_x = nc.dram_tensor("W_x", [2048, 96], F32, kind="ExternalInput").ap()
    W_dt = nc.dram_tensor("W_dt", [64, 2048], F32, kind="ExternalInput").ap()
    W_out = nc.dram_tensor("W_out", [2048, DM], F32, kind="ExternalInput").ap()
    conv_w = nc.dram_tensor("conv_w", [128, 64], F32, kind="ExternalInput").ap()
    conv_b = nc.dram_tensor("conv_b", [128, 16], F32, kind="ExternalInput").ap()
    b_dt = nc.dram_tensor("b_dt", [128, 16], F32, kind="ExternalInput").ap()
    Dv = nc.dram_tensor("Dv", [128, 16], F32, kind="ExternalInput").ap()
    carry = nc.dram_tensor("carry", [128, 256], F32, kind="ExternalInput").ap()
    sext = nc.dram_tensor("sext", [128, 16], F32, kind="ExternalInput").ap()
    selcat = nc.dram_tensor("selcat", [32, 32 * 128], F32, kind="ExternalInput").ap()

    o_x1 = nc.dram_tensor("x1_t", [DM, T], F32, kind="ExternalOutput").ap()
    o_hfin = nc.dram_tensor("hfin", [128, 256], F32, kind="ExternalOutput").ap()
    o_sumd = nc.dram_tensor("sumdelta", [128, 16], F32, kind="ExternalOutput").ap()

    sp_xi = nc.dram_tensor("sp_xi", [2048, T], F32).ap()
    sp_delta = nc.dram_tensor("sp_delta", [2048, T], F32).ap()
    sp_res = nc.dram_tensor("sp_res", [2048, T], F32).ap()
    sp_C = nc.dram_tensor("sp_C", [2048, T], F32).ap()
    sp_bmcm = nc.dram_tensor("sp_bmcm", [32, T], F32).ap()

    NCH = ceil_div(T, chunk)

    with ExitStack() as _es0:
        _ec0 = _es0.enter_context
        blk = _ec0(nc.Block())
        pr = Prog(nc, _es0)
        em = Emitter(nc, _es0, pr, n_psum=7)
        psbf = _ec0(nc.psum_tensor("psbf", [128, 1024], BF16))
        emit_phase1(nc, pr, em, x_t, W_in, W_x, W_dt, W_out, conv_w, conv_b, b_dt,
                    Dv, carry, sext, selcat, o_x1, o_hfin, o_sumd,
                    sp_xi, sp_delta, sp_res, sp_C, sp_bmcm, chunk=chunk, dgroup=dgroup)
    return nc


def emit_ln_feature(pr, em, xtiles, width, strow, sqtile, mub, rstdb, nfeat=8, eps=1e-5):
    nc = pr.nc
    dmf = float(nfeat * 128)
    for s0 in range(0, width, 512):
        w = min(512, width - s0)
        ps = em.take_psum()
        ps2 = em.take_psum()
        em.mm_group(
            [(em.ones_col[:, :], xtiles[:, of * width + s0: of * width + s0 + w])
             for of in range(nfeat)],
            ps[0:1, 0:w])
        for of in range(nfeat):
            sq = sqtile[:, (of % 2) * 512:(of % 2) * 512 + w]
            pr.op("scalar",
                  lambda s, of=of, s0=s0, w=w, sq=sq: s.activation(
                      sq, xtiles[:, of * width + s0: of * width + s0 + w], AF.Square),
                  [xtiles[:, of * width + s0: of * width + s0 + w]], [sq])
            pr.op("tensor",
                  lambda t, of=of, w=w, ps2=ps2, sq=sq: t.matmul(
                      ps2[0:1, 0:w], em.ones_col[:, :], sq,
                      start=(of == 0), stop=(of == nfeat - 1)),
                  [em.ones_col[:, :], sq],
                  [ps2[0:1, 0:w]],
                  check_writes=[ps2[0:1, 0:w]] if of == 0 else [])
        mu = strow[0:1, 0:w]
        var = strow[0:1, 512:512 + w]
        rstd = strow[0:1, 1024:1024 + w]
        pr.op("vector", lambda v, mu=mu, ps=ps, w=w: v.tensor_scalar_mul(mu, ps[0:1, 0:w], 1.0 / dmf),
              [ps[0:1, 0:w]], [mu])
        pr.op("vector", lambda v, var=var, ps2=ps2, w=w: v.tensor_scalar_mul(var, ps2[0:1, 0:w], 1.0 / dmf),
              [ps2[0:1, 0:w]], [var])
        pr.op("vector", lambda v, rstd=rstd, mu=mu: v.tensor_mul(rstd, mu, mu), [mu], [rstd])
        pr.op("vector", lambda v, var=var, rstd=rstd: v.tensor_sub(var, var, rstd), [rstd, var], [var])
        pr.op("scalar", lambda s, var=var: s.activation(var, var, AF.Ln, bias=f32(eps)),
              [var], [var])
        pr.op("scalar", lambda s, rstd=rstd, var=var: s.activation(rstd, var, AF.Exp, scale=f32(-0.5)),
              [var], [rstd])
        ps3 = em.take_psum()
        pr.op("tensor", lambda t, ps3=ps3, mu=mu, w=w: t.matmul(
                  ps3[:, 0:w], em.ones_row[:, :], mu, start=True, stop=True),
              [em.ones_row[:, :], mu], [ps3[:, 0:w]])
        pr.op("vector", lambda v, s0=s0, w=w, ps3=ps3: v.tensor_copy(mub[:, s0:s0 + w], ps3[:, 0:w]),
              [ps3[:, 0:w]], [mub[:, s0:s0 + w]])
        ps4 = em.take_psum()
        pr.op("tensor", lambda t, ps4=ps4, rstd=rstd, w=w: t.matmul(
                  ps4[:, 0:w], em.ones_row[:, :], rstd, start=True, stop=True),
              [em.ones_row[:, :], rstd], [ps4[:, 0:w]])
        pr.op("vector", lambda v, s0=s0, w=w, ps4=ps4: v.tensor_copy(rstdb[:, s0:s0 + w], ps4[:, 0:w]),
              [ps4[:, 0:w]], [rstdb[:, s0:s0 + w]])


HALO2 = 1026
QW = 512
HIDP = 2816


def prep_phase2(inputs, lay):
    wqkv = _np_f32(inputs["L0_ffn_wqkv"][lay])
    qs = _np_f32(inputs["L0_ffn_q_scale"][lay])
    ks = _np_f32(inputs["L0_ffn_k_scale"][lay])
    wo = _np_f32(inputs["L0_ffn_wo"][lay])
    lg = _np_f32(inputs["L0_ffn_ln_g"][lay])
    lb = _np_f32(inputs["L0_ffn_ln_b"][lay])
    w1 = _np_f32(inputs["L0_ffn_w1"][lay])
    b1 = _np_f32(inputs["L0_ffn_b1"][lay])
    w2 = _np_f32(inputs["L0_ffn_w2"][lay])
    b2 = _np_f32(inputs["L0_ffn_b2"][lay])

    w1p = np.zeros((1024, 2 * HIDP), np.float32)
    b1p = np.zeros((128, 2 * (HIDP // 128)), np.float32)
    for j in range(HIDP // 128):
        a0, a1 = j * 128, min((j + 1) * 128, 2730)
        if a0 < 2730:
            n = a1 - a0
            w1p[:, j * 256: j * 256 + n] = w1[:, a0:a1]
            w1p[:, j * 256 + 128: j * 256 + 128 + n] = w1[:, 2730 + a0:2730 + a1]
            b1p[0:n, 2 * j] = b1[a0:a1]
            b1p[0:n, 2 * j + 1] = b1[2730 + a0:2730 + a1]
    w2p = np.zeros((HIDP, 1024), np.float32)
    w2p[:2730] = w2

    def cols8(v):
        return np.ascontiguousarray(v.reshape(8, 128).T)

    shared = {
        "wqkv": wqkv, "wo": wo, "w1p": w1p, "w2p": w2p, "b1p": b1p,
        "b2c": cols8(b2), "lgc": cols8(lg), "lbc": cols8(lb),
        "qs8": np.ascontiguousarray(np.tile(qs * 8.0, 2).reshape(128, 1) * np.ones((1, 4), np.float32)),
        "ksc": np.ascontiguousarray(np.tile(ks, 2).reshape(128, 1) * np.ones((1, 4), np.float32)),
        "selheadT": np.ascontiguousarray(
            np.stack([np.r_[np.ones(64, np.float32), np.zeros(64, np.float32)],
                      np.r_[np.zeros(64, np.float32), np.ones(64, np.float32)]])),
        "selhead": np.ascontiguousarray(
            np.stack([np.r_[np.ones(64, np.float32), np.zeros(64, np.float32)],
                      np.r_[np.zeros(64, np.float32), np.ones(64, np.float32)]]).T),
    }
    i_rel = np.arange(512)[:, None]
    j_rel = np.arange(1024)[None, :]
    m_norm = np.zeros((512, 1024), np.float32)
    m_norm[:, :512][i_rel > j_rel[:, :512]] = -1e38
    m_norm[:, 512:][i_rel < (j_rel[:, 512:] - 512)] = -1e38
    m_w0 = m_norm.copy()
    m_w0[:, :512] = -1e38
    shared["l0mask"] = m_norm

    def rope_tables(tglob):
        n = len(tglob)
        half_idx = np.arange(0, 64, 2, dtype=np.float32)
        inv_freq = 1.0 / 10000.0 ** (half_idx / 64.0)
        tt = np.asarray(tglob, np.float32)
        freqs = tt[:, None] * inv_freq[None, :]
        freqs = np.concatenate([freqs, freqs], axis=-1)
        base = (half_idx + 0.4 * 64) / (1.4 * 64)
        power = (tt - 4096 // 2) / float(QW // 2)
        scale = base[None, :] ** power[:, None]
        scale = np.concatenate([scale, scale], axis=-1)
        cos, sin = np.cos(freqs), np.sin(freqs)
        sgn = np.r_[-np.ones(32, np.float32), np.ones(32, np.float32)]
        csq = (cos * scale).T
        ssq = (sin * scale).T * sgn[:, None]
        csk = (cos / scale).T
        ssk = (sin / scale).T * sgn[:, None]
        dbl = lambda a: np.ascontiguousarray(np.concatenate([a, a], axis=0).astype(np.float32))
        return dbl(csq), dbl(ssq), dbl(csk), dbl(ssk)

    per_core = []
    for c in range(NCORES):
        b, h = c // 2, c % 2
        t0 = h * T
        tq = np.maximum(np.arange(t0 - QW, t0 + T), 0)
        tk = np.maximum(np.arange(t0 - 2 * QW, t0 + T), 0)
        csq, ssq, _, _ = rope_tables(tq)
        _, _, csk, ssk = rope_tables(tk)
        per_core.append({
            "csq": csq, "ssq": ssq, "csk": csk, "ssk": ssk,
            "l0mask_w1": (m_w0 if h == 0 else m_norm),
        })
    return shared, per_core


def slice_stream(x_full, c, halo, zero_halo_for_first=True):
    b, h = c // 2, c % 2
    t0 = h * T
    lo = t0 - halo
    if lo < 0:
        xs = np.concatenate([np.zeros((-lo, DM), np.float32), x_full[b, 0:t0 + T]], axis=0)
    else:
        xs = x_full[b, lo:t0 + T]
    return np.ascontiguousarray(xs.T)


def emit_phase2(nc, pr, em, psbf, xin_t, xres_t, wqkv, wo, w1p, w2p, b1p, b2c, lgc, lbc, qs8, ksc, selheadT, selhead, csq, ssq, csk, ssk, l0mask, l0mask_w1, idn_in, o_out, o_ln, sp_x2m, pfx=""):
    W = HALO2 + T
    NQ = T // QW + 1
    QTOT = T + QW
    KTOT = T + 2 * QW

    with ExitStack() as es:
        ec = es.enter_context

        shT = ec(nc.sbuf_tensor(pfx + "shT", [2, 128], F32))
        shD = ec(nc.sbuf_tensor(pfx + "shD", [128, 2], F32))
        qsct = ec(nc.sbuf_tensor(pfx + "qsct", [128, 4], F32))
        ksct = ec(nc.sbuf_tensor(pfx + "ksct", [128, 4], F32))
        idn_bf = ec(nc.sbuf_tensor(pfx + "idn_bf", [128, 128], BF16))
        o_bf = ec(nc.sbuf_tensor(pfx + "o_bf", [128, 4 * QTOT], BF16))
        pr.dma("sync", shT[:, :], selheadT[:, :])
        pr.dma("sync", shD[:, :], selhead[:, :])
        pr.dma("sync", qsct[:, :], qs8[:, :])
        pr.dma("sync", ksct[:, :], ksc[:, :])
        pr.dma("gpsimd", idn_bf[:, :], idn_in[:, :])

        with ExitStack() as es12:
            ec12 = es12.enter_context
            q_bf = ec12(nc.sbuf_tensor(pfx + "q_bf", [128, 4 * QTOT], BF16))
            k_bf = ec12(nc.sbuf_tensor(pfx + "k_bf", [128, 4 * KTOT], BF16))
            v_tr = ec12(nc.sbuf_tensor(pfx + "v_tr", [128, (KTOT // 128) * 512], BF16))

            with ExitStack() as esa:
                eca = esa.enter_context
                xin_bf = eca(nc.sbuf_tensor(pfx + "xin_bf", [128, 8 * W], BF16))
                wqs = [eca(nc.sbuf_tensor(pfx + f"wqs{i}", [128, 8 * 128], BF16)) for i in range(3)]
                xld = eca(nc.sbuf_tensor(pfx + "p2xld", [128, 512], F32))
                scr = eca(nc.sbuf_tensor(pfx + "scr", [128, 2 * 512], F32))
                sqs = eca(nc.sbuf_tensor(pfx + "sqs", [128, 2 * 512], F32))
                nrm = eca(nc.sbuf_tensor(pfx + "nrm", [2, 2 * 512], F32))
                invb = eca(nc.sbuf_tensor(pfx + "invb", [128, 2 * 512], F32))
                qn2 = eca(nc.sbuf_tensor(pfx + "qn2", [128, 2 * 512], F32))
                qsw = eca(nc.sbuf_tensor(pfx + "qsw", [128, 2 * 512], F32))
                rot_cs = eca(nc.sbuf_tensor(pfx + "rot_cs", [128, 2 * 512], F32))
                rot_ss = eca(nc.sbuf_tensor(pfx + "rot_ss", [128, 2 * 512], F32))
                vev = eca(nc.sbuf_tensor(pfx + "vev", [128, KTOT], BF16))

                for k in range(8):
                    for s0 in range(0, W, 512):
                        wdt = min(512, W - s0)
                        pr.dma("sync", xld[:, 0:wdt], xin_t[k * 128:(k + 1) * 128, s0:s0 + wdt])
                        pr.op("vector",
                              lambda v, k=k, s0=s0, wdt=wdt: v.tensor_copy(
                                  xin_bf[:, k * W + s0:k * W + s0 + wdt], xld[:, 0:wdt]),
                              [xld[:, 0:wdt]], [xin_bf[:, k * W + s0:k * W + s0 + wdt]])

                def qkv_block(of, wtile, out_ap_f32, col0, wdt):
                    ps = em.take_psum()
                    em.mm_group(
                        [(wtile[:, k * 128:(k + 1) * 128],
                          xin_bf[:, k * W + col0 + (0 if k < 4 else -1):
                                 k * W + col0 + (0 if k < 4 else -1) + wdt])
                         for k in range(8)],
                        ps[:, 0:wdt])
                    pr.op("vector",
                          lambda v, ps=ps, out_ap_f32=out_ap_f32, wdt=wdt: v.tensor_copy(
                              out_ap_f32, ps[:, 0:wdt]),
                          [ps[:, 0:wdt]], [out_ap_f32])

                wi = 0
                for hp in range(4):
                    for kind in range(3):
                        of = kind * 4 + hp
                        wtile = wqs[wi % 3]
                        wi += 1
                        for k in range(8):
                            pr.dma("gpsimd", wtile[:, k * 128:(k + 1) * 128],
                                   wqkv[k * 128:(k + 1) * 128, of * 128:(of + 1) * 128])
                        if kind == 2:
                            for s0 in range(0, KTOT, 512):
                                wdt = min(512, KTOT - s0)
                                pp = (s0 // 512) % 2
                                sl = scr[:, pp * 512: pp * 512 + wdt]
                                qkv_block(of, wtile, sl, HALO2 - 2 * QW + s0, wdt)
                                pr.op("vector", lambda v, sl=sl, s0=s0, wdt=wdt: v.tensor_copy(
                                          vev[:, s0:s0 + wdt], sl),
                                      [sl], [vev[:, s0:s0 + wdt]])
                            for tb in range(KTOT // 128):
                                for hh in range(2):
                                    pst = psbf[:, (tb % 8) * 128:(tb % 8) * 128 + 64]
                                    pr.op("tensor",
                                          lambda t, tb=tb, hh=hh, pst=pst: t.transpose(
                                              pst, vev[hh * 64:hh * 64 + 64, tb * 128:(tb + 1) * 128],
                                              idn_bf[hh * 64:hh * 64 + 64, hh * 64:hh * 64 + 64]),
                                          [vev[hh * 64:hh * 64 + 64, tb * 128:(tb + 1) * 128],
                                           idn_bf[hh * 64:hh * 64 + 64, hh * 64:hh * 64 + 64]],
                                          [pst])
                                    h = hp * 2 + hh
                                    pr.op("vector",
                                          lambda v, tb=tb, h=h, pst=pst: v.tensor_copy(
                                              v_tr[:, tb * 512 + h * 64: tb * 512 + h * 64 + 64], pst),
                                          [pst], [v_tr[:, tb * 512 + h * 64: tb * 512 + h * 64 + 64]])
                            continue
                        is_q = kind == 0
                        ntok = QTOT if is_q else KTOT
                        col0 = HALO2 - (QW if is_q else 2 * QW)
                        out_bf = q_bf if is_q else k_bf
                        sc_col = qsct if is_q else ksct
                        cs_in, ss_in = (csq, ssq) if is_q else (csk, ssk)
                        for s0 in range(0, ntok, 512):
                            wdt = min(512, ntok - s0)
                            pp = (s0 // 512) % 2
                            sl = scr[:, pp * 512: pp * 512 + wdt]
                            sq = sqs[:, pp * 512: pp * 512 + wdt]
                            nr = nrm[:, pp * 512: pp * 512 + wdt]
                            iv = invb[:, pp * 512: pp * 512 + wdt]
                            q2 = qn2[:, pp * 512: pp * 512 + wdt]
                            qw_ = qsw[:, pp * 512: pp * 512 + wdt]
                            rc = rot_cs[:, pp * 512: pp * 512 + wdt]
                            rs = rot_ss[:, pp * 512: pp * 512 + wdt]
                            qkv_block(of, wtile, sl, col0 + s0, wdt)
                            pr.dma("scalar", rc, cs_in[:, s0:s0 + wdt])
                            pr.dma("scalar", rs, ss_in[:, s0:s0 + wdt])
                            pr.op("scalar", lambda s, sq=sq, sl=sl: s.activation(sq, sl, AF.Square),
                                  [sl], [sq])
                            ps = em.take_psum()
                            pr.op("tensor", lambda t, ps=ps, sq=sq, wdt=wdt: t.matmul(
                                      ps[0:2, 0:wdt], shD[:, :], sq, start=True, stop=True),
                                  [shD[:, :], sq], [ps[0:2, 0:wdt]])
                            pr.op("vector", lambda v, ps=ps, nr=nr, wdt=wdt: v.tensor_copy(
                                      nr, ps[0:2, 0:wdt]),
                                  [ps[0:2, 0:wdt]], [nr])
                            pr.op("scalar", lambda s, nr=nr: s.activation(nr, nr, AF.Sqrt),
                                  [nr], [nr])
                            pr.op("vector", lambda v, nr=nr: v.tensor_scalar_max(nr, nr, 1e-12),
                                  [nr], [nr])
                            pr.op("scalar", lambda s, nr=nr: s.activation(nr, nr, AF.Ln),
                                  [nr], [nr])
                            pr.op("scalar", lambda s, nr=nr: s.activation(nr, nr, AF.Exp, scale=f32(-1.0)),
                                  [nr], [nr])
                            ps2 = em.take_psum()
                            pr.op("tensor", lambda t, ps2=ps2, nr=nr, wdt=wdt: t.matmul(
                                      ps2[:, 0:wdt], shT[:, :], nr, start=True, stop=True),
                                  [shT[:, :], nr], [ps2[:, 0:wdt]])
                            pr.op("vector", lambda v, ps2=ps2, iv=iv, wdt=wdt: v.tensor_copy(iv, ps2[:, 0:wdt]),
                                  [ps2[:, 0:wdt]], [iv])
                            pr.op("vector", lambda v, q2=q2, sl=sl, iv=iv: v.tensor_mul(q2, sl, iv),
                                  [sl, iv], [q2])
                            pr.op("vector", lambda v, q2=q2, sc_col=sc_col, hp=hp: v.tensor_scalar(
                                      q2, q2, sc_col[:, hp:hp + 1], None, ALU.mult),
                                  [q2, sc_col[:, hp:hp + 1]], [q2])
                            for blk4 in range(4):
                                pr.dma("gpsimd",
                                       qw_[blk4 * 32:blk4 * 32 + 32, :],
                                       q2[(blk4 * 32) ^ 32:((blk4 * 32) ^ 32) + 32, :])
                            pr.op("vector", lambda v, sl=sl, q2=q2, rc=rc: v.tensor_mul(sl, q2, rc),
                                  [q2, rc], [sl])
                            pr.op("vector", lambda v, qw_=qw_, rs=rs: v.tensor_mul(qw_, qw_, rs),
                                  [qw_, rs], [qw_])
                            pr.op("vector",
                                  lambda v, out_bf=out_bf, hp=hp, ntok=ntok, s0=s0, wdt=wdt, sl=sl, qw_=qw_:
                                  v.tensor_add(out_bf[:, hp * ntok + s0: hp * ntok + s0 + wdt], sl, qw_),
                                  [sl, qw_], [out_bf[:, hp * ntok + s0: hp * ntok + s0 + wdt]])
                for e in ENGINES:
                    pr.barrier_to(e)

            with ExitStack() as esb:
                ecb = esb.enter_context
                mask_n = ecb(nc.sbuf_tensor(pfx + "mask_n", [128, 4 * 1024], F32))
                mask_1 = ecb(nc.sbuf_tensor(pfx + "mask_1", [128, 4 * 1024], F32))
                sc_f = ecb(nc.sbuf_tensor(pfx + "sc_f", [128, 1024], F32))
                att_bf = ecb(nc.sbuf_tensor(pfx + "att_bf", [128, 1024], BF16))
                attT_bf = ecb(nc.sbuf_tensor(pfx + "attT_bf", [128, 8 * 128], BF16))
                mrow = ecb(nc.sbuf_tensor(pfx + "mrow", [128, 2], F32))
                rsum = ecb(nc.sbuf_tensor(pfx + "rsum", [128, 2], F32))
                recb = ecb(nc.sbuf_tensor(pfx + "recb", [128, 2], F32))
                for qt4 in range(4):
                    pr.dma("scalar", mask_n[:, qt4 * 1024:(qt4 + 1) * 1024], l0mask[qt4 * 128:(qt4 + 1) * 128, :])
                    pr.dma("scalar", mask_1[:, qt4 * 1024:(qt4 + 1) * 1024], l0mask_w1[qt4 * 128:(qt4 + 1) * 128, :])
                for wq in range(NQ):
                    for h in range(8):
                        hp, hh = h // 2, h % 2
                        for qt in range(4):
                            q0 = wq * QW + qt * 128
                            mask = mask_1 if wq == 1 else mask_n
                            pshs = []
                            for kh in range(2):
                                k0 = wq * QW + kh * 512
                                ps = em.take_psum()
                                pr.op("tensor",
                                      lambda t, ps=ps, hp=hp, hh=hh, q0=q0, k0=k0: t.matmul(
                                          ps[:, 0:512],
                                          q_bf[hh * 64:hh * 64 + 64, hp * QTOT + q0: hp * QTOT + q0 + 128],
                                          k_bf[hh * 64:hh * 64 + 64, hp * KTOT + k0: hp * KTOT + k0 + 512],
                                          start=True, stop=True),
                                      [q_bf[hh * 64:hh * 64 + 64, hp * QTOT + q0: hp * QTOT + q0 + 128],
                                       k_bf[hh * 64:hh * 64 + 64, hp * KTOT + k0: hp * KTOT + k0 + 512]],
                                      [ps[:, 0:512]])
                                pshs.append(ps)
                            for kh in range(2):
                                pr.op("vector",
                                      lambda v, kh=kh, qt=qt, mask=mask, ps=pshs[kh]: v.tensor_add(
                                          sc_f[:, kh * 512:(kh + 1) * 512], ps[:, 0:512],
                                          mask[:, qt * 1024 + kh * 512: qt * 1024 + kh * 512 + 512]),
                                      [pshs[kh][:, 0:512],
                                       mask[:, qt * 1024 + kh * 512: qt * 1024 + kh * 512 + 512]],
                                      [sc_f[:, kh * 512:(kh + 1) * 512]])
                            pr.op("vector", lambda v: v.reduce_max(mrow[:, 0:1], sc_f[:, 0:1024], axis=AX.X, negate=True),
                                  [sc_f[:, 0:1024]], [mrow[:, 0:1]])
                            pr.op("scalar", lambda s: s.activation(sc_f[:, 0:1024], sc_f[:, 0:1024], AF.Exp,
                                                                   bias=mrow[:, 0:1], accum_out=rsum[:, 0:1]),
                                  [sc_f[:, 0:1024], mrow[:, 0:1]], [sc_f[:, 0:1024], rsum[:, 0:1]])
                            pr.op("vector", lambda v: v.reciprocal(recb[:, 0:1], rsum[:, 0:1]),
                                  [rsum[:, 0:1]], [recb[:, 0:1]])
                            pr.op("vector", lambda v: v.tensor_scalar(att_bf[:, 0:1024], sc_f[:, 0:1024],
                                                                      recb[:, 0:1], None, ALU.mult),
                                  [sc_f[:, 0:1024], recb[:, 0:1]], [att_bf[:, 0:1024]])
                            for kt in range(8):
                                pst = psbf[:, kt * 128:(kt + 1) * 128]
                                pr.op("tensor",
                                      lambda t, kt=kt, pst=pst: t.transpose(
                                          pst, att_bf[:, kt * 128:(kt + 1) * 128], idn_bf[:, :]),
                                      [att_bf[:, kt * 128:(kt + 1) * 128], idn_bf[:, :]], [pst])
                                pr.op("vector",
                                      lambda v, kt=kt, pst=pst: v.tensor_copy(
                                          attT_bf[:, kt * 128:(kt + 1) * 128], pst),
                                      [pst], [attT_bf[:, kt * 128:(kt + 1) * 128]])
                            psA = em.take_psum()
                            kb0 = wq * 4
                            em.mm_group(
                                [(v_tr[:, (kb0 + kt) * 512 + h * 64: (kb0 + kt) * 512 + h * 64 + 64],
                                  attT_bf[:, kt * 128:(kt + 1) * 128]) for kt in range(8)],
                                psA[0:64, 0:128])
                            pr.op("vector", lambda v, hp=hp, hh=hh, q0=q0, psA=psA: v.tensor_copy(
                                      o_bf[hh * 64:hh * 64 + 64, hp * QTOT + q0: hp * QTOT + q0 + 128],
                                      psA[0:64, 0:128]),
                                  [psA[0:64, 0:128]],
                                  [o_bf[hh * 64:hh * 64 + 64, hp * QTOT + q0: hp * QTOT + q0 + 128]])
                for e in ENGINES:
                    pr.barrier_to(e)

        TT1 = T + 1
        with ExitStack() as esc:
            ecc = esc.enter_context
            x2m = ecc(nc.sbuf_tensor(pfx + "x2m", [128, 8 * TT1], F32))
            xrl = ecc(nc.sbuf_tensor(pfx + "p2xrl", [128, 512], F32))
            with ExitStack() as esd:
                ecd = esd.enter_context
                wo_bf = ecd(nc.sbuf_tensor(pfx + "wo_bf", [128, 4 * DM], BF16))
                for k in range(4):
                    pr.dma("gpsimd", wo_bf[:, k * DM:(k + 1) * DM], wo[k * 128:(k + 1) * 128, :])
                for of in range(8):
                    for s0 in blocks512(TT1):
                        wdt = 512
                        ps = em.take_psum()
                        em.mm_group(
                            [(wo_bf[:, k * DM + of * 128: k * DM + of * 128 + 128],
                              o_bf[:, k * QTOT + QW - 1 + s0: k * QTOT + QW - 1 + s0 + wdt])
                             for k in range(4)],
                            ps[:, 0:wdt])
                        pr.dma("scalar", xrl[:, 0:wdt], xin_t[of * 128:(of + 1) * 128,
                                                             HALO2 - 1 + s0: HALO2 - 1 + s0 + wdt])
                        pr.op("vector",
                              lambda v, of=of, s0=s0, wdt=wdt, ps=ps: v.tensor_add(
                                  x2m[:, of * TT1 + s0: of * TT1 + s0 + wdt], ps[:, 0:wdt],
                                  xrl[:, 0:wdt]),
                              [ps[:, 0:wdt], xrl[:, 0:wdt]],
                              [x2m[:, of * TT1 + s0: of * TT1 + s0 + wdt]])
                for e in ENGINES:
                    pr.barrier_to(e)
            for of in range(8):
                pr.dma("gpsimd", sp_x2m[of * 128:(of + 1) * 128, :], x2m[:, of * TT1: of * TT1 + TT1])

            lgt = ecc(nc.sbuf_tensor(pfx + "lgt", [128, 8], F32))
            lbt = ecc(nc.sbuf_tensor(pfx + "lbt", [128, 8], F32))
            b1t = ecc(nc.sbuf_tensor(pfx + "b1t", [128, 2 * (HIDP // 128)], F32))
            b2t = ecc(nc.sbuf_tensor(pfx + "b2t", [128, 8], F32))
            pr.dma("sync", lgt[:, :], lgc[:, :])
            pr.dma("sync", lbt[:, :], lbc[:, :])
            pr.dma("sync", b1t[:, :], b1p[:, :])
            pr.dma("sync", b2t[:, :], b2c[:, :])
            FCH = 512
            hsh = ecc(nc.sbuf_tensor(pfx + "hsh", [128, 8 * FCH], F32))
            hln_bf = ecc(nc.sbuf_tensor(pfx + "hln_bf", [128, 8 * FCH], BF16))
            h2_bf = ecc(nc.sbuf_tensor(pfx + "h2_bf", [128, (HIDP // 128) * FCH], BF16))
            sqt = ecc(nc.sbuf_tensor(pfx + "p2sq", [128, 2 * 512], F32))
            strow2 = ecc(nc.sbuf_tensor(pfx + "strow2", [1, 3 * 512], F32))
            mub2 = ecc(nc.sbuf_tensor(pfx + "mub2", [128, FCH], F32))
            rstdb2 = ecc(nc.sbuf_tensor(pfx + "rstdb2", [128, FCH], F32))
            wstr = [ecc(nc.sbuf_tensor(pfx + f"wstr{i}", [128, 8 * 128], BF16)) for i in range(4)]
            gel = ecc(nc.sbuf_tensor(pfx + "gel", [128, 512], F32))
            aev = ecc(nc.sbuf_tensor(pfx + "aev", [128, 512], F32))
            xq2 = ecc(nc.sbuf_tensor(pfx + "xq2", [128, 512], F32))
            shsave = ecc(nc.sbuf_tensor(pfx + "shsave", [128, 8], F32))
            shsave_b = ecc(nc.sbuf_tensor(pfx + "shsave_b", [128, 8], F32))
            shsave2 = None
            NJ = HIDP // 128
            for fc in range(T // FCH):
                c0 = fc * FCH
                for of in range(4, 8):
                    pr.op("vector",
                          lambda v, of=of, c0=c0: v.tensor_copy(
                              shsave[:, of:of + 1], x2m[:, of * TT1 + c0 + FCH: of * TT1 + c0 + FCH + 1]),
                          [x2m[:, of * TT1 + c0 + FCH: of * TT1 + c0 + FCH + 1]],
                          [shsave[:, of:of + 1]])
                for of in range(8):
                    off = 1 if of < 4 else 0
                    pr.op("vector",
                          lambda v, of=of, off=off, c0=c0: v.tensor_copy(
                              hsh[:, of * FCH:(of + 1) * FCH],
                              x2m[:, of * TT1 + c0 + off: of * TT1 + c0 + off + FCH]),
                          [x2m[:, of * TT1 + c0 + off: of * TT1 + c0 + off + FCH]],
                          [hsh[:, of * FCH:(of + 1) * FCH]])
                if fc > 0:
                    for of in range(4, 8):
                        pr.op("vector",
                              lambda v, of=of: v.tensor_copy(
                                  hsh[:, of * FCH: of * FCH + 1], shsave2[:, of:of + 1]),
                              [shsave2[:, of:of + 1]], [hsh[:, of * FCH: of * FCH + 1]])
                shsave, shsave2 = (shsave2, shsave) if fc > 0 else (shsave_b, shsave)
                emit_ln_feature(pr, em, hsh, FCH, strow2, sqt, mub2, rstdb2, nfeat=8)
                for of in range(8):
                    sl = hsh[:, of * FCH:(of + 1) * FCH]
                    pr.op("vector", lambda v, sl=sl: v.tensor_sub(sl, sl, mub2[:, 0:FCH]),
                          [sl, mub2[:, 0:FCH]], [sl])
                    pr.op("vector", lambda v, sl=sl: v.tensor_mul(sl, sl, rstdb2[:, 0:FCH]),
                          [sl, rstdb2[:, 0:FCH]], [sl])
                    pr.op("vector", lambda v, sl=sl, of=of: v.tensor_scalar(
                              hln_bf[:, of * FCH:(of + 1) * FCH], sl,
                              lgt[:, of:of + 1], lbt[:, of:of + 1], ALU.mult, ALU.add),
                          [sl, lgt[:, of:of + 1], lbt[:, of:of + 1]],
                          [hln_bf[:, of * FCH:(of + 1) * FCH]])
                for j in range(NJ):
                    wa = wstr[(2 * j) % 4]
                    wg = wstr[(2 * j + 1) % 4]
                    for k in range(8):
                        pr.dma("gpsimd", wa[:, k * 128:(k + 1) * 128],
                               w1p[k * 128:(k + 1) * 128, j * 256: j * 256 + 128])
                        pr.dma("gpsimd", wg[:, k * 128:(k + 1) * 128],
                               w1p[k * 128:(k + 1) * 128, j * 256 + 128: j * 256 + 256])
                    for s0 in range(0, FCH, 512):
                        wdt = min(512, FCH - s0)
                        psa = em.take_psum()
                        em.mm_group(
                            [(wa[:, k * 128:(k + 1) * 128], hln_bf[:, k * FCH + s0:k * FCH + s0 + wdt])
                             for k in range(8)], psa[:, 0:wdt])
                        psg = em.take_psum()
                        em.mm_group(
                            [(wg[:, k * 128:(k + 1) * 128], hln_bf[:, k * FCH + s0:k * FCH + s0 + wdt])
                             for k in range(8)], psg[:, 0:wdt])
                        pr.op("vector", lambda v, psa=psa, wdt=wdt, j=j: v.tensor_scalar(
                                  aev[:, 0:wdt], psa[:, 0:wdt], b1t[:, 2 * j:2 * j + 1], None, ALU.add),
                              [psa[:, 0:wdt], b1t[:, 2 * j:2 * j + 1]], [aev[:, 0:wdt]])
                        pr.op("scalar", lambda s, psg=psg, wdt=wdt, j=j: s.activation(
                                  gel[:, 0:wdt], psg[:, 0:wdt], AF.Gelu, bias=b1t[:, 2 * j + 1:2 * j + 2]),
                              [psg[:, 0:wdt], b1t[:, 2 * j + 1:2 * j + 2]], [gel[:, 0:wdt]])
                        pr.op("vector", lambda v, j=j, s0=s0, wdt=wdt: v.tensor_mul(
                                  h2_bf[:, j * FCH + s0: j * FCH + s0 + wdt], aev[:, 0:wdt], gel[:, 0:wdt]),
                              [aev[:, 0:wdt], gel[:, 0:wdt]],
                              [h2_bf[:, j * FCH + s0: j * FCH + s0 + wdt]])
                for of in range(8):
                    for s0 in range(0, FCH, 512):
                        wdt = min(512, FCH - s0)
                        ps = em.take_psum()
                        for k in range(NJ):
                            wv = wstr[k % 4]
                            pr.dma("gpsimd", wv[:, 0:128],
                                   w2p[k * 128:(k + 1) * 128, of * 128:(of + 1) * 128])
                            pr.op("tensor",
                                  lambda t, ps=ps, wv=wv, k=k, s0=s0, wdt=wdt:
                                  t.matmul(ps[:, 0:wdt], wv[:, 0:128],
                                           h2_bf[:, k * FCH + s0:k * FCH + s0 + wdt],
                                           start=(k == 0), stop=(k == NJ - 1)),
                                  [wv[:, 0:128], h2_bf[:, k * FCH + s0:k * FCH + s0 + wdt]],
                                  [ps[:, 0:wdt]],
                                  check_writes=[ps[:, 0:wdt]] if k == 0 else [])
                        pr.op("vector", lambda v, ps=ps, wdt=wdt, of=of: v.tensor_scalar(
                                  xq2[:, 0:wdt], ps[:, 0:wdt], b2t[:, of:of + 1], None, ALU.add),
                              [ps[:, 0:wdt], b2t[:, of:of + 1]], [xq2[:, 0:wdt]])
                        pr.op("vector",
                              lambda v, of=of, c0=c0, s0=s0, wdt=wdt: v.tensor_add(
                                  x2m[:, of * TT1 + 1 + c0 + s0: of * TT1 + 1 + c0 + s0 + wdt],
                                  x2m[:, of * TT1 + 1 + c0 + s0: of * TT1 + 1 + c0 + s0 + wdt],
                                  xq2[:, 0:wdt]),
                              [xq2[:, 0:wdt],
                               x2m[:, of * TT1 + 1 + c0 + s0: of * TT1 + 1 + c0 + s0 + wdt]],
                              [x2m[:, of * TT1 + 1 + c0 + s0: of * TT1 + 1 + c0 + s0 + wdt]])
            strow3 = ecc(nc.sbuf_tensor(pfx + "strow3", [1, 3 * 512], F32))
            mub3 = ecc(nc.sbuf_tensor(pfx + "mub3", [128, 512], F32))
            rstdb3 = ecc(nc.sbuf_tensor(pfx + "rstdb3", [128, 512], F32))
            xlnb = ecc(nc.sbuf_tensor(pfx + "xlnb", [128, 8 * 512], F32))
            for of in range(8):
                pr.dma("gpsimd", o_out[of * 128:(of + 1) * 128, :], x2m[:, of * TT1 + 1: of * TT1 + TT1])
            for s0 in range(0, T, 512):
                for of in range(8):
                    pr.dma("scalar", xrl[:, 0:512], xres_t[of * 128:(of + 1) * 128, s0:s0 + 512])
                    pr.op("vector",
                          lambda v, of=of, s0=s0: v.tensor_add(
                              xlnb[:, of * 512:(of + 1) * 512],
                              x2m[:, of * TT1 + 1 + s0: of * TT1 + 1 + s0 + 512], xrl[:, 0:512]),
                          [x2m[:, of * TT1 + 1 + s0: of * TT1 + 1 + s0 + 512], xrl[:, 0:512]],
                          [xlnb[:, of * 512:(of + 1) * 512]])
                emit_ln_feature(pr, em, xlnb, 512, strow3, sqt, mub3, rstdb3, nfeat=8)
                for of in range(8):
                    sl = xlnb[:, of * 512:(of + 1) * 512]
                    pr.op("vector", lambda v, sl=sl: v.tensor_sub(sl, sl, mub3[:, 0:512]),
                          [sl, mub3[:, 0:512]], [sl])
                    pr.op("vector", lambda v, sl=sl: v.tensor_mul(sl, sl, rstdb3[:, 0:512]),
                          [sl, rstdb3[:, 0:512]], [sl])
                    pr.dma("gpsimd", o_ln[of * 128:(of + 1) * 128, s0:s0 + 512], sl)
        for e in ENGINES:
            pr.barrier_to(e)
        print(f"phase2 program: {pr.n_ops} ops, {pr.n_waits} waits")
    return nc


NXK = 32



def build_phase2():
    nc = bass.Bass("TRN2", target_bir_lowering=False, debug=False)
    W = HALO2 + T
    NQ = T // QW + 1
    QTOT = T + QW
    KTOT = T + 2 * QW

    xin_t = nc.dram_tensor("xin_t", [DM, W], F32, kind="ExternalInput").ap()
    xres_t = nc.dram_tensor("xres_t", [DM, T], F32, kind="ExternalInput").ap()
    wqkv = nc.dram_tensor("wqkv", [DM, 1536], F32, kind="ExternalInput").ap()
    wo = nc.dram_tensor("wo", [512, DM], F32, kind="ExternalInput").ap()
    w1p = nc.dram_tensor("w1p", [DM, 2 * HIDP], F32, kind="ExternalInput").ap()
    w2p = nc.dram_tensor("w2p", [HIDP, DM], F32, kind="ExternalInput").ap()
    b1p = nc.dram_tensor("b1p", [128, 2 * (HIDP // 128)], F32, kind="ExternalInput").ap()
    b2c = nc.dram_tensor("b2c", [128, 8], F32, kind="ExternalInput").ap()
    lgc = nc.dram_tensor("lgc", [128, 8], F32, kind="ExternalInput").ap()
    lbc = nc.dram_tensor("lbc", [128, 8], F32, kind="ExternalInput").ap()
    qs8 = nc.dram_tensor("qs8", [128, 4], F32, kind="ExternalInput").ap()
    ksc = nc.dram_tensor("ksc", [128, 4], F32, kind="ExternalInput").ap()
    selheadT = nc.dram_tensor("selheadT", [2, 128], F32, kind="ExternalInput").ap()
    selhead = nc.dram_tensor("selhead", [128, 2], F32, kind="ExternalInput").ap()
    csq = nc.dram_tensor("csq", [128, QTOT], F32, kind="ExternalInput").ap()
    ssq = nc.dram_tensor("ssq", [128, QTOT], F32, kind="ExternalInput").ap()
    csk = nc.dram_tensor("csk", [128, KTOT], F32, kind="ExternalInput").ap()
    ssk = nc.dram_tensor("ssk", [128, KTOT], F32, kind="ExternalInput").ap()
    l0mask = nc.dram_tensor("l0mask", [512, 1024], F32, kind="ExternalInput").ap()
    l0mask_w1 = nc.dram_tensor("l0mask_w1", [512, 1024], F32, kind="ExternalInput").ap()
    idn_in = nc.dram_tensor("idn128", [128, 128], F32, kind="ExternalInput").ap()

    o_out = nc.dram_tensor("xout_t", [DM, T], F32, kind="ExternalOutput").ap()
    o_ln = nc.dram_tensor("xln_t", [DM, T], F32, kind="ExternalOutput").ap()
    sp_x2m = nc.dram_tensor("sp_x2m", [DM, T + 1], F32).ap()

    with ExitStack() as _es0:
        _ec0 = _es0.enter_context
        blk = _ec0(nc.Block())
        pr = Prog(nc, _es0)
        em = Emitter(nc, _es0, pr, n_psum=7)
        psbf = _ec0(nc.psum_tensor("psbf", [128, 1024], BF16))
        emit_phase2(nc, pr, em, psbf, xin_t, xres_t, wqkv, wo, w1p, w2p, b1p, b2c, lgc, lbc, qs8, ksc, selheadT, selhead, csq, ssq, csk, ssk, l0mask, l0mask_w1, idn_in, o_out, o_ln, sp_x2m)
    return nc


def _nu2(d, cap=11):
    d = abs(int(d))
    if d == 0:
        return cap
    v = 0
    while d % 2 == 0 and v < cap:
        v += 1
        d //= 2
    return v


def _count_tables_A():
    mul = np.zeros((16, 256, 256), np.float32)
    add = np.full((16, 256, 256), -1e38, np.float32)
    idx = np.arange(256)
    for h in range(16):
        m = np.array([_nu2(i - h, 7) for i in range(256)])
        x = idx[:, None] ^ idx[None, :]
        bl = np.zeros_like(x)
        nz = x > 0
        bl[nz] = (np.floor(np.log2(x[nz]))).astype(np.int64) + 1
        sseg = np.maximum(0, bl - 2)
        cnt = np.maximum(0, np.minimum(m[:, None], m[None, :]) + 1 - sseg)
        valid = (cnt > 0) & (idx[None, :] <= idx[:, None])
        mul[h][valid] = cnt[valid]
        add[h][valid] = 0.0
    return mul, add


def _count_tables_B(t0, half):
    K = T + NXK
    mul = np.zeros((16, 16, K), np.float32)
    add = np.full((16, 16, K), -1e38, np.float32)
    for h in range(16):
        iv = t0 + h + 128 * np.arange(16)
        jl = t0 + np.arange(T)
        jx = np.concatenate([np.arange(16), 1024 + np.arange(16)])
        jv = np.concatenate([jl, jx])
        mi = np.array([_nu2(i - h) for i in iv])
        mj = np.array([_nu2(j - h) for j in jv])
        x = iv[:, None] ^ jv[None, :]
        bl = np.zeros_like(x)
        nz = x > 0
        bl[nz] = (np.floor(np.log2(x[nz]))).astype(np.int64) + 1
        sseg = np.maximum(0, bl - 2)
        cnt = np.maximum(0, np.minimum(mi[:, None], mj[None, :]) + 1 - sseg)
        valid = (cnt > 0) & (jv[None, :] <= iv[:, None])
        if half == 0:
            valid[:, T:] = False
        else:
            keep = np.zeros(NXK, bool)
            keep[h] = True
            keep[16 + h] = True
            valid[:, T:] &= keep[None, :]
        mul[h][valid] = cnt[valid]
        add[h][valid] = 0.0
    return mul, add


def prep_phase3(inputs):
    wq_t = np.ascontiguousarray(_np_f32(inputs["L1_attn_Wq"]).T)
    wk_t = np.ascontiguousarray(_np_f32(inputs["L1_attn_Wk"]).T)
    wv_t = np.ascontiguousarray(_np_f32(inputs["L1_attn_Wv"]).T)
    wo_t = np.ascontiguousarray(_np_f32(inputs["L1_attn_Wo"]).T)

    def cols8(v):
        return np.ascontiguousarray(_np_f32(v).reshape(8, 128).T)

    tamul, taadd = _count_tables_A()
    shared = {
        "wq_t": wq_t, "wk_t": wk_t, "wv_t": wv_t, "wo_t": wo_t,
        "bqc": cols8(inputs["L1_attn_bq"]), "bkc": cols8(inputs["L1_attn_bk"]),
        "bvc": cols8(inputs["L1_attn_bv"]), "boc": cols8(inputs["L1_attn_bo"]),
        "tamul": tamul.reshape(16 * 256, 256), "taadd": taadd.reshape(16 * 256, 256),
    }

    d = 1024
    inv_freq = 1.0 / 15000.0 ** (np.arange(0, d, 2, dtype=np.float32) / d)

    def rope(tpos):
        freqs = np.asarray(tpos, np.float32)[:, None] * inv_freq[None, :]
        emb = np.concatenate([freqs, freqs], axis=-1)
        cos, sin = np.cos(emb), np.sin(emb)
        sgn = np.r_[-np.ones(512, np.float32), np.ones(512, np.float32)]
        return (np.ascontiguousarray(cos.T.astype(np.float32)),
                np.ascontiguousarray((sin.T * sgn[:, None]).astype(np.float32)))

    per_core = []
    for c in range(NCORES):
        b, h = c // 2, c % 2
        t0 = h * T
        cs, ssg = rope(np.arange(t0, t0 + T))
        csx, ssgx = rope(np.concatenate([np.arange(16), 1024 + np.arange(16)]))
        tbmul, tbadd = _count_tables_B(t0, h)
        per_core.append({
            "cs_t": cs, "ssg_t": ssg, "csx_t": csx, "ssgx_t": ssgx,
            "tbmul": tbmul.reshape(16 * 16, T + NXK),
            "tbadd": tbadd.reshape(16 * 16, T + NXK),
        })
    return shared, per_core


def emit_phase3(nc, pr, em, psbf, xin_t, x2x_t, wq_t, wk_t, wv_t, wo_t, bqc, bkc, bvc, boc, cs_t, ssg_t, csx_t, ssgx_t, tamul, taadd, tbmul, tbadd, idn_in, o_x3, pfx=""):
    KB = T + NXK

    with ExitStack() as es:
        ec = es.enter_context
        idn_bf = ec(nc.sbuf_tensor(pfx + "p3idn", [128, 128], BF16))
        pr.dma("gpsimd", idn_bf[:, :], idn_in[:, :])

        qr_bf = ec(nc.sbuf_tensor(pfx + "qr_bf", [128, 8 * T], BF16))
        kr_bf = ec(nc.sbuf_tensor(pfx + "kr_bf", [128, 8 * T], BF16))
        v_tr = ec(nc.sbuf_tensor(pfx + "v_tr", [128, (T // 128) * 1024], BF16))
        o_bf = ec(nc.sbuf_tensor(pfx + "o_bf", [128, 8 * T], BF16))
        kx_bf = ec(nc.sbuf_tensor(pfx + "kx_bf", [128, 8 * NXK], BF16))
        vx_tr = ec(nc.sbuf_tensor(pfx + "vx_tr", [32, 16 * 64], BF16))

        with ExitStack() as esa:
            eca = esa.enter_context
            xin_bf = eca(nc.sbuf_tensor(pfx + "p3xin_bf", [128, 8 * T], BF16))
            xld = eca(nc.sbuf_tensor(pfx + "p3xld", [128, 512], F32))
            w_bf = [eca(nc.sbuf_tensor(pfx + f"p3w{i}", [128, 8 * 128], BF16)) for i in range(3)]
            qT = eca(nc.sbuf_tensor(pfx + "p3qT", [128, 2 * T], F32))
            bcol = eca(nc.sbuf_tensor(pfx + "p3bcol", [128, 8 * 3], F32))
            cstl = eca(nc.sbuf_tensor(pfx + "p3cs", [128, 2 * 512], F32))
            x2xT = eca(nc.sbuf_tensor(pfx + "p3x2x", [128, 8 * NXK], F32))
            x2x_bf = eca(nc.sbuf_tensor(pfx + "p3x2xb", [128, 8 * NXK], BF16))
            kxf = eca(nc.sbuf_tensor(pfx + "p3kxf", [128, 8 * NXK], F32))
            csx_s = eca(nc.sbuf_tensor(pfx + "p3csx", [128, 8 * NXK], F32))
            ssx_s = eca(nc.sbuf_tensor(pfx + "p3ssx", [128, 8 * NXK], F32))
            vev3 = eca(nc.sbuf_tensor(pfx + "vev3", [128, T], BF16))

            pr.dma("sync", bcol[:, 0:8], bqc[:, :])
            pr.dma("sync", bcol[:, 8:16], bkc[:, :])
            pr.dma("sync", bcol[:, 16:24], bvc[:, :])
            for k in range(8):
                for s0 in range(0, T, 512):
                    pr.dma("sync", xld[:, :], xin_t[k * 128:(k + 1) * 128, s0:s0 + 512])
                    pr.op("vector",
                          lambda v, k=k, s0=s0: v.tensor_copy(
                              xin_bf[:, k * T + s0:k * T + s0 + 512], xld[:, :]),
                          [xld[:, :]], [xin_bf[:, k * T + s0:k * T + s0 + 512]])
                pr.dma("sync", x2xT[:, k * NXK:(k + 1) * NXK], x2x_t[k * 128:(k + 1) * 128, :])
                pr.op("vector",
                      lambda v, k=k: v.tensor_copy(x2x_bf[:, k * NXK:(k + 1) * NXK],
                                                   x2xT[:, k * NXK:(k + 1) * NXK]),
                      [x2xT[:, k * NXK:(k + 1) * NXK]], [x2x_bf[:, k * NXK:(k + 1) * NXK]])
                pr.dma("sync", csx_s[:, k * NXK:(k + 1) * NXK], csx_t[k * 128:(k + 1) * 128, :])
                pr.dma("sync", ssx_s[:, k * NXK:(k + 1) * NXK], ssgx_t[k * 128:(k + 1) * 128, :])

            for kind, wsrc, bof, out_bf in ((0, wq_t, 0, qr_bf), (1, wk_t, 8, kr_bf)):
                for ofp in range(4):
                    for half, of in ((0, ofp), (1, ofp + 4)):
                        wt = w_bf[of % 3]
                        for k in range(8):
                            pr.dma("gpsimd", wt[:, k * 128:(k + 1) * 128],
                                   wsrc[k * 128:(k + 1) * 128, of * 128:(of + 1) * 128])
                        for s0 in range(0, T, 512):
                            ps = em.take_psum()
                            em.mm_group(
                                [(wt[:, k * 128:(k + 1) * 128], xin_bf[:, k * T + s0:k * T + s0 + 512])
                                 for k in range(8)],
                                ps[:, 0:512])
                            pr.op("vector",
                                  lambda v, half=half, s0=s0, ps=ps, bof=bof, of=of: v.tensor_scalar(
                                      qT[:, half * T + s0: half * T + s0 + 512], ps[:, 0:512],
                                      bcol[:, bof + of:bof + of + 1], None, ALU.add),
                                  [ps[:, 0:512], bcol[:, bof + of:bof + of + 1]],
                                  [qT[:, half * T + s0: half * T + s0 + 512]])
                        psx = em.take_psum()
                        em.mm_group(
                            [(wt[:, k * 128:(k + 1) * 128], x2x_bf[:, k * NXK:(k + 1) * NXK])
                             for k in range(8)],
                            psx[:, 0:NXK])
                        pr.op("vector",
                              lambda v, of=of, psx=psx, bof=bof: v.tensor_scalar(
                                  kxf[:, of * NXK:(of + 1) * NXK], psx[:, 0:NXK],
                                  bcol[:, bof + of:bof + of + 1], None, ALU.add),
                              [psx[:, 0:NXK], bcol[:, bof + of:bof + of + 1]],
                              [kxf[:, of * NXK:(of + 1) * NXK]])
                    for half, of in ((0, ofp), (1, ofp + 4)):
                        for s0 in range(0, T, 512):
                            pr.dma("scalar", cstl[:, 0:512], cs_t[of * 128:(of + 1) * 128, s0:s0 + 512])
                            pr.dma("scalar", cstl[:, 512:1024], ssg_t[of * 128:(of + 1) * 128, s0:s0 + 512])
                            pr.op("vector",
                                  lambda v, half=half, s0=s0: v.tensor_mul(
                                      cstl[:, 0:512],
                                      qT[:, half * T + s0: half * T + s0 + 512], cstl[:, 0:512]),
                                  [qT[:, half * T + s0: half * T + s0 + 512], cstl[:, 0:512]],
                                  [cstl[:, 0:512]])
                            pr.op("vector",
                                  lambda v, half=half, s0=s0: v.tensor_mul(
                                      cstl[:, 512:1024],
                                      qT[:, (1 - half) * T + s0: (1 - half) * T + s0 + 512], cstl[:, 512:1024]),
                                  [qT[:, (1 - half) * T + s0: (1 - half) * T + s0 + 512], cstl[:, 512:1024]],
                                  [cstl[:, 512:1024]])
                            pr.op("vector",
                                  lambda v, of=of, s0=s0, out_bf=out_bf: v.tensor_add(
                                      out_bf[:, of * T + s0: of * T + s0 + 512],
                                      cstl[:, 0:512], cstl[:, 512:1024]),
                                  [cstl[:, 0:512], cstl[:, 512:1024]],
                                  [out_bf[:, of * T + s0: of * T + s0 + 512]])
                if kind == 1:
                    for of in range(8):
                        pr.op("vector",
                              lambda v, of=of: v.tensor_mul(
                                  csx_s[:, of * NXK:(of + 1) * NXK],
                                  kxf[:, of * NXK:(of + 1) * NXK], csx_s[:, of * NXK:(of + 1) * NXK]),
                              [kxf[:, of * NXK:(of + 1) * NXK], csx_s[:, of * NXK:(of + 1) * NXK]],
                              [csx_s[:, of * NXK:(of + 1) * NXK]])
                        pr.op("vector",
                              lambda v, of=of: v.tensor_mul(
                                  ssx_s[:, of * NXK:(of + 1) * NXK],
                                  kxf[:, (of ^ 4) * NXK:((of ^ 4) + 1) * NXK], ssx_s[:, of * NXK:(of + 1) * NXK]),
                              [kxf[:, (of ^ 4) * NXK:((of ^ 4) + 1) * NXK], ssx_s[:, of * NXK:(of + 1) * NXK]],
                              [ssx_s[:, of * NXK:(of + 1) * NXK]])
                        pr.op("vector",
                              lambda v, of=of: v.tensor_add(
                                  kx_bf[:, of * NXK:(of + 1) * NXK],
                                  csx_s[:, of * NXK:(of + 1) * NXK], ssx_s[:, of * NXK:(of + 1) * NXK]),
                              [csx_s[:, of * NXK:(of + 1) * NXK], ssx_s[:, of * NXK:(of + 1) * NXK]],
                              [kx_bf[:, of * NXK:(of + 1) * NXK]])
            for of in range(8):
                wt = w_bf[of % 3]
                for k in range(8):
                    pr.dma("gpsimd", wt[:, k * 128:(k + 1) * 128],
                           wv_t[k * 128:(k + 1) * 128, of * 128:(of + 1) * 128])
                for s0 in range(0, T, 512):
                    ps = em.take_psum()
                    em.mm_group(
                        [(wt[:, k * 128:(k + 1) * 128], xin_bf[:, k * T + s0:k * T + s0 + 512])
                         for k in range(8)],
                        ps[:, 0:512])
                    pr.op("vector",
                          lambda v, of=of, s0=s0, ps=ps: v.tensor_scalar(
                              vev3[:, s0:s0 + 512], ps[:, 0:512],
                              bcol[:, 16 + of:16 + of + 1], None, ALU.add),
                          [ps[:, 0:512], bcol[:, 16 + of:16 + of + 1]],
                          [vev3[:, s0:s0 + 512]])
                for tb in range(T // 128):
                    for hh in range(2):
                        pst = psbf[:, (tb % 8) * 128:(tb % 8) * 128 + 64]
                        pr.op("tensor",
                              lambda t, tb=tb, hh=hh, pst=pst: t.transpose(
                                  pst, vev3[hh * 64:hh * 64 + 64, tb * 128:(tb + 1) * 128],
                                  idn_bf[hh * 64:hh * 64 + 64, hh * 64:hh * 64 + 64]),
                              [vev3[hh * 64:hh * 64 + 64, tb * 128:(tb + 1) * 128],
                               idn_bf[hh * 64:hh * 64 + 64, hh * 64:hh * 64 + 64]],
                              [pst])
                        h = of * 2 + hh
                        pr.op("vector",
                              lambda v, tb=tb, h=h, pst=pst: v.tensor_copy(
                                  v_tr[:, tb * 1024 + h * 64: tb * 1024 + h * 64 + 64], pst),
                              [pst], [v_tr[:, tb * 1024 + h * 64: tb * 1024 + h * 64 + 64]])
                psx = em.take_psum()
                em.mm_group(
                    [(wt[:, k * 128:(k + 1) * 128], x2x_bf[:, k * NXK:(k + 1) * NXK])
                     for k in range(8)],
                    psx[:, 0:NXK])
                pr.op("vector",
                      lambda v, of=of, psx=psx: v.tensor_scalar(
                          kxf[:, of * NXK:(of + 1) * NXK], psx[:, 0:NXK],
                          bcol[:, 16 + of:16 + of + 1], None, ALU.add),
                      [psx[:, 0:NXK], bcol[:, 16 + of:16 + of + 1]],
                      [kxf[:, of * NXK:(of + 1) * NXK]])
                pr.op("vector",
                      lambda v, of=of: v.tensor_copy(vev3[:, 0:NXK], kxf[:, of * NXK:(of + 1) * NXK]),
                      [kxf[:, of * NXK:(of + 1) * NXK]], [vev3[:, 0:NXK]])
                for hh in range(2):
                    pst = psbf[0:32, 0:64]
                    pr.op("tensor",
                          lambda t, hh=hh, pst=pst: t.transpose(
                              pst, vev3[hh * 64:hh * 64 + 64, 0:NXK],
                              idn_bf[hh * 64:hh * 64 + 64, hh * 64:hh * 64 + 64]),
                          [vev3[hh * 64:hh * 64 + 64, 0:NXK],
                           idn_bf[hh * 64:hh * 64 + 64, hh * 64:hh * 64 + 64]],
                          [pst])
                    h = of * 2 + hh
                    pr.op("vector",
                          lambda v, h=h, pst=pst: v.tensor_copy(
                              vx_tr[:, h * 64:(h + 1) * 64], pst),
                          [pst], [vx_tr[:, h * 64:(h + 1) * 64]])
            for e in ENGINES:
                pr.barrier_to(e)

        with ExitStack() as esb:
            ecb = esb.enter_context
            ta_m = ecb(nc.sbuf_tensor(pfx + "ta_m", [128, 2 * 256], F32))
            ta_a = ecb(nc.sbuf_tensor(pfx + "ta_a", [128, 2 * 256], F32))
            sc3 = ecb(nc.sbuf_tensor(pfx + "sc3", [128, 256], F32))
            att3 = ecb(nc.sbuf_tensor(pfx + "att3", [128, 256], BF16))
            attT3 = ecb(nc.sbuf_tensor(pfx + "attT3", [128, 2 * 128], BF16))
            mr3 = ecb(nc.sbuf_tensor(pfx + "mr3", [128, 2], F32))
            rs3 = ecb(nc.sbuf_tensor(pfx + "rs3", [128, 2], F32))
            rc3 = ecb(nc.sbuf_tensor(pfx + "rc3", [128, 2], F32))
            for h in range(16):
                for pt in range(2):
                    pr.dma("scalar", ta_m[:, pt * 256:(pt + 1) * 256],
                           tamul[h * 256 + pt * 128: h * 256 + (pt + 1) * 128, :])
                    pr.dma("scalar", ta_a[:, pt * 256:(pt + 1) * 256],
                           taadd[h * 256 + pt * 128: h * 256 + (pt + 1) * 128, :])
                hf, hh = h // 2, h % 2
                for seg in range(T // 256):
                    for qt in range(2):
                        q0 = seg * 256 + qt * 128
                        ps = em.take_psum()
                        pr.op("tensor",
                              lambda t, ps=ps, hf=hf, hh=hh, q0=q0, seg=seg: t.matmul(
                                  ps[:, 0:256],
                                  qr_bf[hh * 64:hh * 64 + 64, hf * T + q0: hf * T + q0 + 128],
                                  kr_bf[hh * 64:hh * 64 + 64, hf * T + seg * 256: hf * T + seg * 256 + 256],
                                  start=True, stop=True),
                              [qr_bf[hh * 64:hh * 64 + 64, hf * T + q0: hf * T + q0 + 128],
                               kr_bf[hh * 64:hh * 64 + 64, hf * T + seg * 256: hf * T + seg * 256 + 256]],
                              [ps[:, 0:256]])
                        pr.op("vector",
                              lambda v, ps=ps, qt=qt: v.tensor_mul(
                                  sc3[:, 0:256], ps[:, 0:256], ta_m[:, qt * 256:(qt + 1) * 256]),
                              [ps[:, 0:256], ta_m[:, qt * 256:(qt + 1) * 256]], [sc3[:, 0:256]])
                        pr.op("vector",
                              lambda v, qt=qt: v.tensor_add(
                                  sc3[:, 0:256], sc3[:, 0:256], ta_a[:, qt * 256:(qt + 1) * 256]),
                              [sc3[:, 0:256], ta_a[:, qt * 256:(qt + 1) * 256]], [sc3[:, 0:256]])
                        pr.op("vector", lambda v: v.reduce_max(mr3[:, 0:1], sc3[:, 0:256], axis=AX.X, negate=True),
                              [sc3[:, 0:256]], [mr3[:, 0:1]])
                        pr.op("scalar", lambda s: s.activation(sc3[:, 0:256], sc3[:, 0:256], AF.Exp,
                                                               bias=mr3[:, 0:1], accum_out=rs3[:, 0:1]),
                              [sc3[:, 0:256], mr3[:, 0:1]], [sc3[:, 0:256], rs3[:, 0:1]])
                        pr.op("vector", lambda v: v.reciprocal(rc3[:, 0:1], rs3[:, 0:1]),
                              [rs3[:, 0:1]], [rc3[:, 0:1]])
                        pr.op("vector", lambda v: v.tensor_scalar(att3[:, 0:256], sc3[:, 0:256],
                                                                  rc3[:, 0:1], None, ALU.mult),
                              [sc3[:, 0:256], rc3[:, 0:1]], [att3[:, 0:256]])
                        for kt in range(2):
                            pst = psbf[:, kt * 128:(kt + 1) * 128]
                            pr.op("tensor",
                                  lambda t, kt=kt, pst=pst: t.transpose(
                                      pst, att3[:, kt * 128:(kt + 1) * 128], idn_bf[:, :]),
                                  [att3[:, kt * 128:(kt + 1) * 128], idn_bf[:, :]], [pst])
                            pr.op("vector",
                                  lambda v, kt=kt, pst=pst: v.tensor_copy(
                                      attT3[:, kt * 128:(kt + 1) * 128], pst),
                                  [pst], [attT3[:, kt * 128:(kt + 1) * 128]])
                        psA = em.take_psum()
                        em.mm_group(
                            [(v_tr[:, (seg * 2 + kt) * 1024 + h * 64: (seg * 2 + kt) * 1024 + h * 64 + 64],
                              attT3[:, kt * 128:(kt + 1) * 128]) for kt in range(2)],
                            psA[0:64, 0:128])
                        pr.op("vector", lambda v, hf=hf, hh=hh, q0=q0, psA=psA: v.tensor_copy(
                                  o_bf[hh * 64:hh * 64 + 64, hf * T + q0: hf * T + q0 + 128],
                                  psA[0:64, 0:128]),
                              [psA[0:64, 0:128]],
                              [o_bf[hh * 64:hh * 64 + 64, hf * T + q0: hf * T + q0 + 128]])
            for e in ENGINES:
                pr.barrier_to(e)

        with ExitStack() as esc:
            ecc = esc.enter_context
            qbx = ecc(nc.sbuf_tensor(pfx + "qbx", [128, 16], BF16))
            tb_m = ecc(nc.sbuf_tensor(pfx + "tb_m", [16, KB], F32))
            tb_a = ecc(nc.sbuf_tensor(pfx + "tb_a", [16, KB], F32))
            scb = ecc(nc.sbuf_tensor(pfx + "scb", [16, KB], F32))
            attb = ecc(nc.sbuf_tensor(pfx + "attb", [16, KB], BF16))
            attbT = ecc(nc.sbuf_tensor(pfx + "attbT", [128, 17 * 16], BF16))
            mrb = ecc(nc.sbuf_tensor(pfx + "mrb", [16, 2], F32))
            rsb = ecc(nc.sbuf_tensor(pfx + "rsb", [16, 2], F32))
            rcb2 = ecc(nc.sbuf_tensor(pfx + "rcb2", [16, 2], F32))
            for h in range(16):
                hf, hh = h // 2, h % 2
                pr.dma("scalar", tb_m[:, :], tbmul[h * 16:(h + 1) * 16, :])
                pr.dma("scalar", tb_a[:, :], tbadd[h * 16:(h + 1) * 16, :])
                pr.dma("gpsimd", qbx[hh * 64:hh * 64 + 64, 0:16],
                       qr_bf[hh * 64:hh * 64 + 64, hf * T + h::128][:, 0:16],
                       allow_slow_non_contiguous=True)
                for s0 in range(0, T, 512):
                    ps = em.take_psum()
                    pr.op("tensor",
                          lambda t, ps=ps, hf=hf, hh=hh, s0=s0: t.matmul(
                              ps[0:16, 0:512], qbx[hh * 64:hh * 64 + 64, 0:16],
                              kr_bf[hh * 64:hh * 64 + 64, hf * T + s0: hf * T + s0 + 512],
                              start=True, stop=True),
                          [qbx[hh * 64:hh * 64 + 64, 0:16],
                           kr_bf[hh * 64:hh * 64 + 64, hf * T + s0: hf * T + s0 + 512]],
                          [ps[0:16, 0:512]])
                    pr.op("vector",
                          lambda v, ps=ps, s0=s0: v.tensor_mul(
                              scb[:, s0:s0 + 512], ps[0:16, 0:512], tb_m[:, s0:s0 + 512]),
                          [ps[0:16, 0:512], tb_m[:, s0:s0 + 512]], [scb[:, s0:s0 + 512]])
                    pr.op("vector",
                          lambda v, s0=s0: v.tensor_add(
                              scb[:, s0:s0 + 512], scb[:, s0:s0 + 512], tb_a[:, s0:s0 + 512]),
                          [scb[:, s0:s0 + 512], tb_a[:, s0:s0 + 512]], [scb[:, s0:s0 + 512]])
                psx = em.take_psum()
                pr.op("tensor",
                      lambda t, psx=psx, hf=hf, hh=hh: t.matmul(
                          psx[0:16, 0:NXK], qbx[hh * 64:hh * 64 + 64, 0:16],
                          kx_bf[hh * 64:hh * 64 + 64, hf * NXK: hf * NXK + NXK],
                          start=True, stop=True),
                      [qbx[hh * 64:hh * 64 + 64, 0:16], kx_bf[hh * 64:hh * 64 + 64, hf * NXK: hf * NXK + NXK]],
                      [psx[0:16, 0:NXK]])
                pr.op("vector",
                      lambda v, psx=psx: v.tensor_mul(
                          scb[:, T:T + NXK], psx[0:16, 0:NXK], tb_m[:, T:T + NXK]),
                      [psx[0:16, 0:NXK], tb_m[:, T:T + NXK]], [scb[:, T:T + NXK]])
                pr.op("vector",
                      lambda v: v.tensor_add(scb[:, T:T + NXK], scb[:, T:T + NXK], tb_a[:, T:T + NXK]),
                      [scb[:, T:T + NXK], tb_a[:, T:T + NXK]], [scb[:, T:T + NXK]])
                pr.op("vector", lambda v: v.reduce_max(mrb[:, 0:1], scb[:, 0:KB], axis=AX.X, negate=True),
                      [scb[:, 0:KB]], [mrb[:, 0:1]])
                pr.op("scalar", lambda s: s.activation(scb[:, 0:KB], scb[:, 0:KB], AF.Exp,
                                                       bias=mrb[:, 0:1], accum_out=rsb[:, 0:1]),
                      [scb[:, 0:KB], mrb[:, 0:1]], [scb[:, 0:KB], rsb[:, 0:1]])
                pr.op("vector", lambda v: v.reciprocal(rcb2[:, 0:1], rsb[:, 0:1]),
                      [rsb[:, 0:1]], [rcb2[:, 0:1]])
                pr.op("vector", lambda v: v.tensor_scalar(attb[:, 0:KB], scb[:, 0:KB],
                                                          rcb2[:, 0:1], None, ALU.mult),
                      [scb[:, 0:KB], rcb2[:, 0:1]], [attb[:, 0:KB]])
                for kt in range(17):
                    wk_ = 128 if kt < 16 else NXK
                    pst = psbf[:, (kt % 8) * 128:(kt % 8) * 128 + 16]
                    pr.op("tensor",
                          lambda t, kt=kt, wk_=wk_, pst=pst: t.transpose(
                              pst[0:wk_, 0:16], attb[0:16, kt * 128:kt * 128 + wk_],
                              idn_bf[0:16, 0:16]),
                          [attb[0:16, kt * 128:kt * 128 + wk_], idn_bf[0:16, 0:16]],
                          [pst[0:wk_, 0:16]])
                    pr.op("vector",
                          lambda v, kt=kt, wk_=wk_, pst=pst: v.tensor_copy(
                              attbT[0:wk_, kt * 16:(kt + 1) * 16], pst[0:wk_, 0:16]),
                          [pst[0:wk_, 0:16]], [attbT[0:wk_, kt * 16:(kt + 1) * 16]])
                psB = em.take_psum()
                pairs = [(v_tr[:, tb * 1024 + h * 64: tb * 1024 + h * 64 + 64],
                          attbT[:, tb * 16:(tb + 1) * 16]) for tb in range(16)]
                pairs.append((vx_tr[0:32, h * 64:(h + 1) * 64], attbT[0:32, 16 * 16:17 * 16]))
                em.mm_group(pairs, psB[0:64, 0:16])
                pr.op("vector",
                      lambda v, hf=hf, hh=hh, h=h, psB=psB: v.tensor_copy(
                          o_bf[hh * 64:hh * 64 + 64, hf * T + h::128][:, 0:16], psB[0:64, 0:16]),
                      [psB[0:64, 0:16]],
                      [o_bf[hh * 64:hh * 64 + 64, hf * T + h: hf * T + h + 128 * 15 + 1]])
            for e in ENGINES:
                pr.barrier_to(e)

        with ExitStack() as esd:
            ecd = esd.enter_context
            wot_bf = ecd(nc.sbuf_tensor(pfx + "wot_bf", [128, 8 * 128], BF16))
            bocl = ecd(nc.sbuf_tensor(pfx + "bocl", [128, 8], F32))
            x3p = ecd(nc.sbuf_tensor(pfx + "x3p", [128, 8 * 512], F32))
            xrl3 = ecd(nc.sbuf_tensor(pfx + "xrl3", [128, 512], F32))
            sq3 = ecd(nc.sbuf_tensor(pfx + "sq3", [128, 2 * 512], F32))
            st3 = ecd(nc.sbuf_tensor(pfx + "st3", [1, 4 * 512], F32))
            mu3 = ecd(nc.sbuf_tensor(pfx + "mu3", [128, 512], F32))
            rst3 = ecd(nc.sbuf_tensor(pfx + "rst3", [128, 512], F32))
            pr.dma("sync", bocl[:, :], boc[:, :])
            for s0 in range(0, T, 512):
                for of in range(8):
                    for k in range(8):
                        pr.dma("gpsimd", wot_bf[:, k * 128:(k + 1) * 128],
                               wo_t[k * 128:(k + 1) * 128, of * 128:(of + 1) * 128])
                    ps = em.take_psum()
                    em.mm_group(
                        [(wot_bf[:, k * 128:(k + 1) * 128], o_bf[:, k * T + s0:k * T + s0 + 512])
                         for k in range(8)],
                        ps[:, 0:512])
                    pr.dma("scalar", xrl3[:, :], xin_t[of * 128:(of + 1) * 128, s0:s0 + 512])
                    pr.op("vector",
                          lambda v, of=of, ps=ps: v.tensor_scalar(
                              x3p[:, of * 512:(of + 1) * 512], ps[:, 0:512],
                              bocl[:, of:of + 1], None, ALU.add),
                          [ps[:, 0:512], bocl[:, of:of + 1]], [x3p[:, of * 512:(of + 1) * 512]])
                    pr.op("vector",
                          lambda v, of=of: v.tensor_add(
                              x3p[:, of * 512:(of + 1) * 512], x3p[:, of * 512:(of + 1) * 512],
                              xrl3[:, :]),
                          [x3p[:, of * 512:(of + 1) * 512], xrl3[:, :]],
                          [x3p[:, of * 512:(of + 1) * 512]])
                emit_ln_feature(pr, em, x3p, 512, st3, sq3, mu3, rst3, nfeat=8)
                for of in range(8):
                    sl = x3p[:, of * 512:(of + 1) * 512]
                    pr.op("vector", lambda v, sl=sl: v.tensor_sub(sl, sl, mu3[:, 0:512]),
                          [sl, mu3[:, 0:512]], [sl])
                    pr.op("vector", lambda v, sl=sl: v.tensor_mul(sl, sl, rst3[:, 0:512]),
                          [sl, rst3[:, 0:512]], [sl])
                    pr.dma("gpsimd", o_x3[of * 128:(of + 1) * 128, s0:s0 + 512], sl)
        for e in ENGINES:
            pr.barrier_to(e)
        print(f"phase3 program: {pr.n_ops} ops, {pr.n_waits} waits")
    return nc


NEXP = 10



def build_phase3():
    nc = bass.Bass("TRN2", target_bir_lowering=False, debug=False)
    KB = T + NXK

    xin_t = nc.dram_tensor("xin_t", [DM, T], F32, kind="ExternalInput").ap()
    x2x_t = nc.dram_tensor("x2x_t", [DM, NXK], F32, kind="ExternalInput").ap()
    wq_t = nc.dram_tensor("wq_t", [DM, DM], F32, kind="ExternalInput").ap()
    wk_t = nc.dram_tensor("wk_t", [DM, DM], F32, kind="ExternalInput").ap()
    wv_t = nc.dram_tensor("wv_t", [DM, DM], F32, kind="ExternalInput").ap()
    wo_t = nc.dram_tensor("wo_t", [DM, DM], F32, kind="ExternalInput").ap()
    bqc = nc.dram_tensor("bqc", [128, 8], F32, kind="ExternalInput").ap()
    bkc = nc.dram_tensor("bkc", [128, 8], F32, kind="ExternalInput").ap()
    bvc = nc.dram_tensor("bvc", [128, 8], F32, kind="ExternalInput").ap()
    boc = nc.dram_tensor("boc", [128, 8], F32, kind="ExternalInput").ap()
    cs_t = nc.dram_tensor("cs_t", [DM, T], F32, kind="ExternalInput").ap()
    ssg_t = nc.dram_tensor("ssg_t", [DM, T], F32, kind="ExternalInput").ap()
    csx_t = nc.dram_tensor("csx_t", [DM, NXK], F32, kind="ExternalInput").ap()
    ssgx_t = nc.dram_tensor("ssgx_t", [DM, NXK], F32, kind="ExternalInput").ap()
    tamul = nc.dram_tensor("tamul", [16 * 256, 256], F32, kind="ExternalInput").ap()
    taadd = nc.dram_tensor("taadd", [16 * 256, 256], F32, kind="ExternalInput").ap()
    tbmul = nc.dram_tensor("tbmul", [16 * 16, KB], F32, kind="ExternalInput").ap()
    tbadd = nc.dram_tensor("tbadd", [16 * 16, KB], F32, kind="ExternalInput").ap()
    idn_in = nc.dram_tensor("idn128", [128, 128], F32, kind="ExternalInput").ap()

    o_x3 = nc.dram_tensor("x3_t", [DM, T], F32, kind="ExternalOutput").ap()

    with ExitStack() as _es0:
        _ec0 = _es0.enter_context
        blk = _ec0(nc.Block())
        pr = Prog(nc, _es0)
        em = Emitter(nc, _es0, pr, n_psum=7)
        psbf = _ec0(nc.psum_tensor("psbf", [128, 1024], BF16))
        emit_phase3(nc, pr, em, psbf, xin_t, x2x_t, wq_t, wk_t, wv_t, wo_t, bqc, bkc, bvc, boc, cs_t, ssg_t, csx_t, ssgx_t, tamul, taadd, tbmul, tbadd, idn_in, o_x3)
    return nc


def prep_phase4(inputs):
    w1 = _np_f32(inputs["L1_ffn_w1"])
    w3 = _np_f32(inputs["L1_ffn_w3"])
    w2 = _np_f32(inputs["L1_ffn_w2"])
    sw1 = _np_f32(inputs["L1_ffn_sw1"])
    sw3 = _np_f32(inputs["L1_ffn_sw3"])
    sw2 = _np_f32(inputs["L1_ffn_sw2"])
    m1 = np.concatenate([w1, sw1], axis=0).transpose(0, 2, 1)
    m3 = np.concatenate([w3, sw3], axis=0).transpose(0, 2, 1)
    m2 = np.concatenate([w2, sw2], axis=0).transpose(0, 2, 1)
    sel8 = np.zeros((8, 8 * 128), np.float32)
    for e in range(8):
        sel8[e, e * 128:(e + 1) * 128] = 1.0
    shared = {
        "m1_t": np.ascontiguousarray(m1.reshape(NEXP * 1024, 2048)),
        "m3_t": np.ascontiguousarray(m3.reshape(NEXP * 1024, 2048)),
        "m2_t": np.ascontiguousarray(m2.reshape(NEXP * 2048, 1024)),
        "rw_t": np.ascontiguousarray(_np_f32(inputs["L1_ffn_router_w"]).T),
        "ebc": np.ascontiguousarray(np.pad(_np_f32(inputs["L1_ffn_expert_bias"]),
                                           (0, 120)).reshape(128, 1)),
        "sel8": sel8,
    }
    return shared


def emit_phase4(nc, pr, em, psbf, xin_t, m1_t, m3_t, m2_t, rw_t, ebc, sel8, idn_in, o_out, pfx=""):
    with ExitStack() as es:
        ec = es.enter_context

        xin_bf = ec(nc.sbuf_tensor(pfx + "p4xin", [128, 8 * T], BF16))
        acc = ec(nc.sbuf_tensor(pfx + "p4acc", [128, 8 * T], F32))
        xld = ec(nc.sbuf_tensor(pfx + "p4xld", [128, 512], F32))
        idn_f = ec(nc.sbuf_tensor(pfx + "p4idn", [128, 128], F32))
        pr.dma("gpsimd", idn_f[:, :], idn_in[:, :])
        for k in range(8):
            for s0 in range(0, T, 512):
                pr.dma("sync", xld[:, :], xin_t[k * 128:(k + 1) * 128, s0:s0 + 512])
                pr.op("vector",
                      lambda v, k=k, s0=s0: v.tensor_copy(
                          xin_bf[:, k * T + s0:k * T + s0 + 512], xld[:, :]),
                      [xld[:, :]], [xin_bf[:, k * T + s0:k * T + s0 + 512]])

        crT = ec(nc.sbuf_tensor(pfx + "p4crT", [8, T], F32))
        with ExitStack() as esr:
            ecr = esr.enter_context
            rwt = ecr(nc.sbuf_tensor(pfx + "p4rw", [128, 8], BF16))
            ebt = ecr(nc.sbuf_tensor(pfx + "p4eb", [128, 1], F32))
            lgT = ecr(nc.sbuf_tensor(pfx + "p4lgT", [8, T], F32))
            lgm = ecr(nc.sbuf_tensor(pfx + "p4lgm", [128, 16 * 8], F32))
            lg2 = ecr(nc.sbuf_tensor(pfx + "p4lg2", [128, 8], F32))
            is1 = ecr(nc.sbuf_tensor(pfx + "p4is1", [128, 8], F32))
            is2 = ecr(nc.sbuf_tensor(pfx + "p4is2", [128, 8], F32))
            mcol = ecr(nc.sbuf_tensor(pfx + "p4m", [128, 4], F32))
            rwk = ecr(nc.sbuf_tensor(pfx + "p4rwk", [128, 8 * 8], F32))
            xf32 = [ecr(nc.sbuf_tensor(pfx + f"p4xf{i}", [128, 512], F32)) for i in range(2)]
            for k in range(8):
                pr.dma("gpsimd", rwk[:, k * 8:(k + 1) * 8], rw_t[k * 128:(k + 1) * 128, :])
            pr.dma("sync", ebt[:, :], ebc[:, :])
            for s0 in range(0, T, 512):
                ps = em.take_psum()
                for k in range(8):
                    xt_ = xf32[k % 2]
                    pr.dma("sync", xt_[:, :], xin_t[k * 128:(k + 1) * 128, s0:s0 + 512])
                    pr.op("tensor",
                          lambda t, ps=ps, k=k, xt_=xt_: t.matmul(
                              ps[0:8, 0:512], rwk[:, k * 8:(k + 1) * 8], xt_[:, :],
                              start=(k == 0), stop=(k == 7)),
                          [rwk[:, k * 8:(k + 1) * 8], xt_[:, :]],
                          [ps[0:8, 0:512]],
                          check_writes=[ps[0:8, 0:512]] if k == 0 else [])
                pr.op("vector",
                      lambda v, ps=ps, s0=s0: v.tensor_scalar(
                          lgT[:, s0:s0 + 512], ps[0:8, 0:512], ebt[0:8, 0:1], None, ALU.add),
                      [ps[0:8, 0:512], ebt[0:8, 0:1]], [lgT[:, s0:s0 + 512]])
            for tt in range(T // 128):
                ps = em.take_psum()
                pr.op("tensor",
                      lambda t, tt=tt, ps=ps: t.transpose(
                          ps[:, 0:8], lgT[0:8, tt * 128:(tt + 1) * 128], idn_f[0:8, 0:8]),
                      [lgT[0:8, tt * 128:(tt + 1) * 128], idn_f[0:8, 0:8]], [ps[:, 0:8]])
                lgs = lgm[:, tt * 8:(tt + 1) * 8]
                pr.op("vector", lambda v, ps=ps, lgs=lgs: v.tensor_copy(lgs, ps[:, 0:8]),
                      [ps[:, 0:8]], [lgs])
                pr.op("vector", lambda v, lgs=lgs: v.reduce_max(mcol[:, 0:1], lgs, axis=AX.X),
                      [lgs], [mcol[:, 0:1]])
                pr.op("vector", lambda v, lgs=lgs: v.tensor_scalar(is1[:, :], lgs, mcol[:, 0:1], None, ALU.is_ge),
                      [lgs, mcol[:, 0:1]], [is1[:, :]])
                pr.op("vector", lambda v, lgs=lgs: v.scalar_tensor_tensor(
                          lg2[:, :], is1[:, :], -1e38, lgs, ALU.mult, ALU.add),
                      [is1[:, :], lgs], [lg2[:, :]])
                pr.op("vector", lambda v: v.reduce_max(mcol[:, 1:2], lg2[:, :], axis=AX.X),
                      [lg2[:, :]], [mcol[:, 1:2]])
                pr.op("vector", lambda v: v.tensor_scalar(is2[:, :], lg2[:, :], mcol[:, 1:2], None, ALU.is_ge),
                      [lg2[:, :], mcol[:, 1:2]], [is2[:, :]])
                pr.op("vector", lambda v: v.tensor_sub(mcol[:, 2:3], mcol[:, 0:1], mcol[:, 1:2]),
                      [mcol[:, 0:1], mcol[:, 1:2]], [mcol[:, 2:3]])
                pr.op("scalar", lambda s: s.activation(mcol[:, 2:3], mcol[:, 2:3], AF.Sigmoid),
                      [mcol[:, 2:3]], [mcol[:, 2:3]])
                pr.op("vector", lambda v: v.tensor_scalar(mcol[:, 3:4], mcol[:, 2:3], -1.0, 1.0, ALU.mult, ALU.add),
                      [mcol[:, 2:3]], [mcol[:, 3:4]])
                pr.op("vector", lambda v: v.tensor_scalar(is1[:, :], is1[:, :], mcol[:, 2:3], None, ALU.mult),
                      [is1[:, :], mcol[:, 2:3]], [is1[:, :]])
                pr.op("vector", lambda v: v.scalar_tensor_tensor(
                          is1[:, :], is2[:, :], mcol[:, 3:4], is1[:, :], ALU.mult, ALU.add),
                      [is2[:, :], mcol[:, 3:4], is1[:, :]], [is1[:, :]])
                ps2 = em.take_psum()
                pr.op("tensor",
                      lambda t, ps2=ps2: t.transpose(ps2[0:8, 0:128], is1[:, 0:8], idn_f[:, :]),
                      [is1[:, 0:8], idn_f[:, :]], [ps2[0:8, 0:128]])
                pr.op("vector",
                      lambda v, tt=tt, ps2=ps2: v.tensor_copy(crT[:, tt * 128:(tt + 1) * 128], ps2[0:8, 0:128]),
                      [ps2[0:8, 0:128]], [crT[:, tt * 128:(tt + 1) * 128]])
            for e in ENGINES:
                pr.barrier_to(e)

        ese = es.enter_context(ExitStack())
        ece = ese.enter_context
        sel8t = ece(nc.sbuf_tensor(pfx + "p4sel8", [8, 8 * 128], F32))
        cbe = ece(nc.sbuf_tensor(pfx + "p4cbe", [128, T], F32))
        hscr = ece(nc.sbuf_tensor(pfx + "p4hscr", [128, 512], F32))
        h3scr = ece(nc.sbuf_tensor(pfx + "p4h3", [128, 512], F32))
        h_bf = ece(nc.sbuf_tensor(pfx + "p4h", [128, 16 * T], BF16))
        wpool = [ece(nc.sbuf_tensor(pfx + f"p4w{i}", [128, 8 * 128], BF16)) for i in range(4)]
        w2pool = [ece(nc.sbuf_tensor(pfx + f"p4w2_{i}", [128, 16 * 128], BF16)) for i in range(2)]
        pr.dma("sync", sel8t[:, :], sel8[:, :])
        for e in range(NEXP):
            routed = e < 8
            if routed:
                for s0 in range(0, T, 512):
                    ps = em.take_psum()
                    pr.op("tensor",
                          lambda t, ps=ps, e=e, s0=s0: t.matmul(
                              ps[:, 0:512], sel8t[:, e * 128:(e + 1) * 128],
                              crT[0:8, s0:s0 + 512], start=True, stop=True),
                          [sel8t[:, e * 128:(e + 1) * 128], crT[0:8, s0:s0 + 512]],
                          [ps[:, 0:512]])
                    pr.op("vector",
                          lambda v, ps=ps, s0=s0: v.tensor_copy(cbe[:, s0:s0 + 512], ps[:, 0:512]),
                          [ps[:, 0:512]], [cbe[:, s0:s0 + 512]])
            for j in range(16):
                w1t = wpool[(2 * j) % 4]
                w3t = wpool[(2 * j + 1) % 4]
                for k in range(8):
                    pr.dma("gpsimd", w1t[:, k * 128:(k + 1) * 128],
                           m1_t[e * 1024 + k * 128: e * 1024 + (k + 1) * 128, j * 128:(j + 1) * 128])
                    pr.dma("gpsimd", w3t[:, k * 128:(k + 1) * 128],
                           m3_t[e * 1024 + k * 128: e * 1024 + (k + 1) * 128, j * 128:(j + 1) * 128])
                for s0 in range(0, T, 512):
                    ps1 = em.take_psum()
                    em.mm_group(
                        [(w1t[:, k * 128:(k + 1) * 128], xin_bf[:, k * T + s0:k * T + s0 + 512])
                         for k in range(8)], ps1[:, 0:512])
                    ps3 = em.take_psum()
                    em.mm_group(
                        [(w3t[:, k * 128:(k + 1) * 128], xin_bf[:, k * T + s0:k * T + s0 + 512])
                         for k in range(8)], ps3[:, 0:512])
                    pr.op("scalar", lambda s, ps1=ps1: s.activation(hscr[:, 0:512], ps1[:, 0:512], AF.Silu),
                          [ps1[:, 0:512]], [hscr[:, 0:512]])
                    pr.op("vector", lambda v, ps3=ps3: v.tensor_mul(h3scr[:, 0:512], hscr[:, 0:512], ps3[:, 0:512]),
                          [hscr[:, 0:512], ps3[:, 0:512]], [h3scr[:, 0:512]])
                    if routed:
                        pr.op("vector",
                              lambda v, j=j, s0=s0: v.tensor_mul(
                                  h_bf[:, j * T + s0:j * T + s0 + 512], h3scr[:, 0:512], cbe[:, s0:s0 + 512]),
                              [h3scr[:, 0:512], cbe[:, s0:s0 + 512]],
                              [h_bf[:, j * T + s0:j * T + s0 + 512]])
                    else:
                        pr.op("vector",
                              lambda v, j=j, s0=s0: v.tensor_copy(
                                  h_bf[:, j * T + s0:j * T + s0 + 512], h3scr[:, 0:512]),
                              [h3scr[:, 0:512]], [h_bf[:, j * T + s0:j * T + s0 + 512]])
            for of in range(8):
                w2t = w2pool[of % 2]
                for k in range(16):
                    pr.dma("gpsimd", w2t[:, k * 128:(k + 1) * 128],
                           m2_t[e * 2048 + k * 128: e * 2048 + (k + 1) * 128, of * 128:(of + 1) * 128])
                for s0 in range(0, T, 512):
                    ps = em.take_psum()
                    em.mm_group(
                        [(w2t[:, k * 128:(k + 1) * 128], h_bf[:, k * T + s0:k * T + s0 + 512])
                         for k in range(16)], ps[:, 0:512])
                    if e == 0:
                        pr.op("vector",
                              lambda v, of=of, s0=s0, ps=ps: v.tensor_copy(
                                  acc[:, of * T + s0: of * T + s0 + 512], ps[:, 0:512]),
                              [ps[:, 0:512]], [acc[:, of * T + s0: of * T + s0 + 512]])
                    else:
                        pr.op("vector",
                              lambda v, of=of, s0=s0, ps=ps: v.tensor_add(
                                  acc[:, of * T + s0: of * T + s0 + 512],
                                  acc[:, of * T + s0: of * T + s0 + 512], ps[:, 0:512]),
                              [ps[:, 0:512], acc[:, of * T + s0: of * T + s0 + 512]],
                              [acc[:, of * T + s0: of * T + s0 + 512]])

        for e in ENGINES:
            pr.barrier_to(e)
        ese.close()

        sq4 = ec(nc.sbuf_tensor(pfx + "p4sq", [128, 2 * 512], F32))
        st4 = ec(nc.sbuf_tensor(pfx + "p4st", [1, 4 * 512], F32))
        mu4 = ec(nc.sbuf_tensor(pfx + "p4mu", [128, 512], F32))
        rst4 = ec(nc.sbuf_tensor(pfx + "p4rst", [128, 512], F32))
        lnb4 = ec(nc.sbuf_tensor(pfx + "p4lnb", [128, 8 * 512], F32))
        for s0 in range(0, T, 512):
            for of in range(8):
                pr.dma("scalar", xld[:, :], xin_t[of * 128:(of + 1) * 128, s0:s0 + 512])
                pr.op("vector",
                      lambda v, of=of, s0=s0: v.tensor_add(
                          lnb4[:, of * 512:(of + 1) * 512],
                          acc[:, of * T + s0: of * T + s0 + 512], xld[:, :]),
                      [acc[:, of * T + s0: of * T + s0 + 512], xld[:, :]],
                      [lnb4[:, of * 512:(of + 1) * 512]])
            emit_ln_feature(pr, em, lnb4, 512, st4, sq4, mu4, rst4, nfeat=8)
            for of in range(8):
                sl = lnb4[:, of * 512:(of + 1) * 512]
                pr.op("vector", lambda v, sl=sl: v.tensor_sub(sl, sl, mu4[:, 0:512]),
                      [sl, mu4[:, 0:512]], [sl])
                pr.op("vector", lambda v, sl=sl: v.tensor_mul(sl, sl, rst4[:, 0:512]),
                      [sl, rst4[:, 0:512]], [sl])
                pr.dma("gpsimd", o_out[of * 128:(of + 1) * 128, s0:s0 + 512], sl)
        for e in ENGINES:
            pr.barrier_to(e)
        print(f"phase4 program: {pr.n_ops} ops, {pr.n_waits} waits")
    return nc


MAGIC0 = 0x5A100000
MAGIC1 = 0x3C700000


def build_fused():
    nc = bass.Bass("TRN2", target_bir_lowering=False, debug=False)
    TH = T + 3
    KB = T + NXK

    def inp(name, shp):
        return nc.dram_tensor(name, shp, F32, kind="ExternalInput").ap()

    x_t = inp("x_t", [DM, TH])
    W_in = inp("W_in", [DM, 4096]); W_x = inp("W_x", [2048, 96])
    W_dt = inp("W_dt", [64, 2048]); W_out = inp("W_out", [2048, DM])
    conv_w = inp("conv_w", [128, 64]); conv_b = inp("conv_b", [128, 16])
    b_dt = inp("b_dt", [128, 16]); Dv = inp("Dv", [128, 16])
    selcat = inp("selcat", [32, 32 * 128])
    zeros256 = inp("zeros256", [128, 256]); zeros16 = inp("zeros16", [128, 16])
    selcol = inp("selcol", [128, 2])
    hsel = inp("hsel", [128, 1])
    pairinfo = nc.dram_tensor("pairinfo", [1, 4], I32, kind="ExternalInput").ap()
    xidx = nc.dram_tensor("xidx", [128, 20], I32, kind="ExternalInput").ap()
    fidx = nc.dram_tensor("fidx", [1, 2], I32, kind="ExternalInput").ap()
    p2w = {}
    for lay in (0, 1):
        sfx = f"_l{lay}"
        p2w[lay] = dict(
            wqkv=inp("wqkv" + sfx, [DM, 1536]), wo=inp("wo" + sfx, [512, DM]),
            w1p=inp("w1p" + sfx, [DM, 2 * HIDP]), w2p=inp("w2p" + sfx, [HIDP, DM]),
            b1p=inp("b1p" + sfx, [128, 2 * (HIDP // 128)]), b2c=inp("b2c" + sfx, [128, 8]),
            lgc=inp("lgc" + sfx, [128, 8]), lbc=inp("lbc" + sfx, [128, 8]),
            qs8=inp("qs8" + sfx, [128, 4]), ksc=inp("ksc" + sfx, [128, 4]))
    selheadT = inp("selheadT", [2, 128]); selhead = inp("selhead", [128, 2])
    csq = inp("csq", [DM // 8, T + QW]) if False else inp("csq", [128, T + QW])
    ssq = inp("ssq", [128, T + QW])
    csk = inp("csk", [128, T + 2 * QW]); ssk = inp("ssk", [128, T + 2 * QW])
    l0mask = inp("l0mask", [512, 1024]); l0mask_w1 = inp("l0mask_w1", [512, 1024])
    idn_in = inp("idn128", [128, 128])
    wq_t = inp("wq_t", [DM, DM]); wk_t = inp("wk_t", [DM, DM])
    wv_t = inp("wv_t", [DM, DM]); wo_t = inp("wo_t", [DM, DM])
    bqc = inp("bqc", [128, 8]); bkc = inp("bkc", [128, 8])
    bvc = inp("bvc", [128, 8]); boc = inp("boc", [128, 8])
    cs_t = inp("cs_t", [DM, T]); ssg_t = inp("ssg_t", [DM, T])
    csx_t = inp("csx_t", [DM, NXK]); ssgx_t = inp("ssgx_t", [DM, NXK])
    tamul = inp("tamul", [16 * 256, 256]); taadd = inp("taadd", [16 * 256, 256])
    tbmul = inp("tbmul", [16 * 16, KB]); tbadd = inp("tbadd", [16 * 16, KB])
    m1_t = inp("m1_t", [NEXP * 1024, 2048]); m3_t = inp("m3_t", [NEXP * 1024, 2048])
    m2_t = inp("m2_t", [NEXP * 2048, 1024])
    rw_t = inp("rw_t", [DM, 8]); ebc = inp("ebc", [128, 1]); sel8 = inp("sel8", [8, 8 * 128])

    out_t = nc.dram_tensor("out_t", [DM, T], F32, kind="ExternalOutput").ap()

    def internal(name, shp, dt=F32):
        return nc.dram_tensor(name, shp, dt).ap()

    x1sp = internal("x1sp", [DM, HALO2 + T])
    x2asp = internal("x2asp", [DM, HALO2 + T])
    x2sp = internal("x2sp", [DM, T])
    x3sp = internal("x3sp", [DM, T])
    x2xsp = internal("x2xsp", [DM, NXK])
    junksp = internal("junksp", [DM, T])
    hfin_i = internal("hfin_i", [128, 256])
    sumd_i = internal("sumd_i", [128, 16])
    carry_i = internal("carry_i", [128, 256])
    sext_i = internal("sext_i", [128, 16])
    sp_xi = internal("sp_xi", [2048, T]); sp_delta = internal("sp_delta", [2048, T])
    sp_res = internal("sp_res", [2048, T]); sp_C = internal("sp_C", [2048, T])
    sp_bmcm = internal("sp_bmcm", [32, T])
    sp_x2m0 = internal("sp_x2m0", [DM, T + 1]); sp_x2m1 = internal("sp_x2m1", [DM, T + 1])

    def shared_t(name, shp, dt=F32):
        return nc.dram_tensor(name, shp, dt, addr_space="Shared").ap()

    xg_sd = shared_t("xg_sd", [256, 16])
    xg_hf = shared_t("xg_hf", [256, 256])
    xg_h1 = shared_t("xg_h1", [2 * 8 * 128, HALO2])
    xg_h2 = shared_t("xg_h2", [2 * 8 * 128, HALO2])
    xg_xx = shared_t("xg_xx", [2 * 8 * 128, NXK])
    shflag = shared_t("shflag", [256, 16], I32)

    with ExitStack() as es:
        ec = es.enter_context
        blk = ec(nc.Block())
        pr = Prog(nc, es)
        em = Emitter(nc, es, pr, n_psum=7)
        psbf = ec(nc.psum_tensor("psbf", [128, 1024], BF16))
        g = nc.gpsimd

        myfl = ec(nc.sbuf_tensor("myfl", [128, 16], I32))
        flrd = ec(nc.sbuf_tensor("flrd", [128, 16], I32))
        hselt = ec(nc.sbuf_tensor("hselt", [128, 1], F32))
        himp = ec(nc.sbuf_tensor("himp", [128, HALO2], F32))
        xixt = ec(nc.sbuf_tensor("xixt", [128, 20], I32))
        psem = ec(nc.semaphore("pollsem"))
        r_rc = ec(g.register("r_rc"))
        r_tmp = ec(g.register("r_tmp"))
        r_cnt = ec(g.register("r_cnt"))
        r_v = ec(g.register("r_v"))
        pr.dma("gpsimd", xixt[:, :], xidx[:, :])
        pr.dma("sync", hselt[:, :], hsel[:, :])
        pr.barrier_to("gpsimd")
        g.reg_mov(r_cnt, 0)
        bar_state = {"k": 0}
        IOA = bass.IndirectOffsetOnAxis if hasattr(bass, "IndirectOffsetOnAxis") else None
        if IOA is None:
            from concourse.bass_core_rust import IndirectOffsetOnAxis as IOA

        def pair_barrier():
            bar_state["k"] += 1
            k = bar_state["k"]
            pr.barrier_to("gpsimd")
            pr.op("gpsimd", lambda gg: gg.memset(myfl[:, 0:8], MAGIC0 + k), [], [myfl[:, 0:8]])
            pr.op("gpsimd", lambda gg: gg.memset(myfl[:, 8:16], MAGIC1 + k), [], [myfl[:, 8:16]])
            pr.idma("gpsimd", shflag[:, :], IOA(xixt[:, 16:17], 0), myfl[:, :], None,
                    extra_reads=[xixt[:, 16:17]])
            g.reg_mov(r_rc, 1)
            with g.While(r_rc):
                g.indirect_dma_start(flrd[:, :], None, shflag[:, :],
                                     IOA(xixt[:, 17:18], 0)).then_inc(psem, 16)
                g.reg_add(r_cnt, r_cnt, 16)
                g.wait_ge(psem, r_cnt)
                g.reg_load(r_v, flrd[0:1, 0:1])
                g.reg_alu(r_rc, r_v, MAGIC0 + k, ALU.not_equal)
                g.reg_load(r_v, flrd[0:1, 8:9])
                g.reg_alu(r_tmp, r_v, MAGIC1 + k, ALU.not_equal)
                g.reg_alu(r_rc, r_rc, r_tmp, ALU.bitwise_or)
            pr._record([], [flrd[0:1, :]], "gpsimd", pr.count["gpsimd"])

        def export_rows(dst_shared, src_ap_fn, width):
            for of in range(8):
                pr.dma("gpsimd", himp[:, 0:width], src_ap_fn(of))
                pr.idma("gpsimd", dst_shared[:, :], IOA(xixt[:, of:of + 1], 0),
                        himp[:, 0:width], None, extra_reads=[xixt[:, of:of + 1]])

        def import_rows(src_shared, dst_dram, width, col0):
            for of in range(8):
                pr.idma("gpsimd", himp[:, 0:width], None,
                        src_shared[:, :], IOA(xixt[:, 8 + of:9 + of], 0),
                        extra_reads=[xixt[:, 8 + of:9 + of]])
                pr.op("vector",
                      lambda v, w=width: v.tensor_scalar(himp[:, 0:w], himp[:, 0:w],
                                                         hselt[:, 0:1], None, ALU.mult),
                      [himp[:, 0:width], hselt[:, 0:1]], [himp[:, 0:width]])
                pr.dma("gpsimd", dst_dram[of * 128:(of + 1) * 128, col0:col0 + width],
                       himp[:, 0:width])

        emit_phase1(nc, pr, em, x_t, W_in, W_x, W_dt, W_out, conv_w, conv_b, b_dt,
                    Dv, zeros256, zeros16, selcat, junksp, hfin_i, sumd_i,
                    sp_xi, sp_delta, sp_res, sp_C, sp_bmcm,
                    do_passA=True, do_y=False, pfx="p1a")
        pr.dma("gpsimd", himp[:, 0:16], sumd_i[:, :])
        pr.idma("gpsimd", xg_sd[:, :], IOA(xixt[:, 16:17], 0), himp[:, 0:16], None,
                extra_reads=[xixt[:, 16:17]])
        pr.dma("gpsimd", himp[:, 16:272], hfin_i[:, :])
        pr.idma("gpsimd", xg_hf[:, :], IOA(xixt[:, 16:17], 0), himp[:, 16:272], None,
                extra_reads=[xixt[:, 16:17]])
        pair_barrier()
        pr.idma("gpsimd", himp[:, 0:16], None, xg_sd[:, :], IOA(xixt[:, 17:18], 0),
                extra_reads=[xixt[:, 17:18]])
        pr.dma("gpsimd", sext_i[:, :], himp[:, 0:16])
        pr.idma("gpsimd", himp[:, 16:272], None, xg_hf[:, :], IOA(xixt[:, 17:18], 0),
                extra_reads=[xixt[:, 17:18]])
        pr.dma("gpsimd", carry_i[:, :], himp[:, 16:272])
        emit_phase1(nc, pr, em, x_t, W_in, W_x, W_dt, W_out, conv_w, conv_b, b_dt,
                    Dv, carry_i, sext_i, selcat, x1sp[:, HALO2:], hfin_i, sumd_i,
                    sp_xi, sp_delta, sp_res, sp_C, sp_bmcm,
                    do_passA=False, do_y=True, carry_sel=selcol, pfx="p1b")
        export_rows(xg_h1, lambda of: x1sp[of * 128:(of + 1) * 128, T:T + HALO2], HALO2)
        pair_barrier()
        import_rows(xg_h1, x1sp, HALO2, 0)

        emit_phase2(nc, pr, em, psbf, x1sp, x1sp[:, HALO2:], p2w[0]["wqkv"], p2w[0]["wo"],
                    p2w[0]["w1p"], p2w[0]["w2p"], p2w[0]["b1p"], p2w[0]["b2c"],
                    p2w[0]["lgc"], p2w[0]["lbc"], p2w[0]["qs8"], p2w[0]["ksc"],
                    selheadT, selhead, csq, ssq, csk, ssk, l0mask, l0mask_w1,
                    idn_in, x2asp[:, HALO2:], junksp, sp_x2m0, pfx="A")
        export_rows(xg_h2, lambda of: x2asp[of * 128:(of + 1) * 128, T:T + HALO2], HALO2)
        pair_barrier()
        import_rows(xg_h2, x2asp, HALO2, 0)
        emit_phase2(nc, pr, em, psbf, x2asp, x1sp[:, HALO2:], p2w[1]["wqkv"], p2w[1]["wo"],
                    p2w[1]["w1p"], p2w[1]["w2p"], p2w[1]["b1p"], p2w[1]["b2c"],
                    p2w[1]["lgc"], p2w[1]["lbc"], p2w[1]["qs8"], p2w[1]["ksc"],
                    selheadT, selhead, csq, ssq, csk, ssk, l0mask, l0mask_w1,
                    idn_in, junksp, x2sp, sp_x2m1, pfx="B")
        for of in range(8):
            pr.dma("gpsimd", himp[:, 0:16], x2sp[of * 128:(of + 1) * 128, 0:16])
            pr.dma("gpsimd", himp[:, 16:32], x2sp[of * 128:(of + 1) * 128, 1024:1040])
            pr.idma("gpsimd", xg_xx[:, :], IOA(xixt[:, of:of + 1], 0), himp[:, 0:32], None,
                    extra_reads=[xixt[:, of:of + 1]])
        pair_barrier()
        import_rows(xg_xx, x2xsp, NXK, 0)

        emit_phase3(nc, pr, em, psbf, x2sp, x2xsp, wq_t, wk_t, wv_t, wo_t,
                    bqc, bkc, bvc, boc, cs_t, ssg_t, csx_t, ssgx_t,
                    tamul, taadd, tbmul, tbadd, idn_in, x3sp, pfx="C")
        emit_phase4(nc, pr, em, psbf, x3sp, m1_t, m3_t, m2_t, rw_t, ebc, sel8,
                    idn_in, out_t, pfx="D")
        for e in ENGINES:
            pr.barrier_to(e)
        print(f"fused program: {pr.n_ops} ops, {pr.n_waits} waits")
    return nc


def prep_fused(inputs):
    sh1, pc1 = prep_phase1(inputs)
    sh3, pc3 = prep_phase3(inputs)
    sh4 = prep_phase4(inputs)
    idn = np.eye(128, dtype=np.float32)
    maps = []
    p2l = [prep_phase2(inputs, 0), prep_phase2(inputs, 1)]
    for c in range(NCORES):
        b, h = c // 2, c % 2
        m = {}
        m.update(sh1); m.update(pc1[c])
        m.pop("carry", None); m.pop("sext", None)
        m["zeros256"] = np.zeros((128, 256), np.float32)
        m["zeros16"] = np.zeros((128, 16), np.float32)
        sc = np.zeros((128, 2), np.float32)
        sc[:, 0] = float(h)
        sc[:, 1] = float(1 - h)
        m["selcol"] = sc
        m["hsel"] = np.full((128, 1), float(h), np.float32)
        m["pairinfo"] = np.array([[h, 1 - h, 0, 0]], np.int32)
        xidx = np.zeros((128, 20), np.int32)
        pp = np.arange(128)
        for of in range(8):
            xidx[:, of] = h * 1024 + of * 128 + pp
            xidx[:, 8 + of] = (1 - h) * 1024 + of * 128 + pp
        xidx[:, 16] = h * 128 + pp
        xidx[:, 17] = (1 - h) * 128 + pp
        m["xidx"] = xidx
        m["fidx"] = np.array([[h, 1 - h]], np.int32)
        for lay in (0, 1):
            sh2, pc2 = p2l[lay]
            for k in ("wqkv", "wo", "w1p", "w2p", "b1p", "b2c", "lgc", "lbc", "qs8", "ksc"):
                m[k + f"_l{lay}"] = sh2[k]
        sh2, pc2 = p2l[0]
        for k in ("selheadT", "selhead", "l0mask"):
            m[k] = sh2[k]
        m.update({k: v for k, v in pc2[c].items()})
        m["idn128"] = idn
        m.update(sh3); m.update(pc3[c])
        m.update(sh4)
        maps.append(m)
    return maps


def kernel_fused(**inputs):
    nc = build_fused()
    maps = prep_fused(inputs)
    r = run_bass_kernel_spmd(nc, maps, core_ids=list(range(NCORES)))
    out = np.zeros((4, 4096, 1024), np.float32)
    for c in range(NCORES):
        b, h = c // 2, c % 2
        out[b, h * T:(h + 1) * T] = np.asarray(r.results[c]["out_t"]).T
    return out



def build_phase4():
    nc = bass.Bass("TRN2", target_bir_lowering=False, debug=False)
    xin_t = nc.dram_tensor("xin_t", [DM, T], F32, kind="ExternalInput").ap()
    m1_t = nc.dram_tensor("m1_t", [NEXP * 1024, 2048], F32, kind="ExternalInput").ap()
    m3_t = nc.dram_tensor("m3_t", [NEXP * 1024, 2048], F32, kind="ExternalInput").ap()
    m2_t = nc.dram_tensor("m2_t", [NEXP * 2048, 1024], F32, kind="ExternalInput").ap()
    rw_t = nc.dram_tensor("rw_t", [DM, 8], F32, kind="ExternalInput").ap()
    ebc = nc.dram_tensor("ebc", [128, 1], F32, kind="ExternalInput").ap()
    sel8 = nc.dram_tensor("sel8", [8, 8 * 128], F32, kind="ExternalInput").ap()
    idn_in = nc.dram_tensor("idn128", [128, 128], F32, kind="ExternalInput").ap()
    o_out = nc.dram_tensor("out_t", [DM, T], F32, kind="ExternalOutput").ap()

    with ExitStack() as _es0:
        _ec0 = _es0.enter_context
        blk = _ec0(nc.Block())
        pr = Prog(nc, _es0)
        em = Emitter(nc, _es0, pr, n_psum=7)
        psbf = _ec0(nc.psum_tensor("psbf", [128, 1024], BF16))
        emit_phase4(nc, pr, em, psbf, xin_t, m1_t, m3_t, m2_t, rw_t, ebc, sel8, idn_in, o_out)
    return nc


def kernel(**inputs):
    return kernel_fused(**inputs)


def kernel_unfused(**inputs):
    idn = np.eye(128, dtype=np.float32)
    x_full = _np_f32(inputs["L0_attn_x"])

    sh1, pc1 = prep_phase1(inputs)
    nc1 = build_phase1()
    zc = np.zeros((128, 256), np.float32)
    zs = np.zeros((128, 16), np.float32)
    maps = []
    for c in range(NCORES):
        m = dict(sh1); m.update(pc1[c]); m["carry"] = zc; m["sext"] = zs
        maps.append(m)
    r1 = run_bass_kernel_spmd(nc1, maps, core_ids=list(range(NCORES)))
    maps2 = []
    for c in range(NCORES):
        m = dict(maps[c])
        if c % 2 == 1:
            m["carry"] = np.asarray(r1.results[c - 1]["hfin"])
        else:
            m["sext"] = np.asarray(r1.results[c + 1]["sumdelta"])
        maps2.append(m)
    r2 = run_bass_kernel_spmd(nc1, maps2, core_ids=list(range(NCORES)))
    x1 = np.zeros((4, 4096, 1024), np.float32)
    for c in range(NCORES):
        b, h = c // 2, c % 2
        x1[b, h * T:(h + 1) * T] = np.asarray(r2.results[c]["x1_t"]).T

    nc2 = build_phase2()

    def run_p2(lay, xin_full, xres_full):
        sh2, pc2 = prep_phase2(inputs, lay)
        mm = []
        for c in range(NCORES):
            m = dict(sh2); m.update(pc2[c])
            m["xin_t"] = slice_stream(xin_full, c, HALO2)
            m["xres_t"] = slice_stream(xres_full, c, 0)
            m["idn128"] = idn
            mm.append(m)
        r = run_bass_kernel_spmd(nc2, mm, core_ids=list(range(NCORES)))
        xo = np.zeros((4, 4096, 1024), np.float32)
        xl = np.zeros((4, 4096, 1024), np.float32)
        for c in range(NCORES):
            b, h = c // 2, c % 2
            xo[b, h * T:(h + 1) * T] = np.asarray(r.results[c]["xout_t"]).T
            xl[b, h * T:(h + 1) * T] = np.asarray(r.results[c]["xln_t"]).T
        return xo, xl

    x2a, _ = run_p2(0, x1, x1)
    _, x2 = run_p2(1, x2a, x1)

    nc3 = build_phase3()
    sh3, pc3 = prep_phase3(inputs)
    rows = np.concatenate([np.arange(16), 1024 + np.arange(16)])
    mm = []
    for c in range(NCORES):
        b, h = c // 2, c % 2
        m = dict(sh3); m.update(pc3[c])
        m["xin_t"] = slice_stream(x2, c, 0)
        m["x2x_t"] = (np.ascontiguousarray(x2[b, rows].T) if h == 1
                      else np.zeros((DM, NXK), np.float32))
        m["idn128"] = idn
        mm.append(m)
    r = run_bass_kernel_spmd(nc3, mm, core_ids=list(range(NCORES)))
    x3 = np.zeros((4, 4096, 1024), np.float32)
    for c in range(NCORES):
        b, h = c // 2, c % 2
        x3[b, h * T:(h + 1) * T] = np.asarray(r.results[c]["x3_t"]).T

    nc4 = build_phase4()
    sh4 = prep_phase4(inputs)
    mm = []
    for c in range(NCORES):
        m = dict(sh4)
        m["xin_t"] = slice_stream(x3, c, 0)
        m["idn128"] = idn
        mm.append(m)
    r = run_bass_kernel_spmd(nc4, mm, core_ids=list(range(NCORES)))
    out = np.zeros((4, 4096, 1024), np.float32)
    for c in range(NCORES):
        b, h = c // 2, c % 2
        out[b, h * T:(h + 1) * T] = np.asarray(r.results[c]["out_t"]).T
    return out
